# version 06
# baseline (speedup 1.0000x reference)
import numpy as np
import ml_dtypes
import concourse.bass as bass
import concourse.mybir as mybir
from concourse.bass_utils import run_bass_kernel_spmd
from contextlib import ExitStack

F32 = mybir.dt.float32
BF16 = mybir.dt.bfloat16
AF = mybir.ActivationFunctionType
ALU = mybir.AluOpType

H = 16
D = 128
DM = 2048
KC = 16
TCORE = 1024
TILE = 512
NT = 2
CH = 256
NCH = 4
FF = 8192
NJ = 64
NCORES = 8
EPS_LN = 1e-5
EPS_RMS = float(np.finfo(np.float32).eps)

DEBUG = 0
UPTO = "C"


class Sem:
    def __init__(self, nc, stack, name):
        self.h = stack.enter_context(nc.semaphore(name))
        self.v = 0

    def inc(self, instr, n=1):
        instr.then_inc(self.h, n)
        self.v += n
        return self.v


class Builder:
    def __init__(self, debug=0, upto="C"):
        self.debug = debug
        self.upto = upto
        nc = self.nc = bass.Bass("TRN2", target_bir_lowering=False)

        dram = lambda nm, shp, kind="ExternalInput", dt=F32: \
            nc.dram_tensor(nm, shp, dt, kind=kind).ap()
        self.x_d = dram("x", [DM, TCORE])
        self.wq_d = dram("wq", [KC * DM, D])
        self.wk_d = dram("wk", [KC * DM, D])
        self.wv_d = dram("wv", [DM, DM])
        self.wg_d = dram("wg", [KC * DM, D])
        self.wo_d = dram("wo", [KC * DM, D])
        self.w0_d = dram("w0", [NJ * DM, D])
        self.w1_d = dram("w1", [FF * KC, D])
        self.slope_d = dram("slope", [128, H])
        self.arr_d = dram("arrb", [128, CH])
        self.arrm_d = dram("arrm", [128, 2])
        self.idxT_d = dram("idxT", [CH, CH])
        self.mask_d = dram("maskT", [CH, CH])
        self.ident_d = dram("ident", [128, 128])
        self.ones_d = dram("onesr", [1, 128])
        self.onesc_d = dram("onesc", [128, 1])
        self.b0_d = dram("b0c", [128, NJ])
        self.b1_d = dram("b1c", [128, KC])
        self.nw_d = dram("nwc", [128, KC])
        self.sel_d = dram("sel", [128, NCORES * 4])
        self.out_d = dram("out", [DM, TCORE], kind="ExternalOutput")

        self.k_sc = dram("k_sc", [DM, TCORE], "Internal", BF16)
        self.v_sc = dram("v_sc", [TCORE, DM], "Internal", BF16)
        self.kvloc_sc = dram("kvloc_sc", [3 * H * D, D], "Internal", BF16)
        self.x2_sc = dram("x2_sc", [DM, TCORE], "Internal")
        self.x2ln_sc = dram("x2ln_sc", [DM, TCORE], "Internal", BF16)
        self.cc_in = dram("cc_in", [H * D, D], "Internal")
        self.cc_out = dram("cc_out", [NCORES * H * D, D], "Internal")

        self.dbg = {}
        if debug:
            for nm, shp in [("dbg_xln", [DM, TCORE]), ("dbg_T", [H * D, D]),
                            ("dbg_kvin", [H * D, D]), ("dbg_og", [DM, TCORE]),
                            ("dbg_x2", [DM, TCORE]), ("dbg_q", [DM, TILE]),
                            ("dbg_g", [DM, TILE]), ("dbg_ob", [DM, CH]),
                            ("dbg_rms", [1, CH])]:
                self.dbg[nm] = dram(nm, shp, kind="ExternalOutput")

        es = self.es = ExitStack()
        self.gL = Sem(nc, es, "gL")
        self.hL = Sem(nc, es, "hL")
        self.sT = Sem(nc, es, "sT")
        self.cc = Sem(nc, es, "ccs")
        self.pe = Sem(nc, es, "pes")
        self.dv = Sem(nc, es, "dvs")
        self.ac = Sem(nc, es, "acs")
        self.dvx = Sem(nc, es, "dvx")
        self.acx = Sem(nc, es, "acx")

        sbuf = self.sbuf
        self.id_bf = sbuf("id_bf", [128, 128], BF16)
        self.ones_bf = sbuf("ones_bf", [1, 128], BF16)
        self.ones_f = sbuf("ones_f", [1, 128], F32)
        self.onesc_bf = sbuf("onesc_bf", [128, 1], BF16)
        self.s_sb = sbuf("s_sb", [128, H], F32)
        self.negs_sb = sbuf("negs_sb", [128, H], F32)
        self.bd_sb = sbuf("bd_sb", [128, H], F32)
        self.bdp_sb = sbuf("bdp_sb", [128, 4 * H], F32)
        self.bd4p_sb = sbuf("bd4p_sb", [128, 4 * H], F32)
        self.u_sb = sbuf("u_sb", [128, NCORES * H], F32)
        self.ut_sb = sbuf("ut_sb", [128, 3 * H], F32)
        self.sel_sb = sbuf("sel_sb", [128, NCORES * 4], F32)
        self.arr_sb = sbuf("arr_sb", [128, CH], F32)
        self.arrm_sb = sbuf("arrm_sb", [128, 2], F32)
        self.kdec_sb = sbuf("kdec_sb", [128, 2 * H], F32)
        self.b0_sb = sbuf("b0_sb", [128, NJ], F32)
        self.b1_sb = sbuf("b1_sb", [128, KC], F32)
        self.nw_sb = sbuf("nw_sb", [128, KC], F32)
        self.row_sb = sbuf("row_sb", [1, 4 * TILE], F32)
        self.nmrow = self.row_sb[0:1, 0:TILE]
        self.rsrow = self.row_sb[0:1, TILE:2 * TILE]
        self.r_t1 = self.row_sb[0:1, 2 * TILE:3 * TILE]
        self.r_t2 = self.row_sb[0:1, 3 * TILE:4 * TILE]

        self.lab = ExitStack()
        self.qdec_sb = sbuf("qdec_sb", [128, H * CH], BF16, self.lab)
        self.diag_sb = sbuf("diag_sb", [128, H * 2 * CH], BF16, self.lab)
        self.xln_sb = sbuf("xln_sb", [128, KC * TCORE], BF16, self.lab)

        self.st_kvl = {}
        self.st_x2 = {}
        self.st_x2ln = {}

    def sbuf(self, nm, shp, dt, stk=None):
        return (stk or self.es).enter_context(self.nc.sbuf_tensor(nm, shp, dt))

    def psum(self, nm, shp, dt, stk=None):
        return (stk or self.es).enter_context(self.nc.psum_tensor(nm, shp, dt))

    def vstep(self, v, instr):
        self.dvx.inc(instr)
        v.wait_ge(self.dvx.h, self.dvx.v)
        return instr

    def sstep(self, sc, instr):
        self.acx.inc(instr)
        sc.wait_ge(self.acx.h, self.acx.v)
        return instr

    def xln(self, k, t0, tw):
        return self.xln_sb[:, k * TCORE + t0: k * TCORE + t0 + tw]

    @staticmethod
    def fmj(ap):
        return ap.rearrange("(k p) t -> p k t", p=128)

    @staticmethod
    def hdj(ap):
        return ap.rearrange("(h d) e -> d h e", d=128)

    def setup_block(self):
        nc = self.nc
        gL, hL, pe, dv, ac = self.gL, self.hL, self.pe, self.dv, self.ac
        ls = ExitStack()
        idx_sb = self.sbuf("idx_sb", [128, 2 * CH], F32, ls)
        mask_sb = self.sbuf("mask_sb", [128, 2 * CH], F32, ls)
        dscr = self.sbuf("dscr", [128, 2 * CH], F32, ls)
        d0, a0 = dv.v, ac.v
        diag_items = [(h, nh) for h in range(H) for nh in range(2)]
        diag_ac = {}

        with nc.Block("setup") as blk:
            @blk.sync
            def _(sy):
                for dst, src in [(self.s_sb, self.slope_d), (self.arr_sb, self.arr_d),
                                 (self.arrm_sb, self.arrm_d), (self.ones_f, self.ones_d),
                                 (self.b0_sb, self.b0_d), (self.b1_sb, self.b1_d),
                                 (self.nw_sb, self.nw_d), (self.sel_sb, self.sel_d)]:
                    hL.inc(sy.dma_start(out=dst[:, :], in_=src), 16)
                hL.inc(sy.dma_start(out=idx_sb[:, 0:CH], in_=self.idxT_d[0:128, :]), 16)
                hL.inc(sy.dma_start(out=idx_sb[:, CH:], in_=self.idxT_d[128:256, :]), 16)
                hL.inc(sy.dma_start(out=mask_sb[:, 0:CH], in_=self.mask_d[0:128, :]), 16)
                hL.inc(sy.dma_start(out=mask_sb[:, CH:], in_=self.mask_d[128:256, :]), 16)

            @blk.gpsimd
            def _(g):
                gL.inc(g.dma_start(out=self.id_bf[:, :], in_=self.ident_d), 16)
                gL.inc(g.dma_start(out=self.ones_bf[:, :], in_=self.ones_d), 16)
                gL.inc(g.dma_start(out=self.onesc_bf[:, :], in_=self.onesc_d), 16)

            @blk.scalar
            def _(sc):
                sc.wait_ge(hL.h, hL.v)
                sc.activation(self.bd_sb[:, :], self.s_sb[:, :], AF.Exp,
                              bias=0.0, scale=-float(CH))
                for j in range(4):
                    sc.activation(self.bdp_sb[:, j * H:(j + 1) * H], self.s_sb[:, :],
                                  AF.Exp, bias=0.0, scale=-float(CH * j))
                for j in range(4):
                    last = sc.activation(self.bd4p_sb[:, j * H:(j + 1) * H],
                                         self.s_sb[:, :], AF.Exp,
                                         bias=0.0, scale=-float(4 * CH * j))
                ac.inc(last)
                for h in range(H):
                    for half in range(2):
                        c = 2 * h + half
                        sc.activation(self.kdec_sb[:, c:c + 1],
                                      self.arrm_sb[:, half:half + 1], AF.Exp,
                                      bias=0.0, scale=self.s_sb[:, h:h + 1])
                sc.wait_ge(dv.h, d0 + 1)
                for h in range(H):
                    sc.activation(self.qdec_sb[:, h * CH:(h + 1) * CH],
                                  self.arr_sb[:, :], AF.Exp,
                                  bias=0.0, scale=self.negs_sb[:, h:h + 1])
                for i, (h, nh) in enumerate(diag_items):
                    if i >= 2:
                        sc.wait_ge(dv.h, d0 + 2 + (i - 2))
                    slot = i % 2
                    diag_ac[i] = ac.inc(sc.activation(
                        dscr[:, slot * CH:(slot + 1) * CH],
                        idx_sb[:, nh * CH:(nh + 1) * CH], AF.Exp,
                        bias=0.0, scale=self.negs_sb[:, h:h + 1]))

            @blk.vector
            def _(v):
                v.wait_ge(hL.h, hL.v)
                dv.inc(v.tensor_scalar_mul(self.negs_sb[:, :], self.s_sb[:, :], -1.0))
                v.wait_ge(ac.h, a0 + 1)
                bd4p = self.bd4p_sb
                for r in range(NCORES):
                    ur = self.u_sb[:, r * H:(r + 1) * H]
                    v.tensor_scalar(ur, bd4p[:, 0:H], self.sel_sb[:, 4 * r:4 * r + 1],
                                    None, ALU.mult)
                    for j in range(1, 4):
                        ts = v.tensor_scalar(self.ut_sb[:, (j - 1) * H:j * H],
                                             bd4p[:, j * H:(j + 1) * H],
                                             self.sel_sb[:, 4 * r + j:4 * r + j + 1],
                                             None, ALU.mult)
                    self.vstep(v, ts)
                    for j in range(1, 4):
                        self.vstep(v, v.tensor_add(
                            ur, ur, self.ut_sb[:, (j - 1) * H:j * H]))
                for i, (h, nh) in enumerate(diag_items):
                    v.wait_ge(ac.h, diag_ac[i])
                    slot = i % 2
                    dv.inc(v.tensor_mul(
                        self.diag_sb[:, (2 * h + nh) * CH:(2 * h + nh + 1) * CH],
                        dscr[:, slot * CH:(slot + 1) * CH],
                        mask_sb[:, nh * CH:(nh + 1) * CH]))

        assert dv.v == d0 + 1 + len(diag_items)
        assert ac.v == a0 + 1 + len(diag_items)
        ls.close()
        self.setup_hL = hL.v
        self.setup_gL = gL.v

    def ln_bf_parts(self, res, src_slice, dst_slice, n_tok, base):
        pe, dv, ac = self.pe, self.dv, self.ac
        p0, a0, d0 = base["pe"], base["ac"], base["dv"]

        def scalar_part(sc):
            for k in range(KC):
                if k >= 2:
                    sc.wait_ge(pe.h, p0 + (k - 1))
                slot = k % 2
                ac.inc(sc.activation(
                    res["sqs"][:, slot * n_tok:(slot + 1) * n_tok],
                    src_slice(k), AF.Square))
            sc.wait_ge(dv.h, d0 + 1)
            ac.inc(sc.activation(self.r_t2[:, 0:n_tok], self.r_t1[:, 0:n_tok],
                                 AF.Sqrt))

        def tensor_part(t):
            for k in range(KC):
                t.matmul(res["pr0"][0:1, 0:n_tok], self.onesc_bf[:, :], src_slice(k),
                         start=(k == 0), stop=(k == KC - 1))
                t.wait_ge(ac.h, a0 + k + 1)
                slot = k % 2
                pe.inc(t.matmul(res["pr1"][0:1, 0:n_tok], self.onesc_bf[:, :],
                                res["sqs"][:, slot * n_tok:(slot + 1) * n_tok],
                                start=(k == 0), stop=(k == KC - 1)))
            t.wait_ge(dv.h, d0 + 2)
            t.matmul(res["pa"][:, 0:n_tok], self.ones_f[:, :],
                     self.nmrow[:, 0:n_tok], start=True, stop=True)
            pe.inc(t.matmul(res["pb"][:, 0:n_tok], self.ones_f[:, :],
                            self.rsrow[:, 0:n_tok], start=True, stop=True))

        def vector_part(v):
            v.wait_ge(pe.h, p0 + KC)
            self.vstep(v, v.tensor_scalar(
                self.nmrow[:, 0:n_tok], res["pr0"][0:1, 0:n_tok],
                -1.0 / DM, None, ALU.mult))
            self.vstep(v, v.tensor_scalar(
                self.r_t1[:, 0:n_tok], res["pr1"][0:1, 0:n_tok],
                1.0 / DM, EPS_LN, ALU.mult, ALU.add))
            self.vstep(v, v.tensor_mul(self.r_t2[:, 0:n_tok],
                                       self.nmrow[:, 0:n_tok],
                                       self.nmrow[:, 0:n_tok]))
            dv.inc(v.tensor_tensor(self.r_t1[:, 0:n_tok], self.r_t1[:, 0:n_tok],
                                   self.r_t2[:, 0:n_tok], ALU.subtract))
            v.wait_ge(ac.h, a0 + KC + 1)
            dv.inc(v.reciprocal(self.rsrow[:, 0:n_tok], self.r_t2[:, 0:n_tok]))
            v.wait_ge(pe.h, p0 + KC + 1)
            for k in range(KC):
                t1 = res["t1s"][:, (k % 2) * n_tok:(k % 2 + 1) * n_tok]
                self.vstep(v, v.tensor_tensor(t1, src_slice(k),
                                              res["pa"][:, 0:n_tok], ALU.add))
                v.tensor_tensor(dst_slice(k), t1,
                                res["pb"][:, 0:n_tok], ALU.mult)

        return scalar_part, tensor_part, vector_part

    def ln_f32_parts(self, res, src_slice, n_tok, base, bias_col=None,
                     dst_slice=None):
        pe, dv, ac = self.pe, self.dv, self.ac
        p0, a0, d0 = base["pe"], base["ac"], base["dv"]

        def scalar_part(sc):
            for k in range(KC):
                if k >= 2:
                    sc.wait_ge(pe.h, p0 + (k - 1))
                slot = k % 2
                ub = res["ubs"][:, slot * n_tok:(slot + 1) * n_tok]
                uq = res["uqs"][:, slot * n_tok:(slot + 1) * n_tok]
                if bias_col is not None:
                    ac.inc(sc.activation(ub, src_slice(k), AF.Identity,
                                         bias=bias_col(k)))
                    ac.inc(sc.activation(uq, src_slice(k), AF.Square,
                                         bias=bias_col(k)))
                else:
                    ac.inc(sc.activation(ub, src_slice(k), AF.Copy))
                    ac.inc(sc.activation(uq, src_slice(k), AF.Square))
            sc.wait_ge(dv.h, d0 + 1)
            ac.inc(sc.activation(self.r_t2[:, 0:n_tok], self.r_t1[:, 0:n_tok],
                                 AF.Sqrt))

        def tensor_part(t):
            for k in range(KC):
                slot = k % 2
                ub = res["ubs"][:, slot * n_tok:(slot + 1) * n_tok]
                uq = res["uqs"][:, slot * n_tok:(slot + 1) * n_tok]
                t.wait_ge(ac.h, a0 + 2 * k + 1)
                t.matmul(res["pr0"][0:1, 0:n_tok], self.onesc_bf[:, :], ub,
                         start=(k == 0), stop=(k == KC - 1))
                t.wait_ge(ac.h, a0 + 2 * k + 2)
                pe.inc(t.matmul(res["pr1"][0:1, 0:n_tok], self.onesc_bf[:, :], uq,
                                start=(k == 0), stop=(k == KC - 1)))
            t.wait_ge(dv.h, d0 + 2)
            t.matmul(res["pa"][:, 0:n_tok], self.ones_f[:, :],
                     self.nmrow[:, 0:n_tok], start=True, stop=True)
            pe.inc(t.matmul(res["pb"][:, 0:n_tok], self.ones_f[:, :],
                            self.rsrow[:, 0:n_tok], start=True, stop=True))

        def vector_part(v):
            v.wait_ge(pe.h, p0 + KC)
            self.vstep(v, v.tensor_scalar(
                self.nmrow[:, 0:n_tok], res["pr0"][0:1, 0:n_tok],
                -1.0 / DM, None, ALU.mult))
            self.vstep(v, v.tensor_scalar(
                self.r_t1[:, 0:n_tok], res["pr1"][0:1, 0:n_tok],
                1.0 / DM, EPS_LN, ALU.mult, ALU.add))
            self.vstep(v, v.tensor_mul(self.r_t2[:, 0:n_tok],
                                       self.nmrow[:, 0:n_tok],
                                       self.nmrow[:, 0:n_tok]))
            dv.inc(v.tensor_tensor(self.r_t1[:, 0:n_tok], self.r_t1[:, 0:n_tok],
                                   self.r_t2[:, 0:n_tok], ALU.subtract))
            v.wait_ge(ac.h, a0 + 2 * KC + 1)
            dv.inc(v.reciprocal(self.rsrow[:, 0:n_tok], self.r_t2[:, 0:n_tok]))
            v.wait_ge(pe.h, p0 + KC + 1)
            for k in range(KC):
                t1 = res["t1s"][:, (k % 2) * n_tok:(k % 2 + 1) * n_tok]
                if bias_col is not None:
                    op = v.scalar_tensor_tensor(t1, src_slice(k), bias_col(k),
                                                res["pa"][:, 0:n_tok],
                                                ALU.add, ALU.add)
                else:
                    op = v.tensor_tensor(t1, src_slice(k),
                                         res["pa"][:, 0:n_tok], ALU.add)
                self.vstep(v, op)
                dst = dst_slice(k) if dst_slice is not None else src_slice(k)
                last = v.tensor_tensor(dst, t1,
                                       res["pb"][:, 0:n_tok], ALU.mult)
            return last

        return scalar_part, tensor_part, vector_part

    def phase_a(self):
        nc = self.nc
        gL, hL, sT, pe, dv, ac = self.gL, self.hL, self.sT, self.pe, self.dv, self.ac
        la = self.la = ExitStack()
        xbf = self.sbuf("xbf", [128, KC * TILE], BF16, la)
        sqs = self.sbuf("sqs", [128, 2 * TILE], BF16, la)
        t1s = self.sbuf("t1s", [128, 2 * TILE], F32, la)
        kfull = self.sbuf("kfull", [128, KC * TCORE], BF16, la)
        vfull = self.sbuf("vfull", [128, 8 * DM], BF16, la)
        wkbuf = self.sbuf("wkbuf", [128, 2 * DM], BF16, la)
        wvbuf = self.sbuf("wvbuf", [128, 2 * KC * 256], BF16, la)
        ktm = self.sbuf("ktm", [128, H * CH], BF16, la)
        kvc = self.sbuf("kvc", [128, H * D], F32, la)
        kvl = self.sbuf("kvl", [128, H * D], BF16, la)
        pa = self.psum("pa", [128, TILE], F32, la)
        pb = self.psum("pb", [128, TILE], F32, la)
        pr0 = self.psum("pr0", [1, TILE], F32, la)
        pr1 = self.psum("pr1", [1, TILE], F32, la)
        ptrA = self.psum("ptrA", [128, 128], BF16, la)
        ptrB = self.psum("ptrB", [128, 128], BF16, la)
        ps_s0 = self.psum("ps_s0", [128, D], F32, la)
        ps_s1 = self.psum("ps_s1", [128, D], F32, la)
        lnres = dict(sqs=sqs, pa=pa, pb=pb, pr0=pr0, pr1=pr1, t1s=t1s)

        for t in range(NT):
            t0 = t * TILE
            base = dict(pe=pe.v, ac=ac.v, dv=dv.v)
            sp, tp, vp = self.ln_bf_parts(
                lnres,
                lambda k: xbf[:, k * TILE:(k + 1) * TILE],
                lambda k, t0=t0: self.xln(k, t0, TILE),
                TILE, base)
            with nc.Block(f"A1_{t}") as blk:
                gx = []

                @blk.gpsimd
                def _(g, t0=t0):
                    gx.append(gL.inc(g.dma_start(
                        out=xbf[:, :].rearrange("p (k t) -> p k t", t=TILE),
                        in_=self.fmj(self.x_d)[:, :, t0:t0 + TILE]), 16))

                @blk.scalar
                def _(sc):
                    sc.wait_ge(gL.h, gx[0])
                    sp(sc)

                @blk.tensor
                def _(t_):
                    t_.wait_ge(gL.h, gx[0])
                    t_.wait_ge(gL.h, self.setup_gL)
                    t_.wait_ge(hL.h, self.setup_hL)
                    tp(t_)

                @blk.vector
                def _(v):
                    vp(v)
            assert pe.v == base["pe"] + 17 and ac.v == base["ac"] + 17
            assert dv.v == base["dv"] + 2
        if not self.lvl("A2"):
            return

        p0, a0 = pe.v, ac.v
        g_wk, g_wv = {}, {}
        with nc.Block("A2") as blk:
            @blk.gpsimd
            def _(g):
                for fo in range(KC):
                    if fo >= 2:
                        g.wait_ge(pe.h, p0 + 2 * (fo - 1))
                    slot = fo % 2
                    g_wk[fo] = gL.inc(g.dma_start(
                        out=wkbuf[:, slot * DM:(slot + 1) * DM]
                            .rearrange("p (k b) -> p k b", b=128),
                        in_=self.wk_d[fo * DM:(fo + 1) * DM, :]
                            .rearrange("(k a) b -> a k b", a=128)), 16)
                for n8 in range(8):
                    if n8 >= 2:
                        g.wait_ge(pe.h, p0 + 32 + 8 * (n8 - 1))
                    slot = n8 % 2
                    g_wv[n8] = gL.inc(g.dma_start(
                        out=wvbuf[:, slot * KC * 256:(slot + 1) * KC * 256]
                            .rearrange("p (k t) -> p k t", t=256),
                        in_=self.wv_d[:, n8 * 256:(n8 + 1) * 256]
                            .rearrange("(k a) t -> a k t", a=128)), 16)
                if self.debug:
                    sT.inc(g.dma_start(
                        out=self.fmj(self.dbg["dbg_xln"]),
                        in_=self.xln_sb[:, :]
                            .rearrange("p (k t) -> p k t", t=TCORE)), 16)

            @blk.tensor
            def _(t_):
                for fo in range(KC):
                    t_.wait_ge(gL.h, g_wk[fo])
                    wslot = fo % 2
                    for half in range(2):
                        idx = fo * 2 + half
                        ps = pa if idx % 2 == 0 else pb
                        if idx >= 2:
                            t_.wait_ge(ac.h, a0 + idx - 1)
                        for k in range(KC):
                            mm = t_.matmul(
                                ps[:, :],
                                wkbuf[:, wslot * DM + k * 128:wslot * DM + (k + 1) * 128],
                                self.xln(k, half * TILE, TILE),
                                start=(k == 0), stop=(k == KC - 1))
                        pe.inc(mm)
                for n8 in range(8):
                    t_.wait_ge(gL.h, g_wv[n8])
                    wslot = n8 % 2
                    for tc in range(8):
                        idx = n8 * 8 + tc
                        ps = pa if idx % 2 == 0 else pb
                        if idx >= 2:
                            t_.wait_ge(ac.h, a0 + 32 + idx - 1)
                        for k in range(KC):
                            mm = t_.matmul(
                                ps[:, 0:256],
                                self.xln(k, tc * 128, 128),
                                wvbuf[:, wslot * KC * 256 + k * 256:
                                      wslot * KC * 256 + (k + 1) * 256],
                                start=(k == 0), stop=(k == KC - 1))
                        pe.inc(mm)

            @blk.scalar
            def _(sc):
                for fo in range(KC):
                    for half in range(2):
                        idx = fo * 2 + half
                        ps = pa if idx % 2 == 0 else pb
                        sc.wait_ge(pe.h, p0 + idx + 1)
                        ac.inc(sc.activation(
                            kfull[:, fo * TCORE + half * TILE:
                                  fo * TCORE + (half + 1) * TILE],
                            ps[:, :], AF.Silu))
                for n8 in range(8):
                    for tc in range(8):
                        idx = n8 * 8 + tc
                        ps = pa if idx % 2 == 0 else pb
                        sc.wait_ge(pe.h, p0 + 32 + idx + 1)
                        ac.inc(sc.activation(
                            vfull[:, tc * DM + n8 * 256:tc * DM + (n8 + 1) * 256],
                            ps[:, 0:256], AF.Silu))

            @blk.sync
            def _(sy):
                sy.wait_ge(ac.h, a0 + 32)
                self.st_k = sT.inc(sy.dma_start(
                    out=self.fmj(self.k_sc),
                    in_=kfull[:, :].rearrange("p (f t) -> p f t", t=TCORE)), 16)
                sy.wait_ge(ac.h, a0 + 96)
                self.st_v = sT.inc(sy.dma_start(
                    out=self.v_sc.rearrange("(tc p) e -> p tc e", p=128),
                    in_=vfull[:, :].rearrange("p (tc e) -> p tc e", e=DM)), 16)

        assert pe.v == p0 + 96 and ac.v == a0 + 96
        if not self.lvl("A"):
            return

        sT0 = sT.v
        with nc.Block("A3") as blk:
            pe0_by_cl = {}
            dv0_by_cl = {}
            p = pe.v
            d = dv.v
            for cl in range(NCH):
                pe0_by_cl[cl] = p
                dv0_by_cl[cl] = d
                p += 48
                d += 48 + (1 if cl < NCH - 1 else 0)
            dv_final = d

            @blk.tensor
            def _(t_):
                for cl in range(NCH):
                    p0c, d0c = pe0_by_cl[cl], dv0_by_cl[cl]
                    for i in range(2 * H):
                        h, nh = i // 2, i % 2
                        if i >= 2:
                            t_.wait_ge(dv.h, d0c + (i - 2) + 1)
                        ptr = ptrA if i % 2 == 0 else ptrB
                        pe.inc(t_.transpose(
                            ptr[:, :],
                            kfull[:, h * TCORE + cl * CH + nh * 128:
                                  h * TCORE + cl * CH + (nh + 1) * 128],
                            self.id_bf[:, :]))
                    for h in range(H):
                        t_.wait_ge(dv.h, d0c + 2 * h + 2)
                        if h >= 2:
                            t_.wait_ge(dv.h, d0c + 32 + (h - 2) + 1)
                        ps = ps_s0 if h % 2 == 0 else ps_s1
                        for nh in range(2):
                            tc = cl * 2 + nh
                            mm = t_.matmul(
                                ps[:, :],
                                ktm[:, (2 * h + nh) * 128:(2 * h + nh + 1) * 128],
                                vfull[:, tc * DM + h * D:tc * DM + (h + 1) * D],
                                start=(nh == 0), stop=(nh == 1))
                        pe.inc(mm)

            @blk.vector
            def _(v):
                for cl in range(NCH):
                    p0c, d0c = pe0_by_cl[cl], dv0_by_cl[cl]
                    for i in range(2 * H):
                        h, nh = i // 2, i % 2
                        v.wait_ge(pe.h, p0c + i + 1)
                        ptr = ptrA if i % 2 == 0 else ptrB
                        dv.inc(v.tensor_scalar(
                            ktm[:, i * 128:(i + 1) * 128],
                            ptr[:, :],
                            self.kdec_sb[:, 2 * h + nh:2 * h + nh + 1],
                            None, ALU.mult))
                    for h in range(H):
                        v.wait_ge(pe.h, p0c + 32 + h + 1)
                        ps = ps_s0 if h % 2 == 0 else ps_s1
                        kvch = kvc[:, h * D:(h + 1) * D]
                        if cl == 0:
                            dv.inc(v.tensor_copy(kvch, ps[:, :]))
                        else:
                            dv.inc(v.scalar_tensor_tensor(
                                kvch, kvch, self.bd_sb[:, h:h + 1], ps[:, :],
                                ALU.mult, ALU.add))
                    if cl < NCH - 1:
                        if cl > 0:
                            v.wait_ge(sT.h, sT0 + 16 * cl)
                        v.wait_ge(dv.h, dv.v)
                        dv.inc(v.tensor_copy(kvl[:, :], kvc[:, :]))

            @blk.sync
            def _(sy):
                for cl in range(NCH - 1):
                    sy.wait_ge(dv.h, dv0_by_cl[cl] + 49)
                    self.st_kvl[cl] = sT.inc(sy.dma_start(
                        out=self.hdj(self.kvloc_sc[cl * H * D:(cl + 1) * H * D, :]),
                        in_=kvl[:, :].rearrange("p (h e) -> p h e", e=D)), 16)
                sy.wait_ge(dv.h, dv_final)
                self.st_T = sT.inc(sy.dma_start(
                    out=self.hdj(self.cc_in),
                    in_=kvc[:, :].rearrange("p (h e) -> p h e", e=D)), 16)
                if self.debug:
                    sT.inc(sy.dma_start(
                        out=self.hdj(self.dbg["dbg_T"]),
                        in_=kvc[:, :].rearrange("p (h e) -> p h e", e=D)), 16)

            @blk.gpsimd
            def _(g):
                g.wait_ge(sT.h, self.st_T)
                self.cc.inc(g.collective_compute(
                    "AllGather", ALU.bypass,
                    replica_groups=[list(range(NCORES))],
                    ins=[self.cc_in.opt()],
                    outs=[self.cc_out.opt()]))

        assert pe.v == pe0_by_cl[NCH - 1] + 48
        assert dv.v == dv_final

    def phase_b(self):
        nc = self.nc
        gL, hL, sT, pe, dv, ac = self.gL, self.hL, self.sT, self.pe, self.dv, self.ac
        self.la.close()
        lb = self.lb = ExitStack()
        q_tile = self.sbuf("q_tile", [128, H * TILE], BF16, lb)
        g_tile = self.sbuf("g_tile", [128, H * TILE], BF16, lb)
        kv_in = self.sbuf("kv_in", [128, H * D], F32, lb)
        kv_bf = self.sbuf("kv_bf", [128, H * D], BF16, lb)
        kvl_in = self.sbuf("kvl_in", [128, H * D], BF16, lb)
        kch = self.sbuf("kch", [128, H * CH], BF16, lb)
        vch = self.sbuf("vch", [128, 2 * DM], BF16, lb)
        outbf = self.sbuf("outbf", [128, H * CH], BF16, lb)
        sqb = self.sbuf("sqb", [128, 2 * CH], BF16, lb)
        qd_scr = self.sbuf("qd_scr", [128, 2 * CH], BF16, lb)
        kqd_scr = self.sbuf("kqd_scr", [128, 2 * CH], BF16, lb)
        wqg = self.sbuf("wqg", [128, 2 * DM], BF16, lb)
        og_tile = self.sbuf("og_tile", [128, H * TILE], BF16, lb)
        z_sb = self.sbuf("z_sb", [128, KC * TILE], F32, lb)
        zb_scr = self.sbuf("zb_scr", [128, 2 * TILE], BF16, lb)
        zq_scr = self.sbuf("zq_scr", [128, 2 * TILE], BF16, lb)
        t1b = self.sbuf("t1b", [128, 2 * TILE], F32, lb)
        pa = self.psum("pba", [128, TILE], F32, lb)
        pb = self.psum("pbb", [128, TILE], F32, lb)
        ps_kq = self.psum("ps_kq", [128, TILE], F32, lb)
        ps_o0 = self.psum("ps_o0", [128, CH], F32, lb)
        ps_o1 = self.psum("ps_o1", [128, CH], F32, lb)
        pr0 = self.psum("pbr0", [1, TILE], F32, lb)
        pr1 = self.psum("pbr1", [1, TILE], F32, lb)

        def b1_block(t, with_kv=False):
            t0 = t * TILE
            p0, a0 = pe.v, ac.v
            d0 = dv.v
            g_w = {}
            h_T = {}
            with nc.Block(f"B1_{t}") as blk:
                if with_kv:
                    @blk.sync
                    def _(sy):
                        sy.wait_ge(self.cc.h, self.cc.v)
                        for r in range(NCORES):
                            if r >= 1:
                                sy.wait_ge(dv.h, d0 + r)
                            h_T[r] = hL.inc(sy.dma_start(
                                out=z_sb[:, 0:H * D]
                                    .rearrange("p (h e) -> p h e", e=D),
                                in_=self.hdj(self.cc_out[r * H * D:
                                                         (r + 1) * H * D, :])), 16)
                        if self.debug:
                            sy.wait_ge(dv.h, d0 + NCORES)
                            sT.inc(sy.dma_start(
                                out=self.hdj(self.dbg["dbg_kvin"]),
                                in_=kv_in[:, :]
                                    .rearrange("p (h e) -> p h e", e=D)), 16)

                @blk.gpsimd
                def _(g):
                    if with_kv:
                        g.wait_ge(self.cc.h, self.cc.v)
                    for i, (wd, dst) in enumerate([(self.wq_d, q_tile),
                                                   (self.wg_d, g_tile)]):
                        for fo in range(KC):
                            idx = i * KC + fo
                            if idx >= 2:
                                g.wait_ge(pe.h, p0 + idx - 1)
                            slot = idx % 2
                            g_w[idx] = gL.inc(g.dma_start(
                                out=wqg[:, slot * DM:(slot + 1) * DM]
                                    .rearrange("p (k b) -> p k b", b=128),
                                in_=wd[fo * DM:(fo + 1) * DM, :]
                                    .rearrange("(k a) b -> a k b", a=128)), 16)
                    if self.debug and t == 0:
                        g.wait_ge(ac.h, a0 + 2 * KC)
                        sT.inc(g.dma_start(
                            out=self.fmj(self.dbg["dbg_q"]),
                            in_=q_tile[:, :].rearrange(
                                "p (h t) -> p h t", t=TILE)), 16)
                        sT.inc(g.dma_start(
                            out=self.fmj(self.dbg["dbg_g"]),
                            in_=g_tile[:, :].rearrange(
                                "p (h t) -> p h t", t=TILE)), 16)

                @blk.tensor
                def _(t_):
                    for idx in range(2 * KC):
                        t_.wait_ge(gL.h, g_w[idx])
                        wslot = idx % 2
                        ps = pa if idx % 2 == 0 else pb
                        if idx >= 2:
                            t_.wait_ge(ac.h, a0 + idx - 1)
                        for k in range(KC):
                            mm = t_.matmul(
                                ps[:, :],
                                wqg[:, wslot * DM + k * 128:wslot * DM + (k + 1) * 128],
                                self.xln(k, t0, TILE),
                                start=(k == 0), stop=(k == KC - 1))
                        pe.inc(mm)

                @blk.scalar
                def _(sc):
                    for idx in range(2 * KC):
                        fo = idx % KC
                        ps = pa if idx % 2 == 0 else pb
                        dst = q_tile if idx < KC else g_tile
                        fn = AF.Silu if idx < KC else AF.Sigmoid
                        sc.wait_ge(pe.h, p0 + idx + 1)
                        ac.inc(sc.activation(
                            dst[:, fo * TILE:(fo + 1) * TILE], ps[:, :], fn))

                if with_kv:
                    @blk.vector
                    def _(v):
                        stg = z_sb[:, H * D:2 * H * D]
                        v.memset(kv_in[:, :], 0.0)
                        for r in range(NCORES):
                            v.wait_ge(hL.h, h_T[r])
                            for h in range(H):
                                ts = v.tensor_scalar(
                                    stg[:, h * D:(h + 1) * D],
                                    z_sb[:, h * D:(h + 1) * D],
                                    self.u_sb[:, r * H + h:r * H + h + 1],
                                    None, ALU.mult)
                            self.vstep(v, ts)
                            tt = v.tensor_add(kv_in[:, :], kv_in[:, :], stg)
                            dv.inc(tt)
                            v.wait_ge(dv.h, dv.v)
            assert pe.v == p0 + 2 * KC and ac.v == a0 + 2 * KC
            assert not with_kv or dv.v == d0 + NCORES

        def b2_block(cl):
            t = cl // 2
            t0 = t * TILE
            co = (cl % 2) * CH
            p0, a0, d0 = pe.v, ac.v, dv.v
            loads = []
            with nc.Block(f"B2_{cl}") as blk:
                @blk.sync
                def _(sy):
                    sy.wait_ge(sT.h, self.st_k)
                    loads.append(hL.inc(sy.dma_start(
                        out=kch[:, :].rearrange("p (h t) -> p h t", t=CH),
                        in_=self.fmj(self.k_sc)[:, :, cl * CH:(cl + 1) * CH]), 16))
                    sy.wait_ge(sT.h, self.st_v)
                    loads.append(hL.inc(sy.dma_start(
                        out=vch[:, :].rearrange("p (tc e) -> p tc e", e=DM),
                        in_=self.v_sc[cl * CH:(cl + 1) * CH, :]
                            .rearrange("(tc p) e -> p tc e", p=128)), 16))
                    if cl > 0:
                        sy.wait_ge(sT.h, self.st_kvl[cl - 1])
                        loads.append(hL.inc(sy.dma_start(
                            out=kvl_in[:, :].rearrange("p (h e) -> p h e", e=D),
                            in_=self.hdj(self.kvloc_sc[(cl - 1) * H * D:
                                                       cl * H * D, :])), 16))

                qd_ev = lambda h: d0 + 2 * h + 2
                kqd_ev = lambda h: d0 + 2 * h + 3
                kq_ev = lambda h: p0 + 3 * h + 1
                out_ev = lambda h: p0 + 3 * h + 2
                stat_ev = lambda h: p0 + 3 * h + 3
                bc_ev = p0 + 49
                oev = lambda h: a0 + 2 * h + 1
                sq_ev = lambda h: a0 + 2 * h + 2

                @blk.tensor
                def _(t_):
                    for lv in loads:
                        t_.wait_ge(hL.h, lv)
                    t_.wait_ge(dv.h, d0 + 1)
                    for h in range(H):
                        if h >= 1:
                            t_.wait_ge(dv.h, kqd_ev(h - 1))
                        for nh in range(2):
                            mm = t_.matmul(
                                ps_kq[:, nh * CH:(nh + 1) * CH],
                                kch[:, h * CH + nh * 128:h * CH + (nh + 1) * 128],
                                q_tile[:, h * TILE + co:h * TILE + co + CH],
                                start=True, stop=True)
                        pe.inc(mm)
                        t_.wait_ge(dv.h, kqd_ev(h))
                        if h >= 2:
                            t_.wait_ge(ac.h, sq_ev(h - 2))
                        ps = ps_o0 if h % 2 == 0 else ps_o1
                        t_.matmul(ps[:, :], vch[:, 0 * DM + h * D:0 * DM + (h + 1) * D],
                                  kqd_scr[:, 0:CH], start=True, stop=False)
                        t_.matmul(ps[:, :], vch[:, 1 * DM + h * D:1 * DM + (h + 1) * D],
                                  kqd_scr[:, CH:2 * CH], start=False, stop=False)
                        qslot = h % 2
                        pe.inc(t_.matmul(ps[:, :], kv_bf[:, h * D:(h + 1) * D],
                                         qd_scr[:, qslot * CH:(qslot + 1) * CH],
                                         start=False, stop=True))
                        t_.wait_ge(ac.h, sq_ev(h))
                        slot = h % 2
                        pe.inc(t_.matmul(pr0[0:1, 0:CH], self.onesc_bf[:, :],
                                         sqb[:, slot * CH:(slot + 1) * CH],
                                         start=(h == 0), stop=(h == H - 1)))
                    t_.wait_ge(dv.h, d0 + 35)
                    t_.wait_ge(ac.h, oev(H - 1))
                    pe.inc(t_.matmul(ps_o0[:, :], self.ones_f[:, :],
                                     self.rsrow[:, 0:CH], start=True, stop=True))

                @blk.scalar
                def _(sc):
                    for h in range(H):
                        sc.wait_ge(pe.h, out_ev(h))
                        ps = ps_o0 if h % 2 == 0 else ps_o1
                        ac.inc(sc.activation(outbf[:, h * CH:(h + 1) * CH],
                                             ps[:, :], AF.Copy))
                        if h >= 2:
                            sc.wait_ge(pe.h, stat_ev(h - 2))
                        slot = h % 2
                        ac.inc(sc.activation(sqb[:, slot * CH:(slot + 1) * CH],
                                             ps[:, :], AF.Square))
                    sc.wait_ge(dv.h, d0 + 34)
                    ac.inc(sc.activation(self.r_t2[0:1, 0:CH],
                                         self.r_t1[0:1, 0:CH], AF.Sqrt))

                @blk.vector
                def _(v):
                    if cl == 0:
                        dv.inc(v.tensor_copy(kv_bf[:, :], kv_in[:, :]))
                    else:
                        v.wait_ge(hL.h, loads[2])
                        for h in range(H):
                            last = v.scalar_tensor_tensor(
                                kv_bf[:, h * D:(h + 1) * D],
                                kv_in[:, h * D:(h + 1) * D],
                                self.bdp_sb[:, cl * H + h:cl * H + h + 1],
                                kvl_in[:, h * D:(h + 1) * D],
                                ALU.mult, ALU.add)
                        dv.inc(last)
                    for h in range(H):
                        qslot = h % 2
                        if h >= 2:
                            v.wait_ge(pe.h, out_ev(h - 2))
                        dv.inc(v.tensor_mul(
                            qd_scr[:, qslot * CH:(qslot + 1) * CH],
                            q_tile[:, h * TILE + co:h * TILE + co + CH],
                            self.qdec_sb[:, h * CH:(h + 1) * CH]))
                        v.wait_ge(pe.h, kq_ev(h))
                        v.tensor_mul(kqd_scr[:, 0:CH], ps_kq[:, 0:CH],
                                     self.diag_sb[:, (2 * h) * CH:(2 * h + 1) * CH])
                        dv.inc(v.tensor_mul(
                            kqd_scr[:, CH:2 * CH], ps_kq[:, CH:2 * CH],
                            self.diag_sb[:, (2 * h + 1) * CH:(2 * h + 2) * CH]))
                    v.wait_ge(pe.h, stat_ev(H - 1))
                    dv.inc(v.tensor_scalar(self.r_t1[0:1, 0:CH], pr0[0:1, 0:CH],
                                           1.0 / DM, EPS_RMS, ALU.mult, ALU.add))
                    v.wait_ge(ac.h, a0 + 33)
                    dv.inc(v.reciprocal(self.rsrow[0:1, 0:CH], self.r_t2[0:1, 0:CH]))
                    v.wait_ge(pe.h, bc_ev)
                    if cl == 2 and 0 in self.st_x2ln:
                        v.wait_ge(sT.h, self.st_x2ln[0])
                    for h in range(H):
                        self.vstep(v, v.scalar_tensor_tensor(
                            og_tile[:, h * TILE + co:h * TILE + co + CH],
                            outbf[:, h * CH:(h + 1) * CH],
                            self.nw_sb[:, h:h + 1],
                            ps_o0[:, :], ALU.mult, ALU.mult))
                        last = v.tensor_mul(
                            og_tile[:, h * TILE + co:h * TILE + co + CH],
                            og_tile[:, h * TILE + co:h * TILE + co + CH],
                            g_tile[:, h * TILE + co:h * TILE + co + CH])
                    dv.inc(last)

                if self.debug and cl == 0:
                    @blk.gpsimd
                    def _(g):
                        g.wait_ge(dv.h, dv.v)
                        sT.inc(g.dma_start(
                            out=self.fmj(self.dbg["dbg_ob"]),
                            in_=outbf[:, :].rearrange(
                                "p (h t) -> p h t", t=CH)), 16)
                        sT.inc(g.dma_start(
                            out=self.dbg["dbg_rms"],
                            in_=self.rsrow[0:1, 0:CH]), 16)

            assert pe.v == p0 + 49, (pe.v, p0)
            assert ac.v == a0 + 33
            assert dv.v == d0 + 36

        def b3_block(t):
            t0 = t * TILE
            p0, a0, d0 = pe.v, ac.v, dv.v
            g_w = {}
            g_acc = []
            ln2res = dict(ubs=zb_scr, uqs=zq_scr, pa=pa, pb=pb, pr0=pr0, pr1=pr1,
                          t1s=t1b)
            ln2base = dict(pe=p0 + 49, ac=a0 + 49, dv=d0 + 3)
            sp2, tp2, vp2 = self.ln_f32_parts(
                ln2res,
                lambda k: z_sb[:, k * TILE:(k + 1) * TILE],
                TILE, ln2base,
                dst_slice=lambda k: og_tile[:, k * TILE:(k + 1) * TILE])
            wo_ev = lambda o: p0 + 3 * o + 1
            zstat_ev = lambda o: p0 + 3 * o + 3
            z_ev = lambda o: a0 + 3 * o + 1
            zb_ev = lambda o: a0 + 3 * o + 2
            zq_ev = lambda o: a0 + 3 * o + 3
            with nc.Block(f"B3_{t}") as blk:
                @blk.gpsimd
                def _(g):
                    for o in range(KC):
                        if o >= 2:
                            g.wait_ge(pe.h, wo_ev(o - 2))
                        slot = o % 2
                        g_w[o] = gL.inc(g.dma_start(
                            out=wqg[:, slot * DM:(slot + 1) * DM]
                                .rearrange("p (f b) -> p f b", b=128),
                            in_=self.wo_d[o * DM:(o + 1) * DM, :]
                                .rearrange("(f a) b -> a f b", a=128)), 16)
                    if self.debug:
                        sT.inc(g.dma_start(
                            out=self.fmj(self.dbg["dbg_og"])[:, :, t0:t0 + TILE],
                            in_=og_tile[:, :].rearrange("p (h t) -> p h t", t=TILE)),
                            16)
                    g.wait_ge(dv.h, d0 + 3)
                    g_acc.append(gL.inc(g.dma_start(
                        out=z_sb[:, :].rearrange("p (k t) -> p k t", t=TILE),
                        in_=self.fmj(self.x_d)[:, :, t0:t0 + TILE],
                        accum_op=ALU.add), 16))

                @blk.tensor
                def _(t_):
                    for o in range(KC):
                        t_.wait_ge(gL.h, g_w[o])
                        wslot = o % 2
                        ps = pa if o % 2 == 0 else pb
                        if o >= 2:
                            t_.wait_ge(ac.h, zq_ev(o - 2))
                        for f in range(KC):
                            mm = t_.matmul(
                                ps[:, :],
                                wqg[:, wslot * DM + f * 128:wslot * DM + (f + 1) * 128],
                                og_tile[:, f * TILE:(f + 1) * TILE],
                                start=(f == 0), stop=(f == KC - 1))
                        pe.inc(mm)
                        slot = o % 2
                        t_.wait_ge(ac.h, zb_ev(o))
                        pe.inc(t_.matmul(pr0[0:1, 0:TILE], self.onesc_bf[:, :],
                                         zb_scr[:, slot * TILE:(slot + 1) * TILE],
                                         start=(o == 0), stop=(o == KC - 1)))
                        t_.wait_ge(ac.h, zq_ev(o))
                        pe.inc(t_.matmul(pr1[0:1, 0:TILE], self.onesc_bf[:, :],
                                         zq_scr[:, slot * TILE:(slot + 1) * TILE],
                                         start=(o == 0), stop=(o == KC - 1)))
                    t_.wait_ge(dv.h, d0 + 2)
                    t_.wait_ge(ac.h, z_ev(KC - 1))
                    t_.matmul(pa[:, :], self.ones_f[:, :], self.nmrow[:, 0:TILE],
                              start=True, stop=True)
                    pe.inc(t_.matmul(pb[:, :], self.ones_f[:, :],
                                     self.rsrow[:, 0:TILE], start=True, stop=True))
                    t_.wait_ge(dv.h, d0 + 1)
                    tp2(t_)

                @blk.scalar
                def _(sc):
                    if t > 0 and 0 in self.st_x2:
                        sc.wait_ge(sT.h, self.st_x2[0])
                    for o in range(KC):
                        sc.wait_ge(pe.h, wo_ev(o))
                        ps = pa if o % 2 == 0 else pb
                        slot = o % 2
                        ac.inc(sc.activation(z_sb[:, o * TILE:(o + 1) * TILE],
                                             ps[:, :], AF.Copy))
                        if o >= 2:
                            sc.wait_ge(pe.h, zstat_ev(o - 2))
                        ac.inc(sc.activation(zb_scr[:, slot * TILE:(slot + 1) * TILE],
                                             ps[:, :], AF.Copy))
                        ac.inc(sc.activation(zq_scr[:, slot * TILE:(slot + 1) * TILE],
                                             ps[:, :], AF.Square))
                    sc.wait_ge(dv.h, d0 + 1)
                    ac.inc(sc.activation(self.r_t2[0:1, 0:TILE],
                                         self.r_t1[0:1, 0:TILE], AF.Sqrt))
                    sc.wait_ge(gL.h, g_acc[0])
                    sp2(sc)

                @blk.vector
                def _(v):
                    v.wait_ge(pe.h, zstat_ev(KC - 1))
                    self.vstep(v, v.tensor_scalar(
                        self.nmrow[0:1, 0:TILE], pr0[0:1, 0:TILE],
                        -1.0 / DM, None, ALU.mult))
                    self.vstep(v, v.tensor_scalar(
                        self.r_t1[0:1, 0:TILE], pr1[0:1, 0:TILE],
                        1.0 / DM, EPS_LN, ALU.mult, ALU.add))
                    self.vstep(v, v.tensor_mul(
                        self.r_t2[0:1, 0:TILE], self.nmrow[0:1, 0:TILE],
                        self.nmrow[0:1, 0:TILE]))
                    dv.inc(v.tensor_tensor(self.r_t1[0:1, 0:TILE],
                                           self.r_t1[0:1, 0:TILE],
                                           self.r_t2[0:1, 0:TILE], ALU.subtract))
                    v.wait_ge(ac.h, a0 + 3 * KC + 1)
                    dv.inc(v.reciprocal(self.rsrow[0:1, 0:TILE],
                                        self.r_t2[0:1, 0:TILE]))
                    v.wait_ge(pe.h, p0 + 49)
                    for k in range(KC):
                        t1 = t1b[:, (k % 2) * TILE:(k % 2 + 1) * TILE]
                        self.vstep(v, v.tensor_tensor(
                            t1, z_sb[:, k * TILE:(k + 1) * TILE],
                            pa[:, :], ALU.add))
                        last = v.tensor_tensor(z_sb[:, k * TILE:(k + 1) * TILE],
                                               t1, pb[:, :], ALU.mult)
                    dv.inc(last)
                    if self.debug:
                        v.wait_ge(sT.h, sT.v)
                    dv.inc(vp2(v))

                @blk.sync
                def _(sy):
                    sy.wait_ge(gL.h, g_acc[0])
                    self.st_x2[t] = sT.inc(sy.dma_start(
                        out=self.fmj(self.x2_sc)[:, :, t0:t0 + TILE],
                        in_=z_sb[:, :].rearrange("p (k t) -> p k t", t=TILE)), 16)
                    sy.wait_ge(dv.h, dv.v)
                    self.st_x2ln[t] = sT.inc(sy.dma_start(
                        out=self.fmj(self.x2ln_sc)[:, :, t0:t0 + TILE],
                        in_=og_tile[:, :].rearrange("p (k t) -> p k t", t=TILE)), 16)
                    if self.debug:
                        sT.inc(sy.dma_start(
                            out=self.fmj(self.dbg["dbg_x2"])[:, :, t0:t0 + TILE],
                            in_=z_sb[:, :].rearrange("p (k t) -> p k t", t=TILE)), 16)
            assert pe.v == p0 + 66 and ac.v == a0 + 82 and dv.v == d0 + 6

        b1_block(0, with_kv=True)
        if not self.lvl("B2"):
            return
        b2_block(0)
        b2_block(1)
        if not self.lvl("B3"):
            return
        b3_block(0)
        if not self.lvl("B"):
            return
        b1_block(1)
        b2_block(2)
        b2_block(3)
        b3_block(1)

    def phase_c(self):
        nc = self.nc
        gL, hL, sT, pe, dv, ac = self.gL, self.hL, self.sT, self.pe, self.dv, self.ac
        self.lb.close()
        self.lab.close()
        lc = self.lc = ExitStack()
        x2_ln = self.sbuf("x2_ln", [128, KC * TCORE], BF16, lc)
        pa = self.psum("pca", [128, TILE], F32, lc)
        pb = self.psum("pcb", [128, TILE], F32, lc)
        pr0 = self.psum("pcr0", [1, TILE], F32, lc)
        pr1 = self.psum("pcr1", [1, TILE], F32, lc)

        def x2ln(k, t0, tw):
            return x2_ln[:, k * TCORE + t0:k * TCORE + t0 + tw]

        h_xln = []
        with nc.Block("C0") as blk:
            @blk.sync
            def _(sy):
                for t in range(NT):
                    sy.wait_ge(sT.h, self.st_x2ln[t])
                    h_xln.append(hL.inc(sy.dma_start(
                        out=x2_ln[:, :].rearrange(
                            "p (k t) -> p k t", t=TCORE)[:, :, t * TILE:
                                                         (t + 1) * TILE],
                        in_=self.fmj(self.x2ln_sc)[:, :, t * TILE:
                                                   (t + 1) * TILE]), 16))

        if not self.lvl("C2"):
            return

        mid = self.sbuf("mid", [128, 32 * TCORE], BF16, lc)
        y_sb = self.sbuf("y_sb", [128, KC * TCORE], F32, lc)
        w0buf = self.sbuf("w0buf", [128, 2 * DM], BF16, lc)
        w1buf = self.sbuf("w1buf", [128, 2 * 32 * 128], BF16, lc)
        ubs = self.sbuf("ubs", [128, 2 * TILE], BF16, lc)
        uqs = self.sbuf("uqs", [128, 2 * TILE], BF16, lc)
        t1d = self.sbuf("t1d", [128, 2 * TILE], F32, lc)

        for h2 in range(2):
            p0, a0, d0 = pe.v, ac.v, dv.v
            pW1 = p0 + 64
            g_w0, g_w1 = {}, {}
            with nc.Block(f"C2_{h2}") as blk:
                @blk.gpsimd
                def _(g, h2=h2):
                    for jl in range(32):
                        if jl >= 2:
                            g.wait_ge(pe.h, p0 + 2 * (jl - 1))
                        slot = jl % 2
                        jo = h2 * 32 + jl
                        g_w0[jl] = gL.inc(g.dma_start(
                            out=w0buf[:, slot * DM:(slot + 1) * DM]
                                .rearrange("p (k b) -> p k b", b=128),
                            in_=self.w0_d[jo * DM:(jo + 1) * DM, :]
                                .rearrange("(k a) b -> a k b", a=128)), 16)
                    for o in range(KC):
                        if o >= 2:
                            g.wait_ge(pe.h, pW1 + 2 * (o - 1))
                        slot = o % 2
                        base_r = h2 * FF * 8 + o * 32 * 128
                        g_w1[o] = gL.inc(g.dma_start(
                            out=w1buf[:, slot * 32 * 128:(slot + 1) * 32 * 128]
                                .rearrange("p (j b) -> p j b", b=128),
                            in_=self.w1_d[base_r:base_r + 32 * 128, :]
                                .rearrange("(j a) b -> a j b", a=128)), 16)

                @blk.tensor
                def _(t_, h2=h2):
                    for lv in h_xln:
                        t_.wait_ge(hL.h, lv)
                    for jl in range(32):
                        t_.wait_ge(gL.h, g_w0[jl])
                        wslot = jl % 2
                        for nt in range(2):
                            idx = 2 * jl + nt
                            ps = pa if idx % 2 == 0 else pb
                            if idx >= 2:
                                t_.wait_ge(ac.h, a0 + idx - 1)
                            for k in range(KC):
                                mm = t_.matmul(
                                    ps[:, :],
                                    w0buf[:, wslot * DM + k * 128:
                                          wslot * DM + (k + 1) * 128],
                                    x2ln(k, nt * TILE, TILE),
                                    start=(k == 0), stop=(k == KC - 1))
                            pe.inc(mm)
                    t_.wait_ge(ac.h, a0 + 64)
                    for o in range(KC):
                        t_.wait_ge(gL.h, g_w1[o])
                        wslot = o % 2
                        for nt in range(2):
                            idx = 2 * o + nt
                            ps = pa if idx % 2 == 0 else pb
                            if idx >= 2:
                                if h2 == 0:
                                    t_.wait_ge(ac.h, a0 + 64 + idx - 1)
                                else:
                                    t_.wait_ge(dv.h, d0 + idx - 1)
                            for jl in range(32):
                                mm = t_.matmul(
                                    ps[:, :],
                                    w1buf[:, wslot * 32 * 128 + jl * 128:
                                          wslot * 32 * 128 + (jl + 1) * 128],
                                    mid[:, jl * TCORE + nt * TILE:
                                        jl * TCORE + (nt + 1) * TILE],
                                    start=(jl == 0), stop=(jl == 31))
                            pe.inc(mm)

                @blk.scalar
                def _(sc, h2=h2):
                    for jl in range(32):
                        jo = h2 * 32 + jl
                        for nt in range(2):
                            idx = 2 * jl + nt
                            ps = pa if idx % 2 == 0 else pb
                            sc.wait_ge(pe.h, p0 + idx + 1)
                            ac.inc(sc.activation(
                                mid[:, jl * TCORE + nt * TILE:
                                    jl * TCORE + (nt + 1) * TILE],
                                ps[:, :], AF.Gelu, bias=self.b0_sb[:, jo:jo + 1]))
                    if h2 == 0:
                        for o in range(KC):
                            for nt in range(2):
                                idx = 2 * o + nt
                                ps = pa if idx % 2 == 0 else pb
                                sc.wait_ge(pe.h, pW1 + idx + 1)
                                ac.inc(sc.activation(
                                    y_sb[:, o * TCORE + nt * TILE:
                                         o * TCORE + (nt + 1) * TILE],
                                    ps[:, :], AF.Copy))

                if h2 == 1:
                    @blk.vector
                    def _(v):
                        for o in range(KC):
                            for nt in range(2):
                                idx = 2 * o + nt
                                ps = pa if idx % 2 == 0 else pb
                                v.wait_ge(pe.h, pW1 + idx + 1)
                                ys = y_sb[:, o * TCORE + nt * TILE:
                                          o * TCORE + (nt + 1) * TILE]
                                dv.inc(v.tensor_tensor(ys, ys, ps[:, :], ALU.add))

        if not self.lvl("C"):
            return
        pe0, ac0, dv0 = pe.v, ac.v, dv.v
        bases = []
        sps, tps, vps = [], [], []
        for nt in range(2):
            base = dict(pe=pe0 + 17 * nt, ac=ac0 + 33 * nt, dv=dv0 + 3 * nt)
            bases.append(base)
            lnres2 = dict(ubs=ubs, uqs=uqs, pa=pa, pb=pb, pr0=pr0, pr1=pr1, t1s=t1d)
            sp, tp, vp = self.ln_f32_parts(
                lnres2,
                lambda k, t0=nt * TILE: y_sb[:, k * TCORE + t0:k * TCORE + t0 + TILE],
                TILE, base,
                bias_col=lambda k: self.b1_sb[:, k:k + 1])
            sps.append(sp)
            tps.append(tp)
            vps.append(vp)
        g_acc = {}
        with nc.Block("C3") as blk:
            @blk.gpsimd
            def _(g):
                for nt in range(2):
                    g.wait_ge(dv.h, dv0 + 3 * nt + 3)
                    t0 = nt * TILE
                    g_acc[nt] = gL.inc(g.dma_start(
                        out=y_sb[:, :].rearrange(
                            "p (k t) -> p k t", t=TCORE)[:, :, t0:t0 + TILE],
                        in_=self.fmj(self.x2_sc)[:, :, t0:t0 + TILE],
                        accum_op=ALU.add), 16)

            @blk.scalar
            def _(sc):
                sps[0](sc)
                sc.wait_ge(pe.h, pe0 + 17)
                sps[1](sc)

            @blk.tensor
            def _(t_):
                tps[0](t_)
                t_.wait_ge(dv.h, dv0 + 3)
                tps[1](t_)

            @blk.vector
            def _(v):
                dv.inc(vps[0](v))
                dv.inc(vps[1](v))

            @blk.sync
            def _(sy):
                for nt in range(2):
                    t0 = nt * TILE
                    sy.wait_ge(gL.h, g_acc[nt])
                    sT.inc(sy.dma_start(
                        out=self.fmj(self.out_d)[:, :, t0:t0 + TILE],
                        in_=y_sb[:, :].rearrange(
                            "p (k t) -> p k t", t=TCORE)[:, :, t0:t0 + TILE]), 16)
        assert pe.v == pe0 + 34 and ac.v == ac0 + 66 and dv.v == dv0 + 6

    def tail_block(self):
        nc = self.nc
        with nc.Block("tail") as blk:
            @blk.sync
            def _(sy):
                sy.wait_ge(self.sT.h, self.sT.v)
                if self.cc.v:
                    sy.wait_ge(self.cc.h, self.cc.v)

    LEVELS = ["S", "A1", "A2", "A", "B1", "Bkv", "B2", "B3", "B", "C1", "C2", "C"]

    def lvl(self, name):
        return self.LEVELS.index(self.upto) >= self.LEVELS.index(name)

    def build(self):
        self.setup_block()
        if self.lvl("A1"):
            self.phase_a()
        if self.lvl("B1"):
            self.phase_b()
        if self.lvl("C1"):
            self.phase_c()
        self.tail_block()
        return self.nc


_CACHE = {}


def _blocked_lhsT(wT, n_out, n_k=KC):
    return np.ascontiguousarray(
        wT.reshape(n_k, 128, n_out, 128).transpose(2, 0, 1, 3).reshape(-1, 128))


def _prep(inputs):
    x = np.ascontiguousarray(np.asarray(inputs["L0_attn_x"], np.float32))
    Wqkv = np.asarray(inputs["L0_attn_Wqkv"], np.float32).reshape(H, 3, D, DM)
    Wq = Wqkv[:, 0].reshape(H * D, DM)
    Wk = Wqkv[:, 1].reshape(H * D, DM)
    Wv = Wqkv[:, 2].reshape(H * D, DM)
    Wg = np.asarray(inputs["L0_attn_Wg"], np.float32)
    Wo = np.asarray(inputs["L0_attn_Wo"], np.float32)
    W0 = np.asarray(inputs["L0_ffn_W0"], np.float32)
    W1 = np.asarray(inputs["L0_ffn_W1"], np.float32)

    d = {}
    d["wq"] = _blocked_lhsT(np.ascontiguousarray(Wq.T), KC)
    d["wk"] = _blocked_lhsT(np.ascontiguousarray(Wk.T), KC)
    d["wv"] = np.ascontiguousarray(Wv.T)
    d["wg"] = _blocked_lhsT(np.ascontiguousarray(Wg.T), KC)
    d["wo"] = _blocked_lhsT(np.ascontiguousarray(Wo.T), KC)
    d["w0"] = _blocked_lhsT(np.ascontiguousarray(W0.T), NJ)
    W1T = np.ascontiguousarray(W1.T).reshape(2, 32, 128, KC, 128)
    d["w1"] = np.ascontiguousarray(W1T.transpose(0, 3, 1, 2, 4).reshape(-1, 128))

    slope = np.asarray(inputs["L0_attn_slope_rate"], np.float32).reshape(1, H)
    d["slope"] = np.ascontiguousarray(np.broadcast_to(slope, (128, H)))
    arr = np.arange(1, CH + 1, dtype=np.float32)
    d["arrb"] = np.ascontiguousarray(np.broadcast_to(arr[None, :], (128, CH)))
    p = np.arange(128, dtype=np.float32)
    d["arrm"] = np.ascontiguousarray(
        np.stack([p + 1 - CH, p + 129 - CH], axis=1))
    mm, nn = np.meshgrid(np.arange(CH, dtype=np.float32),
                         np.arange(CH, dtype=np.float32))
    diff = mm - nn
    d["idxT"] = np.maximum(diff, 0.0).astype(np.float32)
    d["maskT"] = (diff >= 0).astype(np.float32)
    d["ident"] = np.eye(128, dtype=np.float32)
    d["onesr"] = np.ones((1, 128), np.float32)
    d["onesc"] = np.ones((128, 1), np.float32)
    d["b0c"] = np.ascontiguousarray(
        np.asarray(inputs["L0_ffn_bias0"], np.float32).reshape(NJ, 128).T)
    d["b1c"] = np.ascontiguousarray(
        np.asarray(inputs["L0_ffn_bias1"], np.float32).reshape(KC, 128).T)
    d["nwc"] = np.ascontiguousarray(
        np.asarray(inputs["L0_attn_norm_w"], np.float32).reshape(KC, 128).T)
    return x, d


def kernel(**inputs):
    key = (DEBUG, UPTO)
    if key not in _CACHE:
        _CACHE[key] = Builder(debug=DEBUG, upto=UPTO).build()
    nc = _CACHE[key]

    x, shared = _prep(inputs)
    in_maps = []
    for core in range(NCORES):
        b, qq = core // 4, core % 4
        m = dict(shared)
        m["x"] = np.ascontiguousarray(x[b, qq * TCORE:(qq + 1) * TCORE, :].T)
        sel = np.zeros((128, NCORES * 4), np.float32)
        for pidx in range(qq):
            r = b * 4 + pidx
            sel[:, r * 4 + (qq - 1 - pidx)] = 1.0
        m["sel"] = sel
        in_maps.append(m)

    res = run_bass_kernel_spmd(nc, in_maps, core_ids=list(range(NCORES)))
    out = np.empty((2, 4096, DM), np.float32)
    for core in range(NCORES):
        b, qq = core // 4, core % 4
        out[b, qq * TCORE:(qq + 1) * TCORE, :] = res.results[core]["out"].T
    return out


# version 08
# speedup vs baseline: 1.0037x; 1.0037x over previous
import numpy as np
import ml_dtypes
import concourse.bass as bass
import concourse.mybir as mybir
from concourse.bass_utils import run_bass_kernel_spmd
from contextlib import ExitStack

F32 = mybir.dt.float32
BF16 = mybir.dt.bfloat16
AF = mybir.ActivationFunctionType
ALU = mybir.AluOpType

H = 16
D = 128
DM = 2048
KC = 16
TCORE = 1024
TILE = 512
NT = 2
CH = 256
NCH = 4
FF = 8192
NJ = 64
NCORES = 8
EPS_LN = 1e-5
EPS_RMS = float(np.finfo(np.float32).eps)

DEBUG = 0
UPTO = "C"


class Sem:
    def __init__(self, nc, stack, name):
        self.h = stack.enter_context(nc.semaphore(name))
        self.v = 0

    def inc(self, instr, n=1):
        instr.then_inc(self.h, n)
        self.v += n
        return self.v


class Builder:
    def __init__(self, debug=0, upto="C"):
        self.debug = debug
        self.upto = upto
        nc = self.nc = bass.Bass("TRN2", target_bir_lowering=False)

        dram = lambda nm, shp, kind="ExternalInput", dt=F32: \
            nc.dram_tensor(nm, shp, dt, kind=kind).ap()
        self.x_d = dram("x", [DM, TCORE])
        self.wq_d = dram("wq", [KC * DM, D])
        self.wk_d = dram("wk", [KC * DM, D])
        self.wv_d = dram("wv", [DM, DM])
        self.wg_d = dram("wg", [KC * DM, D])
        self.wo_d = dram("wo", [KC * DM, D])
        self.w0_d = dram("w0", [NJ * DM, D])
        self.w1_d = dram("w1", [FF * KC, D])
        self.slope_d = dram("slope", [128, H])
        self.arr_d = dram("arrb", [128, CH])
        self.arrm_d = dram("arrm", [128, 2])
        self.idxT_d = dram("idxT", [CH, CH])
        self.mask_d = dram("maskT", [CH, CH])
        self.ident_d = dram("ident", [128, 128])
        self.ones_d = dram("onesr", [1, 128])
        self.onesc_d = dram("onesc", [128, 1])
        self.b0_d = dram("b0c", [128, NJ])
        self.b1_d = dram("b1c", [128, KC])
        self.nw_d = dram("nwc", [128, KC])
        self.sel_d = dram("sel", [128, NCORES * 4])
        self.out_d = dram("out", [DM, TCORE], kind="ExternalOutput")

        self.k_sc = dram("k_sc", [DM, TCORE], "Internal", BF16)
        self.v_sc = dram("v_sc", [TCORE, DM], "Internal", BF16)
        self.kvloc_sc = dram("kvloc_sc", [3 * H * D, D], "Internal", BF16)
        self.x2_sc = dram("x2_sc", [DM, TCORE], "Internal")
        self.x2ln_sc = dram("x2ln_sc", [DM, TCORE], "Internal", BF16)
        self.cc_in = dram("cc_in", [H * D, D], "Internal")
        self.cc_out = dram("cc_out", [NCORES * H * D, D], "Internal")

        self.dbg = {}
        if debug:
            for nm, shp in [("dbg_xln", [DM, TCORE]), ("dbg_T", [H * D, D]),
                            ("dbg_kvin", [H * D, D]), ("dbg_og", [DM, TCORE]),
                            ("dbg_x2", [DM, TCORE]), ("dbg_q", [DM, TILE]),
                            ("dbg_g", [DM, TILE]), ("dbg_ob", [DM, CH]),
                            ("dbg_rms", [1, CH])]:
                self.dbg[nm] = dram(nm, shp, kind="ExternalOutput")

        es = self.es = ExitStack()
        self.gL = Sem(nc, es, "gL")
        self.hL = Sem(nc, es, "hL")
        self.sT = Sem(nc, es, "sT")
        self.cc = Sem(nc, es, "ccs")
        self.pe = Sem(nc, es, "pes")
        self.dv = Sem(nc, es, "dvs")
        self.ac = Sem(nc, es, "acs")
        self.dvx = Sem(nc, es, "dvx")
        self.acx = Sem(nc, es, "acx")

        sbuf = self.sbuf
        self.id_bf = sbuf("id_bf", [128, 128], BF16)
        self.ones_bf = sbuf("ones_bf", [1, 128], BF16)
        self.ones_f = sbuf("ones_f", [1, 128], F32)
        self.onesc_bf = sbuf("onesc_bf", [128, 1], BF16)
        self.s_sb = sbuf("s_sb", [128, H], F32)
        self.negs_sb = sbuf("negs_sb", [128, H], F32)
        self.bd_sb = sbuf("bd_sb", [128, H], F32)
        self.bdp_sb = sbuf("bdp_sb", [128, 4 * H], F32)
        self.bd4p_sb = sbuf("bd4p_sb", [128, 4 * H], F32)
        self.u_sb = sbuf("u_sb", [128, NCORES * H], F32)
        self.ut_sb = sbuf("ut_sb", [128, 3 * H], F32)
        self.sel_sb = sbuf("sel_sb", [128, NCORES * 4], F32)
        self.arr_sb = sbuf("arr_sb", [128, CH], F32)
        self.arrm_sb = sbuf("arrm_sb", [128, 2], F32)
        self.kdec_sb = sbuf("kdec_sb", [128, 2 * H], F32)
        self.b0_sb = sbuf("b0_sb", [128, NJ], F32)
        self.b1_sb = sbuf("b1_sb", [128, KC], F32)
        self.nw_sb = sbuf("nw_sb", [128, KC], F32)
        self.row_sb = sbuf("row_sb", [1, 4 * TILE], F32)
        self.nmrow = self.row_sb[0:1, 0:TILE]
        self.rsrow = self.row_sb[0:1, TILE:2 * TILE]
        self.r_t1 = self.row_sb[0:1, 2 * TILE:3 * TILE]
        self.r_t2 = self.row_sb[0:1, 3 * TILE:4 * TILE]

        self.lab = ExitStack()
        self.qdec_sb = sbuf("qdec_sb", [128, H * CH], BF16, self.lab)
        self.diag_sb = sbuf("diag_sb", [128, H * 2 * CH], BF16, self.lab)
        self.xln_sb = sbuf("xln_sb", [128, KC * TCORE], BF16, self.lab)

        self.st_kvl = {}
        self.st_x2 = {}
        self.st_x2ln = {}

    def sbuf(self, nm, shp, dt, stk=None):
        return (stk or self.es).enter_context(self.nc.sbuf_tensor(nm, shp, dt))

    def psum(self, nm, shp, dt, stk=None):
        return (stk or self.es).enter_context(self.nc.psum_tensor(nm, shp, dt))

    def vstep(self, v, instr):
        self.dvx.inc(instr)
        v.wait_ge(self.dvx.h, self.dvx.v)
        return instr

    def sstep(self, sc, instr):
        self.acx.inc(instr)
        sc.wait_ge(self.acx.h, self.acx.v)
        return instr

    def xln(self, k, t0, tw):
        return self.xln_sb[:, k * TCORE + t0: k * TCORE + t0 + tw]

    @staticmethod
    def fmj(ap):
        return ap.rearrange("(k p) t -> p k t", p=128)

    @staticmethod
    def hdj(ap):
        return ap.rearrange("(h d) e -> d h e", d=128)

    def setup_block(self):
        nc = self.nc
        gL, hL, pe, dv, ac = self.gL, self.hL, self.pe, self.dv, self.ac
        ls = ExitStack()
        idx_sb = self.sbuf("idx_sb", [128, 2 * CH], F32, ls)
        mask_sb = self.sbuf("mask_sb", [128, 2 * CH], F32, ls)
        dscr = self.sbuf("dscr", [128, 2 * CH], F32, ls)
        d0, a0 = dv.v, ac.v
        diag_items = [(h, nh) for h in range(H) for nh in range(2)]
        diag_ac = {}

        with nc.Block("setup") as blk:
            @blk.sync
            def _(sy):
                for dst, src in [(self.s_sb, self.slope_d), (self.arr_sb, self.arr_d),
                                 (self.arrm_sb, self.arrm_d), (self.ones_f, self.ones_d),
                                 (self.b0_sb, self.b0_d), (self.b1_sb, self.b1_d),
                                 (self.nw_sb, self.nw_d), (self.sel_sb, self.sel_d)]:
                    hL.inc(sy.dma_start(out=dst[:, :], in_=src), 16)
                hL.inc(sy.dma_start(out=idx_sb[:, 0:CH], in_=self.idxT_d[0:128, :]), 16)
                hL.inc(sy.dma_start(out=idx_sb[:, CH:], in_=self.idxT_d[128:256, :]), 16)
                hL.inc(sy.dma_start(out=mask_sb[:, 0:CH], in_=self.mask_d[0:128, :]), 16)
                hL.inc(sy.dma_start(out=mask_sb[:, CH:], in_=self.mask_d[128:256, :]), 16)

            @blk.gpsimd
            def _(g):
                gL.inc(g.dma_start(out=self.id_bf[:, :], in_=self.ident_d), 16)
                gL.inc(g.dma_start(out=self.ones_bf[:, :], in_=self.ones_d), 16)
                gL.inc(g.dma_start(out=self.onesc_bf[:, :], in_=self.onesc_d), 16)

            @blk.scalar
            def _(sc):
                sc.wait_ge(hL.h, hL.v)
                sc.activation(self.bd_sb[:, :], self.s_sb[:, :], AF.Exp,
                              bias=0.0, scale=-float(CH))
                for j in range(4):
                    sc.activation(self.bdp_sb[:, j * H:(j + 1) * H], self.s_sb[:, :],
                                  AF.Exp, bias=0.0, scale=-float(CH * j))
                for j in range(4):
                    last = sc.activation(self.bd4p_sb[:, j * H:(j + 1) * H],
                                         self.s_sb[:, :], AF.Exp,
                                         bias=0.0, scale=-float(4 * CH * j))
                ac.inc(last)
                for h in range(H):
                    for half in range(2):
                        c = 2 * h + half
                        sc.activation(self.kdec_sb[:, c:c + 1],
                                      self.arrm_sb[:, half:half + 1], AF.Exp,
                                      bias=0.0, scale=self.s_sb[:, h:h + 1])
                sc.wait_ge(dv.h, d0 + 1)
                for h in range(H):
                    sc.activation(self.qdec_sb[:, h * CH:(h + 1) * CH],
                                  self.arr_sb[:, :], AF.Exp,
                                  bias=0.0, scale=self.negs_sb[:, h:h + 1])
                for i, (h, nh) in enumerate(diag_items):
                    if i >= 2:
                        sc.wait_ge(dv.h, d0 + 2 + (i - 2))
                    slot = i % 2
                    diag_ac[i] = ac.inc(sc.activation(
                        dscr[:, slot * CH:(slot + 1) * CH],
                        idx_sb[:, nh * CH:(nh + 1) * CH], AF.Exp,
                        bias=0.0, scale=self.negs_sb[:, h:h + 1]))

            @blk.vector
            def _(v):
                v.wait_ge(hL.h, hL.v)
                dv.inc(v.tensor_scalar_mul(self.negs_sb[:, :], self.s_sb[:, :], -1.0))
                v.wait_ge(ac.h, a0 + 1)
                bd4p = self.bd4p_sb
                for r in range(NCORES):
                    ur = self.u_sb[:, r * H:(r + 1) * H]
                    v.tensor_scalar(ur, bd4p[:, 0:H], self.sel_sb[:, 4 * r:4 * r + 1],
                                    None, ALU.mult)
                    for j in range(1, 4):
                        ts = v.tensor_scalar(self.ut_sb[:, (j - 1) * H:j * H],
                                             bd4p[:, j * H:(j + 1) * H],
                                             self.sel_sb[:, 4 * r + j:4 * r + j + 1],
                                             None, ALU.mult)
                    self.vstep(v, ts)
                    for j in range(1, 4):
                        self.vstep(v, v.tensor_add(
                            ur, ur, self.ut_sb[:, (j - 1) * H:j * H]))
                for i, (h, nh) in enumerate(diag_items):
                    v.wait_ge(ac.h, diag_ac[i])
                    slot = i % 2
                    dv.inc(v.tensor_mul(
                        self.diag_sb[:, (2 * h + nh) * CH:(2 * h + nh + 1) * CH],
                        dscr[:, slot * CH:(slot + 1) * CH],
                        mask_sb[:, nh * CH:(nh + 1) * CH]))

        assert dv.v == d0 + 1 + len(diag_items)
        assert ac.v == a0 + 1 + len(diag_items)
        ls.close()
        self.setup_hL = hL.v
        self.setup_gL = gL.v

    def ln_bf_parts(self, res, src_slice, dst_slice, n_tok, base):
        pe, dv, ac = self.pe, self.dv, self.ac
        p0, a0, d0 = base["pe"], base["ac"], base["dv"]

        def scalar_part(sc):
            for k in range(KC):
                if k >= 2:
                    sc.wait_ge(pe.h, p0 + (k - 1))
                slot = k % 2
                ac.inc(sc.activation(
                    res["sqs"][:, slot * n_tok:(slot + 1) * n_tok],
                    src_slice(k), AF.Square))
            sc.wait_ge(dv.h, d0 + 1)
            ac.inc(sc.activation(self.r_t2[:, 0:n_tok], self.r_t1[:, 0:n_tok],
                                 AF.Sqrt))

        def tensor_part(t):
            for k in range(KC):
                t.matmul(res["pr0"][0:1, 0:n_tok], self.onesc_bf[:, :], src_slice(k),
                         start=(k == 0), stop=(k == KC - 1))
                t.wait_ge(ac.h, a0 + k + 1)
                slot = k % 2
                pe.inc(t.matmul(res["pr1"][0:1, 0:n_tok], self.onesc_bf[:, :],
                                res["sqs"][:, slot * n_tok:(slot + 1) * n_tok],
                                start=(k == 0), stop=(k == KC - 1)))
            t.wait_ge(dv.h, d0 + 2)
            t.matmul(res["pa"][:, 0:n_tok], self.ones_f[:, :],
                     self.nmrow[:, 0:n_tok], start=True, stop=True)
            pe.inc(t.matmul(res["pb"][:, 0:n_tok], self.ones_f[:, :],
                            self.rsrow[:, 0:n_tok], start=True, stop=True))

        def vector_part(v):
            v.wait_ge(pe.h, p0 + KC)
            self.vstep(v, v.tensor_scalar(
                self.nmrow[:, 0:n_tok], res["pr0"][0:1, 0:n_tok],
                -1.0 / DM, None, ALU.mult))
            self.vstep(v, v.tensor_scalar(
                self.r_t1[:, 0:n_tok], res["pr1"][0:1, 0:n_tok],
                1.0 / DM, EPS_LN, ALU.mult, ALU.add))
            self.vstep(v, v.tensor_mul(self.r_t2[:, 0:n_tok],
                                       self.nmrow[:, 0:n_tok],
                                       self.nmrow[:, 0:n_tok]))
            dv.inc(v.tensor_tensor(self.r_t1[:, 0:n_tok], self.r_t1[:, 0:n_tok],
                                   self.r_t2[:, 0:n_tok], ALU.subtract))
            v.wait_ge(ac.h, a0 + KC + 1)
            dv.inc(v.reciprocal(self.rsrow[:, 0:n_tok], self.r_t2[:, 0:n_tok]))
            v.wait_ge(pe.h, p0 + KC + 1)
            for k in range(KC):
                t1 = res["t1s"][:, (k % 2) * n_tok:(k % 2 + 1) * n_tok]
                self.vstep(v, v.tensor_tensor(t1, src_slice(k),
                                              res["pa"][:, 0:n_tok], ALU.add))
                v.tensor_tensor(dst_slice(k), t1,
                                res["pb"][:, 0:n_tok], ALU.mult)

        return scalar_part, tensor_part, vector_part

    def ln_f32_parts(self, res, src_slice, n_tok, base, bias_col=None,
                     dst_slice=None):
        pe, dv, ac = self.pe, self.dv, self.ac
        p0, a0, d0 = base["pe"], base["ac"], base["dv"]

        def scalar_part(sc):
            for k in range(KC):
                if k >= 2:
                    sc.wait_ge(pe.h, p0 + (k - 1))
                slot = k % 2
                ub = res["ubs"][:, slot * n_tok:(slot + 1) * n_tok]
                uq = res["uqs"][:, slot * n_tok:(slot + 1) * n_tok]
                if bias_col is not None:
                    ac.inc(sc.activation(ub, src_slice(k), AF.Identity,
                                         bias=bias_col(k)))
                    ac.inc(sc.activation(uq, src_slice(k), AF.Square,
                                         bias=bias_col(k)))
                else:
                    ac.inc(sc.activation(ub, src_slice(k), AF.Copy))
                    ac.inc(sc.activation(uq, src_slice(k), AF.Square))
            sc.wait_ge(dv.h, d0 + 1)
            ac.inc(sc.activation(self.r_t2[:, 0:n_tok], self.r_t1[:, 0:n_tok],
                                 AF.Sqrt))

        def tensor_part(t):
            for k in range(KC):
                slot = k % 2
                ub = res["ubs"][:, slot * n_tok:(slot + 1) * n_tok]
                uq = res["uqs"][:, slot * n_tok:(slot + 1) * n_tok]
                t.wait_ge(ac.h, a0 + 2 * k + 1)
                t.matmul(res["pr0"][0:1, 0:n_tok], self.onesc_bf[:, :], ub,
                         start=(k == 0), stop=(k == KC - 1))
                t.wait_ge(ac.h, a0 + 2 * k + 2)
                pe.inc(t.matmul(res["pr1"][0:1, 0:n_tok], self.onesc_bf[:, :], uq,
                                start=(k == 0), stop=(k == KC - 1)))
            t.wait_ge(dv.h, d0 + 2)
            t.matmul(res["pa"][:, 0:n_tok], self.ones_f[:, :],
                     self.nmrow[:, 0:n_tok], start=True, stop=True)
            pe.inc(t.matmul(res["pb"][:, 0:n_tok], self.ones_f[:, :],
                            self.rsrow[:, 0:n_tok], start=True, stop=True))

        def vector_part(v):
            v.wait_ge(pe.h, p0 + KC)
            self.vstep(v, v.tensor_scalar(
                self.nmrow[:, 0:n_tok], res["pr0"][0:1, 0:n_tok],
                -1.0 / DM, None, ALU.mult))
            self.vstep(v, v.tensor_scalar(
                self.r_t1[:, 0:n_tok], res["pr1"][0:1, 0:n_tok],
                1.0 / DM, EPS_LN, ALU.mult, ALU.add))
            self.vstep(v, v.tensor_mul(self.r_t2[:, 0:n_tok],
                                       self.nmrow[:, 0:n_tok],
                                       self.nmrow[:, 0:n_tok]))
            dv.inc(v.tensor_tensor(self.r_t1[:, 0:n_tok], self.r_t1[:, 0:n_tok],
                                   self.r_t2[:, 0:n_tok], ALU.subtract))
            v.wait_ge(ac.h, a0 + 2 * KC + 1)
            dv.inc(v.reciprocal(self.rsrow[:, 0:n_tok], self.r_t2[:, 0:n_tok]))
            v.wait_ge(pe.h, p0 + KC + 1)
            for k in range(KC):
                t1 = res["t1s"][:, (k % 2) * n_tok:(k % 2 + 1) * n_tok]
                if bias_col is not None:
                    op = v.scalar_tensor_tensor(t1, src_slice(k), bias_col(k),
                                                res["pa"][:, 0:n_tok],
                                                ALU.add, ALU.add)
                else:
                    op = v.tensor_tensor(t1, src_slice(k),
                                         res["pa"][:, 0:n_tok], ALU.add)
                self.vstep(v, op)
                dst = dst_slice(k) if dst_slice is not None else src_slice(k)
                last = v.tensor_tensor(dst, t1,
                                       res["pb"][:, 0:n_tok], ALU.mult)
            return last

        return scalar_part, tensor_part, vector_part

    def phase_a(self):
        nc = self.nc
        gL, hL, sT, pe, dv, ac = self.gL, self.hL, self.sT, self.pe, self.dv, self.ac
        la = self.la = ExitStack()
        xbf = self.sbuf("xbf", [128, KC * TILE], BF16, la)
        sqs = self.sbuf("sqs", [128, 2 * TILE], BF16, la)
        t1s = self.sbuf("t1s", [128, 2 * TILE], F32, la)
        kfull = self.sbuf("kfull", [128, KC * TCORE], BF16, la)
        vfull = self.sbuf("vfull", [128, 8 * DM], BF16, la)
        wkbuf = self.sbuf("wkbuf", [128, 2 * DM], BF16, la)
        wvbuf = self.sbuf("wvbuf", [128, 2 * KC * 256], BF16, la)
        ktm = self.sbuf("ktm", [128, H * CH], BF16, la)
        kvc = self.sbuf("kvc", [128, H * D], F32, la)
        kvl = self.sbuf("kvl", [128, H * D], BF16, la)
        pa = self.psum("pa", [128, TILE], F32, la)
        pb = self.psum("pb", [128, TILE], F32, la)
        pr0 = self.psum("pr0", [1, TILE], F32, la)
        pr1 = self.psum("pr1", [1, TILE], F32, la)
        ptrA = self.psum("ptrA", [128, 128], BF16, la)
        ptrB = self.psum("ptrB", [128, 128], BF16, la)
        ps_s0 = self.psum("ps_s0", [128, D], F32, la)
        ps_s1 = self.psum("ps_s1", [128, D], F32, la)
        lnres = dict(sqs=sqs, pa=pa, pb=pb, pr0=pr0, pr1=pr1, t1s=t1s)

        for t in range(NT):
            t0 = t * TILE
            base = dict(pe=pe.v, ac=ac.v, dv=dv.v)
            sp, tp, vp = self.ln_bf_parts(
                lnres,
                lambda k: xbf[:, k * TILE:(k + 1) * TILE],
                lambda k, t0=t0: self.xln(k, t0, TILE),
                TILE, base)
            with nc.Block(f"A1_{t}") as blk:
                gx = []

                @blk.gpsimd
                def _(g, t0=t0):
                    gx.append(gL.inc(g.dma_start(
                        out=xbf[:, :].rearrange("p (k t) -> p k t", t=TILE),
                        in_=self.fmj(self.x_d)[:, :, t0:t0 + TILE]), 16))

                @blk.scalar
                def _(sc):
                    sc.wait_ge(gL.h, gx[0])
                    sp(sc)

                @blk.tensor
                def _(t_):
                    t_.wait_ge(gL.h, gx[0])
                    t_.wait_ge(gL.h, self.setup_gL)
                    t_.wait_ge(hL.h, self.setup_hL)
                    tp(t_)

                @blk.vector
                def _(v):
                    vp(v)
            assert pe.v == base["pe"] + 17 and ac.v == base["ac"] + 17
            assert dv.v == base["dv"] + 2
        if not self.lvl("A2"):
            return

        p0, a0 = pe.v, ac.v
        g_wk, g_wv = {}, {}
        with nc.Block("A2") as blk:
            @blk.gpsimd
            def _(g):
                for fo in range(KC):
                    if fo >= 2:
                        g.wait_ge(pe.h, p0 + 2 * (fo - 1))
                    slot = fo % 2
                    g_wk[fo] = gL.inc(g.dma_start(
                        out=wkbuf[:, slot * DM:(slot + 1) * DM]
                            .rearrange("p (k b) -> p k b", b=128),
                        in_=self.wk_d[fo * DM:(fo + 1) * DM, :]
                            .rearrange("(k a) b -> a k b", a=128)), 16)
                for n8 in range(8):
                    if n8 >= 2:
                        g.wait_ge(pe.h, p0 + 32 + 8 * (n8 - 1))
                    slot = n8 % 2
                    g_wv[n8] = gL.inc(g.dma_start(
                        out=wvbuf[:, slot * KC * 256:(slot + 1) * KC * 256]
                            .rearrange("p (k t) -> p k t", t=256),
                        in_=self.wv_d[:, n8 * 256:(n8 + 1) * 256]
                            .rearrange("(k a) t -> a k t", a=128)), 16)
                if self.debug:
                    sT.inc(g.dma_start(
                        out=self.fmj(self.dbg["dbg_xln"]),
                        in_=self.xln_sb[:, :]
                            .rearrange("p (k t) -> p k t", t=TCORE)), 16)

            @blk.tensor
            def _(t_):
                for fo in range(KC):
                    t_.wait_ge(gL.h, g_wk[fo])
                    wslot = fo % 2
                    for half in range(2):
                        idx = fo * 2 + half
                        ps = pa if idx % 2 == 0 else pb
                        if idx >= 2:
                            t_.wait_ge(ac.h, a0 + idx - 1)
                        for k in range(KC):
                            mm = t_.matmul(
                                ps[:, :],
                                wkbuf[:, wslot * DM + k * 128:wslot * DM + (k + 1) * 128],
                                self.xln(k, half * TILE, TILE),
                                start=(k == 0), stop=(k == KC - 1))
                        pe.inc(mm)
                for n8 in range(8):
                    t_.wait_ge(gL.h, g_wv[n8])
                    wslot = n8 % 2
                    for tc in range(8):
                        idx = n8 * 8 + tc
                        ps = pa if idx % 2 == 0 else pb
                        if idx >= 2:
                            t_.wait_ge(ac.h, a0 + 32 + idx - 1)
                        for k in range(KC):
                            mm = t_.matmul(
                                ps[:, 0:256],
                                self.xln(k, tc * 128, 128),
                                wvbuf[:, wslot * KC * 256 + k * 256:
                                      wslot * KC * 256 + (k + 1) * 256],
                                start=(k == 0), stop=(k == KC - 1))
                        pe.inc(mm)

            @blk.scalar
            def _(sc):
                for fo in range(KC):
                    for half in range(2):
                        idx = fo * 2 + half
                        ps = pa if idx % 2 == 0 else pb
                        sc.wait_ge(pe.h, p0 + idx + 1)
                        ac.inc(sc.activation(
                            kfull[:, fo * TCORE + half * TILE:
                                  fo * TCORE + (half + 1) * TILE],
                            ps[:, :], AF.Silu))
                for n8 in range(8):
                    for tc in range(8):
                        idx = n8 * 8 + tc
                        ps = pa if idx % 2 == 0 else pb
                        sc.wait_ge(pe.h, p0 + 32 + idx + 1)
                        ac.inc(sc.activation(
                            vfull[:, tc * DM + n8 * 256:tc * DM + (n8 + 1) * 256],
                            ps[:, 0:256], AF.Silu))

            @blk.sync
            def _(sy):
                sy.wait_ge(ac.h, a0 + 32)
                self.st_k = sT.inc(sy.dma_start(
                    out=self.fmj(self.k_sc),
                    in_=kfull[:, :].rearrange("p (f t) -> p f t", t=TCORE)), 16)
                sy.wait_ge(ac.h, a0 + 96)
                self.st_v = sT.inc(sy.dma_start(
                    out=self.v_sc.rearrange("(tc p) e -> p tc e", p=128),
                    in_=vfull[:, :].rearrange("p (tc e) -> p tc e", e=DM)), 16)

        assert pe.v == p0 + 96 and ac.v == a0 + 96
        if not self.lvl("A"):
            return

        sT0 = sT.v
        with nc.Block("A3") as blk:
            pe0_by_cl = {}
            dv0_by_cl = {}
            p = pe.v
            d = dv.v
            for cl in range(NCH):
                pe0_by_cl[cl] = p
                dv0_by_cl[cl] = d
                p += 48
                d += 48 + (1 if cl < NCH - 1 else 0)
            dv_final = d

            @blk.tensor
            def _(t_):
                for cl in range(NCH):
                    p0c, d0c = pe0_by_cl[cl], dv0_by_cl[cl]
                    for i in range(2 * H):
                        h, nh = i // 2, i % 2
                        if i >= 2:
                            t_.wait_ge(dv.h, d0c + (i - 2) + 1)
                        ptr = ptrA if i % 2 == 0 else ptrB
                        pe.inc(t_.transpose(
                            ptr[:, :],
                            kfull[:, h * TCORE + cl * CH + nh * 128:
                                  h * TCORE + cl * CH + (nh + 1) * 128],
                            self.id_bf[:, :]))
                    for h in range(H):
                        t_.wait_ge(dv.h, d0c + 2 * h + 2)
                        if h >= 2:
                            t_.wait_ge(dv.h, d0c + 32 + (h - 2) + 1)
                        ps = ps_s0 if h % 2 == 0 else ps_s1
                        for nh in range(2):
                            tc = cl * 2 + nh
                            mm = t_.matmul(
                                ps[:, :],
                                ktm[:, (2 * h + nh) * 128:(2 * h + nh + 1) * 128],
                                vfull[:, tc * DM + h * D:tc * DM + (h + 1) * D],
                                start=(nh == 0), stop=(nh == 1))
                        pe.inc(mm)

            @blk.vector
            def _(v):
                for cl in range(NCH):
                    p0c, d0c = pe0_by_cl[cl], dv0_by_cl[cl]
                    for i in range(2 * H):
                        h, nh = i // 2, i % 2
                        v.wait_ge(pe.h, p0c + i + 1)
                        ptr = ptrA if i % 2 == 0 else ptrB
                        dv.inc(v.tensor_scalar(
                            ktm[:, i * 128:(i + 1) * 128],
                            ptr[:, :],
                            self.kdec_sb[:, 2 * h + nh:2 * h + nh + 1],
                            None, ALU.mult))
                    for h in range(H):
                        v.wait_ge(pe.h, p0c + 32 + h + 1)
                        ps = ps_s0 if h % 2 == 0 else ps_s1
                        kvch = kvc[:, h * D:(h + 1) * D]
                        if cl == 0:
                            dv.inc(v.tensor_copy(kvch, ps[:, :]))
                        else:
                            dv.inc(v.scalar_tensor_tensor(
                                kvch, kvch, self.bd_sb[:, h:h + 1], ps[:, :],
                                ALU.mult, ALU.add))
                    if cl < NCH - 1:
                        if cl > 0:
                            v.wait_ge(sT.h, sT0 + 16 * cl)
                        v.wait_ge(dv.h, dv.v)
                        dv.inc(v.tensor_copy(kvl[:, :], kvc[:, :]))

            @blk.sync
            def _(sy):
                for cl in range(NCH - 1):
                    sy.wait_ge(dv.h, dv0_by_cl[cl] + 49)
                    self.st_kvl[cl] = sT.inc(sy.dma_start(
                        out=self.hdj(self.kvloc_sc[cl * H * D:(cl + 1) * H * D, :]),
                        in_=kvl[:, :].rearrange("p (h e) -> p h e", e=D)), 16)
                sy.wait_ge(dv.h, dv_final)
                self.st_T = sT.inc(sy.dma_start(
                    out=self.hdj(self.cc_in),
                    in_=kvc[:, :].rearrange("p (h e) -> p h e", e=D)), 16)
                if self.debug:
                    sT.inc(sy.dma_start(
                        out=self.hdj(self.dbg["dbg_T"]),
                        in_=kvc[:, :].rearrange("p (h e) -> p h e", e=D)), 16)

            @blk.gpsimd
            def _(g):
                g.wait_ge(sT.h, self.st_T)
                self.cc.inc(g.collective_compute(
                    "AllGather", ALU.bypass,
                    replica_groups=[list(range(NCORES))],
                    ins=[self.cc_in.opt()],
                    outs=[self.cc_out.opt()]))

        assert pe.v == pe0_by_cl[NCH - 1] + 48
        assert dv.v == dv_final

    def phase_b(self):
        nc = self.nc
        gL, hL, sT, pe, dv, ac = self.gL, self.hL, self.sT, self.pe, self.dv, self.ac
        self.la.close()
        lb = self.lb = ExitStack()
        q_tile = self.sbuf("q_tile", [128, H * TILE], BF16, lb)
        g_tile = self.sbuf("g_tile", [128, H * TILE], BF16, lb)
        kv_in = self.sbuf("kv_in", [128, H * D], F32, lb)
        kv_bf = self.sbuf("kv_bf", [128, H * D], BF16, lb)
        kvl_in = self.sbuf("kvl_in", [128, H * D], BF16, lb)
        kch = self.sbuf("kch", [128, H * CH], BF16, lb)
        vch = self.sbuf("vch", [128, 2 * DM], BF16, lb)
        outbf = self.sbuf("outbf", [128, H * CH], BF16, lb)
        sqb = self.sbuf("sqb", [128, 2 * CH], BF16, lb)
        qd_scr = self.sbuf("qd_scr", [128, 2 * CH], BF16, lb)
        kqd_scr = self.sbuf("kqd_scr", [128, 2 * CH], BF16, lb)
        wqg = self.sbuf("wqg", [128, 2 * DM], BF16, lb)
        og_tile = self.sbuf("og_tile", [128, H * TILE], BF16, lb)
        z_sb = self.sbuf("z_sb", [128, KC * TILE], F32, lb)
        zb_scr = self.sbuf("zb_scr", [128, 2 * TILE], BF16, lb)
        zq_scr = self.sbuf("zq_scr", [128, 2 * TILE], BF16, lb)
        t1b = self.sbuf("t1b", [128, 2 * TILE], F32, lb)
        pa = self.psum("pba", [128, TILE], F32, lb)
        pb = self.psum("pbb", [128, TILE], F32, lb)
        ps_kq = self.psum("ps_kq", [128, TILE], F32, lb)
        ps_o0 = self.psum("ps_o0", [128, CH], F32, lb)
        ps_o1 = self.psum("ps_o1", [128, CH], F32, lb)
        pr0 = self.psum("pbr0", [1, TILE], F32, lb)
        pr1 = self.psum("pbr1", [1, TILE], F32, lb)
        pbc2 = self.psum("pbc2", [128, TILE], F32, lb)

        def b1_parts(t, p_base, a_base, g_w, init_ac_wait=None,
                     init_gp_waits=None):
            t0 = t * TILE

            def gp(g):
                for i, wd in enumerate([self.wq_d, self.wg_d]):
                    for fo in range(KC):
                        idx = i * KC + fo
                        if idx >= 2:
                            g.wait_ge(pe.h, p_base + idx - 1)
                        elif init_gp_waits is not None:
                            g.wait_ge(pe.h, init_gp_waits[idx])
                        slot = idx % 2
                        g_w[idx] = gL.inc(g.dma_start(
                            out=wqg[:, slot * DM:(slot + 1) * DM]
                                .rearrange("p (k b) -> p k b", b=128),
                            in_=wd[fo * DM:(fo + 1) * DM, :]
                                .rearrange("(k a) b -> a k b", a=128)), 16)

            def tp(t_):
                for idx in range(2 * KC):
                    t_.wait_ge(gL.h, g_w[idx])
                    wslot = idx % 2
                    ps = pa if idx % 2 == 0 else pb
                    if idx >= 2:
                        t_.wait_ge(ac.h, a_base + idx - 1)
                    elif init_ac_wait is not None:
                        t_.wait_ge(ac.h, init_ac_wait)
                    for k in range(KC):
                        mm = t_.matmul(
                            ps[:, :],
                            wqg[:, wslot * DM + k * 128:wslot * DM + (k + 1) * 128],
                            self.xln(k, t0, TILE),
                            start=(k == 0), stop=(k == KC - 1))
                    pe.inc(mm)

            def sp(sc):
                for idx in range(2 * KC):
                    fo = idx % KC
                    ps = pa if idx % 2 == 0 else pb
                    dst = q_tile if idx < KC else g_tile
                    fn = AF.Silu if idx < KC else AF.Sigmoid
                    sc.wait_ge(pe.h, p_base + idx + 1)
                    ac.inc(sc.activation(
                        dst[:, fo * TILE:(fo + 1) * TILE], ps[:, :], fn))

            return gp, tp, sp

        def b1_block(t, with_kv=False):
            p0, a0, d0 = pe.v, ac.v, dv.v
            g_w = {}
            h_T = {}
            gp, tp, sp = b1_parts(t, p0, a0, g_w)
            with nc.Block(f"B1_{t}") as blk:
                if with_kv:
                    @blk.sync
                    def _(sy):
                        sy.wait_ge(self.cc.h, self.cc.v)
                        for r in range(NCORES):
                            if r >= 1:
                                sy.wait_ge(dv.h, d0 + r)
                            h_T[r] = hL.inc(sy.dma_start(
                                out=z_sb[:, 0:H * D]
                                    .rearrange("p (h e) -> p h e", e=D),
                                in_=self.hdj(self.cc_out[r * H * D:
                                                         (r + 1) * H * D, :])), 16)
                        if self.debug:
                            sy.wait_ge(dv.h, d0 + NCORES)
                            sT.inc(sy.dma_start(
                                out=self.hdj(self.dbg["dbg_kvin"]),
                                in_=kv_in[:, :]
                                    .rearrange("p (h e) -> p h e", e=D)), 16)

                @blk.gpsimd
                def _(g):
                    if with_kv:
                        g.wait_ge(self.cc.h, self.cc.v)
                    gp(g)
                    if self.debug and t == 0:
                        g.wait_ge(ac.h, a0 + 2 * KC)
                        sT.inc(g.dma_start(
                            out=self.fmj(self.dbg["dbg_q"]),
                            in_=q_tile[:, :].rearrange(
                                "p (h t) -> p h t", t=TILE)), 16)
                        sT.inc(g.dma_start(
                            out=self.fmj(self.dbg["dbg_g"]),
                            in_=g_tile[:, :].rearrange(
                                "p (h t) -> p h t", t=TILE)), 16)

                @blk.tensor
                def _(t_):
                    tp(t_)

                @blk.scalar
                def _(sc):
                    sp(sc)

                if with_kv:
                    @blk.vector
                    def _(v):
                        stg = z_sb[:, H * D:2 * H * D]
                        v.memset(kv_in[:, :], 0.0)
                        for r in range(NCORES):
                            v.wait_ge(hL.h, h_T[r])
                            for h in range(H):
                                ts = v.tensor_scalar(
                                    stg[:, h * D:(h + 1) * D],
                                    z_sb[:, h * D:(h + 1) * D],
                                    self.u_sb[:, r * H + h:r * H + h + 1],
                                    None, ALU.mult)
                            self.vstep(v, ts)
                            tt = v.tensor_add(kv_in[:, :], kv_in[:, :], stg)
                            dv.inc(tt)
                            v.wait_ge(dv.h, dv.v)
            assert pe.v == p0 + 2 * KC and ac.v == a0 + 2 * KC
            assert not with_kv or dv.v == d0 + NCORES

        def b2_block(cl):
            t = cl // 2
            t0 = t * TILE
            co = (cl % 2) * CH
            p0, a0, d0 = pe.v, ac.v, dv.v
            loads = []
            with nc.Block(f"B2_{cl}") as blk:
                @blk.sync
                def _(sy):
                    sy.wait_ge(sT.h, self.st_k)
                    loads.append(hL.inc(sy.dma_start(
                        out=kch[:, :].rearrange("p (h t) -> p h t", t=CH),
                        in_=self.fmj(self.k_sc)[:, :, cl * CH:(cl + 1) * CH]), 16))
                    sy.wait_ge(sT.h, self.st_v)
                    loads.append(hL.inc(sy.dma_start(
                        out=vch[:, :].rearrange("p (tc e) -> p tc e", e=DM),
                        in_=self.v_sc[cl * CH:(cl + 1) * CH, :]
                            .rearrange("(tc p) e -> p tc e", p=128)), 16))
                    if cl > 0:
                        sy.wait_ge(sT.h, self.st_kvl[cl - 1])
                        loads.append(hL.inc(sy.dma_start(
                            out=kvl_in[:, :].rearrange("p (h e) -> p h e", e=D),
                            in_=self.hdj(self.kvloc_sc[(cl - 1) * H * D:
                                                       cl * H * D, :])), 16))

                qd_ev = lambda h: d0 + 2 * h + 2
                kqd_ev = lambda h: d0 + 2 * h + 3
                kq_ev = lambda h: p0 + 3 * h + 1
                out_ev = lambda h: p0 + 3 * h + 2
                stat_ev = lambda h: p0 + 3 * h + 3
                bc_ev = p0 + 49
                oev = lambda h: a0 + 2 * h + 1
                sq_ev = lambda h: a0 + 2 * h + 2

                @blk.tensor
                def _(t_):
                    for lv in loads:
                        t_.wait_ge(hL.h, lv)
                    t_.wait_ge(dv.h, d0 + 1)
                    for h in range(H):
                        if h >= 1:
                            t_.wait_ge(dv.h, kqd_ev(h - 1))
                        for nh in range(2):
                            mm = t_.matmul(
                                ps_kq[:, nh * CH:(nh + 1) * CH],
                                kch[:, h * CH + nh * 128:h * CH + (nh + 1) * 128],
                                q_tile[:, h * TILE + co:h * TILE + co + CH],
                                start=True, stop=True)
                        pe.inc(mm)
                        t_.wait_ge(dv.h, kqd_ev(h))
                        if h >= 2:
                            t_.wait_ge(ac.h, sq_ev(h - 2))
                        ps = ps_o0 if h % 2 == 0 else ps_o1
                        t_.matmul(ps[:, :], vch[:, 0 * DM + h * D:0 * DM + (h + 1) * D],
                                  kqd_scr[:, 0:CH], start=True, stop=False)
                        t_.matmul(ps[:, :], vch[:, 1 * DM + h * D:1 * DM + (h + 1) * D],
                                  kqd_scr[:, CH:2 * CH], start=False, stop=False)
                        qslot = h % 2
                        pe.inc(t_.matmul(ps[:, :], kv_bf[:, h * D:(h + 1) * D],
                                         qd_scr[:, qslot * CH:(qslot + 1) * CH],
                                         start=False, stop=True))
                        t_.wait_ge(ac.h, sq_ev(h))
                        slot = h % 2
                        pe.inc(t_.matmul(pr0[0:1, 0:CH], self.onesc_bf[:, :],
                                         sqb[:, slot * CH:(slot + 1) * CH],
                                         start=(h == 0), stop=(h == H - 1)))
                    t_.wait_ge(dv.h, d0 + 35)
                    t_.wait_ge(ac.h, oev(H - 1))
                    pe.inc(t_.matmul(ps_o0[:, :], self.ones_f[:, :],
                                     self.rsrow[:, 0:CH], start=True, stop=True))

                @blk.scalar
                def _(sc):
                    for h in range(H):
                        sc.wait_ge(pe.h, out_ev(h))
                        ps = ps_o0 if h % 2 == 0 else ps_o1
                        ac.inc(sc.activation(outbf[:, h * CH:(h + 1) * CH],
                                             ps[:, :], AF.Copy))
                        if h >= 2:
                            sc.wait_ge(pe.h, stat_ev(h - 2))
                        slot = h % 2
                        ac.inc(sc.activation(sqb[:, slot * CH:(slot + 1) * CH],
                                             ps[:, :], AF.Square))
                    sc.wait_ge(dv.h, d0 + 34)
                    ac.inc(sc.activation(self.r_t2[0:1, 0:CH],
                                         self.r_t1[0:1, 0:CH], AF.Sqrt))

                @blk.vector
                def _(v):
                    if cl == 0:
                        dv.inc(v.tensor_copy(kv_bf[:, :], kv_in[:, :]))
                    else:
                        v.wait_ge(hL.h, loads[2])
                        for h in range(H):
                            last = v.scalar_tensor_tensor(
                                kv_bf[:, h * D:(h + 1) * D],
                                kv_in[:, h * D:(h + 1) * D],
                                self.bdp_sb[:, cl * H + h:cl * H + h + 1],
                                kvl_in[:, h * D:(h + 1) * D],
                                ALU.mult, ALU.add)
                        dv.inc(last)
                    for h in range(H):
                        qslot = h % 2
                        if h >= 2:
                            v.wait_ge(pe.h, out_ev(h - 2))
                        dv.inc(v.tensor_mul(
                            qd_scr[:, qslot * CH:(qslot + 1) * CH],
                            q_tile[:, h * TILE + co:h * TILE + co + CH],
                            self.qdec_sb[:, h * CH:(h + 1) * CH]))
                        v.wait_ge(pe.h, kq_ev(h))
                        v.tensor_mul(kqd_scr[:, 0:CH], ps_kq[:, 0:CH],
                                     self.diag_sb[:, (2 * h) * CH:(2 * h + 1) * CH])
                        dv.inc(v.tensor_mul(
                            kqd_scr[:, CH:2 * CH], ps_kq[:, CH:2 * CH],
                            self.diag_sb[:, (2 * h + 1) * CH:(2 * h + 2) * CH]))
                    v.wait_ge(pe.h, stat_ev(H - 1))
                    dv.inc(v.tensor_scalar(self.r_t1[0:1, 0:CH], pr0[0:1, 0:CH],
                                           1.0 / DM, EPS_RMS, ALU.mult, ALU.add))
                    v.wait_ge(ac.h, a0 + 33)
                    dv.inc(v.reciprocal(self.rsrow[0:1, 0:CH], self.r_t2[0:1, 0:CH]))
                    v.wait_ge(pe.h, bc_ev)
                    if cl == 2 and 0 in self.st_x2ln:
                        v.wait_ge(sT.h, self.st_x2ln[0])
                    for h in range(H):
                        self.vstep(v, v.scalar_tensor_tensor(
                            og_tile[:, h * TILE + co:h * TILE + co + CH],
                            outbf[:, h * CH:(h + 1) * CH],
                            self.nw_sb[:, h:h + 1],
                            ps_o0[:, :], ALU.mult, ALU.mult))
                        last = v.tensor_mul(
                            og_tile[:, h * TILE + co:h * TILE + co + CH],
                            og_tile[:, h * TILE + co:h * TILE + co + CH],
                            g_tile[:, h * TILE + co:h * TILE + co + CH])
                    dv.inc(last)

                if self.debug and cl == 0:
                    @blk.gpsimd
                    def _(g):
                        g.wait_ge(dv.h, dv.v)
                        sT.inc(g.dma_start(
                            out=self.fmj(self.dbg["dbg_ob"]),
                            in_=outbf[:, :].rearrange(
                                "p (h t) -> p h t", t=CH)), 16)
                        sT.inc(g.dma_start(
                            out=self.dbg["dbg_rms"],
                            in_=self.rsrow[0:1, 0:CH]), 16)

            assert pe.v == p0 + 49, (pe.v, p0)
            assert ac.v == a0 + 33
            assert dv.v == d0 + 36

        def b3_block(t, merge_b1=False):
            t0 = t * TILE
            p0, a0, d0 = pe.v, ac.v, dv.v
            g_w = {}
            g_w1 = {}
            g_acc = []
            nb1 = 32 if merge_b1 else 0
            bc_ev = p0 + 49 + nb1
            ln2res = dict(ubs=zb_scr, uqs=zq_scr, pa=ps_kq, pb=pbc2,
                          pr0=pr0, pr1=pr1, t1s=t1b)
            ln2base = dict(pe=bc_ev, ac=a0 + 49 + nb1, dv=d0 + 3)
            sp2, tp2, vp2 = self.ln_f32_parts(
                ln2res,
                lambda k: z_sb[:, k * TILE:(k + 1) * TILE],
                TILE, ln2base,
                dst_slice=lambda k: og_tile[:, k * TILE:(k + 1) * TILE])
            if merge_b1:
                gp1, tp1, sp1 = b1_parts(1, p0 + 48, a0 + 49, g_w1,
                                         init_ac_wait=a0 + 48,
                                         init_gp_waits=(p0 + 3 * 14 + 1,
                                                        p0 + 3 * 15 + 1))
            wo_ev = lambda o: p0 + 3 * o + 1
            zstat_ev = lambda o: p0 + 3 * o + 3
            z_ev = lambda o: a0 + 3 * o + 1
            zb_ev = lambda o: a0 + 3 * o + 2
            zq_ev = lambda o: a0 + 3 * o + 3
            with nc.Block(f"B3_{t}") as blk:
                @blk.gpsimd
                def _(g):
                    for o in range(KC):
                        if o >= 2:
                            g.wait_ge(pe.h, wo_ev(o - 2))
                        slot = o % 2
                        g_w[o] = gL.inc(g.dma_start(
                            out=wqg[:, slot * DM:(slot + 1) * DM]
                                .rearrange("p (f b) -> p f b", b=128),
                            in_=self.wo_d[o * DM:(o + 1) * DM, :]
                                .rearrange("(f a) b -> a f b", a=128)), 16)
                    if merge_b1:
                        gp1(g)
                    if self.debug:
                        sT.inc(g.dma_start(
                            out=self.fmj(self.dbg["dbg_og"])[:, :, t0:t0 + TILE],
                            in_=og_tile[:, :].rearrange("p (h t) -> p h t", t=TILE)),
                            16)
                    g.wait_ge(dv.h, d0 + 3)
                    g_acc.append(gL.inc(g.dma_start(
                        out=z_sb[:, :].rearrange("p (k t) -> p k t", t=TILE),
                        in_=self.fmj(self.x_d)[:, :, t0:t0 + TILE],
                        accum_op=ALU.add), 16))

                @blk.tensor
                def _(t_):
                    for o in range(KC):
                        t_.wait_ge(gL.h, g_w[o])
                        wslot = o % 2
                        ps = pa if o % 2 == 0 else pb
                        if o >= 2:
                            t_.wait_ge(ac.h, zq_ev(o - 2))
                        for f in range(KC):
                            mm = t_.matmul(
                                ps[:, :],
                                wqg[:, wslot * DM + f * 128:wslot * DM + (f + 1) * 128],
                                og_tile[:, f * TILE:(f + 1) * TILE],
                                start=(f == 0), stop=(f == KC - 1))
                        pe.inc(mm)
                        slot = o % 2
                        t_.wait_ge(ac.h, zb_ev(o))
                        pe.inc(t_.matmul(pr0[0:1, 0:TILE], self.onesc_bf[:, :],
                                         zb_scr[:, slot * TILE:(slot + 1) * TILE],
                                         start=(o == 0), stop=(o == KC - 1)))
                        t_.wait_ge(ac.h, zq_ev(o))
                        pe.inc(t_.matmul(pr1[0:1, 0:TILE], self.onesc_bf[:, :],
                                         zq_scr[:, slot * TILE:(slot + 1) * TILE],
                                         start=(o == 0), stop=(o == KC - 1)))
                    if merge_b1:
                        tp1(t_)
                    t_.wait_ge(dv.h, d0 + 2)
                    t_.matmul(ps_kq[:, :], self.ones_f[:, :], self.nmrow[:, 0:TILE],
                              start=True, stop=True)
                    pe.inc(t_.matmul(pbc2[:, :], self.ones_f[:, :],
                                     self.rsrow[:, 0:TILE], start=True, stop=True))
                    t_.wait_ge(dv.h, d0 + 1)
                    tp2(t_)

                @blk.scalar
                def _(sc):
                    if t > 0 and 0 in self.st_x2:
                        sc.wait_ge(sT.h, self.st_x2[0])
                    for o in range(KC):
                        sc.wait_ge(pe.h, wo_ev(o))
                        ps = pa if o % 2 == 0 else pb
                        slot = o % 2
                        ac.inc(sc.activation(z_sb[:, o * TILE:(o + 1) * TILE],
                                             ps[:, :], AF.Copy))
                        if o >= 2:
                            sc.wait_ge(pe.h, zstat_ev(o - 2))
                        ac.inc(sc.activation(zb_scr[:, slot * TILE:(slot + 1) * TILE],
                                             ps[:, :], AF.Copy))
                        ac.inc(sc.activation(zq_scr[:, slot * TILE:(slot + 1) * TILE],
                                             ps[:, :], AF.Square))
                    sc.wait_ge(dv.h, d0 + 1)
                    ac.inc(sc.activation(self.r_t2[0:1, 0:TILE],
                                         self.r_t1[0:1, 0:TILE], AF.Sqrt))
                    if merge_b1:
                        sp1(sc)
                    sc.wait_ge(gL.h, g_acc[0])
                    sp2(sc)

                @blk.vector
                def _(v):
                    v.wait_ge(pe.h, zstat_ev(KC - 1))
                    self.vstep(v, v.tensor_scalar(
                        self.nmrow[0:1, 0:TILE], pr0[0:1, 0:TILE],
                        -1.0 / DM, None, ALU.mult))
                    self.vstep(v, v.tensor_scalar(
                        self.r_t1[0:1, 0:TILE], pr1[0:1, 0:TILE],
                        1.0 / DM, EPS_LN, ALU.mult, ALU.add))
                    self.vstep(v, v.tensor_mul(
                        self.r_t2[0:1, 0:TILE], self.nmrow[0:1, 0:TILE],
                        self.nmrow[0:1, 0:TILE]))
                    dv.inc(v.tensor_tensor(self.r_t1[0:1, 0:TILE],
                                           self.r_t1[0:1, 0:TILE],
                                           self.r_t2[0:1, 0:TILE], ALU.subtract))
                    v.wait_ge(ac.h, a0 + 3 * KC + 1)
                    dv.inc(v.reciprocal(self.rsrow[0:1, 0:TILE],
                                        self.r_t2[0:1, 0:TILE]))
                    v.wait_ge(pe.h, bc_ev)
                    for k in range(KC):
                        t1 = t1b[:, (k % 2) * TILE:(k % 2 + 1) * TILE]
                        self.vstep(v, v.tensor_tensor(
                            t1, z_sb[:, k * TILE:(k + 1) * TILE],
                            ps_kq[:, :], ALU.add))
                        last = v.tensor_tensor(z_sb[:, k * TILE:(k + 1) * TILE],
                                               t1, pbc2[:, :], ALU.mult)
                    dv.inc(last)
                    if self.debug:
                        v.wait_ge(sT.h, sT.v)
                    dv.inc(vp2(v))

                @blk.sync
                def _(sy):
                    sy.wait_ge(gL.h, g_acc[0])
                    self.st_x2[t] = sT.inc(sy.dma_start(
                        out=self.fmj(self.x2_sc)[:, :, t0:t0 + TILE],
                        in_=z_sb[:, :].rearrange("p (k t) -> p k t", t=TILE)), 16)
                    sy.wait_ge(dv.h, dv.v)
                    self.st_x2ln[t] = sT.inc(sy.dma_start(
                        out=self.fmj(self.x2ln_sc)[:, :, t0:t0 + TILE],
                        in_=og_tile[:, :].rearrange("p (k t) -> p k t", t=TILE)), 16)
                    if self.debug:
                        sT.inc(sy.dma_start(
                            out=self.fmj(self.dbg["dbg_x2"])[:, :, t0:t0 + TILE],
                            in_=z_sb[:, :].rearrange("p (k t) -> p k t", t=TILE)), 16)
            assert pe.v == p0 + 66 + nb1 and ac.v == a0 + 82 + nb1
            assert dv.v == d0 + 6

        b1_block(0, with_kv=True)
        if not self.lvl("B2"):
            return
        b2_block(0)
        b2_block(1)
        if not self.lvl("B3"):
            return
        b3_block(0, merge_b1=True)
        if not self.lvl("B"):
            return
        b2_block(2)
        b2_block(3)
        b3_block(1)

    def phase_c(self):
        nc = self.nc
        gL, hL, sT, pe, dv, ac = self.gL, self.hL, self.sT, self.pe, self.dv, self.ac
        self.lb.close()
        self.lab.close()
        lc = self.lc = ExitStack()
        x2_ln = self.sbuf("x2_ln", [128, KC * TCORE], BF16, lc)
        pa = self.psum("pca", [128, TILE], F32, lc)
        pb = self.psum("pcb", [128, TILE], F32, lc)
        pr0 = self.psum("pcr0", [1, TILE], F32, lc)
        pr1 = self.psum("pcr1", [1, TILE], F32, lc)

        def x2ln(k, t0, tw):
            return x2_ln[:, k * TCORE + t0:k * TCORE + t0 + tw]

        h_xln = []
        with nc.Block("C0") as blk:
            @blk.sync
            def _(sy):
                for t in range(NT):
                    sy.wait_ge(sT.h, self.st_x2ln[t])
                    h_xln.append(hL.inc(sy.dma_start(
                        out=x2_ln[:, :].rearrange(
                            "p (k t) -> p k t", t=TCORE)[:, :, t * TILE:
                                                         (t + 1) * TILE],
                        in_=self.fmj(self.x2ln_sc)[:, :, t * TILE:
                                                   (t + 1) * TILE]), 16))

        if not self.lvl("C2"):
            return

        mid = self.sbuf("mid", [128, 32 * TCORE], BF16, lc)
        y_sb = self.sbuf("y_sb", [128, KC * TCORE], F32, lc)
        w0buf = self.sbuf("w0buf", [128, 2 * DM], BF16, lc)
        w1buf = self.sbuf("w1buf", [128, 2 * 32 * 128], BF16, lc)
        ubs = self.sbuf("ubs", [128, 2 * TILE], BF16, lc)
        uqs = self.sbuf("uqs", [128, 2 * TILE], BF16, lc)
        t1d = self.sbuf("t1d", [128, 2 * TILE], F32, lc)

        for h2 in range(2):
            p0, a0, d0 = pe.v, ac.v, dv.v
            pW1 = p0 + 64
            g_w0, g_w1 = {}, {}
            with nc.Block(f"C2_{h2}") as blk:
                @blk.gpsimd
                def _(g, h2=h2):
                    for jl in range(32):
                        if jl >= 2:
                            g.wait_ge(pe.h, p0 + 2 * (jl - 1))
                        slot = jl % 2
                        jo = h2 * 32 + jl
                        g_w0[jl] = gL.inc(g.dma_start(
                            out=w0buf[:, slot * DM:(slot + 1) * DM]
                                .rearrange("p (k b) -> p k b", b=128),
                            in_=self.w0_d[jo * DM:(jo + 1) * DM, :]
                                .rearrange("(k a) b -> a k b", a=128)), 16)
                    for o in range(KC):
                        if o >= 2:
                            g.wait_ge(pe.h, pW1 + 2 * (o - 1))
                        slot = o % 2
                        base_r = h2 * FF * 8 + o * 32 * 128
                        g_w1[o] = gL.inc(g.dma_start(
                            out=w1buf[:, slot * 32 * 128:(slot + 1) * 32 * 128]
                                .rearrange("p (j b) -> p j b", b=128),
                            in_=self.w1_d[base_r:base_r + 32 * 128, :]
                                .rearrange("(j a) b -> a j b", a=128)), 16)

                @blk.tensor
                def _(t_, h2=h2):
                    for lv in h_xln:
                        t_.wait_ge(hL.h, lv)
                    for jl in range(32):
                        t_.wait_ge(gL.h, g_w0[jl])
                        wslot = jl % 2
                        for nt in range(2):
                            idx = 2 * jl + nt
                            ps = pa if idx % 2 == 0 else pb
                            if idx >= 2:
                                t_.wait_ge(ac.h, a0 + idx - 1)
                            for k in range(KC):
                                mm = t_.matmul(
                                    ps[:, :],
                                    w0buf[:, wslot * DM + k * 128:
                                          wslot * DM + (k + 1) * 128],
                                    x2ln(k, nt * TILE, TILE),
                                    start=(k == 0), stop=(k == KC - 1))
                            pe.inc(mm)
                    t_.wait_ge(ac.h, a0 + 64)
                    for o in range(KC):
                        t_.wait_ge(gL.h, g_w1[o])
                        wslot = o % 2
                        for nt in range(2):
                            idx = 2 * o + nt
                            ps = pa if idx % 2 == 0 else pb
                            if idx >= 2:
                                if h2 == 0:
                                    t_.wait_ge(ac.h, a0 + 64 + idx - 1)
                                else:
                                    t_.wait_ge(dv.h, d0 + idx - 1)
                            for jl in range(32):
                                mm = t_.matmul(
                                    ps[:, :],
                                    w1buf[:, wslot * 32 * 128 + jl * 128:
                                          wslot * 32 * 128 + (jl + 1) * 128],
                                    mid[:, jl * TCORE + nt * TILE:
                                        jl * TCORE + (nt + 1) * TILE],
                                    start=(jl == 0), stop=(jl == 31))
                            pe.inc(mm)

                @blk.scalar
                def _(sc, h2=h2):
                    for jl in range(32):
                        jo = h2 * 32 + jl
                        for nt in range(2):
                            idx = 2 * jl + nt
                            ps = pa if idx % 2 == 0 else pb
                            sc.wait_ge(pe.h, p0 + idx + 1)
                            ac.inc(sc.activation(
                                mid[:, jl * TCORE + nt * TILE:
                                    jl * TCORE + (nt + 1) * TILE],
                                ps[:, :], AF.Gelu, bias=self.b0_sb[:, jo:jo + 1]))
                    if h2 == 0:
                        for o in range(KC):
                            for nt in range(2):
                                idx = 2 * o + nt
                                ps = pa if idx % 2 == 0 else pb
                                sc.wait_ge(pe.h, pW1 + idx + 1)
                                ac.inc(sc.activation(
                                    y_sb[:, o * TCORE + nt * TILE:
                                         o * TCORE + (nt + 1) * TILE],
                                    ps[:, :], AF.Copy))

                if h2 == 1:
                    @blk.vector
                    def _(v):
                        for o in range(KC):
                            for nt in range(2):
                                idx = 2 * o + nt
                                ps = pa if idx % 2 == 0 else pb
                                v.wait_ge(pe.h, pW1 + idx + 1)
                                ys = y_sb[:, o * TCORE + nt * TILE:
                                          o * TCORE + (nt + 1) * TILE]
                                dv.inc(v.tensor_tensor(ys, ys, ps[:, :], ALU.add))

        if not self.lvl("C"):
            return
        pe0, ac0, dv0 = pe.v, ac.v, dv.v
        bases = []
        sps, tps, vps = [], [], []
        for nt in range(2):
            base = dict(pe=pe0 + 17 * nt, ac=ac0 + 33 * nt, dv=dv0 + 3 * nt)
            bases.append(base)
            lnres2 = dict(ubs=ubs, uqs=uqs, pa=pa, pb=pb, pr0=pr0, pr1=pr1, t1s=t1d)
            sp, tp, vp = self.ln_f32_parts(
                lnres2,
                lambda k, t0=nt * TILE: y_sb[:, k * TCORE + t0:k * TCORE + t0 + TILE],
                TILE, base,
                bias_col=lambda k: self.b1_sb[:, k:k + 1])
            sps.append(sp)
            tps.append(tp)
            vps.append(vp)
        g_acc = {}
        with nc.Block("C3") as blk:
            @blk.gpsimd
            def _(g):
                for nt in range(2):
                    g.wait_ge(dv.h, dv0 + 3 * nt + 3)
                    t0 = nt * TILE
                    g_acc[nt] = gL.inc(g.dma_start(
                        out=y_sb[:, :].rearrange(
                            "p (k t) -> p k t", t=TCORE)[:, :, t0:t0 + TILE],
                        in_=self.fmj(self.x2_sc)[:, :, t0:t0 + TILE],
                        accum_op=ALU.add), 16)

            @blk.scalar
            def _(sc):
                sps[0](sc)
                sc.wait_ge(pe.h, pe0 + 17)
                sps[1](sc)

            @blk.tensor
            def _(t_):
                tps[0](t_)
                t_.wait_ge(dv.h, dv0 + 3)
                tps[1](t_)

            @blk.vector
            def _(v):
                dv.inc(vps[0](v))
                dv.inc(vps[1](v))

            @blk.sync
            def _(sy):
                for nt in range(2):
                    t0 = nt * TILE
                    sy.wait_ge(gL.h, g_acc[nt])
                    sT.inc(sy.dma_start(
                        out=self.fmj(self.out_d)[:, :, t0:t0 + TILE],
                        in_=y_sb[:, :].rearrange(
                            "p (k t) -> p k t", t=TCORE)[:, :, t0:t0 + TILE]), 16)
        assert pe.v == pe0 + 34 and ac.v == ac0 + 66 and dv.v == dv0 + 6

    def tail_block(self):
        nc = self.nc
        with nc.Block("tail") as blk:
            @blk.sync
            def _(sy):
                sy.wait_ge(self.sT.h, self.sT.v)
                if self.cc.v:
                    sy.wait_ge(self.cc.h, self.cc.v)

    LEVELS = ["S", "A1", "A2", "A", "B1", "Bkv", "B2", "B3", "B", "C1", "C2", "C"]

    def lvl(self, name):
        return self.LEVELS.index(self.upto) >= self.LEVELS.index(name)

    def build(self):
        self.setup_block()
        if self.lvl("A1"):
            self.phase_a()
        if self.lvl("B1"):
            self.phase_b()
        if self.lvl("C1"):
            self.phase_c()
        self.tail_block()
        return self.nc


_CACHE = {}


def _blocked_lhsT(wT, n_out, n_k=KC):
    return np.ascontiguousarray(
        wT.reshape(n_k, 128, n_out, 128).transpose(2, 0, 1, 3).reshape(-1, 128))


def _prep(inputs):
    x = np.ascontiguousarray(np.asarray(inputs["L0_attn_x"], np.float32))
    Wqkv = np.asarray(inputs["L0_attn_Wqkv"], np.float32).reshape(H, 3, D, DM)
    Wq = Wqkv[:, 0].reshape(H * D, DM)
    Wk = Wqkv[:, 1].reshape(H * D, DM)
    Wv = Wqkv[:, 2].reshape(H * D, DM)
    Wg = np.asarray(inputs["L0_attn_Wg"], np.float32)
    Wo = np.asarray(inputs["L0_attn_Wo"], np.float32)
    W0 = np.asarray(inputs["L0_ffn_W0"], np.float32)
    W1 = np.asarray(inputs["L0_ffn_W1"], np.float32)

    d = {}
    d["wq"] = _blocked_lhsT(np.ascontiguousarray(Wq.T), KC)
    d["wk"] = _blocked_lhsT(np.ascontiguousarray(Wk.T), KC)
    d["wv"] = np.ascontiguousarray(Wv.T)
    d["wg"] = _blocked_lhsT(np.ascontiguousarray(Wg.T), KC)
    d["wo"] = _blocked_lhsT(np.ascontiguousarray(Wo.T), KC)
    d["w0"] = _blocked_lhsT(np.ascontiguousarray(W0.T), NJ)
    W1T = np.ascontiguousarray(W1.T).reshape(2, 32, 128, KC, 128)
    d["w1"] = np.ascontiguousarray(W1T.transpose(0, 3, 1, 2, 4).reshape(-1, 128))

    slope = np.asarray(inputs["L0_attn_slope_rate"], np.float32).reshape(1, H)
    d["slope"] = np.ascontiguousarray(np.broadcast_to(slope, (128, H)))
    arr = np.arange(1, CH + 1, dtype=np.float32)
    d["arrb"] = np.ascontiguousarray(np.broadcast_to(arr[None, :], (128, CH)))
    p = np.arange(128, dtype=np.float32)
    d["arrm"] = np.ascontiguousarray(
        np.stack([p + 1 - CH, p + 129 - CH], axis=1))
    mm, nn = np.meshgrid(np.arange(CH, dtype=np.float32),
                         np.arange(CH, dtype=np.float32))
    diff = mm - nn
    d["idxT"] = np.maximum(diff, 0.0).astype(np.float32)
    d["maskT"] = (diff >= 0).astype(np.float32)
    d["ident"] = np.eye(128, dtype=np.float32)
    d["onesr"] = np.ones((1, 128), np.float32)
    d["onesc"] = np.ones((128, 1), np.float32)
    d["b0c"] = np.ascontiguousarray(
        np.asarray(inputs["L0_ffn_bias0"], np.float32).reshape(NJ, 128).T)
    d["b1c"] = np.ascontiguousarray(
        np.asarray(inputs["L0_ffn_bias1"], np.float32).reshape(KC, 128).T)
    d["nwc"] = np.ascontiguousarray(
        np.asarray(inputs["L0_attn_norm_w"], np.float32).reshape(KC, 128).T)
    return x, d


def kernel(**inputs):
    key = (DEBUG, UPTO)
    if key not in _CACHE:
        _CACHE[key] = Builder(debug=DEBUG, upto=UPTO).build()
    nc = _CACHE[key]

    x, shared = _prep(inputs)
    in_maps = []
    for core in range(NCORES):
        b, qq = core // 4, core % 4
        m = dict(shared)
        m["x"] = np.ascontiguousarray(x[b, qq * TCORE:(qq + 1) * TCORE, :].T)
        sel = np.zeros((128, NCORES * 4), np.float32)
        for pidx in range(qq):
            r = b * 4 + pidx
            sel[:, r * 4 + (qq - 1 - pidx)] = 1.0
        m["sel"] = sel
        in_maps.append(m)

    res = run_bass_kernel_spmd(nc, in_maps, core_ids=list(range(NCORES)))
    out = np.empty((2, 4096, DM), np.float32)
    for core in range(NCORES):
        b, qq = core // 4, core % 4
        out[b, qq * TCORE:(qq + 1) * TCORE, :] = res.results[core]["out"].T
    return out


# version 09
# speedup vs baseline: 1.0130x; 1.0092x over previous
import numpy as np
import ml_dtypes
import concourse.bass as bass
import concourse.mybir as mybir
from concourse.bass_utils import run_bass_kernel_spmd
from contextlib import ExitStack

F32 = mybir.dt.float32
BF16 = mybir.dt.bfloat16
AF = mybir.ActivationFunctionType
ALU = mybir.AluOpType

H = 16
D = 128
DM = 2048
KC = 16
TCORE = 1024
TILE = 512
NT = 2
CH = 256
NCH = 4
FF = 8192
NJ = 64
NCORES = 8
EPS_LN = 1e-5
EPS_RMS = float(np.finfo(np.float32).eps)

DEBUG = 0
UPTO = "C"


class Sem:
    def __init__(self, nc, stack, name):
        self.h = stack.enter_context(nc.semaphore(name))
        self.v = 0

    def inc(self, instr, n=1):
        instr.then_inc(self.h, n)
        self.v += n
        return self.v


class Builder:
    def __init__(self, debug=0, upto="C"):
        self.debug = debug
        self.upto = upto
        nc = self.nc = bass.Bass("TRN2", target_bir_lowering=False)

        dram = lambda nm, shp, kind="ExternalInput", dt=F32: \
            nc.dram_tensor(nm, shp, dt, kind=kind).ap()
        self.x_d = dram("x", [DM, TCORE])
        self.wq_d = dram("wq", [KC * DM, D])
        self.wk_d = dram("wk", [KC * DM, D])
        self.wv_d = dram("wv", [DM, DM])
        self.wg_d = dram("wg", [KC * DM, D])
        self.wo_d = dram("wo", [KC * DM, D])
        self.w0_d = dram("w0", [NJ * DM, D])
        self.w1_d = dram("w1", [FF * KC, D])
        self.slope_d = dram("slope", [128, H])
        self.arr_d = dram("arrb", [128, CH])
        self.arrm_d = dram("arrm", [128, 2])
        self.idxT_d = dram("idxT", [CH, CH])
        self.mask_d = dram("maskT", [CH, CH])
        self.ident_d = dram("ident", [128, 128])
        self.ones_d = dram("onesr", [1, 128])
        self.onesc_d = dram("onesc", [128, 1])
        self.b0_d = dram("b0c", [128, NJ])
        self.b1_d = dram("b1c", [128, KC])
        self.nw_d = dram("nwc", [128, KC])
        self.sel_d = dram("sel", [128, NCORES * 4])
        self.out_d = dram("out", [DM, TCORE], kind="ExternalOutput")

        self.k_sc = dram("k_sc", [DM, TCORE], "Internal", BF16)
        self.v_sc = dram("v_sc", [TCORE, DM], "Internal", BF16)
        self.kvloc_sc = dram("kvloc_sc", [3 * H * D, D], "Internal", BF16)
        self.x2_sc = dram("x2_sc", [DM, TCORE], "Internal")
        self.x2ln_sc = dram("x2ln_sc", [DM, TCORE], "Internal", BF16)
        self.cc_in = dram("cc_in", [H * D, D], "Internal")
        self.cc_out = dram("cc_out", [NCORES * H * D, D], "Internal")

        self.dbg = {}
        if debug:
            for nm, shp in [("dbg_xln", [DM, TCORE]), ("dbg_T", [H * D, D]),
                            ("dbg_kvin", [H * D, D]), ("dbg_og", [DM, TCORE]),
                            ("dbg_x2", [DM, TCORE]), ("dbg_q", [DM, TILE]),
                            ("dbg_g", [DM, TILE]), ("dbg_ob", [DM, CH]),
                            ("dbg_rms", [1, CH])]:
                self.dbg[nm] = dram(nm, shp, kind="ExternalOutput")

        es = self.es = ExitStack()
        self.gL = Sem(nc, es, "gL")
        self.hL = Sem(nc, es, "hL")
        self.sT = Sem(nc, es, "sT")
        self.cc = Sem(nc, es, "ccs")
        self.pe = Sem(nc, es, "pes")
        self.dv = Sem(nc, es, "dvs")
        self.ac = Sem(nc, es, "acs")
        self.dvx = Sem(nc, es, "dvx")
        self.acx = Sem(nc, es, "acx")

        sbuf = self.sbuf
        self.id_bf = sbuf("id_bf", [128, 128], BF16)
        self.ones_bf = sbuf("ones_bf", [1, 128], BF16)
        self.ones_f = sbuf("ones_f", [1, 128], F32)
        self.onesc_bf = sbuf("onesc_bf", [128, 1], BF16)
        self.s_sb = sbuf("s_sb", [128, H], F32)
        self.negs_sb = sbuf("negs_sb", [128, H], F32)
        self.bd_sb = sbuf("bd_sb", [128, H], F32)
        self.bdp_sb = sbuf("bdp_sb", [128, 4 * H], F32)
        self.bd4p_sb = sbuf("bd4p_sb", [128, 4 * H], F32)
        self.u_sb = sbuf("u_sb", [128, NCORES * H], F32)
        self.ut_sb = sbuf("ut_sb", [128, 3 * H], F32)
        self.sel_sb = sbuf("sel_sb", [128, NCORES * 4], F32)
        self.arr_sb = sbuf("arr_sb", [128, CH], F32)
        self.arrm_sb = sbuf("arrm_sb", [128, 2], F32)
        self.kdec_sb = sbuf("kdec_sb", [128, 2 * H], F32)
        self.b0_sb = sbuf("b0_sb", [128, NJ], F32)
        self.b1_sb = sbuf("b1_sb", [128, KC], F32)
        self.nw_sb = sbuf("nw_sb", [128, KC], F32)
        self.row_sb = sbuf("row_sb", [1, 4 * TILE], F32)
        self.nmrow = self.row_sb[0:1, 0:TILE]
        self.rsrow = self.row_sb[0:1, TILE:2 * TILE]
        self.r_t1 = self.row_sb[0:1, 2 * TILE:3 * TILE]
        self.r_t2 = self.row_sb[0:1, 3 * TILE:4 * TILE]

        self.lab = ExitStack()
        self.qdec_sb = sbuf("qdec_sb", [128, H * CH], BF16, self.lab)
        self.diag_sb = sbuf("diag_sb", [128, H * 2 * CH], BF16, self.lab)
        self.xln_sb = sbuf("xln_sb", [128, KC * TCORE], BF16, self.lab)

        self.st_kvl = {}
        self.st_x2 = {}
        self.st_x2ln = {}

    def sbuf(self, nm, shp, dt, stk=None):
        return (stk or self.es).enter_context(self.nc.sbuf_tensor(nm, shp, dt))

    def psum(self, nm, shp, dt, stk=None):
        return (stk or self.es).enter_context(self.nc.psum_tensor(nm, shp, dt))

    def vstep(self, v, instr):
        self.dvx.inc(instr)
        v.wait_ge(self.dvx.h, self.dvx.v)
        return instr

    def sstep(self, sc, instr):
        self.acx.inc(instr)
        sc.wait_ge(self.acx.h, self.acx.v)
        return instr

    def xln(self, k, t0, tw):
        return self.xln_sb[:, k * TCORE + t0: k * TCORE + t0 + tw]

    @staticmethod
    def fmj(ap):
        return ap.rearrange("(k p) t -> p k t", p=128)

    @staticmethod
    def hdj(ap):
        return ap.rearrange("(h d) e -> d h e", d=128)

    def setup_block(self):
        nc = self.nc
        gL, hL, pe, dv, ac = self.gL, self.hL, self.pe, self.dv, self.ac
        ls = ExitStack()
        idx_sb = self.sbuf("idx_sb", [128, 2 * CH], F32, ls)
        mask_sb = self.sbuf("mask_sb", [128, 2 * CH], F32, ls)
        dscr = self.sbuf("dscr", [128, 2 * CH], F32, ls)
        d0, a0 = dv.v, ac.v
        diag_items = [(h, nh) for h in range(H) for nh in range(2)]
        diag_ac = {}

        with nc.Block("setup") as blk:
            @blk.sync
            def _(sy):
                for dst, src in [(self.s_sb, self.slope_d), (self.arr_sb, self.arr_d),
                                 (self.arrm_sb, self.arrm_d), (self.ones_f, self.ones_d),
                                 (self.b0_sb, self.b0_d), (self.b1_sb, self.b1_d),
                                 (self.nw_sb, self.nw_d), (self.sel_sb, self.sel_d)]:
                    hL.inc(sy.dma_start(out=dst[:, :], in_=src), 16)
                hL.inc(sy.dma_start(out=idx_sb[:, 0:CH], in_=self.idxT_d[0:128, :]), 16)
                hL.inc(sy.dma_start(out=idx_sb[:, CH:], in_=self.idxT_d[128:256, :]), 16)
                hL.inc(sy.dma_start(out=mask_sb[:, 0:CH], in_=self.mask_d[0:128, :]), 16)
                hL.inc(sy.dma_start(out=mask_sb[:, CH:], in_=self.mask_d[128:256, :]), 16)

            @blk.gpsimd
            def _(g):
                gL.inc(g.dma_start(out=self.id_bf[:, :], in_=self.ident_d), 16)
                gL.inc(g.dma_start(out=self.ones_bf[:, :], in_=self.ones_d), 16)
                gL.inc(g.dma_start(out=self.onesc_bf[:, :], in_=self.onesc_d), 16)

            @blk.scalar
            def _(sc):
                sc.wait_ge(hL.h, hL.v)
                sc.activation(self.bd_sb[:, :], self.s_sb[:, :], AF.Exp,
                              bias=0.0, scale=-float(CH))
                for j in range(4):
                    sc.activation(self.bdp_sb[:, j * H:(j + 1) * H], self.s_sb[:, :],
                                  AF.Exp, bias=0.0, scale=-float(CH * j))
                for j in range(4):
                    last = sc.activation(self.bd4p_sb[:, j * H:(j + 1) * H],
                                         self.s_sb[:, :], AF.Exp,
                                         bias=0.0, scale=-float(4 * CH * j))
                ac.inc(last)
                for h in range(H):
                    for half in range(2):
                        c = 2 * h + half
                        sc.activation(self.kdec_sb[:, c:c + 1],
                                      self.arrm_sb[:, half:half + 1], AF.Exp,
                                      bias=0.0, scale=self.s_sb[:, h:h + 1])
                sc.wait_ge(dv.h, d0 + 1)
                for h in range(H):
                    sc.activation(self.qdec_sb[:, h * CH:(h + 1) * CH],
                                  self.arr_sb[:, :], AF.Exp,
                                  bias=0.0, scale=self.negs_sb[:, h:h + 1])
                for i, (h, nh) in enumerate(diag_items):
                    if i >= 2:
                        sc.wait_ge(dv.h, d0 + 2 + (i - 2))
                    slot = i % 2
                    diag_ac[i] = ac.inc(sc.activation(
                        dscr[:, slot * CH:(slot + 1) * CH],
                        idx_sb[:, nh * CH:(nh + 1) * CH], AF.Exp,
                        bias=0.0, scale=self.negs_sb[:, h:h + 1]))

            @blk.vector
            def _(v):
                v.wait_ge(hL.h, hL.v)
                dv.inc(v.tensor_scalar_mul(self.negs_sb[:, :], self.s_sb[:, :], -1.0))
                v.wait_ge(ac.h, a0 + 1)
                bd4p = self.bd4p_sb
                for r in range(NCORES):
                    ur = self.u_sb[:, r * H:(r + 1) * H]
                    v.tensor_scalar(ur, bd4p[:, 0:H], self.sel_sb[:, 4 * r:4 * r + 1],
                                    None, ALU.mult)
                    for j in range(1, 4):
                        ts = v.tensor_scalar(self.ut_sb[:, (j - 1) * H:j * H],
                                             bd4p[:, j * H:(j + 1) * H],
                                             self.sel_sb[:, 4 * r + j:4 * r + j + 1],
                                             None, ALU.mult)
                    self.vstep(v, ts)
                    for j in range(1, 4):
                        self.vstep(v, v.tensor_add(
                            ur, ur, self.ut_sb[:, (j - 1) * H:j * H]))
                for i, (h, nh) in enumerate(diag_items):
                    v.wait_ge(ac.h, diag_ac[i])
                    slot = i % 2
                    dv.inc(v.tensor_mul(
                        self.diag_sb[:, (2 * h + nh) * CH:(2 * h + nh + 1) * CH],
                        dscr[:, slot * CH:(slot + 1) * CH],
                        mask_sb[:, nh * CH:(nh + 1) * CH]))

        assert dv.v == d0 + 1 + len(diag_items)
        assert ac.v == a0 + 1 + len(diag_items)
        ls.close()
        self.setup_hL = hL.v
        self.setup_gL = gL.v

    def ln_bf_parts(self, res, src_slice, dst_slice, n_tok, base):
        pe, dv, ac = self.pe, self.dv, self.ac
        p0, a0, d0 = base["pe"], base["ac"], base["dv"]

        def scalar_part(sc):
            for k in range(KC):
                if k >= 2:
                    sc.wait_ge(pe.h, p0 + (k - 1))
                slot = k % 2
                ac.inc(sc.activation(
                    res["sqs"][:, slot * n_tok:(slot + 1) * n_tok],
                    src_slice(k), AF.Square))
            sc.wait_ge(dv.h, d0 + 1)
            ac.inc(sc.activation(self.r_t2[:, 0:n_tok], self.r_t1[:, 0:n_tok],
                                 AF.Sqrt))

        def tensor_part(t):
            for k in range(KC):
                t.matmul(res["pr0"][0:1, 0:n_tok], self.onesc_bf[:, :], src_slice(k),
                         start=(k == 0), stop=(k == KC - 1))
                t.wait_ge(ac.h, a0 + k + 1)
                slot = k % 2
                pe.inc(t.matmul(res["pr1"][0:1, 0:n_tok], self.onesc_bf[:, :],
                                res["sqs"][:, slot * n_tok:(slot + 1) * n_tok],
                                start=(k == 0), stop=(k == KC - 1)))
            t.wait_ge(dv.h, d0 + 2)
            t.matmul(res["pa"][:, 0:n_tok], self.ones_f[:, :],
                     self.nmrow[:, 0:n_tok], start=True, stop=True)
            pe.inc(t.matmul(res["pb"][:, 0:n_tok], self.ones_f[:, :],
                            self.rsrow[:, 0:n_tok], start=True, stop=True))

        def vector_part(v):
            v.wait_ge(pe.h, p0 + KC)
            self.vstep(v, v.tensor_scalar(
                self.nmrow[:, 0:n_tok], res["pr0"][0:1, 0:n_tok],
                -1.0 / DM, None, ALU.mult))
            self.vstep(v, v.tensor_scalar(
                self.r_t1[:, 0:n_tok], res["pr1"][0:1, 0:n_tok],
                1.0 / DM, EPS_LN, ALU.mult, ALU.add))
            self.vstep(v, v.tensor_mul(self.r_t2[:, 0:n_tok],
                                       self.nmrow[:, 0:n_tok],
                                       self.nmrow[:, 0:n_tok]))
            dv.inc(v.tensor_tensor(self.r_t1[:, 0:n_tok], self.r_t1[:, 0:n_tok],
                                   self.r_t2[:, 0:n_tok], ALU.subtract))
            v.wait_ge(ac.h, a0 + KC + 1)
            dv.inc(v.reciprocal(self.rsrow[:, 0:n_tok], self.r_t2[:, 0:n_tok]))
            v.wait_ge(pe.h, p0 + KC + 1)
            for k in range(KC):
                t1 = res["t1s"][:, (k % 2) * n_tok:(k % 2 + 1) * n_tok]
                self.vstep(v, v.tensor_tensor(t1, src_slice(k),
                                              res["pa"][:, 0:n_tok], ALU.add))
                v.tensor_tensor(dst_slice(k), t1,
                                res["pb"][:, 0:n_tok], ALU.mult)

        return scalar_part, tensor_part, vector_part

    def ln_f32_parts(self, res, src_slice, n_tok, base, bias_col=None,
                     dst_slice=None):
        pe, dv, ac = self.pe, self.dv, self.ac
        p0, a0, d0 = base["pe"], base["ac"], base["dv"]

        def scalar_part(sc):
            for k in range(KC):
                if k >= 2:
                    sc.wait_ge(pe.h, p0 + (k - 1))
                slot = k % 2
                ub = res["ubs"][:, slot * n_tok:(slot + 1) * n_tok]
                uq = res["uqs"][:, slot * n_tok:(slot + 1) * n_tok]
                if bias_col is not None:
                    ac.inc(sc.activation(ub, src_slice(k), AF.Identity,
                                         bias=bias_col(k)))
                    ac.inc(sc.activation(uq, src_slice(k), AF.Square,
                                         bias=bias_col(k)))
                else:
                    ac.inc(sc.activation(ub, src_slice(k), AF.Copy))
                    ac.inc(sc.activation(uq, src_slice(k), AF.Square))
            sc.wait_ge(dv.h, d0 + 1)
            ac.inc(sc.activation(self.r_t2[:, 0:n_tok], self.r_t1[:, 0:n_tok],
                                 AF.Sqrt))

        def tensor_part(t):
            for k in range(KC):
                slot = k % 2
                ub = res["ubs"][:, slot * n_tok:(slot + 1) * n_tok]
                uq = res["uqs"][:, slot * n_tok:(slot + 1) * n_tok]
                t.wait_ge(ac.h, a0 + 2 * k + 1)
                t.matmul(res["pr0"][0:1, 0:n_tok], self.onesc_bf[:, :], ub,
                         start=(k == 0), stop=(k == KC - 1))
                t.wait_ge(ac.h, a0 + 2 * k + 2)
                pe.inc(t.matmul(res["pr1"][0:1, 0:n_tok], self.onesc_bf[:, :], uq,
                                start=(k == 0), stop=(k == KC - 1)))
            t.wait_ge(dv.h, d0 + 2)
            t.matmul(res["pa"][:, 0:n_tok], self.ones_f[:, :],
                     self.nmrow[:, 0:n_tok], start=True, stop=True)
            pe.inc(t.matmul(res["pb"][:, 0:n_tok], self.ones_f[:, :],
                            self.rsrow[:, 0:n_tok], start=True, stop=True))

        def vector_part(v):
            v.wait_ge(pe.h, p0 + KC)
            self.vstep(v, v.tensor_scalar(
                self.nmrow[:, 0:n_tok], res["pr0"][0:1, 0:n_tok],
                -1.0 / DM, None, ALU.mult))
            self.vstep(v, v.tensor_scalar(
                self.r_t1[:, 0:n_tok], res["pr1"][0:1, 0:n_tok],
                1.0 / DM, EPS_LN, ALU.mult, ALU.add))
            self.vstep(v, v.tensor_mul(self.r_t2[:, 0:n_tok],
                                       self.nmrow[:, 0:n_tok],
                                       self.nmrow[:, 0:n_tok]))
            dv.inc(v.tensor_tensor(self.r_t1[:, 0:n_tok], self.r_t1[:, 0:n_tok],
                                   self.r_t2[:, 0:n_tok], ALU.subtract))
            v.wait_ge(ac.h, a0 + 2 * KC + 1)
            dv.inc(v.reciprocal(self.rsrow[:, 0:n_tok], self.r_t2[:, 0:n_tok]))
            v.wait_ge(pe.h, p0 + KC + 1)
            for k in range(KC):
                t1 = res["t1s"][:, (k % 2) * n_tok:(k % 2 + 1) * n_tok]
                if bias_col is not None:
                    op = v.scalar_tensor_tensor(t1, src_slice(k), bias_col(k),
                                                res["pa"][:, 0:n_tok],
                                                ALU.add, ALU.add)
                else:
                    op = v.tensor_tensor(t1, src_slice(k),
                                         res["pa"][:, 0:n_tok], ALU.add)
                self.vstep(v, op)
                dst = dst_slice(k) if dst_slice is not None else src_slice(k)
                last = v.tensor_tensor(dst, t1,
                                       res["pb"][:, 0:n_tok], ALU.mult)
            return last

        return scalar_part, tensor_part, vector_part

    def phase_a(self):
        nc = self.nc
        gL, hL, sT, pe, dv, ac = self.gL, self.hL, self.sT, self.pe, self.dv, self.ac
        la = self.la = ExitStack()
        xbf = self.sbuf("xbf", [128, KC * TILE], BF16, la)
        sqs = self.sbuf("sqs", [128, 2 * TILE], BF16, la)
        t1s = self.sbuf("t1s", [128, 2 * TILE], F32, la)
        kfull = self.sbuf("kfull", [128, KC * TCORE], BF16, la)
        vfull = self.sbuf("vfull", [128, 8 * DM], BF16, la)
        wkbuf = self.sbuf("wkbuf", [128, 2 * DM], BF16, la)
        wvbuf = self.sbuf("wvbuf", [128, 2 * KC * 256], BF16, la)
        ktm = self.sbuf("ktm", [128, H * CH], BF16, la)
        kvc = self.sbuf("kvc", [128, H * D], F32, la)
        kvl = self.sbuf("kvl", [128, H * D], BF16, la)
        pa = self.psum("pa", [128, TILE], F32, la)
        pb = self.psum("pb", [128, TILE], F32, la)
        pr0 = self.psum("pr0", [1, TILE], F32, la)
        pr1 = self.psum("pr1", [1, TILE], F32, la)
        ptrA = self.psum("ptrA", [128, 128], BF16, la)
        ptrB = self.psum("ptrB", [128, 128], BF16, la)
        ps_s0 = self.psum("ps_s0", [128, D], F32, la)
        ps_s1 = self.psum("ps_s1", [128, D], F32, la)
        lnres = dict(sqs=sqs, pa=pa, pb=pb, pr0=pr0, pr1=pr1, t1s=t1s)

        for t in range(NT):
            t0 = t * TILE
            base = dict(pe=pe.v, ac=ac.v, dv=dv.v)
            sp, tp, vp = self.ln_bf_parts(
                lnres,
                lambda k: xbf[:, k * TILE:(k + 1) * TILE],
                lambda k, t0=t0: self.xln(k, t0, TILE),
                TILE, base)
            with nc.Block(f"A1_{t}") as blk:
                gx = []

                @blk.gpsimd
                def _(g, t0=t0):
                    gx.append(gL.inc(g.dma_start(
                        out=xbf[:, :].rearrange("p (k t) -> p k t", t=TILE),
                        in_=self.fmj(self.x_d)[:, :, t0:t0 + TILE]), 16))

                @blk.scalar
                def _(sc):
                    sc.wait_ge(gL.h, gx[0])
                    sp(sc)

                @blk.tensor
                def _(t_):
                    t_.wait_ge(gL.h, gx[0])
                    t_.wait_ge(gL.h, self.setup_gL)
                    t_.wait_ge(hL.h, self.setup_hL)
                    tp(t_)

                @blk.vector
                def _(v):
                    vp(v)
            assert pe.v == base["pe"] + 17 and ac.v == base["ac"] + 17
            assert dv.v == base["dv"] + 2
        if not self.lvl("A2"):
            return

        p0, a0 = pe.v, ac.v
        g_wk, g_wv = {}, {}
        with nc.Block("A2") as blk:
            @blk.gpsimd
            def _(g):
                for fo in range(KC):
                    if fo >= 2:
                        g.wait_ge(pe.h, p0 + 2 * (fo - 1))
                    slot = fo % 2
                    g_wk[fo] = gL.inc(g.dma_start(
                        out=wkbuf[:, slot * DM:(slot + 1) * DM]
                            .rearrange("p (k b) -> p k b", b=128),
                        in_=self.wk_d[fo * DM:(fo + 1) * DM, :]
                            .rearrange("(k a) b -> a k b", a=128)), 16)
                for n8 in range(8):
                    if n8 >= 2:
                        g.wait_ge(pe.h, p0 + 32 + 8 * (n8 - 1))
                    slot = n8 % 2
                    g_wv[n8] = gL.inc(g.dma_start(
                        out=wvbuf[:, slot * KC * 256:(slot + 1) * KC * 256]
                            .rearrange("p (k t) -> p k t", t=256),
                        in_=self.wv_d[:, n8 * 256:(n8 + 1) * 256]
                            .rearrange("(k a) t -> a k t", a=128)), 16)
                if self.debug:
                    sT.inc(g.dma_start(
                        out=self.fmj(self.dbg["dbg_xln"]),
                        in_=self.xln_sb[:, :]
                            .rearrange("p (k t) -> p k t", t=TCORE)), 16)

            @blk.tensor
            def _(t_):
                for fo in range(KC):
                    t_.wait_ge(gL.h, g_wk[fo])
                    wslot = fo % 2
                    for half in range(2):
                        idx = fo * 2 + half
                        ps = pa if idx % 2 == 0 else pb
                        if idx >= 2:
                            t_.wait_ge(ac.h, a0 + idx - 1)
                        for k in range(KC):
                            mm = t_.matmul(
                                ps[:, :],
                                wkbuf[:, wslot * DM + k * 128:wslot * DM + (k + 1) * 128],
                                self.xln(k, half * TILE, TILE),
                                start=(k == 0), stop=(k == KC - 1))
                        pe.inc(mm)
                for n8 in range(8):
                    t_.wait_ge(gL.h, g_wv[n8])
                    wslot = n8 % 2
                    for tc in range(8):
                        idx = n8 * 8 + tc
                        ps = pa if idx % 2 == 0 else pb
                        if idx >= 2:
                            t_.wait_ge(ac.h, a0 + 32 + idx - 1)
                        for k in range(KC):
                            mm = t_.matmul(
                                ps[:, 0:256],
                                self.xln(k, tc * 128, 128),
                                wvbuf[:, wslot * KC * 256 + k * 256:
                                      wslot * KC * 256 + (k + 1) * 256],
                                start=(k == 0), stop=(k == KC - 1))
                        pe.inc(mm)

            @blk.scalar
            def _(sc):
                for fo in range(KC):
                    for half in range(2):
                        idx = fo * 2 + half
                        ps = pa if idx % 2 == 0 else pb
                        sc.wait_ge(pe.h, p0 + idx + 1)
                        ac.inc(sc.activation(
                            kfull[:, fo * TCORE + half * TILE:
                                  fo * TCORE + (half + 1) * TILE],
                            ps[:, :], AF.Silu))
                for n8 in range(8):
                    for tc in range(8):
                        idx = n8 * 8 + tc
                        ps = pa if idx % 2 == 0 else pb
                        sc.wait_ge(pe.h, p0 + 32 + idx + 1)
                        ac.inc(sc.activation(
                            vfull[:, tc * DM + n8 * 256:tc * DM + (n8 + 1) * 256],
                            ps[:, 0:256], AF.Silu))

            @blk.sync
            def _(sy):
                sy.wait_ge(ac.h, a0 + 32)
                self.st_k = sT.inc(sy.dma_start(
                    out=self.fmj(self.k_sc),
                    in_=kfull[:, :].rearrange("p (f t) -> p f t", t=TCORE)), 16)
                sy.wait_ge(ac.h, a0 + 96)
                self.st_v = sT.inc(sy.dma_start(
                    out=self.v_sc.rearrange("(tc p) e -> p tc e", p=128),
                    in_=vfull[:, :].rearrange("p (tc e) -> p tc e", e=DM)), 16)

        assert pe.v == p0 + 96 and ac.v == a0 + 96
        if not self.lvl("A"):
            return

        sT0 = sT.v
        with nc.Block("A3") as blk:
            pe0_by_cl = {}
            dv0_by_cl = {}
            p = pe.v
            d = dv.v
            for cl in range(NCH):
                pe0_by_cl[cl] = p
                dv0_by_cl[cl] = d
                p += 48
                d += 48 + (1 if cl < NCH - 1 else 0)
            dv_final = d

            @blk.tensor
            def _(t_):
                for cl in range(NCH):
                    p0c, d0c = pe0_by_cl[cl], dv0_by_cl[cl]
                    for i in range(2 * H):
                        h, nh = i // 2, i % 2
                        if i >= 2:
                            t_.wait_ge(dv.h, d0c + (i - 2) + 1)
                        ptr = ptrA if i % 2 == 0 else ptrB
                        pe.inc(t_.transpose(
                            ptr[:, :],
                            kfull[:, h * TCORE + cl * CH + nh * 128:
                                  h * TCORE + cl * CH + (nh + 1) * 128],
                            self.id_bf[:, :]))
                    for h in range(H):
                        t_.wait_ge(dv.h, d0c + 2 * h + 2)
                        if h >= 2:
                            t_.wait_ge(dv.h, d0c + 32 + (h - 2) + 1)
                        ps = ps_s0 if h % 2 == 0 else ps_s1
                        for nh in range(2):
                            tc = cl * 2 + nh
                            mm = t_.matmul(
                                ps[:, :],
                                ktm[:, (2 * h + nh) * 128:(2 * h + nh + 1) * 128],
                                vfull[:, tc * DM + h * D:tc * DM + (h + 1) * D],
                                start=(nh == 0), stop=(nh == 1))
                        pe.inc(mm)

            @blk.vector
            def _(v):
                for cl in range(NCH):
                    p0c, d0c = pe0_by_cl[cl], dv0_by_cl[cl]
                    for i in range(2 * H):
                        h, nh = i // 2, i % 2
                        v.wait_ge(pe.h, p0c + i + 1)
                        ptr = ptrA if i % 2 == 0 else ptrB
                        dv.inc(v.tensor_scalar(
                            ktm[:, i * 128:(i + 1) * 128],
                            ptr[:, :],
                            self.kdec_sb[:, 2 * h + nh:2 * h + nh + 1],
                            None, ALU.mult))
                    for h in range(H):
                        v.wait_ge(pe.h, p0c + 32 + h + 1)
                        ps = ps_s0 if h % 2 == 0 else ps_s1
                        kvch = kvc[:, h * D:(h + 1) * D]
                        if cl == 0:
                            dv.inc(v.tensor_copy(kvch, ps[:, :]))
                        else:
                            dv.inc(v.scalar_tensor_tensor(
                                kvch, kvch, self.bd_sb[:, h:h + 1], ps[:, :],
                                ALU.mult, ALU.add))
                    if cl < NCH - 1:
                        if cl > 0:
                            v.wait_ge(sT.h, sT0 + 16 * cl)
                        v.wait_ge(dv.h, dv.v)
                        dv.inc(v.tensor_copy(kvl[:, :], kvc[:, :]))

            @blk.sync
            def _(sy):
                for cl in range(NCH - 1):
                    sy.wait_ge(dv.h, dv0_by_cl[cl] + 49)
                    self.st_kvl[cl] = sT.inc(sy.dma_start(
                        out=self.hdj(self.kvloc_sc[cl * H * D:(cl + 1) * H * D, :]),
                        in_=kvl[:, :].rearrange("p (h e) -> p h e", e=D)), 16)
                sy.wait_ge(dv.h, dv_final)
                self.st_T = sT.inc(sy.dma_start(
                    out=self.hdj(self.cc_in),
                    in_=kvc[:, :].rearrange("p (h e) -> p h e", e=D)), 16)
                if self.debug:
                    sT.inc(sy.dma_start(
                        out=self.hdj(self.dbg["dbg_T"]),
                        in_=kvc[:, :].rearrange("p (h e) -> p h e", e=D)), 16)

            @blk.gpsimd
            def _(g):
                g.wait_ge(sT.h, self.st_T)
                self.cc.inc(g.collective_compute(
                    "AllGather", ALU.bypass,
                    replica_groups=[list(range(NCORES))],
                    ins=[self.cc_in.opt()],
                    outs=[self.cc_out.opt()]))

        assert pe.v == pe0_by_cl[NCH - 1] + 48
        assert dv.v == dv_final

    def phase_b(self):
        nc = self.nc
        gL, hL, sT, pe, dv, ac = self.gL, self.hL, self.sT, self.pe, self.dv, self.ac
        self.la.close()
        lb = self.lb = ExitStack()
        q_tile = self.sbuf("q_tile", [128, H * TILE], BF16, lb)
        g_tile = self.sbuf("g_tile", [128, H * TILE], BF16, lb)
        kv_in = self.sbuf("kv_in", [128, H * D], F32, lb)
        kv_bf = self.sbuf("kv_bf", [128, H * D], BF16, lb)
        kvl_in = self.sbuf("kvl_in", [128, H * D], BF16, lb)
        kch = self.sbuf("kch", [128, H * CH], BF16, lb)
        vch = self.sbuf("vch", [128, 2 * DM], BF16, lb)
        outbf = self.sbuf("outbf", [128, H * CH], BF16, lb)
        sqb = self.sbuf("sqb", [128, 2 * CH], BF16, lb)
        qd_scr = self.sbuf("qd_scr", [128, 2 * CH], BF16, lb)
        kqd_scr = self.sbuf("kqd_scr", [128, 2 * CH], BF16, lb)
        wqg = self.sbuf("wqg", [128, 2 * DM], BF16, lb)
        og_tile = self.sbuf("og_tile", [128, H * TILE], BF16, lb)
        z_sb = self.sbuf("z_sb", [128, KC * TILE], F32, lb)
        zb_scr = self.sbuf("zb_scr", [128, 2 * TILE], BF16, lb)
        zq_scr = self.sbuf("zq_scr", [128, 2 * TILE], BF16, lb)
        t1b = self.sbuf("t1b", [128, 2 * TILE], F32, lb)
        pa = self.psum("pba", [128, TILE], F32, lb)
        pb = self.psum("pbb", [128, TILE], F32, lb)
        ps_kq = self.psum("ps_kq", [128, TILE], F32, lb)
        ps_o0 = self.psum("ps_o0", [128, CH], F32, lb)
        ps_o1 = self.psum("ps_o1", [128, CH], F32, lb)
        pr0 = self.psum("pbr0", [1, TILE], F32, lb)
        pr1 = self.psum("pbr1", [1, TILE], F32, lb)
        pbc2 = self.psum("pbc2", [128, TILE], F32, lb)

        def b1_parts(t, p_base, a_base, g_w, init_ac_wait=None,
                     init_gp_waits=None):
            t0 = t * TILE

            def gp(g):
                for i, wd in enumerate([self.wq_d, self.wg_d]):
                    for fo in range(KC):
                        idx = i * KC + fo
                        if idx >= 2:
                            g.wait_ge(pe.h, p_base + idx - 1)
                        elif init_gp_waits is not None:
                            g.wait_ge(pe.h, init_gp_waits[idx])
                        slot = idx % 2
                        g_w[idx] = gL.inc(g.dma_start(
                            out=wqg[:, slot * DM:(slot + 1) * DM]
                                .rearrange("p (k b) -> p k b", b=128),
                            in_=wd[fo * DM:(fo + 1) * DM, :]
                                .rearrange("(k a) b -> a k b", a=128)), 16)

            def tp(t_):
                for idx in range(2 * KC):
                    t_.wait_ge(gL.h, g_w[idx])
                    wslot = idx % 2
                    ps = pa if idx % 2 == 0 else pb
                    if idx >= 2:
                        t_.wait_ge(ac.h, a_base + idx - 1)
                    elif init_ac_wait is not None:
                        t_.wait_ge(ac.h, init_ac_wait)
                    for k in range(KC):
                        mm = t_.matmul(
                            ps[:, :],
                            wqg[:, wslot * DM + k * 128:wslot * DM + (k + 1) * 128],
                            self.xln(k, t0, TILE),
                            start=(k == 0), stop=(k == KC - 1))
                    pe.inc(mm)

            def sp(sc):
                for idx in range(2 * KC):
                    fo = idx % KC
                    ps = pa if idx % 2 == 0 else pb
                    dst = q_tile if idx < KC else g_tile
                    fn = AF.Silu if idx < KC else AF.Sigmoid
                    sc.wait_ge(pe.h, p_base + idx + 1)
                    ac.inc(sc.activation(
                        dst[:, fo * TILE:(fo + 1) * TILE], ps[:, :], fn))

            return gp, tp, sp

        def b1_block(t, with_kv=False):
            p0, a0, d0 = pe.v, ac.v, dv.v
            g_w = {}
            h_T = {}
            gp, tp, sp = b1_parts(t, p0, a0, g_w)
            with nc.Block(f"B1_{t}") as blk:
                if with_kv:
                    @blk.sync
                    def _(sy):
                        sy.wait_ge(self.cc.h, self.cc.v)
                        for r in range(NCORES):
                            if r >= 1:
                                sy.wait_ge(dv.h, d0 + r)
                            h_T[r] = hL.inc(sy.dma_start(
                                out=z_sb[:, 0:H * D]
                                    .rearrange("p (h e) -> p h e", e=D),
                                in_=self.hdj(self.cc_out[r * H * D:
                                                         (r + 1) * H * D, :])), 16)
                        if self.debug:
                            sy.wait_ge(dv.h, d0 + NCORES)
                            sT.inc(sy.dma_start(
                                out=self.hdj(self.dbg["dbg_kvin"]),
                                in_=kv_in[:, :]
                                    .rearrange("p (h e) -> p h e", e=D)), 16)

                @blk.gpsimd
                def _(g):
                    if with_kv:
                        g.wait_ge(self.cc.h, self.cc.v)
                    gp(g)
                    if self.debug and t == 0:
                        g.wait_ge(ac.h, a0 + 2 * KC)
                        sT.inc(g.dma_start(
                            out=self.fmj(self.dbg["dbg_q"]),
                            in_=q_tile[:, :].rearrange(
                                "p (h t) -> p h t", t=TILE)), 16)
                        sT.inc(g.dma_start(
                            out=self.fmj(self.dbg["dbg_g"]),
                            in_=g_tile[:, :].rearrange(
                                "p (h t) -> p h t", t=TILE)), 16)

                @blk.tensor
                def _(t_):
                    tp(t_)

                @blk.scalar
                def _(sc):
                    sp(sc)

                if with_kv:
                    @blk.vector
                    def _(v):
                        stg = z_sb[:, H * D:2 * H * D]
                        v.memset(kv_in[:, :], 0.0)
                        for r in range(NCORES):
                            v.wait_ge(hL.h, h_T[r])
                            for h in range(H):
                                ts = v.tensor_scalar(
                                    stg[:, h * D:(h + 1) * D],
                                    z_sb[:, h * D:(h + 1) * D],
                                    self.u_sb[:, r * H + h:r * H + h + 1],
                                    None, ALU.mult)
                            self.vstep(v, ts)
                            tt = v.tensor_add(kv_in[:, :], kv_in[:, :], stg)
                            dv.inc(tt)
                            v.wait_ge(dv.h, dv.v)
            assert pe.v == p0 + 2 * KC and ac.v == a0 + 2 * KC
            assert not with_kv or dv.v == d0 + NCORES

        def b2_block(cl):
            t = cl // 2
            t0 = t * TILE
            co = (cl % 2) * CH
            p0, a0, d0 = pe.v, ac.v, dv.v
            loads = []
            with nc.Block(f"B2_{cl}") as blk:
                @blk.sync
                def _(sy):
                    sy.wait_ge(sT.h, self.st_k)
                    loads.append(hL.inc(sy.dma_start(
                        out=kch[:, :].rearrange("p (h t) -> p h t", t=CH),
                        in_=self.fmj(self.k_sc)[:, :, cl * CH:(cl + 1) * CH]), 16))
                    sy.wait_ge(sT.h, self.st_v)
                    loads.append(hL.inc(sy.dma_start(
                        out=vch[:, :].rearrange("p (tc e) -> p tc e", e=DM),
                        in_=self.v_sc[cl * CH:(cl + 1) * CH, :]
                            .rearrange("(tc p) e -> p tc e", p=128)), 16))
                    if cl > 0:
                        sy.wait_ge(sT.h, self.st_kvl[cl - 1])
                        loads.append(hL.inc(sy.dma_start(
                            out=kvl_in[:, :].rearrange("p (h e) -> p h e", e=D),
                            in_=self.hdj(self.kvloc_sc[(cl - 1) * H * D:
                                                       cl * H * D, :])), 16))

                qd_ev = lambda h: d0 + 2 * h + 2
                kqd_ev = lambda h: d0 + 2 * h + 3
                kq_ev = lambda h: p0 + 3 * h + 1
                out_ev = lambda h: p0 + 3 * h + 2
                stat_ev = lambda h: p0 + 3 * h + 3
                bc_ev = p0 + 49
                oev = lambda h: a0 + 2 * h + 1
                sq_ev = lambda h: a0 + 2 * h + 2

                @blk.tensor
                def _(t_):
                    for lv in loads:
                        t_.wait_ge(hL.h, lv)
                    t_.wait_ge(dv.h, d0 + 1)
                    for h in range(H):
                        if h >= 1:
                            t_.wait_ge(dv.h, kqd_ev(h - 1))
                        for nh in range(2):
                            mm = t_.matmul(
                                ps_kq[:, nh * CH:(nh + 1) * CH],
                                kch[:, h * CH + nh * 128:h * CH + (nh + 1) * 128],
                                q_tile[:, h * TILE + co:h * TILE + co + CH],
                                start=True, stop=True)
                        pe.inc(mm)
                        t_.wait_ge(dv.h, kqd_ev(h))
                        if h >= 2:
                            t_.wait_ge(ac.h, sq_ev(h - 2))
                        ps = ps_o0 if h % 2 == 0 else ps_o1
                        t_.matmul(ps[:, :], vch[:, 0 * DM + h * D:0 * DM + (h + 1) * D],
                                  kqd_scr[:, 0:CH], start=True, stop=False)
                        t_.matmul(ps[:, :], vch[:, 1 * DM + h * D:1 * DM + (h + 1) * D],
                                  kqd_scr[:, CH:2 * CH], start=False, stop=False)
                        qslot = h % 2
                        pe.inc(t_.matmul(ps[:, :], kv_bf[:, h * D:(h + 1) * D],
                                         qd_scr[:, qslot * CH:(qslot + 1) * CH],
                                         start=False, stop=True))
                        t_.wait_ge(ac.h, sq_ev(h))
                        slot = h % 2
                        pe.inc(t_.matmul(pr0[0:1, 0:CH], self.onesc_bf[:, :],
                                         sqb[:, slot * CH:(slot + 1) * CH],
                                         start=(h == 0), stop=(h == H - 1)))
                    t_.wait_ge(dv.h, d0 + 35)
                    t_.wait_ge(ac.h, oev(H - 1))
                    pe.inc(t_.matmul(ps_o0[:, :], self.ones_f[:, :],
                                     self.rsrow[:, 0:CH], start=True, stop=True))

                @blk.scalar
                def _(sc):
                    for h in range(H):
                        sc.wait_ge(pe.h, out_ev(h))
                        ps = ps_o0 if h % 2 == 0 else ps_o1
                        ac.inc(sc.activation(outbf[:, h * CH:(h + 1) * CH],
                                             ps[:, :], AF.Copy))
                        if h >= 2:
                            sc.wait_ge(pe.h, stat_ev(h - 2))
                        slot = h % 2
                        ac.inc(sc.activation(sqb[:, slot * CH:(slot + 1) * CH],
                                             ps[:, :], AF.Square))
                    sc.wait_ge(dv.h, d0 + 34)
                    ac.inc(sc.activation(self.r_t2[0:1, 0:CH],
                                         self.r_t1[0:1, 0:CH], AF.Sqrt))

                @blk.vector
                def _(v):
                    if cl == 0:
                        dv.inc(v.tensor_copy(kv_bf[:, :], kv_in[:, :]))
                    else:
                        v.wait_ge(hL.h, loads[2])
                        for h in range(H):
                            last = v.scalar_tensor_tensor(
                                kv_bf[:, h * D:(h + 1) * D],
                                kv_in[:, h * D:(h + 1) * D],
                                self.bdp_sb[:, cl * H + h:cl * H + h + 1],
                                kvl_in[:, h * D:(h + 1) * D],
                                ALU.mult, ALU.add)
                        dv.inc(last)
                    for h in range(H):
                        qslot = h % 2
                        if h >= 2:
                            v.wait_ge(pe.h, out_ev(h - 2))
                        dv.inc(v.tensor_mul(
                            qd_scr[:, qslot * CH:(qslot + 1) * CH],
                            q_tile[:, h * TILE + co:h * TILE + co + CH],
                            self.qdec_sb[:, h * CH:(h + 1) * CH]))
                        v.wait_ge(pe.h, kq_ev(h))
                        v.tensor_mul(kqd_scr[:, 0:CH], ps_kq[:, 0:CH],
                                     self.diag_sb[:, (2 * h) * CH:(2 * h + 1) * CH])
                        dv.inc(v.tensor_mul(
                            kqd_scr[:, CH:2 * CH], ps_kq[:, CH:2 * CH],
                            self.diag_sb[:, (2 * h + 1) * CH:(2 * h + 2) * CH]))
                    v.wait_ge(pe.h, stat_ev(H - 1))
                    dv.inc(v.tensor_scalar(self.r_t1[0:1, 0:CH], pr0[0:1, 0:CH],
                                           1.0 / DM, EPS_RMS, ALU.mult, ALU.add))
                    v.wait_ge(ac.h, a0 + 33)
                    dv.inc(v.reciprocal(self.rsrow[0:1, 0:CH], self.r_t2[0:1, 0:CH]))
                    v.wait_ge(pe.h, bc_ev)
                    if cl == 2 and 0 in self.st_x2ln:
                        v.wait_ge(sT.h, self.st_x2ln[0])
                    for h in range(H):
                        self.vstep(v, v.scalar_tensor_tensor(
                            og_tile[:, h * TILE + co:h * TILE + co + CH],
                            outbf[:, h * CH:(h + 1) * CH],
                            self.nw_sb[:, h:h + 1],
                            ps_o0[:, :], ALU.mult, ALU.mult))
                        last = v.tensor_mul(
                            og_tile[:, h * TILE + co:h * TILE + co + CH],
                            og_tile[:, h * TILE + co:h * TILE + co + CH],
                            g_tile[:, h * TILE + co:h * TILE + co + CH])
                    dv.inc(last)

                if self.debug and cl == 0:
                    @blk.gpsimd
                    def _(g):
                        g.wait_ge(dv.h, dv.v)
                        sT.inc(g.dma_start(
                            out=self.fmj(self.dbg["dbg_ob"]),
                            in_=outbf[:, :].rearrange(
                                "p (h t) -> p h t", t=CH)), 16)
                        sT.inc(g.dma_start(
                            out=self.dbg["dbg_rms"],
                            in_=self.rsrow[0:1, 0:CH]), 16)

            assert pe.v == p0 + 49, (pe.v, p0)
            assert ac.v == a0 + 33
            assert dv.v == d0 + 36

        def b3_block(t, merge_b1=False):
            t0 = t * TILE
            p0, a0, d0 = pe.v, ac.v, dv.v
            g_w = {}
            g_w1 = {}
            g_acc = []
            nb1 = 32 if merge_b1 else 0
            bc_ev = p0 + 49
            ln2res = dict(ubs=zb_scr, uqs=zq_scr, pa=ps_kq, pb=pbc2,
                          pr0=pr0, pr1=pr1, t1s=t1b)
            ln2base = dict(pe=p0 + 49 + nb1, ac=a0 + 49 + nb1, dv=d0 + 3)
            sp2, tp2, vp2 = self.ln_f32_parts(
                ln2res,
                lambda k: z_sb[:, k * TILE:(k + 1) * TILE],
                TILE, ln2base,
                dst_slice=lambda k: og_tile[:, k * TILE:(k + 1) * TILE])
            if merge_b1:
                gp1, tp1, sp1 = b1_parts(1, p0 + 49, a0 + 49, g_w1,
                                         init_ac_wait=a0 + 48,
                                         init_gp_waits=(p0 + 3 * 14 + 1,
                                                        p0 + 3 * 15 + 1))
            wo_ev = lambda o: p0 + 3 * o + 1
            zstat_ev = lambda o: p0 + 3 * o + 3
            z_ev = lambda o: a0 + 3 * o + 1
            zb_ev = lambda o: a0 + 3 * o + 2
            zq_ev = lambda o: a0 + 3 * o + 3
            with nc.Block(f"B3_{t}") as blk:
                @blk.gpsimd
                def _(g):
                    for o in range(KC):
                        if o >= 2:
                            g.wait_ge(pe.h, wo_ev(o - 2))
                        slot = o % 2
                        g_w[o] = gL.inc(g.dma_start(
                            out=wqg[:, slot * DM:(slot + 1) * DM]
                                .rearrange("p (f b) -> p f b", b=128),
                            in_=self.wo_d[o * DM:(o + 1) * DM, :]
                                .rearrange("(f a) b -> a f b", a=128)), 16)
                    if merge_b1:
                        gp1(g)
                    if self.debug:
                        sT.inc(g.dma_start(
                            out=self.fmj(self.dbg["dbg_og"])[:, :, t0:t0 + TILE],
                            in_=og_tile[:, :].rearrange("p (h t) -> p h t", t=TILE)),
                            16)
                    g.wait_ge(dv.h, d0 + 3)
                    g_acc.append(gL.inc(g.dma_start(
                        out=z_sb[:, :].rearrange("p (k t) -> p k t", t=TILE),
                        in_=self.fmj(self.x_d)[:, :, t0:t0 + TILE],
                        accum_op=ALU.add), 16))

                @blk.tensor
                def _(t_):
                    for o in range(KC):
                        t_.wait_ge(gL.h, g_w[o])
                        wslot = o % 2
                        ps = pa if o % 2 == 0 else pb
                        if o >= 2:
                            t_.wait_ge(ac.h, zq_ev(o - 2))
                        for f in range(KC):
                            mm = t_.matmul(
                                ps[:, :],
                                wqg[:, wslot * DM + f * 128:wslot * DM + (f + 1) * 128],
                                og_tile[:, f * TILE:(f + 1) * TILE],
                                start=(f == 0), stop=(f == KC - 1))
                        pe.inc(mm)
                        slot = o % 2
                        t_.wait_ge(ac.h, zb_ev(o))
                        pe.inc(t_.matmul(pr0[0:1, 0:TILE], self.onesc_bf[:, :],
                                         zb_scr[:, slot * TILE:(slot + 1) * TILE],
                                         start=(o == 0), stop=(o == KC - 1)))
                        t_.wait_ge(ac.h, zq_ev(o))
                        pe.inc(t_.matmul(pr1[0:1, 0:TILE], self.onesc_bf[:, :],
                                         zq_scr[:, slot * TILE:(slot + 1) * TILE],
                                         start=(o == 0), stop=(o == KC - 1)))
                    t_.wait_ge(dv.h, d0 + 2)
                    t_.matmul(ps_kq[:, :], self.ones_f[:, :], self.nmrow[:, 0:TILE],
                              start=True, stop=True)
                    pe.inc(t_.matmul(pbc2[:, :], self.ones_f[:, :],
                                     self.rsrow[:, 0:TILE], start=True, stop=True))
                    if merge_b1:
                        tp1(t_)
                    t_.wait_ge(dv.h, d0 + 1)
                    tp2(t_)

                @blk.scalar
                def _(sc):
                    if t > 0 and 0 in self.st_x2:
                        sc.wait_ge(sT.h, self.st_x2[0])
                    for o in range(KC):
                        sc.wait_ge(pe.h, wo_ev(o))
                        ps = pa if o % 2 == 0 else pb
                        slot = o % 2
                        ac.inc(sc.activation(z_sb[:, o * TILE:(o + 1) * TILE],
                                             ps[:, :], AF.Copy))
                        if o >= 2:
                            sc.wait_ge(pe.h, zstat_ev(o - 2))
                        ac.inc(sc.activation(zb_scr[:, slot * TILE:(slot + 1) * TILE],
                                             ps[:, :], AF.Copy))
                        ac.inc(sc.activation(zq_scr[:, slot * TILE:(slot + 1) * TILE],
                                             ps[:, :], AF.Square))
                    sc.wait_ge(dv.h, d0 + 1)
                    ac.inc(sc.activation(self.r_t2[0:1, 0:TILE],
                                         self.r_t1[0:1, 0:TILE], AF.Sqrt))
                    if merge_b1:
                        sp1(sc)
                    sc.wait_ge(gL.h, g_acc[0])
                    sp2(sc)

                @blk.vector
                def _(v):
                    v.wait_ge(pe.h, zstat_ev(KC - 1))
                    self.vstep(v, v.tensor_scalar(
                        self.nmrow[0:1, 0:TILE], pr0[0:1, 0:TILE],
                        -1.0 / DM, None, ALU.mult))
                    self.vstep(v, v.tensor_scalar(
                        self.r_t1[0:1, 0:TILE], pr1[0:1, 0:TILE],
                        1.0 / DM, EPS_LN, ALU.mult, ALU.add))
                    self.vstep(v, v.tensor_mul(
                        self.r_t2[0:1, 0:TILE], self.nmrow[0:1, 0:TILE],
                        self.nmrow[0:1, 0:TILE]))
                    dv.inc(v.tensor_tensor(self.r_t1[0:1, 0:TILE],
                                           self.r_t1[0:1, 0:TILE],
                                           self.r_t2[0:1, 0:TILE], ALU.subtract))
                    v.wait_ge(ac.h, a0 + 3 * KC + 1)
                    dv.inc(v.reciprocal(self.rsrow[0:1, 0:TILE],
                                        self.r_t2[0:1, 0:TILE]))
                    v.wait_ge(pe.h, bc_ev)
                    for k in range(KC):
                        t1 = t1b[:, (k % 2) * TILE:(k % 2 + 1) * TILE]
                        self.vstep(v, v.tensor_tensor(
                            t1, z_sb[:, k * TILE:(k + 1) * TILE],
                            ps_kq[:, :], ALU.add))
                        last = v.tensor_tensor(z_sb[:, k * TILE:(k + 1) * TILE],
                                               t1, pbc2[:, :], ALU.mult)
                    dv.inc(last)
                    if self.debug:
                        v.wait_ge(sT.h, sT.v)
                    dv.inc(vp2(v))

                @blk.sync
                def _(sy):
                    sy.wait_ge(gL.h, g_acc[0])
                    self.st_x2[t] = sT.inc(sy.dma_start(
                        out=self.fmj(self.x2_sc)[:, :, t0:t0 + TILE],
                        in_=z_sb[:, :].rearrange("p (k t) -> p k t", t=TILE)), 16)
                    sy.wait_ge(dv.h, dv.v)
                    self.st_x2ln[t] = sT.inc(sy.dma_start(
                        out=self.fmj(self.x2ln_sc)[:, :, t0:t0 + TILE],
                        in_=og_tile[:, :].rearrange("p (k t) -> p k t", t=TILE)), 16)
                    if self.debug:
                        sT.inc(sy.dma_start(
                            out=self.fmj(self.dbg["dbg_x2"])[:, :, t0:t0 + TILE],
                            in_=z_sb[:, :].rearrange("p (k t) -> p k t", t=TILE)), 16)
            assert pe.v == p0 + 66 + nb1 and ac.v == a0 + 82 + nb1
            assert dv.v == d0 + 6

        b1_block(0, with_kv=True)
        if not self.lvl("B2"):
            return
        b2_block(0)
        b2_block(1)
        if not self.lvl("B3"):
            return
        b3_block(0, merge_b1=True)
        if not self.lvl("B"):
            return
        b2_block(2)
        b2_block(3)
        b3_block(1)

    def phase_c(self):
        nc = self.nc
        gL, hL, sT, pe, dv, ac = self.gL, self.hL, self.sT, self.pe, self.dv, self.ac
        self.lb.close()
        self.lab.close()
        lc = self.lc = ExitStack()
        x2_ln = self.sbuf("x2_ln", [128, KC * TCORE], BF16, lc)
        pa = self.psum("pca", [128, TILE], F32, lc)
        pb = self.psum("pcb", [128, TILE], F32, lc)
        pr0 = self.psum("pcr0", [1, TILE], F32, lc)
        pr1 = self.psum("pcr1", [1, TILE], F32, lc)

        def x2ln(k, t0, tw):
            return x2_ln[:, k * TCORE + t0:k * TCORE + t0 + tw]

        h_xln = []
        with nc.Block("C0") as blk:
            @blk.sync
            def _(sy):
                for t in range(NT):
                    sy.wait_ge(sT.h, self.st_x2ln[t])
                    h_xln.append(hL.inc(sy.dma_start(
                        out=x2_ln[:, :].rearrange(
                            "p (k t) -> p k t", t=TCORE)[:, :, t * TILE:
                                                         (t + 1) * TILE],
                        in_=self.fmj(self.x2ln_sc)[:, :, t * TILE:
                                                   (t + 1) * TILE]), 16))

        if not self.lvl("C2"):
            return

        mid = self.sbuf("mid", [128, 32 * TCORE], BF16, lc)
        y_sb = self.sbuf("y_sb", [128, KC * TCORE], F32, lc)
        w0buf = self.sbuf("w0buf", [128, 2 * DM], BF16, lc)
        w1buf = self.sbuf("w1buf", [128, 2 * 32 * 128], BF16, lc)
        ubs = self.sbuf("ubs", [128, 2 * TILE], BF16, lc)
        uqs = self.sbuf("uqs", [128, 2 * TILE], BF16, lc)
        t1d = self.sbuf("t1d", [128, 2 * TILE], F32, lc)

        for h2 in range(2):
            p0, a0, d0 = pe.v, ac.v, dv.v
            pW1 = p0 + 64
            g_w0, g_w1 = {}, {}
            with nc.Block(f"C2_{h2}") as blk:
                @blk.gpsimd
                def _(g, h2=h2):
                    for jl in range(32):
                        if jl >= 2:
                            g.wait_ge(pe.h, p0 + 2 * (jl - 1))
                        slot = jl % 2
                        jo = h2 * 32 + jl
                        g_w0[jl] = gL.inc(g.dma_start(
                            out=w0buf[:, slot * DM:(slot + 1) * DM]
                                .rearrange("p (k b) -> p k b", b=128),
                            in_=self.w0_d[jo * DM:(jo + 1) * DM, :]
                                .rearrange("(k a) b -> a k b", a=128)), 16)
                    for o in range(KC):
                        if o >= 2:
                            g.wait_ge(pe.h, pW1 + 2 * (o - 1))
                        slot = o % 2
                        base_r = h2 * FF * 8 + o * 32 * 128
                        g_w1[o] = gL.inc(g.dma_start(
                            out=w1buf[:, slot * 32 * 128:(slot + 1) * 32 * 128]
                                .rearrange("p (j b) -> p j b", b=128),
                            in_=self.w1_d[base_r:base_r + 32 * 128, :]
                                .rearrange("(j a) b -> a j b", a=128)), 16)

                @blk.tensor
                def _(t_, h2=h2):
                    for lv in h_xln:
                        t_.wait_ge(hL.h, lv)
                    for jl in range(32):
                        t_.wait_ge(gL.h, g_w0[jl])
                        wslot = jl % 2
                        for nt in range(2):
                            idx = 2 * jl + nt
                            ps = pa if idx % 2 == 0 else pb
                            if idx >= 2:
                                t_.wait_ge(ac.h, a0 + idx - 1)
                            for k in range(KC):
                                mm = t_.matmul(
                                    ps[:, :],
                                    w0buf[:, wslot * DM + k * 128:
                                          wslot * DM + (k + 1) * 128],
                                    x2ln(k, nt * TILE, TILE),
                                    start=(k == 0), stop=(k == KC - 1))
                            pe.inc(mm)
                    t_.wait_ge(ac.h, a0 + 64)
                    for o in range(KC):
                        t_.wait_ge(gL.h, g_w1[o])
                        wslot = o % 2
                        for nt in range(2):
                            idx = 2 * o + nt
                            ps = pa if idx % 2 == 0 else pb
                            if idx >= 2:
                                if h2 == 0:
                                    t_.wait_ge(ac.h, a0 + 64 + idx - 1)
                                else:
                                    t_.wait_ge(dv.h, d0 + idx - 1)
                            for jl in range(32):
                                mm = t_.matmul(
                                    ps[:, :],
                                    w1buf[:, wslot * 32 * 128 + jl * 128:
                                          wslot * 32 * 128 + (jl + 1) * 128],
                                    mid[:, jl * TCORE + nt * TILE:
                                        jl * TCORE + (nt + 1) * TILE],
                                    start=(jl == 0), stop=(jl == 31))
                            pe.inc(mm)

                @blk.scalar
                def _(sc, h2=h2):
                    for jl in range(32):
                        jo = h2 * 32 + jl
                        for nt in range(2):
                            idx = 2 * jl + nt
                            ps = pa if idx % 2 == 0 else pb
                            sc.wait_ge(pe.h, p0 + idx + 1)
                            ac.inc(sc.activation(
                                mid[:, jl * TCORE + nt * TILE:
                                    jl * TCORE + (nt + 1) * TILE],
                                ps[:, :], AF.Gelu, bias=self.b0_sb[:, jo:jo + 1]))
                    if h2 == 0:
                        for o in range(KC):
                            for nt in range(2):
                                idx = 2 * o + nt
                                ps = pa if idx % 2 == 0 else pb
                                sc.wait_ge(pe.h, pW1 + idx + 1)
                                ac.inc(sc.activation(
                                    y_sb[:, o * TCORE + nt * TILE:
                                         o * TCORE + (nt + 1) * TILE],
                                    ps[:, :], AF.Copy))

                if h2 == 1:
                    @blk.vector
                    def _(v):
                        for o in range(KC):
                            for nt in range(2):
                                idx = 2 * o + nt
                                ps = pa if idx % 2 == 0 else pb
                                v.wait_ge(pe.h, pW1 + idx + 1)
                                ys = y_sb[:, o * TCORE + nt * TILE:
                                          o * TCORE + (nt + 1) * TILE]
                                dv.inc(v.tensor_tensor(ys, ys, ps[:, :], ALU.add))

        if not self.lvl("C"):
            return
        pe0, ac0, dv0 = pe.v, ac.v, dv.v
        bases = []
        sps, tps, vps = [], [], []
        for nt in range(2):
            base = dict(pe=pe0 + 17 * nt, ac=ac0 + 33 * nt, dv=dv0 + 3 * nt)
            bases.append(base)
            lnres2 = dict(ubs=ubs, uqs=uqs, pa=pa, pb=pb, pr0=pr0, pr1=pr1, t1s=t1d)
            sp, tp, vp = self.ln_f32_parts(
                lnres2,
                lambda k, t0=nt * TILE: y_sb[:, k * TCORE + t0:k * TCORE + t0 + TILE],
                TILE, base,
                bias_col=lambda k: self.b1_sb[:, k:k + 1])
            sps.append(sp)
            tps.append(tp)
            vps.append(vp)
        g_acc = {}
        with nc.Block("C3") as blk:
            @blk.gpsimd
            def _(g):
                for nt in range(2):
                    g.wait_ge(dv.h, dv0 + 3 * nt + 3)
                    t0 = nt * TILE
                    g_acc[nt] = gL.inc(g.dma_start(
                        out=y_sb[:, :].rearrange(
                            "p (k t) -> p k t", t=TCORE)[:, :, t0:t0 + TILE],
                        in_=self.fmj(self.x2_sc)[:, :, t0:t0 + TILE],
                        accum_op=ALU.add), 16)

            @blk.scalar
            def _(sc):
                sps[0](sc)
                sc.wait_ge(pe.h, pe0 + 17)
                sps[1](sc)

            @blk.tensor
            def _(t_):
                tps[0](t_)
                t_.wait_ge(dv.h, dv0 + 3)
                tps[1](t_)

            @blk.vector
            def _(v):
                dv.inc(vps[0](v))
                dv.inc(vps[1](v))

            @blk.sync
            def _(sy):
                for nt in range(2):
                    t0 = nt * TILE
                    sy.wait_ge(gL.h, g_acc[nt])
                    sT.inc(sy.dma_start(
                        out=self.fmj(self.out_d)[:, :, t0:t0 + TILE],
                        in_=y_sb[:, :].rearrange(
                            "p (k t) -> p k t", t=TCORE)[:, :, t0:t0 + TILE]), 16)
        assert pe.v == pe0 + 34 and ac.v == ac0 + 66 and dv.v == dv0 + 6

    def tail_block(self):
        nc = self.nc
        with nc.Block("tail") as blk:
            @blk.sync
            def _(sy):
                sy.wait_ge(self.sT.h, self.sT.v)
                if self.cc.v:
                    sy.wait_ge(self.cc.h, self.cc.v)

    LEVELS = ["S", "A1", "A2", "A", "B1", "Bkv", "B2", "B3", "B", "C1", "C2", "C"]

    def lvl(self, name):
        return self.LEVELS.index(self.upto) >= self.LEVELS.index(name)

    def build(self):
        self.setup_block()
        if self.lvl("A1"):
            self.phase_a()
        if self.lvl("B1"):
            self.phase_b()
        if self.lvl("C1"):
            self.phase_c()
        self.tail_block()
        return self.nc


_CACHE = {}


def _blocked_lhsT(wT, n_out, n_k=KC):
    return np.ascontiguousarray(
        wT.reshape(n_k, 128, n_out, 128).transpose(2, 0, 1, 3).reshape(-1, 128))


def _prep(inputs):
    x = np.ascontiguousarray(np.asarray(inputs["L0_attn_x"], np.float32))
    Wqkv = np.asarray(inputs["L0_attn_Wqkv"], np.float32).reshape(H, 3, D, DM)
    Wq = Wqkv[:, 0].reshape(H * D, DM)
    Wk = Wqkv[:, 1].reshape(H * D, DM)
    Wv = Wqkv[:, 2].reshape(H * D, DM)
    Wg = np.asarray(inputs["L0_attn_Wg"], np.float32)
    Wo = np.asarray(inputs["L0_attn_Wo"], np.float32)
    W0 = np.asarray(inputs["L0_ffn_W0"], np.float32)
    W1 = np.asarray(inputs["L0_ffn_W1"], np.float32)

    d = {}
    d["wq"] = _blocked_lhsT(np.ascontiguousarray(Wq.T), KC)
    d["wk"] = _blocked_lhsT(np.ascontiguousarray(Wk.T), KC)
    d["wv"] = np.ascontiguousarray(Wv.T)
    d["wg"] = _blocked_lhsT(np.ascontiguousarray(Wg.T), KC)
    d["wo"] = _blocked_lhsT(np.ascontiguousarray(Wo.T), KC)
    d["w0"] = _blocked_lhsT(np.ascontiguousarray(W0.T), NJ)
    W1T = np.ascontiguousarray(W1.T).reshape(2, 32, 128, KC, 128)
    d["w1"] = np.ascontiguousarray(W1T.transpose(0, 3, 1, 2, 4).reshape(-1, 128))

    slope = np.asarray(inputs["L0_attn_slope_rate"], np.float32).reshape(1, H)
    d["slope"] = np.ascontiguousarray(np.broadcast_to(slope, (128, H)))
    arr = np.arange(1, CH + 1, dtype=np.float32)
    d["arrb"] = np.ascontiguousarray(np.broadcast_to(arr[None, :], (128, CH)))
    p = np.arange(128, dtype=np.float32)
    d["arrm"] = np.ascontiguousarray(
        np.stack([p + 1 - CH, p + 129 - CH], axis=1))
    mm, nn = np.meshgrid(np.arange(CH, dtype=np.float32),
                         np.arange(CH, dtype=np.float32))
    diff = mm - nn
    d["idxT"] = np.maximum(diff, 0.0).astype(np.float32)
    d["maskT"] = (diff >= 0).astype(np.float32)
    d["ident"] = np.eye(128, dtype=np.float32)
    d["onesr"] = np.ones((1, 128), np.float32)
    d["onesc"] = np.ones((128, 1), np.float32)
    d["b0c"] = np.ascontiguousarray(
        np.asarray(inputs["L0_ffn_bias0"], np.float32).reshape(NJ, 128).T)
    d["b1c"] = np.ascontiguousarray(
        np.asarray(inputs["L0_ffn_bias1"], np.float32).reshape(KC, 128).T)
    d["nwc"] = np.ascontiguousarray(
        np.asarray(inputs["L0_attn_norm_w"], np.float32).reshape(KC, 128).T)
    return x, d


def kernel(**inputs):
    key = (DEBUG, UPTO)
    if key not in _CACHE:
        _CACHE[key] = Builder(debug=DEBUG, upto=UPTO).build()
    nc = _CACHE[key]

    x, shared = _prep(inputs)
    in_maps = []
    for core in range(NCORES):
        b, qq = core // 4, core % 4
        m = dict(shared)
        m["x"] = np.ascontiguousarray(x[b, qq * TCORE:(qq + 1) * TCORE, :].T)
        sel = np.zeros((128, NCORES * 4), np.float32)
        for pidx in range(qq):
            r = b * 4 + pidx
            sel[:, r * 4 + (qq - 1 - pidx)] = 1.0
        m["sel"] = sel
        in_maps.append(m)

    res = run_bass_kernel_spmd(nc, in_maps, core_ids=list(range(NCORES)))
    out = np.empty((2, 4096, DM), np.float32)
    for core in range(NCORES):
        b, qq = core // 4, core % 4
        out[b, qq * TCORE:(qq + 1) * TCORE, :] = res.results[core]["out"].T
    return out


# version 10
# speedup vs baseline: 1.0193x; 1.0063x over previous
import numpy as np
import ml_dtypes
import concourse.bass as bass
import concourse.mybir as mybir
from concourse.bass_utils import run_bass_kernel_spmd
from contextlib import ExitStack

F32 = mybir.dt.float32
BF16 = mybir.dt.bfloat16
AF = mybir.ActivationFunctionType
ALU = mybir.AluOpType

H = 16
D = 128
DM = 2048
KC = 16
TCORE = 1024
TILE = 512
NT = 2
CH = 256
NCH = 4
FF = 8192
NJ = 64
NCORES = 8
EPS_LN = 1e-5
EPS_RMS = float(np.finfo(np.float32).eps)

DEBUG = 0
UPTO = "C"


class Sem:
    def __init__(self, nc, stack, name):
        self.h = stack.enter_context(nc.semaphore(name))
        self.v = 0

    def inc(self, instr, n=1):
        instr.then_inc(self.h, n)
        self.v += n
        return self.v


class Builder:
    def __init__(self, debug=0, upto="C"):
        self.debug = debug
        self.upto = upto
        nc = self.nc = bass.Bass("TRN2", target_bir_lowering=False)

        dram = lambda nm, shp, kind="ExternalInput", dt=F32: \
            nc.dram_tensor(nm, shp, dt, kind=kind).ap()
        self.x_d = dram("x", [DM, TCORE])
        self.wq_d = dram("wq", [KC * DM, D])
        self.wk_d = dram("wk", [KC * DM, D])
        self.wv_d = dram("wv", [DM, DM])
        self.wg_d = dram("wg", [KC * DM, D])
        self.wo_d = dram("wo", [KC * DM, D])
        self.w0_d = dram("w0", [NJ * DM, D])
        self.w1_d = dram("w1", [FF * KC, D])
        self.slope_d = dram("slope", [128, H])
        self.arr_d = dram("arrb", [128, CH])
        self.arrm_d = dram("arrm", [128, 2])
        self.idxT_d = dram("idxT", [CH, CH])
        self.mask_d = dram("maskT", [CH, CH])
        self.ident_d = dram("ident", [128, 128])
        self.ones_d = dram("onesr", [1, 128])
        self.onesc_d = dram("onesc", [128, 1])
        self.b0_d = dram("b0c", [128, NJ])
        self.b1_d = dram("b1c", [128, KC])
        self.nw_d = dram("nwc", [128, KC])
        self.sel_d = dram("sel", [128, NCORES * 4])
        self.out_d = dram("out", [DM, TCORE], kind="ExternalOutput")

        self.k_sc = dram("k_sc", [DM, TCORE], "Internal", BF16)
        self.v_sc = dram("v_sc", [TCORE, DM], "Internal", BF16)
        self.kvloc_sc = dram("kvloc_sc", [3 * H * D, D], "Internal", BF16)
        self.x2_sc = dram("x2_sc", [DM, TCORE], "Internal")
        self.cc_in = dram("cc_in", [H * D, D], "Internal")
        self.cc_out = dram("cc_out", [NCORES * H * D, D], "Internal")

        self.dbg = {}
        if debug == 2:
            self.dbg["dbg_x2ln"] = dram("dbg_x2ln", [DM, TCORE],
                                        kind="ExternalOutput")
            self.dbg["dbg_y"] = dram("dbg_y", [DM, TCORE],
                                     kind="ExternalOutput")
            self.dbg["dbg_mid"] = dram("dbg_mid", [32 * 128, TCORE],
                                       kind="ExternalOutput")
        if debug == 1:
            for nm, shp in [("dbg_xln", [DM, TCORE]), ("dbg_T", [H * D, D]),
                            ("dbg_kvin", [H * D, D]), ("dbg_og", [DM, TCORE]),
                            ("dbg_x2", [DM, TCORE]), ("dbg_q", [DM, TILE]),
                            ("dbg_g", [DM, TILE]), ("dbg_ob", [DM, CH]),
                            ("dbg_rms", [1, CH])]:
                self.dbg[nm] = dram(nm, shp, kind="ExternalOutput")

        es = self.es = ExitStack()
        self.gL = Sem(nc, es, "gL")
        self.hL = Sem(nc, es, "hL")
        self.sT = Sem(nc, es, "sT")
        self.cc = Sem(nc, es, "ccs")
        self.pe = Sem(nc, es, "pes")
        self.dv = Sem(nc, es, "dvs")
        self.ac = Sem(nc, es, "acs")
        self.dvx = Sem(nc, es, "dvx")
        self.acx = Sem(nc, es, "acx")

        sbuf = self.sbuf
        self.id_bf = sbuf("id_bf", [128, 128], BF16)
        self.ones_bf = sbuf("ones_bf", [1, 128], BF16)
        self.ones_f = sbuf("ones_f", [1, 128], F32)
        self.onesc_bf = sbuf("onesc_bf", [128, 1], BF16)
        self.s_sb = sbuf("s_sb", [128, H], F32)
        self.negs_sb = sbuf("negs_sb", [128, H], F32)
        self.bd_sb = sbuf("bd_sb", [128, H], F32)
        self.bdp_sb = sbuf("bdp_sb", [128, 4 * H], F32)
        self.bd4p_sb = sbuf("bd4p_sb", [128, 4 * H], F32)
        self.u_sb = sbuf("u_sb", [128, NCORES * H], F32)
        self.ut_sb = sbuf("ut_sb", [128, 3 * H], F32)
        self.sel_sb = sbuf("sel_sb", [128, NCORES * 4], F32)
        self.arr_sb = sbuf("arr_sb", [128, CH], F32)
        self.arrm_sb = sbuf("arrm_sb", [128, 2], F32)
        self.kdec_sb = sbuf("kdec_sb", [128, 2 * H], F32)
        self.b0_sb = sbuf("b0_sb", [128, NJ], F32)
        self.b1_sb = sbuf("b1_sb", [128, KC], F32)
        self.nw_sb = sbuf("nw_sb", [128, KC], F32)
        self.row_sb = sbuf("row_sb", [1, 4 * TILE], F32)
        self.nmrow = self.row_sb[0:1, 0:TILE]
        self.rsrow = self.row_sb[0:1, TILE:2 * TILE]
        self.r_t1 = self.row_sb[0:1, 2 * TILE:3 * TILE]
        self.r_t2 = self.row_sb[0:1, 3 * TILE:4 * TILE]

        self.lab_b = ExitStack()
        self.xln_sb = sbuf("xln_sb", [128, KC * TCORE], BF16, self.lab_b)
        self.lab = ExitStack()
        self.qdec_sb = sbuf("qdec_sb", [128, H * CH], BF16, self.lab)
        self.diag_sb = sbuf("diag_sb", [128, H * 2 * CH], BF16, self.lab)

        self.st_kvl = {}
        self.st_x2 = {}
        self.st_x2ln = {}

    def sbuf(self, nm, shp, dt, stk=None):
        return (stk or self.es).enter_context(self.nc.sbuf_tensor(nm, shp, dt))

    def psum(self, nm, shp, dt, stk=None):
        return (stk or self.es).enter_context(self.nc.psum_tensor(nm, shp, dt))

    def vstep(self, v, instr):
        self.dvx.inc(instr)
        v.wait_ge(self.dvx.h, self.dvx.v)
        return instr

    def sstep(self, sc, instr):
        self.acx.inc(instr)
        sc.wait_ge(self.acx.h, self.acx.v)
        return instr

    def xln(self, k, t0, tw):
        return self.xln_sb[:, k * TCORE + t0: k * TCORE + t0 + tw]

    @staticmethod
    def fmj(ap):
        return ap.rearrange("(k p) t -> p k t", p=128)

    @staticmethod
    def hdj(ap):
        return ap.rearrange("(h d) e -> d h e", d=128)

    def setup_block(self):
        nc = self.nc
        gL, hL, pe, dv, ac = self.gL, self.hL, self.pe, self.dv, self.ac
        ls = ExitStack()
        idx_sb = self.sbuf("idx_sb", [128, 2 * CH], F32, ls)
        mask_sb = self.sbuf("mask_sb", [128, 2 * CH], F32, ls)
        dscr = self.sbuf("dscr", [128, 2 * CH], F32, ls)
        d0, a0 = dv.v, ac.v
        diag_items = [(h, nh) for h in range(H) for nh in range(2)]
        diag_ac = {}

        with nc.Block("setup") as blk:
            @blk.sync
            def _(sy):
                for dst, src in [(self.s_sb, self.slope_d), (self.arr_sb, self.arr_d),
                                 (self.arrm_sb, self.arrm_d), (self.ones_f, self.ones_d),
                                 (self.b0_sb, self.b0_d), (self.b1_sb, self.b1_d),
                                 (self.nw_sb, self.nw_d), (self.sel_sb, self.sel_d)]:
                    hL.inc(sy.dma_start(out=dst[:, :], in_=src), 16)
                hL.inc(sy.dma_start(out=idx_sb[:, 0:CH], in_=self.idxT_d[0:128, :]), 16)
                hL.inc(sy.dma_start(out=idx_sb[:, CH:], in_=self.idxT_d[128:256, :]), 16)
                hL.inc(sy.dma_start(out=mask_sb[:, 0:CH], in_=self.mask_d[0:128, :]), 16)
                hL.inc(sy.dma_start(out=mask_sb[:, CH:], in_=self.mask_d[128:256, :]), 16)

            @blk.gpsimd
            def _(g):
                gL.inc(g.dma_start(out=self.id_bf[:, :], in_=self.ident_d), 16)
                gL.inc(g.dma_start(out=self.ones_bf[:, :], in_=self.ones_d), 16)
                gL.inc(g.dma_start(out=self.onesc_bf[:, :], in_=self.onesc_d), 16)

            @blk.scalar
            def _(sc):
                sc.wait_ge(hL.h, hL.v)
                sc.activation(self.bd_sb[:, :], self.s_sb[:, :], AF.Exp,
                              bias=0.0, scale=-float(CH))
                for j in range(4):
                    sc.activation(self.bdp_sb[:, j * H:(j + 1) * H], self.s_sb[:, :],
                                  AF.Exp, bias=0.0, scale=-float(CH * j))
                for j in range(4):
                    last = sc.activation(self.bd4p_sb[:, j * H:(j + 1) * H],
                                         self.s_sb[:, :], AF.Exp,
                                         bias=0.0, scale=-float(4 * CH * j))
                ac.inc(last)
                for h in range(H):
                    for half in range(2):
                        c = 2 * h + half
                        sc.activation(self.kdec_sb[:, c:c + 1],
                                      self.arrm_sb[:, half:half + 1], AF.Exp,
                                      bias=0.0, scale=self.s_sb[:, h:h + 1])
                sc.wait_ge(dv.h, d0 + 1)
                for h in range(H):
                    sc.activation(self.qdec_sb[:, h * CH:(h + 1) * CH],
                                  self.arr_sb[:, :], AF.Exp,
                                  bias=0.0, scale=self.negs_sb[:, h:h + 1])
                for i, (h, nh) in enumerate(diag_items):
                    if i >= 2:
                        sc.wait_ge(dv.h, d0 + 2 + (i - 2))
                    slot = i % 2
                    diag_ac[i] = ac.inc(sc.activation(
                        dscr[:, slot * CH:(slot + 1) * CH],
                        idx_sb[:, nh * CH:(nh + 1) * CH], AF.Exp,
                        bias=0.0, scale=self.negs_sb[:, h:h + 1]))

            @blk.vector
            def _(v):
                v.wait_ge(hL.h, hL.v)
                dv.inc(v.tensor_scalar_mul(self.negs_sb[:, :], self.s_sb[:, :], -1.0))
                v.wait_ge(ac.h, a0 + 1)
                bd4p = self.bd4p_sb
                for r in range(NCORES):
                    ur = self.u_sb[:, r * H:(r + 1) * H]
                    v.tensor_scalar(ur, bd4p[:, 0:H], self.sel_sb[:, 4 * r:4 * r + 1],
                                    None, ALU.mult)
                    for j in range(1, 4):
                        ts = v.tensor_scalar(self.ut_sb[:, (j - 1) * H:j * H],
                                             bd4p[:, j * H:(j + 1) * H],
                                             self.sel_sb[:, 4 * r + j:4 * r + j + 1],
                                             None, ALU.mult)
                    self.vstep(v, ts)
                    for j in range(1, 4):
                        self.vstep(v, v.tensor_add(
                            ur, ur, self.ut_sb[:, (j - 1) * H:j * H]))
                for i, (h, nh) in enumerate(diag_items):
                    v.wait_ge(ac.h, diag_ac[i])
                    slot = i % 2
                    dv.inc(v.tensor_mul(
                        self.diag_sb[:, (2 * h + nh) * CH:(2 * h + nh + 1) * CH],
                        dscr[:, slot * CH:(slot + 1) * CH],
                        mask_sb[:, nh * CH:(nh + 1) * CH]))

        assert dv.v == d0 + 1 + len(diag_items)
        assert ac.v == a0 + 1 + len(diag_items)
        ls.close()
        self.setup_hL = hL.v
        self.setup_gL = gL.v

    def ln_bf_parts(self, res, src_slice, dst_slice, n_tok, base):
        pe, dv, ac = self.pe, self.dv, self.ac
        p0, a0, d0 = base["pe"], base["ac"], base["dv"]

        def scalar_part(sc):
            for k in range(KC):
                if k >= 2:
                    sc.wait_ge(pe.h, p0 + (k - 1))
                slot = k % 2
                ac.inc(sc.activation(
                    res["sqs"][:, slot * n_tok:(slot + 1) * n_tok],
                    src_slice(k), AF.Square))
            sc.wait_ge(dv.h, d0 + 1)
            ac.inc(sc.activation(self.r_t2[:, 0:n_tok], self.r_t1[:, 0:n_tok],
                                 AF.Sqrt))

        def tensor_part(t):
            for k in range(KC):
                t.matmul(res["pr0"][0:1, 0:n_tok], self.onesc_bf[:, :], src_slice(k),
                         start=(k == 0), stop=(k == KC - 1))
                t.wait_ge(ac.h, a0 + k + 1)
                slot = k % 2
                pe.inc(t.matmul(res["pr1"][0:1, 0:n_tok], self.onesc_bf[:, :],
                                res["sqs"][:, slot * n_tok:(slot + 1) * n_tok],
                                start=(k == 0), stop=(k == KC - 1)))
            t.wait_ge(dv.h, d0 + 2)
            t.matmul(res["pa"][:, 0:n_tok], self.ones_f[:, :],
                     self.nmrow[:, 0:n_tok], start=True, stop=True)
            pe.inc(t.matmul(res["pb"][:, 0:n_tok], self.ones_f[:, :],
                            self.rsrow[:, 0:n_tok], start=True, stop=True))

        def vector_part(v):
            v.wait_ge(pe.h, p0 + KC)
            self.vstep(v, v.tensor_scalar(
                self.nmrow[:, 0:n_tok], res["pr0"][0:1, 0:n_tok],
                -1.0 / DM, None, ALU.mult))
            self.vstep(v, v.tensor_scalar(
                self.r_t1[:, 0:n_tok], res["pr1"][0:1, 0:n_tok],
                1.0 / DM, EPS_LN, ALU.mult, ALU.add))
            self.vstep(v, v.tensor_mul(self.r_t2[:, 0:n_tok],
                                       self.nmrow[:, 0:n_tok],
                                       self.nmrow[:, 0:n_tok]))
            dv.inc(v.tensor_tensor(self.r_t1[:, 0:n_tok], self.r_t1[:, 0:n_tok],
                                   self.r_t2[:, 0:n_tok], ALU.subtract))
            v.wait_ge(ac.h, a0 + KC + 1)
            dv.inc(v.reciprocal(self.rsrow[:, 0:n_tok], self.r_t2[:, 0:n_tok]))
            v.wait_ge(pe.h, p0 + KC + 1)
            for k in range(KC):
                t1 = res["t1s"][:, (k % 2) * n_tok:(k % 2 + 1) * n_tok]
                self.vstep(v, v.tensor_tensor(t1, src_slice(k),
                                              res["pa"][:, 0:n_tok], ALU.add))
                v.tensor_tensor(dst_slice(k), t1,
                                res["pb"][:, 0:n_tok], ALU.mult)

        return scalar_part, tensor_part, vector_part

    def ln_f32_parts(self, res, src_slice, n_tok, base, bias_col=None,
                     dst_slice=None):
        pe, dv, ac = self.pe, self.dv, self.ac
        p0, a0, d0 = base["pe"], base["ac"], base["dv"]

        def scalar_part(sc):
            for k in range(KC):
                if k >= 2:
                    sc.wait_ge(pe.h, p0 + (k - 1))
                slot = k % 2
                ub = res["ubs"][:, slot * n_tok:(slot + 1) * n_tok]
                uq = res["uqs"][:, slot * n_tok:(slot + 1) * n_tok]
                if bias_col is not None:
                    ac.inc(sc.activation(ub, src_slice(k), AF.Identity,
                                         bias=bias_col(k)))
                    ac.inc(sc.activation(uq, src_slice(k), AF.Square,
                                         bias=bias_col(k)))
                else:
                    ac.inc(sc.activation(ub, src_slice(k), AF.Copy))
                    ac.inc(sc.activation(uq, src_slice(k), AF.Square))
            sc.wait_ge(dv.h, d0 + 1)
            ac.inc(sc.activation(self.r_t2[:, 0:n_tok], self.r_t1[:, 0:n_tok],
                                 AF.Sqrt))

        def tensor_part(t):
            for k in range(KC):
                slot = k % 2
                ub = res["ubs"][:, slot * n_tok:(slot + 1) * n_tok]
                uq = res["uqs"][:, slot * n_tok:(slot + 1) * n_tok]
                t.wait_ge(ac.h, a0 + 2 * k + 1)
                t.matmul(res["pr0"][0:1, 0:n_tok], self.onesc_bf[:, :], ub,
                         start=(k == 0), stop=(k == KC - 1))
                t.wait_ge(ac.h, a0 + 2 * k + 2)
                pe.inc(t.matmul(res["pr1"][0:1, 0:n_tok], self.onesc_bf[:, :], uq,
                                start=(k == 0), stop=(k == KC - 1)))
            t.wait_ge(dv.h, d0 + 2)
            t.matmul(res["pa"][:, 0:n_tok], self.ones_f[:, :],
                     self.nmrow[:, 0:n_tok], start=True, stop=True)
            pe.inc(t.matmul(res["pb"][:, 0:n_tok], self.ones_f[:, :],
                            self.rsrow[:, 0:n_tok], start=True, stop=True))

        def vector_part(v):
            v.wait_ge(pe.h, p0 + KC)
            self.vstep(v, v.tensor_scalar(
                self.nmrow[:, 0:n_tok], res["pr0"][0:1, 0:n_tok],
                -1.0 / DM, None, ALU.mult))
            self.vstep(v, v.tensor_scalar(
                self.r_t1[:, 0:n_tok], res["pr1"][0:1, 0:n_tok],
                1.0 / DM, EPS_LN, ALU.mult, ALU.add))
            self.vstep(v, v.tensor_mul(self.r_t2[:, 0:n_tok],
                                       self.nmrow[:, 0:n_tok],
                                       self.nmrow[:, 0:n_tok]))
            dv.inc(v.tensor_tensor(self.r_t1[:, 0:n_tok], self.r_t1[:, 0:n_tok],
                                   self.r_t2[:, 0:n_tok], ALU.subtract))
            v.wait_ge(ac.h, a0 + 2 * KC + 1)
            dv.inc(v.reciprocal(self.rsrow[:, 0:n_tok], self.r_t2[:, 0:n_tok]))
            v.wait_ge(pe.h, p0 + KC + 1)
            for k in range(KC):
                t1 = res["t1s"][:, (k % 2) * n_tok:(k % 2 + 1) * n_tok]
                if bias_col is not None:
                    op = v.scalar_tensor_tensor(t1, src_slice(k), bias_col(k),
                                                res["pa"][:, 0:n_tok],
                                                ALU.add, ALU.add)
                else:
                    op = v.tensor_tensor(t1, src_slice(k),
                                         res["pa"][:, 0:n_tok], ALU.add)
                self.vstep(v, op)
                dst = dst_slice(k) if dst_slice is not None else src_slice(k)
                last = v.tensor_tensor(dst, t1,
                                       res["pb"][:, 0:n_tok], ALU.mult)
            return last

        return scalar_part, tensor_part, vector_part

    def phase_a(self):
        nc = self.nc
        gL, hL, sT, pe, dv, ac = self.gL, self.hL, self.sT, self.pe, self.dv, self.ac
        la = self.la = ExitStack()
        xbf = self.sbuf("xbf", [128, KC * TILE], BF16, la)
        sqs = self.sbuf("sqs", [128, 2 * TILE], BF16, la)
        t1s = self.sbuf("t1s", [128, 2 * TILE], F32, la)
        kfull = self.sbuf("kfull", [128, KC * TCORE], BF16, la)
        vfull = self.sbuf("vfull", [128, 8 * DM], BF16, la)
        wkbuf = self.sbuf("wkbuf", [128, 2 * DM], BF16, la)
        wvbuf = self.sbuf("wvbuf", [128, 2 * KC * 256], BF16, la)
        ktm = self.sbuf("ktm", [128, H * CH], BF16, la)
        kvc = self.sbuf("kvc", [128, H * D], F32, la)
        kvl = self.sbuf("kvl", [128, H * D], BF16, la)
        pa = self.psum("pa", [128, TILE], F32, la)
        pb = self.psum("pb", [128, TILE], F32, la)
        pr0 = self.psum("pr0", [1, TILE], F32, la)
        pr1 = self.psum("pr1", [1, TILE], F32, la)
        ptrA = self.psum("ptrA", [128, 128], BF16, la)
        ptrB = self.psum("ptrB", [128, 128], BF16, la)
        ps_s0 = self.psum("ps_s0", [128, D], F32, la)
        ps_s1 = self.psum("ps_s1", [128, D], F32, la)
        lnres = dict(sqs=sqs, pa=pa, pb=pb, pr0=pr0, pr1=pr1, t1s=t1s)

        for t in range(NT):
            t0 = t * TILE
            base = dict(pe=pe.v, ac=ac.v, dv=dv.v)
            sp, tp, vp = self.ln_bf_parts(
                lnres,
                lambda k: xbf[:, k * TILE:(k + 1) * TILE],
                lambda k, t0=t0: self.xln(k, t0, TILE),
                TILE, base)
            with nc.Block(f"A1_{t}") as blk:
                gx = []

                @blk.gpsimd
                def _(g, t0=t0):
                    gx.append(gL.inc(g.dma_start(
                        out=xbf[:, :].rearrange("p (k t) -> p k t", t=TILE),
                        in_=self.fmj(self.x_d)[:, :, t0:t0 + TILE]), 16))

                @blk.scalar
                def _(sc):
                    sc.wait_ge(gL.h, gx[0])
                    sp(sc)

                @blk.tensor
                def _(t_):
                    t_.wait_ge(gL.h, gx[0])
                    t_.wait_ge(gL.h, self.setup_gL)
                    t_.wait_ge(hL.h, self.setup_hL)
                    tp(t_)

                @blk.vector
                def _(v):
                    vp(v)
            assert pe.v == base["pe"] + 17 and ac.v == base["ac"] + 17
            assert dv.v == base["dv"] + 2
        if not self.lvl("A2"):
            return

        p0, a0 = pe.v, ac.v
        g_wk, g_wv = {}, {}
        with nc.Block("A2") as blk:
            @blk.gpsimd
            def _(g):
                for fo in range(KC):
                    if fo >= 2:
                        g.wait_ge(pe.h, p0 + 2 * (fo - 1))
                    slot = fo % 2
                    g_wk[fo] = gL.inc(g.dma_start(
                        out=wkbuf[:, slot * DM:(slot + 1) * DM]
                            .rearrange("p (k b) -> p k b", b=128),
                        in_=self.wk_d[fo * DM:(fo + 1) * DM, :]
                            .rearrange("(k a) b -> a k b", a=128)), 16)
                for n8 in range(8):
                    if n8 >= 2:
                        g.wait_ge(pe.h, p0 + 32 + 8 * (n8 - 1))
                    slot = n8 % 2
                    g_wv[n8] = gL.inc(g.dma_start(
                        out=wvbuf[:, slot * KC * 256:(slot + 1) * KC * 256]
                            .rearrange("p (k t) -> p k t", t=256),
                        in_=self.wv_d[:, n8 * 256:(n8 + 1) * 256]
                            .rearrange("(k a) t -> a k t", a=128)), 16)
                if self.debug == 1:
                    sT.inc(g.dma_start(
                        out=self.fmj(self.dbg["dbg_xln"]),
                        in_=self.xln_sb[:, :]
                            .rearrange("p (k t) -> p k t", t=TCORE)), 16)

            @blk.tensor
            def _(t_):
                for fo in range(KC):
                    t_.wait_ge(gL.h, g_wk[fo])
                    wslot = fo % 2
                    for half in range(2):
                        idx = fo * 2 + half
                        ps = pa if idx % 2 == 0 else pb
                        if idx >= 2:
                            t_.wait_ge(ac.h, a0 + idx - 1)
                        for k in range(KC):
                            mm = t_.matmul(
                                ps[:, :],
                                wkbuf[:, wslot * DM + k * 128:wslot * DM + (k + 1) * 128],
                                self.xln(k, half * TILE, TILE),
                                start=(k == 0), stop=(k == KC - 1))
                        pe.inc(mm)
                for n8 in range(8):
                    t_.wait_ge(gL.h, g_wv[n8])
                    wslot = n8 % 2
                    for tc in range(8):
                        idx = n8 * 8 + tc
                        ps = pa if idx % 2 == 0 else pb
                        if idx >= 2:
                            t_.wait_ge(ac.h, a0 + 32 + idx - 1)
                        for k in range(KC):
                            mm = t_.matmul(
                                ps[:, 0:256],
                                self.xln(k, tc * 128, 128),
                                wvbuf[:, wslot * KC * 256 + k * 256:
                                      wslot * KC * 256 + (k + 1) * 256],
                                start=(k == 0), stop=(k == KC - 1))
                        pe.inc(mm)

            @blk.scalar
            def _(sc):
                for fo in range(KC):
                    for half in range(2):
                        idx = fo * 2 + half
                        ps = pa if idx % 2 == 0 else pb
                        sc.wait_ge(pe.h, p0 + idx + 1)
                        ac.inc(sc.activation(
                            kfull[:, fo * TCORE + half * TILE:
                                  fo * TCORE + (half + 1) * TILE],
                            ps[:, :], AF.Silu))
                for n8 in range(8):
                    for tc in range(8):
                        idx = n8 * 8 + tc
                        ps = pa if idx % 2 == 0 else pb
                        sc.wait_ge(pe.h, p0 + 32 + idx + 1)
                        ac.inc(sc.activation(
                            vfull[:, tc * DM + n8 * 256:tc * DM + (n8 + 1) * 256],
                            ps[:, 0:256], AF.Silu))

            @blk.sync
            def _(sy):
                sy.wait_ge(ac.h, a0 + 32)
                self.st_k = sT.inc(sy.dma_start(
                    out=self.fmj(self.k_sc),
                    in_=kfull[:, :].rearrange("p (f t) -> p f t", t=TCORE)), 16)
                sy.wait_ge(ac.h, a0 + 96)
                self.st_v = sT.inc(sy.dma_start(
                    out=self.v_sc.rearrange("(tc p) e -> p tc e", p=128),
                    in_=vfull[:, :].rearrange("p (tc e) -> p tc e", e=DM)), 16)

        assert pe.v == p0 + 96 and ac.v == a0 + 96
        if not self.lvl("A"):
            return

        sT0 = sT.v
        with nc.Block("A3") as blk:
            pe0_by_cl = {}
            dv0_by_cl = {}
            p = pe.v
            d = dv.v
            for cl in range(NCH):
                pe0_by_cl[cl] = p
                dv0_by_cl[cl] = d
                p += 48
                d += 48 + (1 if cl < NCH - 1 else 0)
            dv_final = d

            @blk.tensor
            def _(t_):
                for cl in range(NCH):
                    p0c, d0c = pe0_by_cl[cl], dv0_by_cl[cl]
                    for i in range(2 * H):
                        h, nh = i // 2, i % 2
                        if i >= 2:
                            t_.wait_ge(dv.h, d0c + (i - 2) + 1)
                        ptr = ptrA if i % 2 == 0 else ptrB
                        pe.inc(t_.transpose(
                            ptr[:, :],
                            kfull[:, h * TCORE + cl * CH + nh * 128:
                                  h * TCORE + cl * CH + (nh + 1) * 128],
                            self.id_bf[:, :]))
                    for h in range(H):
                        t_.wait_ge(dv.h, d0c + 2 * h + 2)
                        if h >= 2:
                            t_.wait_ge(dv.h, d0c + 32 + (h - 2) + 1)
                        ps = ps_s0 if h % 2 == 0 else ps_s1
                        for nh in range(2):
                            tc = cl * 2 + nh
                            mm = t_.matmul(
                                ps[:, :],
                                ktm[:, (2 * h + nh) * 128:(2 * h + nh + 1) * 128],
                                vfull[:, tc * DM + h * D:tc * DM + (h + 1) * D],
                                start=(nh == 0), stop=(nh == 1))
                        pe.inc(mm)

            @blk.vector
            def _(v):
                for cl in range(NCH):
                    p0c, d0c = pe0_by_cl[cl], dv0_by_cl[cl]
                    for i in range(2 * H):
                        h, nh = i // 2, i % 2
                        v.wait_ge(pe.h, p0c + i + 1)
                        ptr = ptrA if i % 2 == 0 else ptrB
                        dv.inc(v.tensor_scalar(
                            ktm[:, i * 128:(i + 1) * 128],
                            ptr[:, :],
                            self.kdec_sb[:, 2 * h + nh:2 * h + nh + 1],
                            None, ALU.mult))
                    for h in range(H):
                        v.wait_ge(pe.h, p0c + 32 + h + 1)
                        ps = ps_s0 if h % 2 == 0 else ps_s1
                        kvch = kvc[:, h * D:(h + 1) * D]
                        if cl == 0:
                            dv.inc(v.tensor_copy(kvch, ps[:, :]))
                        else:
                            dv.inc(v.scalar_tensor_tensor(
                                kvch, kvch, self.bd_sb[:, h:h + 1], ps[:, :],
                                ALU.mult, ALU.add))
                    if cl < NCH - 1:
                        if cl > 0:
                            v.wait_ge(sT.h, sT0 + 16 * cl)
                        v.wait_ge(dv.h, dv.v)
                        dv.inc(v.tensor_copy(kvl[:, :], kvc[:, :]))

            @blk.sync
            def _(sy):
                for cl in range(NCH - 1):
                    sy.wait_ge(dv.h, dv0_by_cl[cl] + 49)
                    self.st_kvl[cl] = sT.inc(sy.dma_start(
                        out=self.hdj(self.kvloc_sc[cl * H * D:(cl + 1) * H * D, :]),
                        in_=kvl[:, :].rearrange("p (h e) -> p h e", e=D)), 16)
                sy.wait_ge(dv.h, dv_final)
                self.st_T = sT.inc(sy.dma_start(
                    out=self.hdj(self.cc_in),
                    in_=kvc[:, :].rearrange("p (h e) -> p h e", e=D)), 16)
                if self.debug == 1:
                    sT.inc(sy.dma_start(
                        out=self.hdj(self.dbg["dbg_T"]),
                        in_=kvc[:, :].rearrange("p (h e) -> p h e", e=D)), 16)

            @blk.gpsimd
            def _(g):
                g.wait_ge(sT.h, self.st_T)
                self.cc.inc(g.collective_compute(
                    "AllGather", ALU.bypass,
                    replica_groups=[list(range(NCORES))],
                    ins=[self.cc_in.opt()],
                    outs=[self.cc_out.opt()]))

        assert pe.v == pe0_by_cl[NCH - 1] + 48
        assert dv.v == dv_final

    def phase_b(self):
        nc = self.nc
        gL, hL, sT, pe, dv, ac = self.gL, self.hL, self.sT, self.pe, self.dv, self.ac
        self.la.close()
        lb = self.lb = ExitStack()
        q_tile = self.sbuf("q_tile", [128, H * TILE], BF16, lb)
        g_tile = self.sbuf("g_tile", [128, H * TILE], BF16, lb)
        kv_in = self.sbuf("kv_in", [128, H * D], F32, lb)
        kv_bf = self.sbuf("kv_bf", [128, H * D], BF16, lb)
        kvl_in = self.sbuf("kvl_in", [128, H * D], BF16, lb)
        kch = self.sbuf("kch", [128, H * CH], BF16, lb)
        vch = self.sbuf("vch", [128, 2 * DM], BF16, lb)
        outbf = self.sbuf("outbf", [128, H * CH], BF16, lb)
        sqb = self.sbuf("sqb", [128, 2 * CH], BF16, lb)
        qd_scr = self.sbuf("qd_scr", [128, 2 * CH], BF16, lb)
        kqd_scr = self.sbuf("kqd_scr", [128, 2 * CH], BF16, lb)
        wqg = self.sbuf("wqg", [128, 2 * DM], BF16, lb)
        og_tile = self.sbuf("og_tile", [128, H * TILE], BF16, lb)
        z_sb = self.sbuf("z_sb", [128, KC * TILE], F32, lb)
        zb_scr = self.sbuf("zb_scr", [128, 2 * TILE], BF16, lb)
        zq_scr = self.sbuf("zq_scr", [128, 2 * TILE], BF16, lb)
        t1b = self.sbuf("t1b", [128, 2 * TILE], F32, lb)
        pa = self.psum("pba", [128, TILE], F32, lb)
        pb = self.psum("pbb", [128, TILE], F32, lb)
        ps_kq = self.psum("ps_kq", [128, TILE], F32, lb)
        ps_o0 = self.psum("ps_o0", [128, CH], F32, lb)
        ps_o1 = self.psum("ps_o1", [128, CH], F32, lb)
        pr0 = self.psum("pbr0", [1, TILE], F32, lb)
        pr1 = self.psum("pbr1", [1, TILE], F32, lb)
        pbc2 = self.psum("pbc2", [128, TILE], F32, lb)

        def b1_parts(t, p_base, a_base, g_w, init_ac_wait=None,
                     init_gp_waits=None):
            t0 = t * TILE

            def gp(g):
                for i, wd in enumerate([self.wq_d, self.wg_d]):
                    for fo in range(KC):
                        idx = i * KC + fo
                        if idx >= 2:
                            g.wait_ge(pe.h, p_base + idx - 1)
                        elif init_gp_waits is not None:
                            g.wait_ge(pe.h, init_gp_waits[idx])
                        slot = idx % 2
                        g_w[idx] = gL.inc(g.dma_start(
                            out=wqg[:, slot * DM:(slot + 1) * DM]
                                .rearrange("p (k b) -> p k b", b=128),
                            in_=wd[fo * DM:(fo + 1) * DM, :]
                                .rearrange("(k a) b -> a k b", a=128)), 16)

            def tp(t_):
                for idx in range(2 * KC):
                    t_.wait_ge(gL.h, g_w[idx])
                    wslot = idx % 2
                    ps = pa if idx % 2 == 0 else pb
                    if idx >= 2:
                        t_.wait_ge(ac.h, a_base + idx - 1)
                    elif init_ac_wait is not None:
                        t_.wait_ge(ac.h, init_ac_wait)
                    for k in range(KC):
                        mm = t_.matmul(
                            ps[:, :],
                            wqg[:, wslot * DM + k * 128:wslot * DM + (k + 1) * 128],
                            self.xln(k, t0, TILE),
                            start=(k == 0), stop=(k == KC - 1))
                    pe.inc(mm)

            def sp(sc):
                for idx in range(2 * KC):
                    fo = idx % KC
                    ps = pa if idx % 2 == 0 else pb
                    dst = q_tile if idx < KC else g_tile
                    fn = AF.Silu if idx < KC else AF.Sigmoid
                    sc.wait_ge(pe.h, p_base + idx + 1)
                    ac.inc(sc.activation(
                        dst[:, fo * TILE:(fo + 1) * TILE], ps[:, :], fn))

            return gp, tp, sp

        def b1_block(t, with_kv=False):
            p0, a0, d0 = pe.v, ac.v, dv.v
            g_w = {}
            h_T = {}
            gp, tp, sp = b1_parts(t, p0, a0, g_w)
            with nc.Block(f"B1_{t}") as blk:
                if with_kv:
                    @blk.sync
                    def _(sy):
                        sy.wait_ge(self.cc.h, self.cc.v)
                        for r in range(NCORES):
                            if r >= 1:
                                sy.wait_ge(dv.h, d0 + r)
                            h_T[r] = hL.inc(sy.dma_start(
                                out=z_sb[:, 0:H * D]
                                    .rearrange("p (h e) -> p h e", e=D),
                                in_=self.hdj(self.cc_out[r * H * D:
                                                         (r + 1) * H * D, :])), 16)
                        if self.debug == 1:
                            sy.wait_ge(dv.h, d0 + NCORES)
                            sT.inc(sy.dma_start(
                                out=self.hdj(self.dbg["dbg_kvin"]),
                                in_=kv_in[:, :]
                                    .rearrange("p (h e) -> p h e", e=D)), 16)

                @blk.gpsimd
                def _(g):
                    if with_kv:
                        g.wait_ge(self.cc.h, self.cc.v)
                    gp(g)
                    if self.debug == 1 and t == 0:
                        g.wait_ge(ac.h, a0 + 2 * KC)
                        sT.inc(g.dma_start(
                            out=self.fmj(self.dbg["dbg_q"]),
                            in_=q_tile[:, :].rearrange(
                                "p (h t) -> p h t", t=TILE)), 16)
                        sT.inc(g.dma_start(
                            out=self.fmj(self.dbg["dbg_g"]),
                            in_=g_tile[:, :].rearrange(
                                "p (h t) -> p h t", t=TILE)), 16)

                @blk.tensor
                def _(t_):
                    tp(t_)

                @blk.scalar
                def _(sc):
                    sp(sc)

                if with_kv:
                    @blk.vector
                    def _(v):
                        stg = z_sb[:, H * D:2 * H * D]
                        v.memset(kv_in[:, :], 0.0)
                        for r in range(NCORES):
                            v.wait_ge(hL.h, h_T[r])
                            for h in range(H):
                                ts = v.tensor_scalar(
                                    stg[:, h * D:(h + 1) * D],
                                    z_sb[:, h * D:(h + 1) * D],
                                    self.u_sb[:, r * H + h:r * H + h + 1],
                                    None, ALU.mult)
                            self.vstep(v, ts)
                            tt = v.tensor_add(kv_in[:, :], kv_in[:, :], stg)
                            dv.inc(tt)
                            v.wait_ge(dv.h, dv.v)
            assert pe.v == p0 + 2 * KC and ac.v == a0 + 2 * KC
            assert not with_kv or dv.v == d0 + NCORES

        def b2_block(cl):
            t = cl // 2
            t0 = t * TILE
            co = (cl % 2) * CH
            p0, a0, d0 = pe.v, ac.v, dv.v
            loads = []
            with nc.Block(f"B2_{cl}") as blk:
                @blk.sync
                def _(sy):
                    sy.wait_ge(sT.h, self.st_k)
                    loads.append(hL.inc(sy.dma_start(
                        out=kch[:, :].rearrange("p (h t) -> p h t", t=CH),
                        in_=self.fmj(self.k_sc)[:, :, cl * CH:(cl + 1) * CH]), 16))
                    sy.wait_ge(sT.h, self.st_v)
                    loads.append(hL.inc(sy.dma_start(
                        out=vch[:, :].rearrange("p (tc e) -> p tc e", e=DM),
                        in_=self.v_sc[cl * CH:(cl + 1) * CH, :]
                            .rearrange("(tc p) e -> p tc e", p=128)), 16))
                    if cl > 0:
                        sy.wait_ge(sT.h, self.st_kvl[cl - 1])
                        loads.append(hL.inc(sy.dma_start(
                            out=kvl_in[:, :].rearrange("p (h e) -> p h e", e=D),
                            in_=self.hdj(self.kvloc_sc[(cl - 1) * H * D:
                                                       cl * H * D, :])), 16))

                qd_ev = lambda h: d0 + 2 * h + 2
                kqd_ev = lambda h: d0 + 2 * h + 3
                kq_ev = lambda h: p0 + 3 * h + 1
                out_ev = lambda h: p0 + 3 * h + 2
                stat_ev = lambda h: p0 + 3 * h + 3
                bc_ev = p0 + 49
                oev = lambda h: a0 + 2 * h + 1
                sq_ev = lambda h: a0 + 2 * h + 2

                @blk.tensor
                def _(t_):
                    for lv in loads:
                        t_.wait_ge(hL.h, lv)
                    t_.wait_ge(dv.h, d0 + 1)
                    for h in range(H):
                        if h >= 1:
                            t_.wait_ge(dv.h, kqd_ev(h - 1))
                        for nh in range(2):
                            mm = t_.matmul(
                                ps_kq[:, nh * CH:(nh + 1) * CH],
                                kch[:, h * CH + nh * 128:h * CH + (nh + 1) * 128],
                                q_tile[:, h * TILE + co:h * TILE + co + CH],
                                start=True, stop=True)
                        pe.inc(mm)
                        t_.wait_ge(dv.h, kqd_ev(h))
                        if h >= 2:
                            t_.wait_ge(ac.h, sq_ev(h - 2))
                        ps = ps_o0 if h % 2 == 0 else ps_o1
                        t_.matmul(ps[:, :], vch[:, 0 * DM + h * D:0 * DM + (h + 1) * D],
                                  kqd_scr[:, 0:CH], start=True, stop=False)
                        t_.matmul(ps[:, :], vch[:, 1 * DM + h * D:1 * DM + (h + 1) * D],
                                  kqd_scr[:, CH:2 * CH], start=False, stop=False)
                        qslot = h % 2
                        pe.inc(t_.matmul(ps[:, :], kv_bf[:, h * D:(h + 1) * D],
                                         qd_scr[:, qslot * CH:(qslot + 1) * CH],
                                         start=False, stop=True))
                        t_.wait_ge(ac.h, sq_ev(h))
                        slot = h % 2
                        pe.inc(t_.matmul(pr0[0:1, 0:CH], self.onesc_bf[:, :],
                                         sqb[:, slot * CH:(slot + 1) * CH],
                                         start=(h == 0), stop=(h == H - 1)))
                    t_.wait_ge(dv.h, d0 + 35)
                    t_.wait_ge(ac.h, oev(H - 1))
                    pe.inc(t_.matmul(ps_o0[:, :], self.ones_f[:, :],
                                     self.rsrow[:, 0:CH], start=True, stop=True))

                @blk.scalar
                def _(sc):
                    for h in range(H):
                        sc.wait_ge(pe.h, out_ev(h))
                        ps = ps_o0 if h % 2 == 0 else ps_o1
                        ac.inc(sc.activation(outbf[:, h * CH:(h + 1) * CH],
                                             ps[:, :], AF.Copy))
                        if h >= 2:
                            sc.wait_ge(pe.h, stat_ev(h - 2))
                        slot = h % 2
                        ac.inc(sc.activation(sqb[:, slot * CH:(slot + 1) * CH],
                                             ps[:, :], AF.Square))
                    sc.wait_ge(dv.h, d0 + 34)
                    ac.inc(sc.activation(self.r_t2[0:1, 0:CH],
                                         self.r_t1[0:1, 0:CH], AF.Sqrt))

                @blk.vector
                def _(v):
                    if cl == 0:
                        dv.inc(v.tensor_copy(kv_bf[:, :], kv_in[:, :]))
                    else:
                        v.wait_ge(hL.h, loads[2])
                        for h in range(H):
                            last = v.scalar_tensor_tensor(
                                kv_bf[:, h * D:(h + 1) * D],
                                kv_in[:, h * D:(h + 1) * D],
                                self.bdp_sb[:, cl * H + h:cl * H + h + 1],
                                kvl_in[:, h * D:(h + 1) * D],
                                ALU.mult, ALU.add)
                        dv.inc(last)
                    for h in range(H):
                        qslot = h % 2
                        if h >= 2:
                            v.wait_ge(pe.h, out_ev(h - 2))
                        dv.inc(v.tensor_mul(
                            qd_scr[:, qslot * CH:(qslot + 1) * CH],
                            q_tile[:, h * TILE + co:h * TILE + co + CH],
                            self.qdec_sb[:, h * CH:(h + 1) * CH]))
                        v.wait_ge(pe.h, kq_ev(h))
                        v.tensor_mul(kqd_scr[:, 0:CH], ps_kq[:, 0:CH],
                                     self.diag_sb[:, (2 * h) * CH:(2 * h + 1) * CH])
                        dv.inc(v.tensor_mul(
                            kqd_scr[:, CH:2 * CH], ps_kq[:, CH:2 * CH],
                            self.diag_sb[:, (2 * h + 1) * CH:(2 * h + 2) * CH]))
                    v.wait_ge(pe.h, stat_ev(H - 1))
                    dv.inc(v.tensor_scalar(self.r_t1[0:1, 0:CH], pr0[0:1, 0:CH],
                                           1.0 / DM, EPS_RMS, ALU.mult, ALU.add))
                    v.wait_ge(ac.h, a0 + 33)
                    dv.inc(v.reciprocal(self.rsrow[0:1, 0:CH], self.r_t2[0:1, 0:CH]))
                    v.wait_ge(pe.h, bc_ev)
                    for h in range(H):
                        self.vstep(v, v.scalar_tensor_tensor(
                            og_tile[:, h * TILE + co:h * TILE + co + CH],
                            outbf[:, h * CH:(h + 1) * CH],
                            self.nw_sb[:, h:h + 1],
                            ps_o0[:, :], ALU.mult, ALU.mult))
                        last = v.tensor_mul(
                            og_tile[:, h * TILE + co:h * TILE + co + CH],
                            og_tile[:, h * TILE + co:h * TILE + co + CH],
                            g_tile[:, h * TILE + co:h * TILE + co + CH])
                    dv.inc(last)

                if self.debug == 1 and cl == 0:
                    @blk.gpsimd
                    def _(g):
                        g.wait_ge(dv.h, dv.v)
                        sT.inc(g.dma_start(
                            out=self.fmj(self.dbg["dbg_ob"]),
                            in_=outbf[:, :].rearrange(
                                "p (h t) -> p h t", t=CH)), 16)
                        sT.inc(g.dma_start(
                            out=self.dbg["dbg_rms"],
                            in_=self.rsrow[0:1, 0:CH]), 16)

            assert pe.v == p0 + 49, (pe.v, p0)
            assert ac.v == a0 + 33
            assert dv.v == d0 + 36

        def b3_block(t, merge_b1=False):
            t0 = t * TILE
            p0, a0, d0 = pe.v, ac.v, dv.v
            g_w = {}
            g_w1 = {}
            g_acc = []
            nb1 = 32 if merge_b1 else 0
            bc_ev = p0 + 49
            ln2res = dict(ubs=zb_scr, uqs=zq_scr, pa=ps_kq, pb=pbc2,
                          pr0=pr0, pr1=pr1, t1s=t1b)
            ln2base = dict(pe=p0 + 49 + nb1, ac=a0 + 49 + nb1, dv=d0 + 3)
            sp2, tp2, vp2 = self.ln_f32_parts(
                ln2res,
                lambda k: z_sb[:, k * TILE:(k + 1) * TILE],
                TILE, ln2base,
                dst_slice=lambda k, t0=t0: self.xln(k, t0, TILE))
            if merge_b1:
                gp1, tp1, sp1 = b1_parts(1, p0 + 49, a0 + 49, g_w1,
                                         init_ac_wait=a0 + 48,
                                         init_gp_waits=(p0 + 3 * 14 + 1,
                                                        p0 + 3 * 15 + 1))
            wo_ev = lambda o: p0 + 3 * o + 1
            zstat_ev = lambda o: p0 + 3 * o + 3
            z_ev = lambda o: a0 + 3 * o + 1
            zb_ev = lambda o: a0 + 3 * o + 2
            zq_ev = lambda o: a0 + 3 * o + 3
            with nc.Block(f"B3_{t}") as blk:
                @blk.gpsimd
                def _(g):
                    for o in range(KC):
                        if o >= 2:
                            g.wait_ge(pe.h, wo_ev(o - 2))
                        slot = o % 2
                        g_w[o] = gL.inc(g.dma_start(
                            out=wqg[:, slot * DM:(slot + 1) * DM]
                                .rearrange("p (f b) -> p f b", b=128),
                            in_=self.wo_d[o * DM:(o + 1) * DM, :]
                                .rearrange("(f a) b -> a f b", a=128)), 16)
                    if merge_b1:
                        gp1(g)
                    if self.debug == 1:
                        sT.inc(g.dma_start(
                            out=self.fmj(self.dbg["dbg_og"])[:, :, t0:t0 + TILE],
                            in_=og_tile[:, :].rearrange("p (h t) -> p h t", t=TILE)),
                            16)
                    g.wait_ge(dv.h, d0 + 3)
                    g_acc.append(gL.inc(g.dma_start(
                        out=z_sb[:, :].rearrange("p (k t) -> p k t", t=TILE),
                        in_=self.fmj(self.x_d)[:, :, t0:t0 + TILE],
                        accum_op=ALU.add), 16))

                @blk.tensor
                def _(t_):
                    for o in range(KC):
                        t_.wait_ge(gL.h, g_w[o])
                        wslot = o % 2
                        ps = pa if o % 2 == 0 else pb
                        if o >= 2:
                            t_.wait_ge(ac.h, zq_ev(o - 2))
                        for f in range(KC):
                            mm = t_.matmul(
                                ps[:, :],
                                wqg[:, wslot * DM + f * 128:wslot * DM + (f + 1) * 128],
                                og_tile[:, f * TILE:(f + 1) * TILE],
                                start=(f == 0), stop=(f == KC - 1))
                        pe.inc(mm)
                        slot = o % 2
                        t_.wait_ge(ac.h, zb_ev(o))
                        pe.inc(t_.matmul(pr0[0:1, 0:TILE], self.onesc_bf[:, :],
                                         zb_scr[:, slot * TILE:(slot + 1) * TILE],
                                         start=(o == 0), stop=(o == KC - 1)))
                        t_.wait_ge(ac.h, zq_ev(o))
                        pe.inc(t_.matmul(pr1[0:1, 0:TILE], self.onesc_bf[:, :],
                                         zq_scr[:, slot * TILE:(slot + 1) * TILE],
                                         start=(o == 0), stop=(o == KC - 1)))
                    t_.wait_ge(dv.h, d0 + 2)
                    t_.matmul(ps_kq[:, :], self.ones_f[:, :], self.nmrow[:, 0:TILE],
                              start=True, stop=True)
                    pe.inc(t_.matmul(pbc2[:, :], self.ones_f[:, :],
                                     self.rsrow[:, 0:TILE], start=True, stop=True))
                    if merge_b1:
                        tp1(t_)
                    t_.wait_ge(dv.h, d0 + 1)
                    tp2(t_)

                @blk.scalar
                def _(sc):
                    if t > 0 and 0 in self.st_x2:
                        sc.wait_ge(sT.h, self.st_x2[0])
                    for o in range(KC):
                        sc.wait_ge(pe.h, wo_ev(o))
                        ps = pa if o % 2 == 0 else pb
                        slot = o % 2
                        ac.inc(sc.activation(z_sb[:, o * TILE:(o + 1) * TILE],
                                             ps[:, :], AF.Copy))
                        if o >= 2:
                            sc.wait_ge(pe.h, zstat_ev(o - 2))
                        ac.inc(sc.activation(zb_scr[:, slot * TILE:(slot + 1) * TILE],
                                             ps[:, :], AF.Copy))
                        ac.inc(sc.activation(zq_scr[:, slot * TILE:(slot + 1) * TILE],
                                             ps[:, :], AF.Square))
                    sc.wait_ge(dv.h, d0 + 1)
                    ac.inc(sc.activation(self.r_t2[0:1, 0:TILE],
                                         self.r_t1[0:1, 0:TILE], AF.Sqrt))
                    if merge_b1:
                        sp1(sc)
                    sc.wait_ge(gL.h, g_acc[0])
                    sp2(sc)

                @blk.vector
                def _(v):
                    v.wait_ge(pe.h, zstat_ev(KC - 1))
                    self.vstep(v, v.tensor_scalar(
                        self.nmrow[0:1, 0:TILE], pr0[0:1, 0:TILE],
                        -1.0 / DM, None, ALU.mult))
                    self.vstep(v, v.tensor_scalar(
                        self.r_t1[0:1, 0:TILE], pr1[0:1, 0:TILE],
                        1.0 / DM, EPS_LN, ALU.mult, ALU.add))
                    self.vstep(v, v.tensor_mul(
                        self.r_t2[0:1, 0:TILE], self.nmrow[0:1, 0:TILE],
                        self.nmrow[0:1, 0:TILE]))
                    dv.inc(v.tensor_tensor(self.r_t1[0:1, 0:TILE],
                                           self.r_t1[0:1, 0:TILE],
                                           self.r_t2[0:1, 0:TILE], ALU.subtract))
                    v.wait_ge(ac.h, a0 + 3 * KC + 1)
                    dv.inc(v.reciprocal(self.rsrow[0:1, 0:TILE],
                                        self.r_t2[0:1, 0:TILE]))
                    v.wait_ge(pe.h, bc_ev)
                    for k in range(KC):
                        t1 = t1b[:, (k % 2) * TILE:(k % 2 + 1) * TILE]
                        self.vstep(v, v.tensor_tensor(
                            t1, z_sb[:, k * TILE:(k + 1) * TILE],
                            ps_kq[:, :], ALU.add))
                        last = v.tensor_tensor(z_sb[:, k * TILE:(k + 1) * TILE],
                                               t1, pbc2[:, :], ALU.mult)
                    dv.inc(last)
                    if self.debug == 1:
                        v.wait_ge(sT.h, sT.v)
                    dv.inc(vp2(v))

                @blk.sync
                def _(sy):
                    sy.wait_ge(gL.h, g_acc[0])
                    self.st_x2[t] = sT.inc(sy.dma_start(
                        out=self.fmj(self.x2_sc)[:, :, t0:t0 + TILE],
                        in_=z_sb[:, :].rearrange("p (k t) -> p k t", t=TILE)), 16)
                    if self.debug == 1:
                        sT.inc(sy.dma_start(
                            out=self.fmj(self.dbg["dbg_x2"])[:, :, t0:t0 + TILE],
                            in_=z_sb[:, :].rearrange("p (k t) -> p k t", t=TILE)), 16)
            assert pe.v == p0 + 66 + nb1 and ac.v == a0 + 82 + nb1
            assert dv.v == d0 + 6

        b1_block(0, with_kv=True)
        if not self.lvl("B2"):
            return
        b2_block(0)
        b2_block(1)
        if not self.lvl("B3"):
            return
        b3_block(0, merge_b1=True)
        if not self.lvl("B"):
            return
        b2_block(2)
        b2_block(3)
        b3_block(1)

    def phase_c(self):
        nc = self.nc
        gL, hL, sT, pe, dv, ac = self.gL, self.hL, self.sT, self.pe, self.dv, self.ac
        self.lb.close()
        self.lab.close()
        lc = self.lc = ExitStack()
        pa = self.psum("pca", [128, TILE], F32, lc)
        pb = self.psum("pcb", [128, TILE], F32, lc)
        pr0 = self.psum("pcr0", [1, TILE], F32, lc)
        pr1 = self.psum("pcr1", [1, TILE], F32, lc)

        x2ln = self.xln

        if not self.lvl("C2"):
            return

        mid = self.sbuf("mid", [128, 32 * TCORE], BF16, lc)
        y_sb = self.sbuf("y_sb", [128, KC * TCORE], F32, lc)
        w0buf = self.sbuf("w0buf", [128, 2 * DM], BF16, lc)
        w1buf = self.sbuf("w1buf", [128, 2 * 32 * 128], BF16, lc)
        ubs = self.sbuf("ubs", [128, 2 * TILE], BF16, lc)
        uqs = self.sbuf("uqs", [128, 2 * TILE], BF16, lc)
        t1d = self.sbuf("t1d", [128, 2 * TILE], F32, lc)

        for h2 in range(2):
            p0, a0, d0 = pe.v, ac.v, dv.v
            pW1 = p0 + 64
            g_w0, g_w1 = {}, {}
            with nc.Block(f"C2_{h2}") as blk:
                @blk.gpsimd
                def _(g, h2=h2):
                    for jl in range(32):
                        if jl >= 2:
                            g.wait_ge(pe.h, p0 + 2 * (jl - 1))
                        slot = jl % 2
                        jo = h2 * 32 + jl
                        g_w0[jl] = gL.inc(g.dma_start(
                            out=w0buf[:, slot * DM:(slot + 1) * DM]
                                .rearrange("p (k b) -> p k b", b=128),
                            in_=self.w0_d[jo * DM:(jo + 1) * DM, :]
                                .rearrange("(k a) b -> a k b", a=128)), 16)
                    for o in range(KC):
                        if o >= 2:
                            g.wait_ge(pe.h, pW1 + 2 * (o - 1))
                        slot = o % 2
                        base_r = h2 * FF * 8 + o * 32 * 128
                        g_w1[o] = gL.inc(g.dma_start(
                            out=w1buf[:, slot * 32 * 128:(slot + 1) * 32 * 128]
                                .rearrange("p (j b) -> p j b", b=128),
                            in_=self.w1_d[base_r:base_r + 32 * 128, :]
                                .rearrange("(j a) b -> a j b", a=128)), 16)

                @blk.tensor
                def _(t_, h2=h2):
                    for jl in range(32):
                        t_.wait_ge(gL.h, g_w0[jl])
                        wslot = jl % 2
                        for nt in range(2):
                            idx = 2 * jl + nt
                            ps = pa if idx % 2 == 0 else pb
                            if idx >= 2:
                                t_.wait_ge(ac.h, a0 + idx - 1)
                            for k in range(KC):
                                mm = t_.matmul(
                                    ps[:, :],
                                    w0buf[:, wslot * DM + k * 128:
                                          wslot * DM + (k + 1) * 128],
                                    x2ln(k, nt * TILE, TILE),
                                    start=(k == 0), stop=(k == KC - 1))
                            pe.inc(mm)
                    t_.wait_ge(ac.h, a0 + 64)
                    for o in range(KC):
                        t_.wait_ge(gL.h, g_w1[o])
                        wslot = o % 2
                        for nt in range(2):
                            idx = 2 * o + nt
                            ps = pa if idx % 2 == 0 else pb
                            if idx >= 2:
                                if h2 == 0:
                                    t_.wait_ge(ac.h, a0 + 64 + idx - 1)
                                else:
                                    t_.wait_ge(dv.h, d0 + idx - 1)
                            for jl in range(32):
                                mm = t_.matmul(
                                    ps[:, :],
                                    w1buf[:, wslot * 32 * 128 + jl * 128:
                                          wslot * 32 * 128 + (jl + 1) * 128],
                                    mid[:, jl * TCORE + nt * TILE:
                                        jl * TCORE + (nt + 1) * TILE],
                                    start=(jl == 0), stop=(jl == 31))
                            pe.inc(mm)

                @blk.scalar
                def _(sc, h2=h2):
                    for jl in range(32):
                        jo = h2 * 32 + jl
                        for nt in range(2):
                            idx = 2 * jl + nt
                            ps = pa if idx % 2 == 0 else pb
                            sc.wait_ge(pe.h, p0 + idx + 1)
                            ac.inc(sc.activation(
                                mid[:, jl * TCORE + nt * TILE:
                                    jl * TCORE + (nt + 1) * TILE],
                                ps[:, :], AF.Gelu, bias=self.b0_sb[:, jo:jo + 1]))
                    if h2 == 0:
                        for o in range(KC):
                            for nt in range(2):
                                idx = 2 * o + nt
                                ps = pa if idx % 2 == 0 else pb
                                sc.wait_ge(pe.h, pW1 + idx + 1)
                                ac.inc(sc.activation(
                                    y_sb[:, o * TCORE + nt * TILE:
                                         o * TCORE + (nt + 1) * TILE],
                                    ps[:, :], AF.Copy))

                if h2 == 1:
                    @blk.vector
                    def _(v):
                        for o in range(KC):
                            for nt in range(2):
                                idx = 2 * o + nt
                                ps = pa if idx % 2 == 0 else pb
                                v.wait_ge(pe.h, pW1 + idx + 1)
                                ys = y_sb[:, o * TCORE + nt * TILE:
                                          o * TCORE + (nt + 1) * TILE]
                                dv.inc(v.tensor_tensor(ys, ys, ps[:, :], ALU.add))

        if not self.lvl("C"):
            return
        pe0, ac0, dv0 = pe.v, ac.v, dv.v
        bases = []
        sps, tps, vps = [], [], []
        for nt in range(2):
            base = dict(pe=pe0 + 17 * nt, ac=ac0 + 33 * nt, dv=dv0 + 3 * nt)
            bases.append(base)
            lnres2 = dict(ubs=ubs, uqs=uqs, pa=pa, pb=pb, pr0=pr0, pr1=pr1, t1s=t1d)
            sp, tp, vp = self.ln_f32_parts(
                lnres2,
                lambda k, t0=nt * TILE: y_sb[:, k * TCORE + t0:k * TCORE + t0 + TILE],
                TILE, base,
                bias_col=lambda k: self.b1_sb[:, k:k + 1])
            sps.append(sp)
            tps.append(tp)
            vps.append(vp)
        if self.debug == 2:
            with nc.Block("C3dbg") as blk:
                @blk.sync
                def _(sy):
                    v = sT.inc(sy.dma_start(
                        out=self.fmj(self.dbg["dbg_y"]),
                        in_=y_sb[:, :].rearrange("p (k t) -> p k t", t=TCORE)), 16)
                    sy.wait_ge(sT.h, v)

                @blk.gpsimd
                def _(g):
                    sT.inc(g.dma_start(
                        out=self.fmj(self.dbg["dbg_x2ln"]),
                        in_=self.xln_sb[:, :]
                            .rearrange("p (k t) -> p k t", t=TCORE)), 16)
                    sT.inc(g.dma_start(
                        out=self.dbg["dbg_mid"]
                            .rearrange("(j p) t -> p j t", p=128),
                        in_=mid[:, :].rearrange("p (j t) -> p j t", t=TCORE)), 16)
        g_acc = {}
        gL0, hL0, sT0 = gL.v, hL.v, sT.v
        with nc.Block("C3") as blk:
            @blk.gpsimd
            def _(g):
                for nt in range(2):
                    g.wait_ge(dv.h, dv0 + 3 * nt + 3)
                    t0 = nt * TILE
                    g_acc[nt] = gL.inc(g.dma_start(
                        out=y_sb[:, :].rearrange(
                            "p (k t) -> p k t", t=TCORE)[:, :, t0:t0 + TILE],
                        in_=self.fmj(self.x2_sc)[:, :, t0:t0 + TILE],
                        accum_op=ALU.add), 16)

            @blk.scalar
            def _(sc):
                sc.wait_ge(gL.h, gL0)
                sc.wait_ge(hL.h, hL0)
                sc.wait_ge(sT.h, sT0)
                sps[0](sc)
                sc.wait_ge(dv.h, dv0 + 3)
                sps[1](sc)

            @blk.tensor
            def _(t_):
                t_.wait_ge(gL.h, gL0)
                t_.wait_ge(hL.h, hL0)
                t_.wait_ge(sT.h, sT0)
                tps[0](t_)
                t_.wait_ge(dv.h, dv0 + 3)
                tps[1](t_)

            @blk.vector
            def _(v):
                dv.inc(vps[0](v))
                dv.inc(vps[1](v))

            @blk.sync
            def _(sy):
                for nt in range(2):
                    t0 = nt * TILE
                    sy.wait_ge(gL.h, g_acc[nt])
                    sT.inc(sy.dma_start(
                        out=self.fmj(self.out_d)[:, :, t0:t0 + TILE],
                        in_=y_sb[:, :].rearrange(
                            "p (k t) -> p k t", t=TCORE)[:, :, t0:t0 + TILE]), 16)
        assert pe.v == pe0 + 34 and ac.v == ac0 + 66 and dv.v == dv0 + 6

    def tail_block(self):
        nc = self.nc
        with nc.Block("tail") as blk:
            @blk.sync
            def _(sy):
                sy.wait_ge(self.sT.h, self.sT.v)
                if self.cc.v:
                    sy.wait_ge(self.cc.h, self.cc.v)

    LEVELS = ["S", "A1", "A2", "A", "B1", "Bkv", "B2", "B3", "B", "C1", "C2", "C"]

    def lvl(self, name):
        return self.LEVELS.index(self.upto) >= self.LEVELS.index(name)

    def build(self):
        self.setup_block()
        if self.lvl("A1"):
            self.phase_a()
        if self.lvl("B1"):
            self.phase_b()
        if self.lvl("C1"):
            self.phase_c()
        self.tail_block()
        return self.nc


_CACHE = {}


def _blocked_lhsT(wT, n_out, n_k=KC):
    return np.ascontiguousarray(
        wT.reshape(n_k, 128, n_out, 128).transpose(2, 0, 1, 3).reshape(-1, 128))


def _prep(inputs):
    x = np.ascontiguousarray(np.asarray(inputs["L0_attn_x"], np.float32))
    Wqkv = np.asarray(inputs["L0_attn_Wqkv"], np.float32).reshape(H, 3, D, DM)
    Wq = Wqkv[:, 0].reshape(H * D, DM)
    Wk = Wqkv[:, 1].reshape(H * D, DM)
    Wv = Wqkv[:, 2].reshape(H * D, DM)
    Wg = np.asarray(inputs["L0_attn_Wg"], np.float32)
    Wo = np.asarray(inputs["L0_attn_Wo"], np.float32)
    W0 = np.asarray(inputs["L0_ffn_W0"], np.float32)
    W1 = np.asarray(inputs["L0_ffn_W1"], np.float32)

    d = {}
    d["wq"] = _blocked_lhsT(np.ascontiguousarray(Wq.T), KC)
    d["wk"] = _blocked_lhsT(np.ascontiguousarray(Wk.T), KC)
    d["wv"] = np.ascontiguousarray(Wv.T)
    d["wg"] = _blocked_lhsT(np.ascontiguousarray(Wg.T), KC)
    d["wo"] = _blocked_lhsT(np.ascontiguousarray(Wo.T), KC)
    d["w0"] = _blocked_lhsT(np.ascontiguousarray(W0.T), NJ)
    W1T = np.ascontiguousarray(W1.T).reshape(2, 32, 128, KC, 128)
    d["w1"] = np.ascontiguousarray(W1T.transpose(0, 3, 1, 2, 4).reshape(-1, 128))

    slope = np.asarray(inputs["L0_attn_slope_rate"], np.float32).reshape(1, H)
    d["slope"] = np.ascontiguousarray(np.broadcast_to(slope, (128, H)))
    arr = np.arange(1, CH + 1, dtype=np.float32)
    d["arrb"] = np.ascontiguousarray(np.broadcast_to(arr[None, :], (128, CH)))
    p = np.arange(128, dtype=np.float32)
    d["arrm"] = np.ascontiguousarray(
        np.stack([p + 1 - CH, p + 129 - CH], axis=1))
    mm, nn = np.meshgrid(np.arange(CH, dtype=np.float32),
                         np.arange(CH, dtype=np.float32))
    diff = mm - nn
    d["idxT"] = np.maximum(diff, 0.0).astype(np.float32)
    d["maskT"] = (diff >= 0).astype(np.float32)
    d["ident"] = np.eye(128, dtype=np.float32)
    d["onesr"] = np.ones((1, 128), np.float32)
    d["onesc"] = np.ones((128, 1), np.float32)
    d["b0c"] = np.ascontiguousarray(
        np.asarray(inputs["L0_ffn_bias0"], np.float32).reshape(NJ, 128).T)
    d["b1c"] = np.ascontiguousarray(
        np.asarray(inputs["L0_ffn_bias1"], np.float32).reshape(KC, 128).T)
    d["nwc"] = np.ascontiguousarray(
        np.asarray(inputs["L0_attn_norm_w"], np.float32).reshape(KC, 128).T)
    return x, d


def kernel(**inputs):
    key = (DEBUG, UPTO)
    if key not in _CACHE:
        _CACHE[key] = Builder(debug=DEBUG, upto=UPTO).build()
    nc = _CACHE[key]

    x, shared = _prep(inputs)
    in_maps = []
    for core in range(NCORES):
        b, qq = core // 4, core % 4
        m = dict(shared)
        m["x"] = np.ascontiguousarray(x[b, qq * TCORE:(qq + 1) * TCORE, :].T)
        sel = np.zeros((128, NCORES * 4), np.float32)
        for pidx in range(qq):
            r = b * 4 + pidx
            sel[:, r * 4 + (qq - 1 - pidx)] = 1.0
        m["sel"] = sel
        in_maps.append(m)

    res = run_bass_kernel_spmd(nc, in_maps, core_ids=list(range(NCORES)))
    out = np.empty((2, 4096, DM), np.float32)
    for core in range(NCORES):
        b, qq = core // 4, core % 4
        out[b, qq * TCORE:(qq + 1) * TCORE, :] = res.results[core]["out"].T
    return out


# version 13
# speedup vs baseline: 1.0302x; 1.0106x over previous
import numpy as np
import ml_dtypes
import concourse.bass as bass
import concourse.mybir as mybir
from concourse.bass_utils import run_bass_kernel_spmd
from contextlib import ExitStack

F32 = mybir.dt.float32
BF16 = mybir.dt.bfloat16
AF = mybir.ActivationFunctionType
ALU = mybir.AluOpType

H = 16
D = 128
DM = 2048
KC = 16
TCORE = 1024
TILE = 512
NT = 2
CH = 256
NCH = 4
FF = 8192
NJ = 64
NCORES = 8
EPS_LN = 1e-5
EPS_RMS = float(np.finfo(np.float32).eps)

DEBUG = 0
UPTO = "C"


class Sem:
    def __init__(self, nc, stack, name):
        self.h = stack.enter_context(nc.semaphore(name))
        self.v = 0

    def inc(self, instr, n=1):
        instr.then_inc(self.h, n)
        self.v += n
        return self.v


class Builder:
    def __init__(self, debug=0, upto="C"):
        self.debug = debug
        self.upto = upto
        nc = self.nc = bass.Bass("TRN2", target_bir_lowering=False)

        dram = lambda nm, shp, kind="ExternalInput", dt=F32: \
            nc.dram_tensor(nm, shp, dt, kind=kind).ap()
        self.x_d = dram("x", [DM, TCORE])
        self.wq_d = dram("wq", [KC * DM, D])
        self.wk_d = dram("wk", [KC * DM, D])
        self.wv_d = dram("wv", [DM, DM])
        self.wg_d = dram("wg", [KC * DM, D])
        self.wo_d = dram("wo", [KC * DM, D])
        self.w0_d = dram("w0", [NJ * DM, D])
        self.w1_d = dram("w1", [FF * KC, D])
        self.slope_d = dram("slope", [128, H])
        self.arr_d = dram("arrb", [128, CH])
        self.arrm_d = dram("arrm", [128, 2])
        self.idxT_d = dram("idxT", [CH, CH])
        self.mask_d = dram("maskT", [CH, CH])
        self.ident_d = dram("ident", [128, 128])
        self.ones_d = dram("onesr", [1, 128])
        self.onesc_d = dram("onesc", [128, 1])
        self.b0_d = dram("b0c", [128, NJ])
        self.b1_d = dram("b1c", [128, KC])
        self.nw_d = dram("nwc", [128, KC])
        self.sel_d = dram("sel", [128, NCORES * 4])
        self.out_d = dram("out", [DM, TCORE], kind="ExternalOutput")

        self.k_sc = dram("k_sc", [DM, TCORE], "Internal", BF16)
        self.v_sc = dram("v_sc", [TCORE, DM], "Internal", BF16)
        self.kvloc_sc = dram("kvloc_sc", [3 * H * D, D], "Internal", BF16)
        self.x2_sc = dram("x2_sc", [DM, TCORE], "Internal")
        self.cc_in = dram("cc_in", [H * D, D], "Internal")
        self.cc_out = dram("cc_out", [NCORES * H * D, D], "Internal")

        self.dbg = {}
        if debug == 2:
            self.dbg["dbg_x2ln"] = dram("dbg_x2ln", [DM, TCORE],
                                        kind="ExternalOutput")
            self.dbg["dbg_y"] = dram("dbg_y", [DM, TCORE],
                                     kind="ExternalOutput")
            self.dbg["dbg_mid"] = dram("dbg_mid", [32 * 128, TCORE],
                                       kind="ExternalOutput")
        if debug == 1:
            for nm, shp in [("dbg_xln", [DM, TCORE]), ("dbg_T", [H * D, D]),
                            ("dbg_kvin", [H * D, D]), ("dbg_og", [DM, TCORE]),
                            ("dbg_x2", [DM, TCORE]), ("dbg_q", [DM, TILE]),
                            ("dbg_g", [DM, TILE]), ("dbg_ob", [DM, CH]),
                            ("dbg_rms", [1, CH])]:
                self.dbg[nm] = dram(nm, shp, kind="ExternalOutput")

        es = self.es = ExitStack()
        self.gL = Sem(nc, es, "gL")
        self.hL = Sem(nc, es, "hL")
        self.sT = Sem(nc, es, "sT")
        self.cc = Sem(nc, es, "ccs")
        self.pe = Sem(nc, es, "pes")
        self.dv = Sem(nc, es, "dvs")
        self.ac = Sem(nc, es, "acs")
        self.dvx = Sem(nc, es, "dvx")
        self.acx = Sem(nc, es, "acx")

        sbuf = self.sbuf
        self.id_bf = sbuf("id_bf", [128, 128], BF16)
        self.ones_bf = sbuf("ones_bf", [1, 128], BF16)
        self.ones_f = sbuf("ones_f", [1, 128], F32)
        self.onesc_bf = sbuf("onesc_bf", [128, 1], BF16)
        self.s_sb = sbuf("s_sb", [128, H], F32)
        self.negs_sb = sbuf("negs_sb", [128, H], F32)
        self.bd_sb = sbuf("bd_sb", [128, H], F32)
        self.bdp_sb = sbuf("bdp_sb", [128, 4 * H], F32)
        self.bd4p_sb = sbuf("bd4p_sb", [128, 4 * H], F32)
        self.u_sb = sbuf("u_sb", [128, NCORES * H], F32)
        self.ut_sb = sbuf("ut_sb", [128, 3 * H], F32)
        self.sel_sb = sbuf("sel_sb", [128, NCORES * 4], F32)
        self.arr_sb = sbuf("arr_sb", [128, CH], F32)
        self.arrm_sb = sbuf("arrm_sb", [128, 2], F32)
        self.kdec_sb = sbuf("kdec_sb", [128, 2 * H], F32)
        self.b0_sb = sbuf("b0_sb", [128, NJ], F32)
        self.b1_sb = sbuf("b1_sb", [128, KC], F32)
        self.nw_sb = sbuf("nw_sb", [128, KC], F32)
        self.row_sb = sbuf("row_sb", [1, 4 * TILE], F32)
        self.junk_sb = sbuf("junk_sb", [1, 8], F32)
        self.nmrow = self.row_sb[0:1, 0:TILE]
        self.rsrow = self.row_sb[0:1, TILE:2 * TILE]
        self.r_t1 = self.row_sb[0:1, 2 * TILE:3 * TILE]
        self.r_t2 = self.row_sb[0:1, 3 * TILE:4 * TILE]

        self.lab_b = ExitStack()
        self.xln_sb = sbuf("xln_sb", [128, KC * TCORE], BF16, self.lab_b)
        self.lab = ExitStack()
        self.qdec_sb = sbuf("qdec_sb", [128, H * CH], BF16, self.lab)
        self.diag_sb = sbuf("diag_sb", [128, H * 2 * CH], BF16, self.lab)

        self.st_kvl = {}
        self.st_x2 = {}
        self.st_x2ln = {}
        self.pre_wk = {}
        self.gx0 = None
        self.b2_loads = {}

    def sbuf(self, nm, shp, dt, stk=None):
        return (stk or self.es).enter_context(self.nc.sbuf_tensor(nm, shp, dt))

    def psum(self, nm, shp, dt, stk=None):
        return (stk or self.es).enter_context(self.nc.psum_tensor(nm, shp, dt))

    def vstep(self, v, instr):
        self.dvx.inc(instr)
        v.wait_ge(self.dvx.h, self.dvx.v)
        return instr

    def sstep(self, sc, instr):
        self.acx.inc(instr)
        sc.wait_ge(self.acx.h, self.acx.v)
        return instr

    def xln(self, k, t0, tw):
        return self.xln_sb[:, k * TCORE + t0: k * TCORE + t0 + tw]

    @staticmethod
    def fmj(ap):
        return ap.rearrange("(k p) t -> p k t", p=128)

    @staticmethod
    def hdj(ap):
        return ap.rearrange("(h d) e -> d h e", d=128)

    def setup_block(self):
        nc = self.nc
        gL, hL, pe, dv, ac = self.gL, self.hL, self.pe, self.dv, self.ac
        ls = ExitStack()
        idx_sb = self.sbuf("idx_sb", [128, 2 * CH], F32, ls)
        mask_sb = self.sbuf("mask_sb", [128, 2 * CH], F32, ls)
        dscr = self.sbuf("dscr", [128, 2 * CH], F32, ls)
        d0, a0 = dv.v, ac.v
        diag_items = [(h, nh) for h in range(H) for nh in range(2)]
        diag_ac = {}

        with nc.Block("setup") as blk:
            @blk.sync
            def _(sy):
                for dst, src in [(self.s_sb, self.slope_d), (self.arr_sb, self.arr_d),
                                 (self.arrm_sb, self.arrm_d), (self.ones_f, self.ones_d),
                                 (self.b0_sb, self.b0_d), (self.b1_sb, self.b1_d),
                                 (self.nw_sb, self.nw_d), (self.sel_sb, self.sel_d)]:
                    hL.inc(sy.dma_start(out=dst[:, :], in_=src), 16)
                hL.inc(sy.dma_start(out=idx_sb[:, 0:CH], in_=self.idxT_d[0:128, :]), 16)
                hL.inc(sy.dma_start(out=idx_sb[:, CH:], in_=self.idxT_d[128:256, :]), 16)
                hL.inc(sy.dma_start(out=mask_sb[:, 0:CH], in_=self.mask_d[0:128, :]), 16)
                hL.inc(sy.dma_start(out=mask_sb[:, CH:], in_=self.mask_d[128:256, :]), 16)

            @blk.gpsimd
            def _(g):
                self.gx0 = gL.inc(g.dma_start(
                    out=self.a_["xbf"][:, :].rearrange("p (k t) -> p k t", t=TILE),
                    in_=self.fmj(self.x_d)[:, :, 0:TILE]), 16)
                gL.inc(g.dma_start(out=self.id_bf[:, :], in_=self.ident_d), 16)
                gL.inc(g.dma_start(out=self.ones_bf[:, :], in_=self.ones_d), 16)
                gL.inc(g.dma_start(out=self.onesc_bf[:, :], in_=self.onesc_d), 16)
                for fo in range(2):
                    self.pre_wk[fo] = gL.inc(g.dma_start(
                        out=self.a_["wkbuf"][:, fo * DM:(fo + 1) * DM]
                            .rearrange("p (k b) -> p k b", b=128),
                        in_=self.wk_d[fo * DM:(fo + 1) * DM, :]
                            .rearrange("(k a) b -> a k b", a=128)), 16)

            @blk.scalar
            def _(sc):
                sc.wait_ge(hL.h, hL.v)
                sc.activation(self.bd_sb[:, :], self.s_sb[:, :], AF.Exp,
                              bias=0.0, scale=-float(CH))
                for j in range(4):
                    sc.activation(self.bdp_sb[:, j * H:(j + 1) * H], self.s_sb[:, :],
                                  AF.Exp, bias=0.0, scale=-float(CH * j))
                for j in range(4):
                    last = sc.activation(self.bd4p_sb[:, j * H:(j + 1) * H],
                                         self.s_sb[:, :], AF.Exp,
                                         bias=0.0, scale=-float(4 * CH * j))
                ac.inc(last)
                for h in range(H):
                    for half in range(2):
                        c = 2 * h + half
                        sc.activation(self.kdec_sb[:, c:c + 1],
                                      self.arrm_sb[:, half:half + 1], AF.Exp,
                                      bias=0.0, scale=self.s_sb[:, h:h + 1])
                sc.wait_ge(dv.h, d0 + 1)
                for h in range(H):
                    sc.activation(self.qdec_sb[:, h * CH:(h + 1) * CH],
                                  self.arr_sb[:, :], AF.Exp,
                                  bias=0.0, scale=self.negs_sb[:, h:h + 1])
                for i, (h, nh) in enumerate(diag_items):
                    if i >= 2:
                        sc.wait_ge(dv.h, d0 + 2 + (i - 2))
                    slot = i % 2
                    diag_ac[i] = ac.inc(sc.activation(
                        dscr[:, slot * CH:(slot + 1) * CH],
                        idx_sb[:, nh * CH:(nh + 1) * CH], AF.Exp,
                        bias=0.0, scale=self.negs_sb[:, h:h + 1]))

            @blk.vector
            def _(v):
                v.wait_ge(hL.h, hL.v)
                dv.inc(v.tensor_scalar_mul(self.negs_sb[:, :], self.s_sb[:, :], -1.0))
                v.wait_ge(ac.h, a0 + 1)
                bd4p = self.bd4p_sb
                for r in range(NCORES):
                    ur = self.u_sb[:, r * H:(r + 1) * H]
                    v.tensor_scalar(ur, bd4p[:, 0:H], self.sel_sb[:, 4 * r:4 * r + 1],
                                    None, ALU.mult)
                    for j in range(1, 4):
                        ts = v.tensor_scalar(self.ut_sb[:, (j - 1) * H:j * H],
                                             bd4p[:, j * H:(j + 1) * H],
                                             self.sel_sb[:, 4 * r + j:4 * r + j + 1],
                                             None, ALU.mult)
                    self.vstep(v, ts)
                    for j in range(1, 4):
                        self.vstep(v, v.tensor_add(
                            ur, ur, self.ut_sb[:, (j - 1) * H:j * H]))
                for i, (h, nh) in enumerate(diag_items):
                    v.wait_ge(ac.h, diag_ac[i])
                    slot = i % 2
                    dv.inc(v.tensor_mul(
                        self.diag_sb[:, (2 * h + nh) * CH:(2 * h + nh + 1) * CH],
                        dscr[:, slot * CH:(slot + 1) * CH],
                        mask_sb[:, nh * CH:(nh + 1) * CH]))

        assert dv.v == d0 + 1 + len(diag_items)
        assert ac.v == a0 + 1 + len(diag_items)
        ls.close()
        self.setup_hL = hL.v
        self.setup_gL = gL.v

    def ln_bf_parts(self, res, src_slice, dst_slice, n_tok, base):
        pe, dv, ac = self.pe, self.dv, self.ac
        p0, a0, d0 = base["pe"], base["ac"], base["dv"]

        def scalar_part(sc):
            for k in range(KC):
                if k >= 2:
                    sc.wait_ge(pe.h, p0 + (k - 1))
                slot = k % 2
                ac.inc(sc.activation(
                    res["sqs"][:, slot * n_tok:(slot + 1) * n_tok],
                    src_slice(k), AF.Square))
            sc.wait_ge(dv.h, d0 + 1)
            ac.inc(sc.activation(self.r_t2[:, 0:n_tok], self.r_t1[:, 0:n_tok],
                                 AF.Sqrt))

        def tensor_part(t):
            for k in range(KC):
                t.matmul(res["pr0"][0:1, 0:n_tok], self.onesc_bf[:, :], src_slice(k),
                         start=(k == 0), stop=(k == KC - 1))
                t.wait_ge(ac.h, a0 + k + 1)
                slot = k % 2
                pe.inc(t.matmul(res["pr1"][0:1, 0:n_tok], self.onesc_bf[:, :],
                                res["sqs"][:, slot * n_tok:(slot + 1) * n_tok],
                                start=(k == 0), stop=(k == KC - 1)))
            t.wait_ge(dv.h, d0 + 2)
            t.matmul(res["pa"][:, 0:n_tok], self.ones_f[:, :],
                     self.nmrow[:, 0:n_tok], start=True, stop=True)
            pe.inc(t.matmul(res["pb"][:, 0:n_tok], self.ones_f[:, :],
                            self.rsrow[:, 0:n_tok], start=True, stop=True))

        def vector_part(v):
            v.wait_ge(pe.h, p0 + KC)
            self.vstep(v, v.tensor_scalar(
                self.nmrow[:, 0:n_tok], res["pr0"][0:1, 0:n_tok],
                -1.0 / DM, None, ALU.mult))
            self.vstep(v, v.tensor_scalar(
                self.r_t1[:, 0:n_tok], res["pr1"][0:1, 0:n_tok],
                1.0 / DM, EPS_LN, ALU.mult, ALU.add))
            self.vstep(v, v.tensor_mul(self.r_t2[:, 0:n_tok],
                                       self.nmrow[:, 0:n_tok],
                                       self.nmrow[:, 0:n_tok]))
            dv.inc(v.tensor_tensor(self.r_t1[:, 0:n_tok], self.r_t1[:, 0:n_tok],
                                   self.r_t2[:, 0:n_tok], ALU.subtract))
            v.wait_ge(ac.h, a0 + KC + 1)
            dv.inc(v.reciprocal(self.rsrow[:, 0:n_tok], self.r_t2[:, 0:n_tok]))
            v.wait_ge(pe.h, p0 + KC + 1)
            for k in range(KC):
                t1 = res["t1s"][:, (k % 2) * n_tok:(k % 2 + 1) * n_tok]
                self.vstep(v, v.tensor_tensor(t1, src_slice(k),
                                              res["pa"][:, 0:n_tok], ALU.add))
                last = v.tensor_tensor(dst_slice(k), t1,
                                       res["pb"][:, 0:n_tok], ALU.mult)
            return last

        return scalar_part, tensor_part, vector_part

    def ln_f32_parts(self, res, src_slice, n_tok, base, bias_col=None,
                     dst_slice=None):
        pe, dv, ac = self.pe, self.dv, self.ac
        p0, a0, d0 = base["pe"], base["ac"], base["dv"]

        def scalar_part(sc):
            for k in range(KC):
                if k >= 2:
                    sc.wait_ge(pe.h, p0 + (k - 1))
                slot = k % 2
                ub = res["ubs"][:, slot * n_tok:(slot + 1) * n_tok]
                uq = res["uqs"][:, slot * n_tok:(slot + 1) * n_tok]
                if bias_col is not None:
                    ac.inc(sc.activation(ub, src_slice(k), AF.Identity,
                                         bias=bias_col(k)))
                    ac.inc(sc.activation(uq, src_slice(k), AF.Square,
                                         bias=bias_col(k)))
                else:
                    ac.inc(sc.activation(ub, src_slice(k), AF.Copy))
                    ac.inc(sc.activation(uq, src_slice(k), AF.Square))
            ac.inc(sc.activation(self.junk_sb[0:1, 0:8], self.junk_sb[0:1, 0:8],
                                 AF.Copy))
            sc.wait_ge(dv.h, d0 + 1)
            ac.inc(sc.activation(self.r_t2[:, 0:n_tok], self.r_t1[:, 0:n_tok],
                                 AF.Sqrt))

        def tensor_part(t):
            for k in range(KC):
                slot = k % 2
                ub = res["ubs"][:, slot * n_tok:(slot + 1) * n_tok]
                uq = res["uqs"][:, slot * n_tok:(slot + 1) * n_tok]
                t.wait_ge(ac.h, a0 + 2 * k + 2)
                t.matmul(res["pr0"][0:1, 0:n_tok], self.onesc_bf[:, :], ub,
                         start=(k == 0), stop=(k == KC - 1))
                t.wait_ge(ac.h, a0 + 2 * k + 3)
                pe.inc(t.matmul(res["pr1"][0:1, 0:n_tok], self.onesc_bf[:, :], uq,
                                start=(k == 0), stop=(k == KC - 1)))
            t.wait_ge(dv.h, d0 + 2)
            t.matmul(res["pa"][:, 0:n_tok], self.ones_f[:, :],
                     self.nmrow[:, 0:n_tok], start=True, stop=True)
            pe.inc(t.matmul(res["pb"][:, 0:n_tok], self.ones_f[:, :],
                            self.rsrow[:, 0:n_tok], start=True, stop=True))

        def vector_part(v):
            v.wait_ge(pe.h, p0 + KC)
            self.vstep(v, v.tensor_scalar(
                self.nmrow[:, 0:n_tok], res["pr0"][0:1, 0:n_tok],
                -1.0 / DM, None, ALU.mult))
            self.vstep(v, v.tensor_scalar(
                self.r_t1[:, 0:n_tok], res["pr1"][0:1, 0:n_tok],
                1.0 / DM, EPS_LN, ALU.mult, ALU.add))
            self.vstep(v, v.tensor_mul(self.r_t2[:, 0:n_tok],
                                       self.nmrow[:, 0:n_tok],
                                       self.nmrow[:, 0:n_tok]))
            dv.inc(v.tensor_tensor(self.r_t1[:, 0:n_tok], self.r_t1[:, 0:n_tok],
                                   self.r_t2[:, 0:n_tok], ALU.subtract))
            v.wait_ge(ac.h, a0 + 2 * KC + 2)
            dv.inc(v.reciprocal(self.rsrow[:, 0:n_tok], self.r_t2[:, 0:n_tok]))
            v.wait_ge(pe.h, p0 + KC + 1)
            for k in range(KC):
                t1 = res["t1s"][:, (k % 2) * n_tok:(k % 2 + 1) * n_tok]
                if bias_col is not None:
                    op = v.scalar_tensor_tensor(t1, src_slice(k), bias_col(k),
                                                res["pa"][:, 0:n_tok],
                                                ALU.add, ALU.add)
                else:
                    op = v.tensor_tensor(t1, src_slice(k),
                                         res["pa"][:, 0:n_tok], ALU.add)
                self.vstep(v, op)
                dst = dst_slice(k) if dst_slice is not None else src_slice(k)
                last = v.tensor_tensor(dst, t1,
                                       res["pb"][:, 0:n_tok], ALU.mult)
            return last

        return scalar_part, tensor_part, vector_part

    def alloc_a(self):
        la = self.la = ExitStack()
        self.a_ = a_ = {}
        a_["xbf"] = self.sbuf("xbf", [128, KC * TILE], BF16, la)
        a_["sqs"] = self.sbuf("sqs", [128, 2 * TILE], BF16, la)
        a_["t1s"] = self.sbuf("t1s", [128, 2 * TILE], F32, la)
        a_["kfull"] = self.sbuf("kfull", [128, KC * TCORE], BF16, la)
        a_["vfull"] = self.sbuf("vfull", [128, 8 * DM], BF16, la)
        a_["wkbuf"] = self.sbuf("wkbuf", [128, 2 * DM], BF16, la)
        a_["wvbuf"] = self.sbuf("wvbuf", [128, 2 * KC * 256], BF16, la)
        a_["ktm"] = self.sbuf("ktm", [128, H * CH], BF16, la)
        a_["kvc"] = self.sbuf("kvc", [128, H * D], F32, la)
        a_["kvl"] = self.sbuf("kvl", [128, H * D], BF16, la)
        a_["pa"] = self.psum("pa", [128, TILE], F32, la)
        a_["pb"] = self.psum("pb", [128, TILE], F32, la)
        a_["pr0"] = self.psum("pr0", [1, TILE], F32, la)
        a_["pr1"] = self.psum("pr1", [1, TILE], F32, la)
        a_["ptrA"] = self.psum("ptrA", [128, 128], BF16, la)
        a_["ptrB"] = self.psum("ptrB", [128, 128], BF16, la)
        a_["ps_s0"] = self.psum("ps_s0", [128, D], F32, la)
        a_["ps_s1"] = self.psum("ps_s1", [128, D], F32, la)

    def phase_a(self):
        nc = self.nc
        gL, hL, sT, pe, dv, ac = self.gL, self.hL, self.sT, self.pe, self.dv, self.ac
        a_ = self.a_
        xbf = a_["xbf"]; sqs = a_["sqs"]; t1s = a_["t1s"]
        kfull = a_["kfull"]; vfull = a_["vfull"]
        wkbuf = a_["wkbuf"]; wvbuf = a_["wvbuf"]
        ktm = a_["ktm"]; kvc = a_["kvc"]; kvl = a_["kvl"]
        pa = a_["pa"]; pb = a_["pb"]; pr0 = a_["pr0"]; pr1 = a_["pr1"]
        ptrA = a_["ptrA"]; ptrB = a_["ptrB"]
        ps_s0 = a_["ps_s0"]; ps_s1 = a_["ps_s1"]
        lnres = dict(sqs=sqs, pa=pa, pb=pb, pr0=pr0, pr1=pr1, t1s=t1s)

        gx = {0: self.gx0}
        for t in range(NT):
            t0 = t * TILE
            base = dict(pe=pe.v, ac=ac.v, dv=dv.v)
            sp, tp, vp = self.ln_bf_parts(
                lnres,
                lambda k: xbf[:, k * TILE:(k + 1) * TILE],
                lambda k, t0=t0: self.xln(k, t0, TILE),
                TILE, base)
            with nc.Block(f"A1_{t}") as blk:
                @blk.gpsimd
                def _(g, t=t, base=base):
                    if t == 0:
                        g.wait_ge(dv.h, base["dv"] + 3)
                        gx[1] = gL.inc(g.dma_start(
                            out=xbf[:, :].rearrange("p (k t) -> p k t", t=TILE),
                            in_=self.fmj(self.x_d)[:, :, TILE:2 * TILE]), 16)
                    else:
                        pass

                @blk.scalar
                def _(sc):
                    sc.wait_ge(gL.h, gx[t])
                    sp(sc)

                @blk.tensor
                def _(t_):
                    t_.wait_ge(gL.h, gx[t])
                    t_.wait_ge(gL.h, self.setup_gL)
                    t_.wait_ge(hL.h, self.setup_hL)
                    tp(t_)

                @blk.vector
                def _(v):
                    dv.inc(vp(v))
            assert pe.v == base["pe"] + 17 and ac.v == base["ac"] + 17
            assert dv.v == base["dv"] + 3
        if not self.lvl("A2"):
            return

        p0, a0 = pe.v, ac.v
        g_wk, g_wv = {}, {}
        with nc.Block("A2") as blk:
            @blk.gpsimd
            def _(g):
                g_wk.update(self.pre_wk)
                for fo in range(KC):
                    if fo < 2:
                        continue
                    g.wait_ge(pe.h, p0 + 2 * (fo - 1))
                    slot = fo % 2
                    g_wk[fo] = gL.inc(g.dma_start(
                        out=wkbuf[:, slot * DM:(slot + 1) * DM]
                            .rearrange("p (k b) -> p k b", b=128),
                        in_=self.wk_d[fo * DM:(fo + 1) * DM, :]
                            .rearrange("(k a) b -> a k b", a=128)), 16)
                for n8 in range(8):
                    if n8 >= 2:
                        g.wait_ge(pe.h, p0 + 32 + 8 * (n8 - 1))
                    slot = n8 % 2
                    g_wv[n8] = gL.inc(g.dma_start(
                        out=wvbuf[:, slot * KC * 256:(slot + 1) * KC * 256]
                            .rearrange("p (k t) -> p k t", t=256),
                        in_=self.wv_d[:, n8 * 256:(n8 + 1) * 256]
                            .rearrange("(k a) t -> a k t", a=128)), 16)
                if self.debug == 1:
                    sT.inc(g.dma_start(
                        out=self.fmj(self.dbg["dbg_xln"]),
                        in_=self.xln_sb[:, :]
                            .rearrange("p (k t) -> p k t", t=TCORE)), 16)

            @blk.tensor
            def _(t_):
                for fo in range(KC):
                    t_.wait_ge(gL.h, g_wk[fo])
                    wslot = fo % 2
                    for half in range(2):
                        idx = fo * 2 + half
                        ps = pa if idx % 2 == 0 else pb
                        if idx >= 2:
                            t_.wait_ge(ac.h, a0 + idx - 1)
                        for k in range(KC):
                            mm = t_.matmul(
                                ps[:, :],
                                wkbuf[:, wslot * DM + k * 128:wslot * DM + (k + 1) * 128],
                                self.xln(k, half * TILE, TILE),
                                start=(k == 0), stop=(k == KC - 1))
                        pe.inc(mm)
                for n8 in range(8):
                    t_.wait_ge(gL.h, g_wv[n8])
                    wslot = n8 % 2
                    for tc in range(8):
                        idx = n8 * 8 + tc
                        ps = pa if idx % 2 == 0 else pb
                        if idx >= 2:
                            t_.wait_ge(ac.h, a0 + 32 + idx - 1)
                        for k in range(KC):
                            mm = t_.matmul(
                                ps[:, 0:256],
                                self.xln(k, tc * 128, 128),
                                wvbuf[:, wslot * KC * 256 + k * 256:
                                      wslot * KC * 256 + (k + 1) * 256],
                                start=(k == 0), stop=(k == KC - 1))
                        pe.inc(mm)

            @blk.scalar
            def _(sc):
                for fo in range(KC):
                    for half in range(2):
                        idx = fo * 2 + half
                        ps = pa if idx % 2 == 0 else pb
                        sc.wait_ge(pe.h, p0 + idx + 1)
                        ac.inc(sc.activation(
                            kfull[:, fo * TCORE + half * TILE:
                                  fo * TCORE + (half + 1) * TILE],
                            ps[:, :], AF.Silu))
                for n8 in range(8):
                    for tc in range(8):
                        idx = n8 * 8 + tc
                        ps = pa if idx % 2 == 0 else pb
                        sc.wait_ge(pe.h, p0 + 32 + idx + 1)
                        ac.inc(sc.activation(
                            vfull[:, tc * DM + n8 * 256:tc * DM + (n8 + 1) * 256],
                            ps[:, 0:256], AF.Silu))

            @blk.sync
            def _(sy):
                sy.wait_ge(ac.h, a0 + 32)
                self.st_k = sT.inc(sy.dma_start(
                    out=self.fmj(self.k_sc),
                    in_=kfull[:, :].rearrange("p (f t) -> p f t", t=TCORE)), 16)
                sy.wait_ge(ac.h, a0 + 96)
                self.st_v = sT.inc(sy.dma_start(
                    out=self.v_sc.rearrange("(tc p) e -> p tc e", p=128),
                    in_=vfull[:, :].rearrange("p (tc e) -> p tc e", e=DM)), 16)

        assert pe.v == p0 + 96 and ac.v == a0 + 96
        if not self.lvl("A"):
            return

        sT0 = sT.v
        with nc.Block("A3") as blk:
            pe0_by_cl = {}
            dv0_by_cl = {}
            p = pe.v
            d = dv.v
            for cl in range(NCH):
                pe0_by_cl[cl] = p
                dv0_by_cl[cl] = d
                p += 48
                d += 48 + (1 if cl < NCH - 1 else 0)
            dv_final = d

            @blk.tensor
            def _(t_):
                for cl in range(NCH):
                    p0c, d0c = pe0_by_cl[cl], dv0_by_cl[cl]
                    for i in range(2 * H):
                        h, nh = i // 2, i % 2
                        if i >= 2:
                            t_.wait_ge(dv.h, d0c + (i - 2) + 1)
                        ptr = ptrA if i % 2 == 0 else ptrB
                        pe.inc(t_.transpose(
                            ptr[:, :],
                            kfull[:, h * TCORE + cl * CH + nh * 128:
                                  h * TCORE + cl * CH + (nh + 1) * 128],
                            self.id_bf[:, :]))
                    for h in range(H):
                        t_.wait_ge(dv.h, d0c + 2 * h + 2)
                        if h >= 2:
                            t_.wait_ge(dv.h, d0c + 32 + (h - 2) + 1)
                        ps = ps_s0 if h % 2 == 0 else ps_s1
                        for nh in range(2):
                            tc = cl * 2 + nh
                            mm = t_.matmul(
                                ps[:, :],
                                ktm[:, (2 * h + nh) * 128:(2 * h + nh + 1) * 128],
                                vfull[:, tc * DM + h * D:tc * DM + (h + 1) * D],
                                start=(nh == 0), stop=(nh == 1))
                        pe.inc(mm)

            @blk.vector
            def _(v):
                for cl in range(NCH):
                    p0c, d0c = pe0_by_cl[cl], dv0_by_cl[cl]
                    for i in range(2 * H):
                        h, nh = i // 2, i % 2
                        v.wait_ge(pe.h, p0c + i + 1)
                        ptr = ptrA if i % 2 == 0 else ptrB
                        dv.inc(v.tensor_scalar(
                            ktm[:, i * 128:(i + 1) * 128],
                            ptr[:, :],
                            self.kdec_sb[:, 2 * h + nh:2 * h + nh + 1],
                            None, ALU.mult))
                    for h in range(H):
                        v.wait_ge(pe.h, p0c + 32 + h + 1)
                        ps = ps_s0 if h % 2 == 0 else ps_s1
                        kvch = kvc[:, h * D:(h + 1) * D]
                        if cl == 0:
                            dv.inc(v.tensor_copy(kvch, ps[:, :]))
                        else:
                            dv.inc(v.scalar_tensor_tensor(
                                kvch, kvch, self.bd_sb[:, h:h + 1], ps[:, :],
                                ALU.mult, ALU.add))
                    if cl < NCH - 1:
                        if cl > 0:
                            v.wait_ge(sT.h, sT0 + 16 * cl)
                        v.wait_ge(dv.h, dv.v)
                        dv.inc(v.tensor_copy(kvl[:, :], kvc[:, :]))

            @blk.sync
            def _(sy):
                for cl in range(NCH - 1):
                    sy.wait_ge(dv.h, dv0_by_cl[cl] + 49)
                    self.st_kvl[cl] = sT.inc(sy.dma_start(
                        out=self.hdj(self.kvloc_sc[cl * H * D:(cl + 1) * H * D, :]),
                        in_=kvl[:, :].rearrange("p (h e) -> p h e", e=D)), 16)
                sy.wait_ge(dv.h, dv_final)
                self.st_T = sT.inc(sy.dma_start(
                    out=self.hdj(self.cc_in),
                    in_=kvc[:, :].rearrange("p (h e) -> p h e", e=D)), 16)
                if self.debug == 1:
                    sT.inc(sy.dma_start(
                        out=self.hdj(self.dbg["dbg_T"]),
                        in_=kvc[:, :].rearrange("p (h e) -> p h e", e=D)), 16)

            @blk.gpsimd
            def _(g):
                g.wait_ge(sT.h, self.st_T)
                self.cc.inc(g.collective_compute(
                    "AllGather", ALU.bypass,
                    replica_groups=[list(range(NCORES))],
                    ins=[self.cc_in.opt()],
                    outs=[self.cc_out.opt()]))

        assert pe.v == pe0_by_cl[NCH - 1] + 48
        assert dv.v == dv_final

    def phase_b(self):
        nc = self.nc
        gL, hL, sT, pe, dv, ac = self.gL, self.hL, self.sT, self.pe, self.dv, self.ac
        self.la.close()
        lb = self.lb = ExitStack()
        q_tile = self.sbuf("q_tile", [128, H * TILE], BF16, lb)
        g_tile = self.sbuf("g_tile", [128, H * TILE], BF16, lb)
        kv_in = self.sbuf("kv_in", [128, H * D], F32, lb)
        kv_bf = self.sbuf("kv_bf", [128, H * D], BF16, lb)
        kvl_in = self.sbuf("kvl_in", [128, H * D], BF16, lb)
        kch = self.sbuf("kch", [128, H * CH], BF16, lb)
        vch = self.sbuf("vch", [128, 2 * DM], BF16, lb)
        outbf = self.sbuf("outbf", [128, H * CH], BF16, lb)
        sqb = self.sbuf("sqb", [128, 2 * CH], BF16, lb)
        qd_scr = self.sbuf("qd_scr", [128, 2 * CH], BF16, lb)
        kqd_scr = self.sbuf("kqd_scr", [128, 2 * CH], BF16, lb)
        wqg = self.sbuf("wqg", [128, 2 * DM], BF16, lb)
        og_tile = self.sbuf("og_tile", [128, H * TILE], BF16, lb)
        z_sb = self.sbuf("z_sb", [128, KC * TILE], F32, lb)
        zb_scr = self.sbuf("zb_scr", [128, 2 * TILE], BF16, lb)
        zq_scr = self.sbuf("zq_scr", [128, 2 * TILE], BF16, lb)
        t1b = self.sbuf("t1b", [128, 2 * TILE], F32, lb)
        pa = self.psum("pba", [128, TILE], F32, lb)
        pb = self.psum("pbb", [128, TILE], F32, lb)
        ps_kq = self.psum("ps_kq", [128, TILE], F32, lb)
        ps_o0 = self.psum("ps_o0", [128, CH], F32, lb)
        ps_o1 = self.psum("ps_o1", [128, CH], F32, lb)
        pr0 = self.psum("pbr0", [1, TILE], F32, lb)
        pr1 = self.psum("pbr1", [1, TILE], F32, lb)
        pbc2 = self.psum("pbc2", [128, TILE], F32, lb)

        def b1_parts(t, p_base, a_base, g_w, init_ac_wait=None,
                     init_gp_waits=None):
            t0 = t * TILE

            def gp(g):
                for i, wd in enumerate([self.wq_d, self.wg_d]):
                    for fo in range(KC):
                        idx = i * KC + fo
                        if idx >= 2:
                            g.wait_ge(pe.h, p_base + idx - 1)
                        elif init_gp_waits is not None:
                            g.wait_ge(pe.h, init_gp_waits[idx])
                        slot = idx % 2
                        g_w[idx] = gL.inc(g.dma_start(
                            out=wqg[:, slot * DM:(slot + 1) * DM]
                                .rearrange("p (k b) -> p k b", b=128),
                            in_=wd[fo * DM:(fo + 1) * DM, :]
                                .rearrange("(k a) b -> a k b", a=128)), 16)

            def tp(t_):
                for idx in range(2 * KC):
                    t_.wait_ge(gL.h, g_w[idx])
                    wslot = idx % 2
                    ps = pa if idx % 2 == 0 else pb
                    if idx >= 2:
                        t_.wait_ge(ac.h, a_base + idx - 1)
                    elif init_ac_wait is not None:
                        t_.wait_ge(ac.h, init_ac_wait)
                    for k in range(KC):
                        mm = t_.matmul(
                            ps[:, :],
                            wqg[:, wslot * DM + k * 128:wslot * DM + (k + 1) * 128],
                            self.xln(k, t0, TILE),
                            start=(k == 0), stop=(k == KC - 1))
                    pe.inc(mm)

            def sp(sc):
                for idx in range(2 * KC):
                    fo = idx % KC
                    ps = pa if idx % 2 == 0 else pb
                    dst = q_tile if idx < KC else g_tile
                    fn = AF.Silu if idx < KC else AF.Sigmoid
                    sc.wait_ge(pe.h, p_base + idx + 1)
                    ac.inc(sc.activation(
                        dst[:, fo * TILE:(fo + 1) * TILE], ps[:, :], fn))

            return gp, tp, sp

        def b1_block(t, with_kv=False):
            p0, a0, d0 = pe.v, ac.v, dv.v
            g_w = {}
            h_T = {}
            gp, tp, sp = b1_parts(t, p0, a0, g_w)
            with nc.Block(f"B1_{t}") as blk:
                if with_kv:
                    @blk.sync
                    def _(sy):
                        sy.wait_ge(self.cc.h, self.cc.v)
                        issue_b2_loads(sy, 0)
                        for r in range(NCORES):
                            if r >= 1:
                                sy.wait_ge(dv.h, d0 + r)
                            h_T[r] = hL.inc(sy.dma_start(
                                out=z_sb[:, 0:H * D]
                                    .rearrange("p (h e) -> p h e", e=D),
                                in_=self.hdj(self.cc_out[r * H * D:
                                                         (r + 1) * H * D, :])), 16)
                        if self.debug == 1:
                            sy.wait_ge(dv.h, d0 + NCORES)
                            sT.inc(sy.dma_start(
                                out=self.hdj(self.dbg["dbg_kvin"]),
                                in_=kv_in[:, :]
                                    .rearrange("p (h e) -> p h e", e=D)), 16)

                @blk.gpsimd
                def _(g):
                    if with_kv:
                        g.wait_ge(self.cc.h, self.cc.v)
                    gp(g)
                    if self.debug == 1 and t == 0:
                        g.wait_ge(ac.h, a0 + 2 * KC)
                        sT.inc(g.dma_start(
                            out=self.fmj(self.dbg["dbg_q"]),
                            in_=q_tile[:, :].rearrange(
                                "p (h t) -> p h t", t=TILE)), 16)
                        sT.inc(g.dma_start(
                            out=self.fmj(self.dbg["dbg_g"]),
                            in_=g_tile[:, :].rearrange(
                                "p (h t) -> p h t", t=TILE)), 16)

                @blk.tensor
                def _(t_):
                    tp(t_)

                @blk.scalar
                def _(sc):
                    sp(sc)

                if with_kv:
                    @blk.vector
                    def _(v):
                        stg = z_sb[:, H * D:2 * H * D]
                        v.memset(kv_in[:, :], 0.0)
                        for r in range(NCORES):
                            v.wait_ge(hL.h, h_T[r])
                            for h in range(H):
                                ts = v.tensor_scalar(
                                    stg[:, h * D:(h + 1) * D],
                                    z_sb[:, h * D:(h + 1) * D],
                                    self.u_sb[:, r * H + h:r * H + h + 1],
                                    None, ALU.mult)
                            self.vstep(v, ts)
                            tt = v.tensor_add(kv_in[:, :], kv_in[:, :], stg)
                            dv.inc(tt)
                            v.wait_ge(dv.h, dv.v)
            assert pe.v == p0 + 2 * KC and ac.v == a0 + 2 * KC
            assert not with_kv or dv.v == d0 + NCORES

        def issue_b2_loads(sy, cl):
            loads = []
            sy.wait_ge(sT.h, self.st_k)
            loads.append(hL.inc(sy.dma_start(
                out=kch[:, :].rearrange("p (h t) -> p h t", t=CH),
                in_=self.fmj(self.k_sc)[:, :, cl * CH:(cl + 1) * CH]), 16))
            sy.wait_ge(sT.h, self.st_v)
            loads.append(hL.inc(sy.dma_start(
                out=vch[:, :].rearrange("p (tc e) -> p tc e", e=DM),
                in_=self.v_sc[cl * CH:(cl + 1) * CH, :]
                    .rearrange("(tc p) e -> p tc e", p=128)), 16))
            if cl > 0:
                sy.wait_ge(sT.h, self.st_kvl[cl - 1])
                loads.append(hL.inc(sy.dma_start(
                    out=kvl_in[:, :].rearrange("p (h e) -> p h e", e=D),
                    in_=self.hdj(self.kvloc_sc[(cl - 1) * H * D:
                                               cl * H * D, :])), 16))
            self.b2_loads[cl] = loads
            return loads

        self.issue_b2_loads = issue_b2_loads

        def b2_block(cl):
            t = cl // 2
            t0 = t * TILE
            co = (cl % 2) * CH
            p0, a0, d0 = pe.v, ac.v, dv.v
            loads = self.b2_loads[cl]
            with nc.Block(f"B2_{cl}") as blk:
                @blk.sync
                def _(sy):
                    if cl < NCH - 1:
                        sy.wait_ge(pe.h, p0 + 47)
                        sy.wait_ge(dv.h, d0 + 1)
                        issue_b2_loads(sy, cl + 1)

                qd_ev = lambda h: d0 + 2 * h + 2
                kqd_ev = lambda h: d0 + 2 * h + 3
                kq_ev = lambda h: p0 + 3 * h + 1
                out_ev = lambda h: p0 + 3 * h + 2
                stat_ev = lambda h: p0 + 3 * h + 3
                bc_ev = p0 + 49
                oev = lambda h: a0 + 2 * h + 1
                sq_ev = lambda h: a0 + 2 * h + 2

                @blk.tensor
                def _(t_):
                    for lv in loads:
                        t_.wait_ge(hL.h, lv)
                    t_.wait_ge(dv.h, d0 + 1)
                    for h in range(H):
                        if h >= 1:
                            t_.wait_ge(dv.h, kqd_ev(h - 1))
                        for nh in range(2):
                            mm = t_.matmul(
                                ps_kq[:, nh * CH:(nh + 1) * CH],
                                kch[:, h * CH + nh * 128:h * CH + (nh + 1) * 128],
                                q_tile[:, h * TILE + co:h * TILE + co + CH],
                                start=True, stop=True)
                        pe.inc(mm)
                        t_.wait_ge(dv.h, kqd_ev(h))
                        if h >= 2:
                            t_.wait_ge(ac.h, sq_ev(h - 2))
                        ps = ps_o0 if h % 2 == 0 else ps_o1
                        t_.matmul(ps[:, :], vch[:, 0 * DM + h * D:0 * DM + (h + 1) * D],
                                  kqd_scr[:, 0:CH], start=True, stop=False)
                        t_.matmul(ps[:, :], vch[:, 1 * DM + h * D:1 * DM + (h + 1) * D],
                                  kqd_scr[:, CH:2 * CH], start=False, stop=False)
                        qslot = h % 2
                        pe.inc(t_.matmul(ps[:, :], kv_bf[:, h * D:(h + 1) * D],
                                         qd_scr[:, qslot * CH:(qslot + 1) * CH],
                                         start=False, stop=True))
                        t_.wait_ge(ac.h, sq_ev(h))
                        slot = h % 2
                        pe.inc(t_.matmul(pr0[0:1, 0:CH], self.onesc_bf[:, :],
                                         sqb[:, slot * CH:(slot + 1) * CH],
                                         start=(h == 0), stop=(h == H - 1)))
                    t_.wait_ge(dv.h, d0 + 35)
                    t_.wait_ge(ac.h, oev(H - 1))
                    pe.inc(t_.matmul(ps_o0[:, :], self.ones_f[:, :],
                                     self.rsrow[:, 0:CH], start=True, stop=True))

                @blk.scalar
                def _(sc):
                    for h in range(H):
                        sc.wait_ge(pe.h, out_ev(h))
                        ps = ps_o0 if h % 2 == 0 else ps_o1
                        ac.inc(sc.activation(outbf[:, h * CH:(h + 1) * CH],
                                             ps[:, :], AF.Copy))
                        if h >= 2:
                            sc.wait_ge(pe.h, stat_ev(h - 2))
                        slot = h % 2
                        ac.inc(sc.activation(sqb[:, slot * CH:(slot + 1) * CH],
                                             ps[:, :], AF.Square))
                    sc.wait_ge(dv.h, d0 + 34)
                    ac.inc(sc.activation(self.r_t2[0:1, 0:CH],
                                         self.r_t1[0:1, 0:CH], AF.Sqrt))

                @blk.vector
                def _(v):
                    if cl == 0:
                        dv.inc(v.tensor_copy(kv_bf[:, :], kv_in[:, :]))
                    else:
                        v.wait_ge(hL.h, loads[2])
                        for h in range(H):
                            last = v.scalar_tensor_tensor(
                                kv_bf[:, h * D:(h + 1) * D],
                                kv_in[:, h * D:(h + 1) * D],
                                self.bdp_sb[:, cl * H + h:cl * H + h + 1],
                                kvl_in[:, h * D:(h + 1) * D],
                                ALU.mult, ALU.add)
                        dv.inc(last)
                    for h in range(H):
                        qslot = h % 2
                        if h >= 2:
                            v.wait_ge(pe.h, out_ev(h - 2))
                        dv.inc(v.tensor_mul(
                            qd_scr[:, qslot * CH:(qslot + 1) * CH],
                            q_tile[:, h * TILE + co:h * TILE + co + CH],
                            self.qdec_sb[:, h * CH:(h + 1) * CH]))
                        v.wait_ge(pe.h, kq_ev(h))
                        v.tensor_mul(kqd_scr[:, 0:CH], ps_kq[:, 0:CH],
                                     self.diag_sb[:, (2 * h) * CH:(2 * h + 1) * CH])
                        dv.inc(v.tensor_mul(
                            kqd_scr[:, CH:2 * CH], ps_kq[:, CH:2 * CH],
                            self.diag_sb[:, (2 * h + 1) * CH:(2 * h + 2) * CH]))
                    v.wait_ge(pe.h, stat_ev(H - 1))
                    dv.inc(v.tensor_scalar(self.r_t1[0:1, 0:CH], pr0[0:1, 0:CH],
                                           1.0 / DM, EPS_RMS, ALU.mult, ALU.add))
                    v.wait_ge(ac.h, a0 + 33)
                    dv.inc(v.reciprocal(self.rsrow[0:1, 0:CH], self.r_t2[0:1, 0:CH]))
                    v.wait_ge(pe.h, bc_ev)
                    for h in range(H):
                        self.vstep(v, v.scalar_tensor_tensor(
                            og_tile[:, h * TILE + co:h * TILE + co + CH],
                            outbf[:, h * CH:(h + 1) * CH],
                            self.nw_sb[:, h:h + 1],
                            ps_o0[:, :], ALU.mult, ALU.mult))
                        last = v.tensor_mul(
                            og_tile[:, h * TILE + co:h * TILE + co + CH],
                            og_tile[:, h * TILE + co:h * TILE + co + CH],
                            g_tile[:, h * TILE + co:h * TILE + co + CH])
                    dv.inc(last)

                if self.debug == 1 and cl == 0:
                    @blk.gpsimd
                    def _(g):
                        g.wait_ge(dv.h, dv.v)
                        sT.inc(g.dma_start(
                            out=self.fmj(self.dbg["dbg_ob"]),
                            in_=outbf[:, :].rearrange(
                                "p (h t) -> p h t", t=CH)), 16)
                        sT.inc(g.dma_start(
                            out=self.dbg["dbg_rms"],
                            in_=self.rsrow[0:1, 0:CH]), 16)

            assert pe.v == p0 + 49, (pe.v, p0)
            assert ac.v == a0 + 33
            assert dv.v == d0 + 36

        def b3_block(t, merge_b1=False):
            t0 = t * TILE
            p0, a0, d0 = pe.v, ac.v, dv.v
            g_w = {}
            g_w1 = {}
            g_acc = []
            nb1 = 32 if merge_b1 else 0
            bc_ev = p0 + 49
            ln2res = dict(ubs=zb_scr, uqs=zq_scr, pa=ps_kq, pb=pbc2,
                          pr0=pr0, pr1=pr1, t1s=t1b)
            ln2base = dict(pe=p0 + 49 + nb1, ac=a0 + 49 + nb1, dv=d0 + 3)
            sp2, tp2, vp2 = self.ln_f32_parts(
                ln2res,
                lambda k: z_sb[:, k * TILE:(k + 1) * TILE],
                TILE, ln2base,
                dst_slice=lambda k, t0=t0: self.xln(k, t0, TILE))
            if merge_b1:
                gp1, tp1, sp1 = b1_parts(1, p0 + 49, a0 + 49, g_w1,
                                         init_ac_wait=a0 + 48,
                                         init_gp_waits=(p0 + 3 * 14 + 1,
                                                        p0 + 3 * 15 + 1))
            wo_ev = lambda o: p0 + 3 * o + 1
            zstat_ev = lambda o: p0 + 3 * o + 3
            z_ev = lambda o: a0 + 3 * o + 1
            zb_ev = lambda o: a0 + 3 * o + 2
            zq_ev = lambda o: a0 + 3 * o + 3
            with nc.Block(f"B3_{t}") as blk:
                @blk.gpsimd
                def _(g):
                    for o in range(KC):
                        if o >= 2:
                            g.wait_ge(pe.h, wo_ev(o - 2))
                        slot = o % 2
                        g_w[o] = gL.inc(g.dma_start(
                            out=wqg[:, slot * DM:(slot + 1) * DM]
                                .rearrange("p (f b) -> p f b", b=128),
                            in_=self.wo_d[o * DM:(o + 1) * DM, :]
                                .rearrange("(f a) b -> a f b", a=128)), 16)
                    if merge_b1:
                        gp1(g)
                    if self.debug == 1:
                        sT.inc(g.dma_start(
                            out=self.fmj(self.dbg["dbg_og"])[:, :, t0:t0 + TILE],
                            in_=og_tile[:, :].rearrange("p (h t) -> p h t", t=TILE)),
                            16)
                    g.wait_ge(dv.h, d0 + 3)
                    g_acc.append(gL.inc(g.dma_start(
                        out=z_sb[:, :].rearrange("p (k t) -> p k t", t=TILE),
                        in_=self.fmj(self.x_d)[:, :, t0:t0 + TILE],
                        accum_op=ALU.add), 16))

                @blk.tensor
                def _(t_):
                    for o in range(KC):
                        t_.wait_ge(gL.h, g_w[o])
                        wslot = o % 2
                        ps = pa if o % 2 == 0 else pb
                        if o >= 2:
                            t_.wait_ge(ac.h, zq_ev(o - 2))
                        for f in range(KC):
                            mm = t_.matmul(
                                ps[:, :],
                                wqg[:, wslot * DM + f * 128:wslot * DM + (f + 1) * 128],
                                og_tile[:, f * TILE:(f + 1) * TILE],
                                start=(f == 0), stop=(f == KC - 1))
                        pe.inc(mm)
                        slot = o % 2
                        t_.wait_ge(ac.h, zb_ev(o))
                        pe.inc(t_.matmul(pr0[0:1, 0:TILE], self.onesc_bf[:, :],
                                         zb_scr[:, slot * TILE:(slot + 1) * TILE],
                                         start=(o == 0), stop=(o == KC - 1)))
                        t_.wait_ge(ac.h, zq_ev(o))
                        pe.inc(t_.matmul(pr1[0:1, 0:TILE], self.onesc_bf[:, :],
                                         zq_scr[:, slot * TILE:(slot + 1) * TILE],
                                         start=(o == 0), stop=(o == KC - 1)))
                    t_.wait_ge(dv.h, d0 + 2)
                    t_.matmul(ps_kq[:, :], self.ones_f[:, :], self.nmrow[:, 0:TILE],
                              start=True, stop=True)
                    pe.inc(t_.matmul(pbc2[:, :], self.ones_f[:, :],
                                     self.rsrow[:, 0:TILE], start=True, stop=True))
                    if merge_b1:
                        tp1(t_)
                    t_.wait_ge(dv.h, d0 + 1)
                    tp2(t_)

                @blk.scalar
                def _(sc):
                    if t > 0 and 0 in self.st_x2:
                        sc.wait_ge(sT.h, self.st_x2[0])
                    for o in range(KC):
                        sc.wait_ge(pe.h, wo_ev(o))
                        ps = pa if o % 2 == 0 else pb
                        slot = o % 2
                        ac.inc(sc.activation(z_sb[:, o * TILE:(o + 1) * TILE],
                                             ps[:, :], AF.Copy))
                        if o >= 2:
                            sc.wait_ge(pe.h, zstat_ev(o - 2))
                        ac.inc(sc.activation(zb_scr[:, slot * TILE:(slot + 1) * TILE],
                                             ps[:, :], AF.Copy))
                        ac.inc(sc.activation(zq_scr[:, slot * TILE:(slot + 1) * TILE],
                                             ps[:, :], AF.Square))
                    sc.wait_ge(dv.h, d0 + 1)
                    ac.inc(sc.activation(self.r_t2[0:1, 0:TILE],
                                         self.r_t1[0:1, 0:TILE], AF.Sqrt))
                    if merge_b1:
                        sp1(sc)
                    sc.wait_ge(gL.h, g_acc[0])
                    sp2(sc)

                @blk.vector
                def _(v):
                    v.wait_ge(pe.h, zstat_ev(KC - 1))
                    self.vstep(v, v.tensor_scalar(
                        self.nmrow[0:1, 0:TILE], pr0[0:1, 0:TILE],
                        -1.0 / DM, None, ALU.mult))
                    self.vstep(v, v.tensor_scalar(
                        self.r_t1[0:1, 0:TILE], pr1[0:1, 0:TILE],
                        1.0 / DM, EPS_LN, ALU.mult, ALU.add))
                    self.vstep(v, v.tensor_mul(
                        self.r_t2[0:1, 0:TILE], self.nmrow[0:1, 0:TILE],
                        self.nmrow[0:1, 0:TILE]))
                    dv.inc(v.tensor_tensor(self.r_t1[0:1, 0:TILE],
                                           self.r_t1[0:1, 0:TILE],
                                           self.r_t2[0:1, 0:TILE], ALU.subtract))
                    v.wait_ge(ac.h, a0 + 3 * KC + 1)
                    dv.inc(v.reciprocal(self.rsrow[0:1, 0:TILE],
                                        self.r_t2[0:1, 0:TILE]))
                    v.wait_ge(pe.h, bc_ev)
                    for k in range(KC):
                        t1 = t1b[:, (k % 2) * TILE:(k % 2 + 1) * TILE]
                        self.vstep(v, v.tensor_tensor(
                            t1, z_sb[:, k * TILE:(k + 1) * TILE],
                            ps_kq[:, :], ALU.add))
                        last = v.tensor_tensor(z_sb[:, k * TILE:(k + 1) * TILE],
                                               t1, pbc2[:, :], ALU.mult)
                    dv.inc(last)
                    if self.debug == 1:
                        v.wait_ge(sT.h, sT.v)
                    dv.inc(vp2(v))

                @blk.sync
                def _(sy):
                    sy.wait_ge(gL.h, g_acc[0])
                    self.st_x2[t] = sT.inc(sy.dma_start(
                        out=self.fmj(self.x2_sc)[:, :, t0:t0 + TILE],
                        in_=z_sb[:, :].rearrange("p (k t) -> p k t", t=TILE)), 16)
                    if self.debug == 1:
                        sT.inc(sy.dma_start(
                            out=self.fmj(self.dbg["dbg_x2"])[:, :, t0:t0 + TILE],
                            in_=z_sb[:, :].rearrange("p (k t) -> p k t", t=TILE)), 16)
            assert pe.v == p0 + 66 + nb1 and ac.v == a0 + 83 + nb1
            assert dv.v == d0 + 6

        b1_block(0, with_kv=True)
        if not self.lvl("B2"):
            return
        b2_block(0)
        b2_block(1)
        if not self.lvl("B3"):
            return
        b3_block(0, merge_b1=True)
        if not self.lvl("B"):
            return
        b2_block(2)
        b2_block(3)
        b3_block(1)

    def phase_c(self):
        nc = self.nc
        gL, hL, sT, pe, dv, ac = self.gL, self.hL, self.sT, self.pe, self.dv, self.ac
        self.lb.close()
        self.lab.close()
        lc = self.lc = ExitStack()
        pa = self.psum("pca", [128, TILE], F32, lc)
        pb = self.psum("pcb", [128, TILE], F32, lc)
        pr0 = self.psum("pcr0", [1, TILE], F32, lc)
        pr1 = self.psum("pcr1", [1, TILE], F32, lc)

        x2ln = self.xln

        if not self.lvl("C2"):
            return

        mid = self.sbuf("mid", [128, 32 * TCORE], BF16, lc)
        y_sb = self.sbuf("y_sb", [128, KC * TCORE], F32, lc)
        w0buf = self.sbuf("w0buf", [128, 2 * DM], BF16, lc)
        w1buf = self.sbuf("w1buf", [128, 2 * 32 * 128], BF16, lc)
        ubs = self.sbuf("ubs", [128, 2 * TILE], BF16, lc)
        uqs = self.sbuf("uqs", [128, 2 * TILE], BF16, lc)
        t1d = self.sbuf("t1d", [128, 2 * TILE], F32, lc)

        for h2 in range(2):
            p0, a0, d0 = pe.v, ac.v, dv.v
            pW1 = p0 + 64
            g_w0, g_w1 = {}, {}
            with nc.Block(f"C2_{h2}") as blk:
                @blk.gpsimd
                def _(g, h2=h2):
                    for jl in range(32):
                        if jl >= 2:
                            g.wait_ge(pe.h, p0 + 2 * (jl - 1))
                        slot = jl % 2
                        jo = h2 * 32 + jl
                        g_w0[jl] = gL.inc(g.dma_start(
                            out=w0buf[:, slot * DM:(slot + 1) * DM]
                                .rearrange("p (k b) -> p k b", b=128),
                            in_=self.w0_d[jo * DM:(jo + 1) * DM, :]
                                .rearrange("(k a) b -> a k b", a=128)), 16)
                    for o in range(KC):
                        if o >= 2:
                            g.wait_ge(pe.h, pW1 + 2 * (o - 1))
                        slot = o % 2
                        base_r = h2 * FF * 8 + o * 32 * 128
                        g_w1[o] = gL.inc(g.dma_start(
                            out=w1buf[:, slot * 32 * 128:(slot + 1) * 32 * 128]
                                .rearrange("p (j b) -> p j b", b=128),
                            in_=self.w1_d[base_r:base_r + 32 * 128, :]
                                .rearrange("(j a) b -> a j b", a=128)), 16)

                @blk.tensor
                def _(t_, h2=h2):
                    for jl in range(32):
                        t_.wait_ge(gL.h, g_w0[jl])
                        wslot = jl % 2
                        for nt in range(2):
                            idx = 2 * jl + nt
                            ps = pa if idx % 2 == 0 else pb
                            if idx >= 2:
                                t_.wait_ge(ac.h, a0 + idx - 1)
                            for k in range(KC):
                                mm = t_.matmul(
                                    ps[:, :],
                                    w0buf[:, wslot * DM + k * 128:
                                          wslot * DM + (k + 1) * 128],
                                    x2ln(k, nt * TILE, TILE),
                                    start=(k == 0), stop=(k == KC - 1))
                            pe.inc(mm)
                    t_.wait_ge(ac.h, a0 + 64)
                    for o in range(KC):
                        t_.wait_ge(gL.h, g_w1[o])
                        wslot = o % 2
                        for nt in range(2):
                            idx = 2 * o + nt
                            ps = pa if idx % 2 == 0 else pb
                            if idx >= 2:
                                if h2 == 0:
                                    t_.wait_ge(ac.h, a0 + 64 + idx - 1)
                                else:
                                    t_.wait_ge(dv.h, d0 + idx - 1)
                            for jl in range(32):
                                mm = t_.matmul(
                                    ps[:, :],
                                    w1buf[:, wslot * 32 * 128 + jl * 128:
                                          wslot * 32 * 128 + (jl + 1) * 128],
                                    mid[:, jl * TCORE + nt * TILE:
                                        jl * TCORE + (nt + 1) * TILE],
                                    start=(jl == 0), stop=(jl == 31))
                            pe.inc(mm)

                @blk.scalar
                def _(sc, h2=h2):
                    for jl in range(32):
                        jo = h2 * 32 + jl
                        for nt in range(2):
                            idx = 2 * jl + nt
                            ps = pa if idx % 2 == 0 else pb
                            sc.wait_ge(pe.h, p0 + idx + 1)
                            ac.inc(sc.activation(
                                mid[:, jl * TCORE + nt * TILE:
                                    jl * TCORE + (nt + 1) * TILE],
                                ps[:, :], AF.Gelu, bias=self.b0_sb[:, jo:jo + 1]))
                    if h2 == 0:
                        for o in range(KC):
                            for nt in range(2):
                                idx = 2 * o + nt
                                ps = pa if idx % 2 == 0 else pb
                                sc.wait_ge(pe.h, pW1 + idx + 1)
                                ac.inc(sc.activation(
                                    y_sb[:, o * TCORE + nt * TILE:
                                         o * TCORE + (nt + 1) * TILE],
                                    ps[:, :], AF.Copy))

                if h2 == 1:
                    @blk.vector
                    def _(v):
                        for o in range(KC):
                            for nt in range(2):
                                idx = 2 * o + nt
                                ps = pa if idx % 2 == 0 else pb
                                v.wait_ge(pe.h, pW1 + idx + 1)
                                ys = y_sb[:, o * TCORE + nt * TILE:
                                          o * TCORE + (nt + 1) * TILE]
                                dv.inc(v.tensor_tensor(ys, ys, ps[:, :], ALU.add))

        if not self.lvl("C"):
            return
        pe0, ac0, dv0 = pe.v, ac.v, dv.v
        bases = []
        sps, tps, vps = [], [], []
        for nt in range(2):
            base = dict(pe=pe0 + 17 * nt, ac=ac0 + 34 * nt, dv=dv0 + 3 * nt)
            bases.append(base)
            lnres2 = dict(ubs=ubs, uqs=uqs, pa=pa, pb=pb, pr0=pr0, pr1=pr1, t1s=t1d)
            sp, tp, vp = self.ln_f32_parts(
                lnres2,
                lambda k, t0=nt * TILE: y_sb[:, k * TCORE + t0:k * TCORE + t0 + TILE],
                TILE, base,
                bias_col=lambda k: self.b1_sb[:, k:k + 1])
            sps.append(sp)
            tps.append(tp)
            vps.append(vp)
        if self.debug == 2:
            with nc.Block("C3dbg") as blk:
                @blk.sync
                def _(sy):
                    v = sT.inc(sy.dma_start(
                        out=self.fmj(self.dbg["dbg_y"]),
                        in_=y_sb[:, :].rearrange("p (k t) -> p k t", t=TCORE)), 16)
                    sy.wait_ge(sT.h, v)

                @blk.gpsimd
                def _(g):
                    sT.inc(g.dma_start(
                        out=self.fmj(self.dbg["dbg_x2ln"]),
                        in_=self.xln_sb[:, :]
                            .rearrange("p (k t) -> p k t", t=TCORE)), 16)
                    sT.inc(g.dma_start(
                        out=self.dbg["dbg_mid"]
                            .rearrange("(j p) t -> p j t", p=128),
                        in_=mid[:, :].rearrange("p (j t) -> p j t", t=TCORE)), 16)
        g_acc = {}
        gL0, hL0, sT0 = gL.v, hL.v, sT.v
        with nc.Block("C3") as blk:
            @blk.gpsimd
            def _(g):
                for nt in range(2):
                    g.wait_ge(dv.h, dv0 + 3 * nt + 3)
                    t0 = nt * TILE
                    g_acc[nt] = gL.inc(g.dma_start(
                        out=y_sb[:, :].rearrange(
                            "p (k t) -> p k t", t=TCORE)[:, :, t0:t0 + TILE],
                        in_=self.fmj(self.x2_sc)[:, :, t0:t0 + TILE],
                        accum_op=ALU.add), 16)

            @blk.scalar
            def _(sc):
                sc.wait_ge(gL.h, gL0)
                sc.wait_ge(hL.h, hL0)
                sc.wait_ge(sT.h, sT0)
                sps[0](sc)
                sc.wait_ge(dv.h, dv0 + 3)
                sps[1](sc)

            @blk.tensor
            def _(t_):
                t_.wait_ge(gL.h, gL0)
                t_.wait_ge(hL.h, hL0)
                t_.wait_ge(sT.h, sT0)
                tps[0](t_)
                t_.wait_ge(dv.h, dv0 + 3)
                tps[1](t_)

            @blk.vector
            def _(v):
                dv.inc(vps[0](v))
                dv.inc(vps[1](v))

            @blk.sync
            def _(sy):
                for nt in range(2):
                    t0 = nt * TILE
                    sy.wait_ge(gL.h, g_acc[nt])
                    sT.inc(sy.dma_start(
                        out=self.fmj(self.out_d)[:, :, t0:t0 + TILE],
                        in_=y_sb[:, :].rearrange(
                            "p (k t) -> p k t", t=TCORE)[:, :, t0:t0 + TILE]), 16)
        assert pe.v == pe0 + 34 and ac.v == ac0 + 68 and dv.v == dv0 + 6

    def tail_block(self):
        nc = self.nc
        with nc.Block("tail") as blk:
            @blk.sync
            def _(sy):
                sy.wait_ge(self.sT.h, self.sT.v)
                if self.cc.v:
                    sy.wait_ge(self.cc.h, self.cc.v)

    LEVELS = ["S", "A1", "A2", "A", "B1", "Bkv", "B2", "B3", "B", "C1", "C2", "C"]

    def lvl(self, name):
        return self.LEVELS.index(self.upto) >= self.LEVELS.index(name)

    def build(self):
        self.alloc_a()
        self.setup_block()
        if self.lvl("A1"):
            self.phase_a()
        if self.lvl("B1"):
            self.phase_b()
        if self.lvl("C1"):
            self.phase_c()
        self.tail_block()
        return self.nc


_CACHE = {}


def _blocked_lhsT(wT, n_out, n_k=KC):
    return np.ascontiguousarray(
        wT.reshape(n_k, 128, n_out, 128).transpose(2, 0, 1, 3).reshape(-1, 128))


def _prep(inputs):
    x = np.ascontiguousarray(np.asarray(inputs["L0_attn_x"], np.float32))
    Wqkv = np.asarray(inputs["L0_attn_Wqkv"], np.float32).reshape(H, 3, D, DM)
    Wq = Wqkv[:, 0].reshape(H * D, DM)
    Wk = Wqkv[:, 1].reshape(H * D, DM)
    Wv = Wqkv[:, 2].reshape(H * D, DM)
    Wg = np.asarray(inputs["L0_attn_Wg"], np.float32)
    Wo = np.asarray(inputs["L0_attn_Wo"], np.float32)
    W0 = np.asarray(inputs["L0_ffn_W0"], np.float32)
    W1 = np.asarray(inputs["L0_ffn_W1"], np.float32)

    d = {}
    d["wq"] = _blocked_lhsT(np.ascontiguousarray(Wq.T), KC)
    d["wk"] = _blocked_lhsT(np.ascontiguousarray(Wk.T), KC)
    d["wv"] = np.ascontiguousarray(Wv.T)
    d["wg"] = _blocked_lhsT(np.ascontiguousarray(Wg.T), KC)
    d["wo"] = _blocked_lhsT(np.ascontiguousarray(Wo.T), KC)
    d["w0"] = _blocked_lhsT(np.ascontiguousarray(W0.T), NJ)
    W1T = np.ascontiguousarray(W1.T).reshape(2, 32, 128, KC, 128)
    d["w1"] = np.ascontiguousarray(W1T.transpose(0, 3, 1, 2, 4).reshape(-1, 128))

    slope = np.asarray(inputs["L0_attn_slope_rate"], np.float32).reshape(1, H)
    d["slope"] = np.ascontiguousarray(np.broadcast_to(slope, (128, H)))
    arr = np.arange(1, CH + 1, dtype=np.float32)
    d["arrb"] = np.ascontiguousarray(np.broadcast_to(arr[None, :], (128, CH)))
    p = np.arange(128, dtype=np.float32)
    d["arrm"] = np.ascontiguousarray(
        np.stack([p + 1 - CH, p + 129 - CH], axis=1))
    mm, nn = np.meshgrid(np.arange(CH, dtype=np.float32),
                         np.arange(CH, dtype=np.float32))
    diff = mm - nn
    d["idxT"] = np.maximum(diff, 0.0).astype(np.float32)
    d["maskT"] = (diff >= 0).astype(np.float32)
    d["ident"] = np.eye(128, dtype=np.float32)
    d["onesr"] = np.ones((1, 128), np.float32)
    d["onesc"] = np.ones((128, 1), np.float32)
    d["b0c"] = np.ascontiguousarray(
        np.asarray(inputs["L0_ffn_bias0"], np.float32).reshape(NJ, 128).T)
    d["b1c"] = np.ascontiguousarray(
        np.asarray(inputs["L0_ffn_bias1"], np.float32).reshape(KC, 128).T)
    d["nwc"] = np.ascontiguousarray(
        np.asarray(inputs["L0_attn_norm_w"], np.float32).reshape(KC, 128).T)
    return x, d


def kernel(**inputs):
    key = (DEBUG, UPTO)
    if key not in _CACHE:
        _CACHE[key] = Builder(debug=DEBUG, upto=UPTO).build()
    nc = _CACHE[key]

    x, shared = _prep(inputs)
    in_maps = []
    for core in range(NCORES):
        b, qq = core // 4, core % 4
        m = dict(shared)
        m["x"] = np.ascontiguousarray(x[b, qq * TCORE:(qq + 1) * TCORE, :].T)
        sel = np.zeros((128, NCORES * 4), np.float32)
        for pidx in range(qq):
            r = b * 4 + pidx
            sel[:, r * 4 + (qq - 1 - pidx)] = 1.0
        m["sel"] = sel
        in_maps.append(m)

    res = run_bass_kernel_spmd(nc, in_maps, core_ids=list(range(NCORES)))
    out = np.empty((2, 4096, DM), np.float32)
    for core in range(NCORES):
        b, qq = core // 4, core % 4
        out[b, qq * TCORE:(qq + 1) * TCORE, :] = res.results[core]["out"].T
    return out


# version 14 (final)
# speedup vs baseline: 1.0392x; 1.0088x over previous
import numpy as np
import ml_dtypes
import concourse.bass as bass
import concourse.mybir as mybir
from concourse.bass_utils import run_bass_kernel_spmd
from contextlib import ExitStack

F32 = mybir.dt.float32
BF16 = mybir.dt.bfloat16
AF = mybir.ActivationFunctionType
ALU = mybir.AluOpType

H = 16
D = 128
DM = 2048
KC = 16
TCORE = 1024
TILE = 512
NT = 2
CH = 256
NCH = 4
FF = 8192
NJ = 64
NCORES = 8
EPS_LN = 1e-5
EPS_RMS = float(np.finfo(np.float32).eps)

DEBUG = 0
UPTO = "C"


class Sem:
    def __init__(self, nc, stack, name):
        self.h = stack.enter_context(nc.semaphore(name))
        self.v = 0

    def inc(self, instr, n=1):
        instr.then_inc(self.h, n)
        self.v += n
        return self.v


class Builder:
    def __init__(self, debug=0, upto="C"):
        self.debug = debug
        self.upto = upto
        nc = self.nc = bass.Bass("TRN2", target_bir_lowering=False)

        dram = lambda nm, shp, kind="ExternalInput", dt=F32: \
            nc.dram_tensor(nm, shp, dt, kind=kind).ap()
        self.x_d = dram("x", [DM, TCORE])
        self.wq_d = dram("wq", [KC * DM, D])
        self.wk_d = dram("wk", [KC * DM, D])
        self.wv_d = dram("wv", [DM, DM])
        self.wg_d = dram("wg", [KC * DM, D])
        self.wo_d = dram("wo", [KC * DM, D])
        self.w0_d = dram("w0", [NJ * DM, D])
        self.w1_d = dram("w1", [FF * KC, D])
        self.slope_d = dram("slope", [128, H])
        self.arr_d = dram("arrb", [128, CH])
        self.arrm_d = dram("arrm", [128, 2])
        self.idxT_d = dram("idxT", [CH, CH])
        self.mask_d = dram("maskT", [CH, CH])
        self.ident_d = dram("ident", [128, 128])
        self.ones_d = dram("onesr", [1, 128])
        self.onesc_d = dram("onesc", [128, 1])
        self.b0_d = dram("b0c", [128, NJ])
        self.b1_d = dram("b1c", [128, KC])
        self.nw_d = dram("nwc", [128, KC])
        self.sel_d = dram("sel", [128, NCORES * 4])
        self.out_d = dram("out", [DM, TCORE], kind="ExternalOutput")

        self.k_sc = dram("k_sc", [DM, TCORE], "Internal", BF16)
        self.v_sc = dram("v_sc", [TCORE, DM], "Internal", BF16)
        self.kvloc_sc = dram("kvloc_sc", [3 * H * D, D], "Internal", BF16)
        self.x2_sc = dram("x2_sc", [DM, TCORE], "Internal")
        self.cc_in = dram("cc_in", [H * D, D], "Internal")
        self.cc_out = nc.dram_tensor("cc_out", [NCORES * H * D, D], F32,
                                     kind="Internal",
                                     addr_space="Shared").ap()

        self.dbg = {}
        if debug == 2:
            self.dbg["dbg_x2ln"] = dram("dbg_x2ln", [DM, TCORE],
                                        kind="ExternalOutput")
            self.dbg["dbg_y"] = dram("dbg_y", [DM, TCORE],
                                     kind="ExternalOutput")
            self.dbg["dbg_mid"] = dram("dbg_mid", [32 * 128, TCORE],
                                       kind="ExternalOutput")
        if debug == 1:
            for nm, shp in [("dbg_xln", [DM, TCORE]), ("dbg_T", [H * D, D]),
                            ("dbg_kvin", [H * D, D]), ("dbg_og", [DM, TCORE]),
                            ("dbg_x2", [DM, TCORE]), ("dbg_q", [DM, TILE]),
                            ("dbg_g", [DM, TILE]), ("dbg_ob", [DM, CH]),
                            ("dbg_rms", [1, CH])]:
                self.dbg[nm] = dram(nm, shp, kind="ExternalOutput")

        es = self.es = ExitStack()
        self.gL = Sem(nc, es, "gL")
        self.hL = Sem(nc, es, "hL")
        self.sT = Sem(nc, es, "sT")
        self.cc = Sem(nc, es, "ccs")
        self.pe = Sem(nc, es, "pes")
        self.dv = Sem(nc, es, "dvs")
        self.ac = Sem(nc, es, "acs")
        self.dvx = Sem(nc, es, "dvx")
        self.acx = Sem(nc, es, "acx")

        sbuf = self.sbuf
        self.id_bf = sbuf("id_bf", [128, 128], BF16)
        self.ones_bf = sbuf("ones_bf", [1, 128], BF16)
        self.ones_f = sbuf("ones_f", [1, 128], F32)
        self.onesc_bf = sbuf("onesc_bf", [128, 1], BF16)
        self.s_sb = sbuf("s_sb", [128, H], F32)
        self.negs_sb = sbuf("negs_sb", [128, H], F32)
        self.bd_sb = sbuf("bd_sb", [128, H], F32)
        self.bdp_sb = sbuf("bdp_sb", [128, 4 * H], F32)
        self.bd4p_sb = sbuf("bd4p_sb", [128, 4 * H], F32)
        self.u_sb = sbuf("u_sb", [128, NCORES * H], F32)
        self.ut_sb = sbuf("ut_sb", [128, 3 * H], F32)
        self.sel_sb = sbuf("sel_sb", [128, NCORES * 4], F32)
        self.arr_sb = sbuf("arr_sb", [128, CH], F32)
        self.arrm_sb = sbuf("arrm_sb", [128, 2], F32)
        self.kdec_sb = sbuf("kdec_sb", [128, 2 * H], F32)
        self.b0_sb = sbuf("b0_sb", [128, NJ], F32)
        self.b1_sb = sbuf("b1_sb", [128, KC], F32)
        self.nw_sb = sbuf("nw_sb", [128, KC], F32)
        self.row_sb = sbuf("row_sb", [1, 4 * TILE], F32)
        self.junk_sb = sbuf("junk_sb", [1, 8], F32)
        self.nmrow = self.row_sb[0:1, 0:TILE]
        self.rsrow = self.row_sb[0:1, TILE:2 * TILE]
        self.r_t1 = self.row_sb[0:1, 2 * TILE:3 * TILE]
        self.r_t2 = self.row_sb[0:1, 3 * TILE:4 * TILE]

        self.lab_b = ExitStack()
        self.xln_sb = sbuf("xln_sb", [128, KC * TCORE], BF16, self.lab_b)
        self.lab = ExitStack()
        self.qdec_sb = sbuf("qdec_sb", [128, H * CH], BF16, self.lab)
        self.diag_sb = sbuf("diag_sb", [128, H * 2 * CH], BF16, self.lab)

        self.st_kvl = {}
        self.st_x2 = {}
        self.st_x2ln = {}
        self.pre_wk = {}
        self.gx0 = None
        self.b2_loads = {}

    def sbuf(self, nm, shp, dt, stk=None):
        return (stk or self.es).enter_context(self.nc.sbuf_tensor(nm, shp, dt))

    def psum(self, nm, shp, dt, stk=None):
        return (stk or self.es).enter_context(self.nc.psum_tensor(nm, shp, dt))

    def vstep(self, v, instr):
        self.dvx.inc(instr)
        v.wait_ge(self.dvx.h, self.dvx.v)
        return instr

    def sstep(self, sc, instr):
        self.acx.inc(instr)
        sc.wait_ge(self.acx.h, self.acx.v)
        return instr

    def xln(self, k, t0, tw):
        return self.xln_sb[:, k * TCORE + t0: k * TCORE + t0 + tw]

    @staticmethod
    def fmj(ap):
        return ap.rearrange("(k p) t -> p k t", p=128)

    @staticmethod
    def hdj(ap):
        return ap.rearrange("(h d) e -> d h e", d=128)

    def setup_block(self):
        nc = self.nc
        gL, hL, pe, dv, ac = self.gL, self.hL, self.pe, self.dv, self.ac
        ls = ExitStack()
        idx_sb = self.sbuf("idx_sb", [128, 2 * CH], F32, ls)
        mask_sb = self.sbuf("mask_sb", [128, 2 * CH], F32, ls)
        dscr = self.sbuf("dscr", [128, 2 * CH], F32, ls)
        d0, a0 = dv.v, ac.v
        diag_items = [(h, nh) for h in range(H) for nh in range(2)]
        diag_ac = {}

        with nc.Block("setup") as blk:
            @blk.sync
            def _(sy):
                for dst, src in [(self.s_sb, self.slope_d), (self.arr_sb, self.arr_d),
                                 (self.arrm_sb, self.arrm_d), (self.ones_f, self.ones_d),
                                 (self.b0_sb, self.b0_d), (self.b1_sb, self.b1_d),
                                 (self.nw_sb, self.nw_d), (self.sel_sb, self.sel_d)]:
                    hL.inc(sy.dma_start(out=dst[:, :], in_=src), 16)
                hL.inc(sy.dma_start(out=idx_sb[:, 0:CH], in_=self.idxT_d[0:128, :]), 16)
                hL.inc(sy.dma_start(out=idx_sb[:, CH:], in_=self.idxT_d[128:256, :]), 16)
                hL.inc(sy.dma_start(out=mask_sb[:, 0:CH], in_=self.mask_d[0:128, :]), 16)
                hL.inc(sy.dma_start(out=mask_sb[:, CH:], in_=self.mask_d[128:256, :]), 16)

            @blk.gpsimd
            def _(g):
                self.gx0 = gL.inc(g.dma_start(
                    out=self.a_["xbf"][:, :].rearrange("p (k t) -> p k t", t=TILE),
                    in_=self.fmj(self.x_d)[:, :, 0:TILE]), 16)
                gL.inc(g.dma_start(out=self.id_bf[:, :], in_=self.ident_d), 16)
                gL.inc(g.dma_start(out=self.ones_bf[:, :], in_=self.ones_d), 16)
                gL.inc(g.dma_start(out=self.onesc_bf[:, :], in_=self.onesc_d), 16)
                for fo in range(2):
                    self.pre_wk[fo] = gL.inc(g.dma_start(
                        out=self.a_["wkbuf"][:, fo * DM:(fo + 1) * DM]
                            .rearrange("p (k b) -> p k b", b=128),
                        in_=self.wk_d[fo * DM:(fo + 1) * DM, :]
                            .rearrange("(k a) b -> a k b", a=128)), 16)

            @blk.scalar
            def _(sc):
                sc.wait_ge(hL.h, hL.v)
                sc.activation(self.bd_sb[:, :], self.s_sb[:, :], AF.Exp,
                              bias=0.0, scale=-float(CH))
                for j in range(4):
                    sc.activation(self.bdp_sb[:, j * H:(j + 1) * H], self.s_sb[:, :],
                                  AF.Exp, bias=0.0, scale=-float(CH * j))
                for j in range(4):
                    last = sc.activation(self.bd4p_sb[:, j * H:(j + 1) * H],
                                         self.s_sb[:, :], AF.Exp,
                                         bias=0.0, scale=-float(4 * CH * j))
                ac.inc(last)
                for h in range(H):
                    for half in range(2):
                        c = 2 * h + half
                        sc.activation(self.kdec_sb[:, c:c + 1],
                                      self.arrm_sb[:, half:half + 1], AF.Exp,
                                      bias=0.0, scale=self.s_sb[:, h:h + 1])
                sc.wait_ge(dv.h, d0 + 1)
                for h in range(H):
                    sc.activation(self.qdec_sb[:, h * CH:(h + 1) * CH],
                                  self.arr_sb[:, :], AF.Exp,
                                  bias=0.0, scale=self.negs_sb[:, h:h + 1])
                for i, (h, nh) in enumerate(diag_items):
                    if i >= 2:
                        sc.wait_ge(dv.h, d0 + 2 + (i - 2))
                    slot = i % 2
                    diag_ac[i] = ac.inc(sc.activation(
                        dscr[:, slot * CH:(slot + 1) * CH],
                        idx_sb[:, nh * CH:(nh + 1) * CH], AF.Exp,
                        bias=0.0, scale=self.negs_sb[:, h:h + 1]))

            @blk.vector
            def _(v):
                v.wait_ge(hL.h, hL.v)
                dv.inc(v.tensor_scalar_mul(self.negs_sb[:, :], self.s_sb[:, :], -1.0))
                v.wait_ge(ac.h, a0 + 1)
                bd4p = self.bd4p_sb
                for r in range(NCORES):
                    ur = self.u_sb[:, r * H:(r + 1) * H]
                    v.tensor_scalar(ur, bd4p[:, 0:H], self.sel_sb[:, 4 * r:4 * r + 1],
                                    None, ALU.mult)
                    for j in range(1, 4):
                        ts = v.tensor_scalar(self.ut_sb[:, (j - 1) * H:j * H],
                                             bd4p[:, j * H:(j + 1) * H],
                                             self.sel_sb[:, 4 * r + j:4 * r + j + 1],
                                             None, ALU.mult)
                    self.vstep(v, ts)
                    for j in range(1, 4):
                        self.vstep(v, v.tensor_add(
                            ur, ur, self.ut_sb[:, (j - 1) * H:j * H]))
                for i, (h, nh) in enumerate(diag_items):
                    v.wait_ge(ac.h, diag_ac[i])
                    slot = i % 2
                    dv.inc(v.tensor_mul(
                        self.diag_sb[:, (2 * h + nh) * CH:(2 * h + nh + 1) * CH],
                        dscr[:, slot * CH:(slot + 1) * CH],
                        mask_sb[:, nh * CH:(nh + 1) * CH]))

        assert dv.v == d0 + 1 + len(diag_items)
        assert ac.v == a0 + 1 + len(diag_items)
        ls.close()
        self.setup_hL = hL.v
        self.setup_gL = gL.v

    def ln_bf_parts(self, res, src_slice, dst_slice, n_tok, base):
        pe, dv, ac = self.pe, self.dv, self.ac
        p0, a0, d0 = base["pe"], base["ac"], base["dv"]

        def scalar_part(sc):
            for k in range(KC):
                if k >= 2:
                    sc.wait_ge(pe.h, p0 + (k - 1))
                slot = k % 2
                ac.inc(sc.activation(
                    res["sqs"][:, slot * n_tok:(slot + 1) * n_tok],
                    src_slice(k), AF.Square))
            sc.wait_ge(dv.h, d0 + 1)
            ac.inc(sc.activation(self.r_t2[:, 0:n_tok], self.r_t1[:, 0:n_tok],
                                 AF.Sqrt))

        def tensor_part(t):
            for k in range(KC):
                t.matmul(res["pr0"][0:1, 0:n_tok], self.onesc_bf[:, :], src_slice(k),
                         start=(k == 0), stop=(k == KC - 1))
                t.wait_ge(ac.h, a0 + k + 1)
                slot = k % 2
                pe.inc(t.matmul(res["pr1"][0:1, 0:n_tok], self.onesc_bf[:, :],
                                res["sqs"][:, slot * n_tok:(slot + 1) * n_tok],
                                start=(k == 0), stop=(k == KC - 1)))
            t.wait_ge(dv.h, d0 + 2)
            t.matmul(res["pa"][:, 0:n_tok], self.ones_f[:, :],
                     self.nmrow[:, 0:n_tok], start=True, stop=True)
            pe.inc(t.matmul(res["pb"][:, 0:n_tok], self.ones_f[:, :],
                            self.rsrow[:, 0:n_tok], start=True, stop=True))

        def vector_part(v):
            v.wait_ge(pe.h, p0 + KC)
            self.vstep(v, v.tensor_scalar(
                self.nmrow[:, 0:n_tok], res["pr0"][0:1, 0:n_tok],
                -1.0 / DM, None, ALU.mult))
            self.vstep(v, v.tensor_scalar(
                self.r_t1[:, 0:n_tok], res["pr1"][0:1, 0:n_tok],
                1.0 / DM, EPS_LN, ALU.mult, ALU.add))
            self.vstep(v, v.tensor_mul(self.r_t2[:, 0:n_tok],
                                       self.nmrow[:, 0:n_tok],
                                       self.nmrow[:, 0:n_tok]))
            dv.inc(v.tensor_tensor(self.r_t1[:, 0:n_tok], self.r_t1[:, 0:n_tok],
                                   self.r_t2[:, 0:n_tok], ALU.subtract))
            v.wait_ge(ac.h, a0 + KC + 1)
            dv.inc(v.reciprocal(self.rsrow[:, 0:n_tok], self.r_t2[:, 0:n_tok]))
            v.wait_ge(pe.h, p0 + KC + 1)
            for k in range(KC):
                t1 = res["t1s"][:, (k % 2) * n_tok:(k % 2 + 1) * n_tok]
                self.vstep(v, v.tensor_tensor(t1, src_slice(k),
                                              res["pa"][:, 0:n_tok], ALU.add))
                last = v.tensor_tensor(dst_slice(k), t1,
                                       res["pb"][:, 0:n_tok], ALU.mult)
            return last

        return scalar_part, tensor_part, vector_part

    def ln_f32_parts(self, res, src_slice, n_tok, base, bias_col=None,
                     dst_slice=None):
        pe, dv, ac = self.pe, self.dv, self.ac
        p0, a0, d0 = base["pe"], base["ac"], base["dv"]

        def scalar_part(sc):
            for k in range(KC):
                if k >= 2:
                    sc.wait_ge(pe.h, p0 + (k - 1))
                slot = k % 2
                ub = res["ubs"][:, slot * n_tok:(slot + 1) * n_tok]
                uq = res["uqs"][:, slot * n_tok:(slot + 1) * n_tok]
                if bias_col is not None:
                    ac.inc(sc.activation(ub, src_slice(k), AF.Identity,
                                         bias=bias_col(k)))
                    ac.inc(sc.activation(uq, src_slice(k), AF.Square,
                                         bias=bias_col(k)))
                else:
                    ac.inc(sc.activation(ub, src_slice(k), AF.Copy))
                    ac.inc(sc.activation(uq, src_slice(k), AF.Square))
            ac.inc(sc.activation(self.junk_sb[0:1, 0:8], self.junk_sb[0:1, 0:8],
                                 AF.Copy))
            sc.wait_ge(dv.h, d0 + 1)
            ac.inc(sc.activation(self.r_t2[:, 0:n_tok], self.r_t1[:, 0:n_tok],
                                 AF.Sqrt))

        def tensor_part(t):
            for k in range(KC):
                slot = k % 2
                ub = res["ubs"][:, slot * n_tok:(slot + 1) * n_tok]
                uq = res["uqs"][:, slot * n_tok:(slot + 1) * n_tok]
                t.wait_ge(ac.h, a0 + 2 * k + 2)
                t.matmul(res["pr0"][0:1, 0:n_tok], self.onesc_bf[:, :], ub,
                         start=(k == 0), stop=(k == KC - 1))
                t.wait_ge(ac.h, a0 + 2 * k + 3)
                pe.inc(t.matmul(res["pr1"][0:1, 0:n_tok], self.onesc_bf[:, :], uq,
                                start=(k == 0), stop=(k == KC - 1)))
            t.wait_ge(dv.h, d0 + 2)
            t.matmul(res["pa"][:, 0:n_tok], self.ones_f[:, :],
                     self.nmrow[:, 0:n_tok], start=True, stop=True)
            pe.inc(t.matmul(res["pb"][:, 0:n_tok], self.ones_f[:, :],
                            self.rsrow[:, 0:n_tok], start=True, stop=True))

        def vector_part(v):
            v.wait_ge(pe.h, p0 + KC)
            self.vstep(v, v.tensor_scalar(
                self.nmrow[:, 0:n_tok], res["pr0"][0:1, 0:n_tok],
                -1.0 / DM, None, ALU.mult))
            self.vstep(v, v.tensor_scalar(
                self.r_t1[:, 0:n_tok], res["pr1"][0:1, 0:n_tok],
                1.0 / DM, EPS_LN, ALU.mult, ALU.add))
            self.vstep(v, v.tensor_mul(self.r_t2[:, 0:n_tok],
                                       self.nmrow[:, 0:n_tok],
                                       self.nmrow[:, 0:n_tok]))
            dv.inc(v.tensor_tensor(self.r_t1[:, 0:n_tok], self.r_t1[:, 0:n_tok],
                                   self.r_t2[:, 0:n_tok], ALU.subtract))
            v.wait_ge(ac.h, a0 + 2 * KC + 2)
            dv.inc(v.reciprocal(self.rsrow[:, 0:n_tok], self.r_t2[:, 0:n_tok]))
            v.wait_ge(pe.h, p0 + KC + 1)
            for k in range(KC):
                t1 = res["t1s"][:, (k % 2) * n_tok:(k % 2 + 1) * n_tok]
                if bias_col is not None:
                    op = v.scalar_tensor_tensor(t1, src_slice(k), bias_col(k),
                                                res["pa"][:, 0:n_tok],
                                                ALU.add, ALU.add)
                else:
                    op = v.tensor_tensor(t1, src_slice(k),
                                         res["pa"][:, 0:n_tok], ALU.add)
                self.vstep(v, op)
                dst = dst_slice(k) if dst_slice is not None else src_slice(k)
                last = v.tensor_tensor(dst, t1,
                                       res["pb"][:, 0:n_tok], ALU.mult)
            return last

        return scalar_part, tensor_part, vector_part

    def alloc_a(self):
        la = self.la = ExitStack()
        self.a_ = a_ = {}
        a_["xbf"] = self.sbuf("xbf", [128, KC * TILE], BF16, la)
        a_["sqs"] = self.sbuf("sqs", [128, 2 * TILE], BF16, la)
        a_["t1s"] = self.sbuf("t1s", [128, 2 * TILE], F32, la)
        a_["kfull"] = self.sbuf("kfull", [128, KC * TCORE], BF16, la)
        a_["vfull"] = self.sbuf("vfull", [128, 8 * DM], BF16, la)
        a_["wkbuf"] = self.sbuf("wkbuf", [128, 2 * DM], BF16, la)
        a_["wvbuf"] = self.sbuf("wvbuf", [128, 2 * KC * 256], BF16, la)
        a_["ktm"] = self.sbuf("ktm", [128, H * CH], BF16, la)
        a_["kvc"] = self.sbuf("kvc", [128, H * D], F32, la)
        a_["kvl"] = self.sbuf("kvl", [128, H * D], BF16, la)
        a_["pa"] = self.psum("pa", [128, TILE], F32, la)
        a_["pb"] = self.psum("pb", [128, TILE], F32, la)
        a_["pr0"] = self.psum("pr0", [1, TILE], F32, la)
        a_["pr1"] = self.psum("pr1", [1, TILE], F32, la)
        a_["ptrA"] = self.psum("ptrA", [128, 128], BF16, la)
        a_["ptrB"] = self.psum("ptrB", [128, 128], BF16, la)
        a_["ps_s0"] = self.psum("ps_s0", [128, D], F32, la)
        a_["ps_s1"] = self.psum("ps_s1", [128, D], F32, la)

    def phase_a(self):
        nc = self.nc
        gL, hL, sT, pe, dv, ac = self.gL, self.hL, self.sT, self.pe, self.dv, self.ac
        a_ = self.a_
        xbf = a_["xbf"]; sqs = a_["sqs"]; t1s = a_["t1s"]
        kfull = a_["kfull"]; vfull = a_["vfull"]
        wkbuf = a_["wkbuf"]; wvbuf = a_["wvbuf"]
        ktm = a_["ktm"]; kvc = a_["kvc"]; kvl = a_["kvl"]
        pa = a_["pa"]; pb = a_["pb"]; pr0 = a_["pr0"]; pr1 = a_["pr1"]
        ptrA = a_["ptrA"]; ptrB = a_["ptrB"]
        ps_s0 = a_["ps_s0"]; ps_s1 = a_["ps_s1"]
        lnres = dict(sqs=sqs, pa=pa, pb=pb, pr0=pr0, pr1=pr1, t1s=t1s)

        gx = {0: self.gx0}
        for t in range(NT):
            t0 = t * TILE
            base = dict(pe=pe.v, ac=ac.v, dv=dv.v)
            sp, tp, vp = self.ln_bf_parts(
                lnres,
                lambda k: xbf[:, k * TILE:(k + 1) * TILE],
                lambda k, t0=t0: self.xln(k, t0, TILE),
                TILE, base)
            with nc.Block(f"A1_{t}") as blk:
                @blk.gpsimd
                def _(g, t=t, base=base):
                    if t == 0:
                        g.wait_ge(dv.h, base["dv"] + 3)
                        gx[1] = gL.inc(g.dma_start(
                            out=xbf[:, :].rearrange("p (k t) -> p k t", t=TILE),
                            in_=self.fmj(self.x_d)[:, :, TILE:2 * TILE]), 16)
                    else:
                        pass

                @blk.scalar
                def _(sc):
                    sc.wait_ge(gL.h, gx[t])
                    sp(sc)

                @blk.tensor
                def _(t_):
                    t_.wait_ge(gL.h, gx[t])
                    t_.wait_ge(gL.h, self.setup_gL)
                    t_.wait_ge(hL.h, self.setup_hL)
                    tp(t_)

                @blk.vector
                def _(v):
                    dv.inc(vp(v))
            assert pe.v == base["pe"] + 17 and ac.v == base["ac"] + 17
            assert dv.v == base["dv"] + 3
        if not self.lvl("A2"):
            return

        p0, a0 = pe.v, ac.v
        g_wk, g_wv = {}, {}
        with nc.Block("A2") as blk:
            @blk.gpsimd
            def _(g):
                g_wk.update(self.pre_wk)
                for fo in range(KC):
                    if fo < 2:
                        continue
                    g.wait_ge(pe.h, p0 + 2 * (fo - 1))
                    slot = fo % 2
                    g_wk[fo] = gL.inc(g.dma_start(
                        out=wkbuf[:, slot * DM:(slot + 1) * DM]
                            .rearrange("p (k b) -> p k b", b=128),
                        in_=self.wk_d[fo * DM:(fo + 1) * DM, :]
                            .rearrange("(k a) b -> a k b", a=128)), 16)
                for n8 in range(8):
                    if n8 >= 2:
                        g.wait_ge(pe.h, p0 + 32 + 8 * (n8 - 1))
                    slot = n8 % 2
                    g_wv[n8] = gL.inc(g.dma_start(
                        out=wvbuf[:, slot * KC * 256:(slot + 1) * KC * 256]
                            .rearrange("p (k t) -> p k t", t=256),
                        in_=self.wv_d[:, n8 * 256:(n8 + 1) * 256]
                            .rearrange("(k a) t -> a k t", a=128)), 16)
                if self.debug == 1:
                    sT.inc(g.dma_start(
                        out=self.fmj(self.dbg["dbg_xln"]),
                        in_=self.xln_sb[:, :]
                            .rearrange("p (k t) -> p k t", t=TCORE)), 16)

            @blk.tensor
            def _(t_):
                for fo in range(KC):
                    t_.wait_ge(gL.h, g_wk[fo])
                    wslot = fo % 2
                    for half in range(2):
                        idx = fo * 2 + half
                        ps = pa if idx % 2 == 0 else pb
                        if idx >= 2:
                            t_.wait_ge(ac.h, a0 + idx - 1)
                        for k in range(KC):
                            mm = t_.matmul(
                                ps[:, :],
                                wkbuf[:, wslot * DM + k * 128:wslot * DM + (k + 1) * 128],
                                self.xln(k, half * TILE, TILE),
                                start=(k == 0), stop=(k == KC - 1))
                        pe.inc(mm)
                for n8 in range(8):
                    t_.wait_ge(gL.h, g_wv[n8])
                    wslot = n8 % 2
                    for tc in range(8):
                        idx = n8 * 8 + tc
                        ps = pa if idx % 2 == 0 else pb
                        if idx >= 2:
                            t_.wait_ge(ac.h, a0 + 32 + idx - 1)
                        for k in range(KC):
                            mm = t_.matmul(
                                ps[:, 0:256],
                                self.xln(k, tc * 128, 128),
                                wvbuf[:, wslot * KC * 256 + k * 256:
                                      wslot * KC * 256 + (k + 1) * 256],
                                start=(k == 0), stop=(k == KC - 1))
                        pe.inc(mm)

            @blk.scalar
            def _(sc):
                for fo in range(KC):
                    for half in range(2):
                        idx = fo * 2 + half
                        ps = pa if idx % 2 == 0 else pb
                        sc.wait_ge(pe.h, p0 + idx + 1)
                        ac.inc(sc.activation(
                            kfull[:, fo * TCORE + half * TILE:
                                  fo * TCORE + (half + 1) * TILE],
                            ps[:, :], AF.Silu))
                for n8 in range(8):
                    for tc in range(8):
                        idx = n8 * 8 + tc
                        ps = pa if idx % 2 == 0 else pb
                        sc.wait_ge(pe.h, p0 + 32 + idx + 1)
                        ac.inc(sc.activation(
                            vfull[:, tc * DM + n8 * 256:tc * DM + (n8 + 1) * 256],
                            ps[:, 0:256], AF.Silu))

            @blk.sync
            def _(sy):
                sy.wait_ge(ac.h, a0 + 32)
                self.st_k = sT.inc(sy.dma_start(
                    out=self.fmj(self.k_sc),
                    in_=kfull[:, :].rearrange("p (f t) -> p f t", t=TCORE)), 16)
                sy.wait_ge(ac.h, a0 + 96)
                self.st_v = sT.inc(sy.dma_start(
                    out=self.v_sc.rearrange("(tc p) e -> p tc e", p=128),
                    in_=vfull[:, :].rearrange("p (tc e) -> p tc e", e=DM)), 16)

        assert pe.v == p0 + 96 and ac.v == a0 + 96
        if not self.lvl("A"):
            return

        sT0 = sT.v
        with nc.Block("A3") as blk:
            pe0_by_cl = {}
            dv0_by_cl = {}
            p = pe.v
            d = dv.v
            for cl in range(NCH):
                pe0_by_cl[cl] = p
                dv0_by_cl[cl] = d
                p += 48
                d += 48 + (1 if cl < NCH - 1 else 0)
            dv_final = d

            @blk.tensor
            def _(t_):
                for cl in range(NCH):
                    p0c, d0c = pe0_by_cl[cl], dv0_by_cl[cl]
                    for i in range(2 * H):
                        h, nh = i // 2, i % 2
                        if i >= 2:
                            t_.wait_ge(dv.h, d0c + (i - 2) + 1)
                        ptr = ptrA if i % 2 == 0 else ptrB
                        pe.inc(t_.transpose(
                            ptr[:, :],
                            kfull[:, h * TCORE + cl * CH + nh * 128:
                                  h * TCORE + cl * CH + (nh + 1) * 128],
                            self.id_bf[:, :]))
                    for h in range(H):
                        t_.wait_ge(dv.h, d0c + 2 * h + 2)
                        if h >= 2:
                            t_.wait_ge(dv.h, d0c + 32 + (h - 2) + 1)
                        ps = ps_s0 if h % 2 == 0 else ps_s1
                        for nh in range(2):
                            tc = cl * 2 + nh
                            mm = t_.matmul(
                                ps[:, :],
                                ktm[:, (2 * h + nh) * 128:(2 * h + nh + 1) * 128],
                                vfull[:, tc * DM + h * D:tc * DM + (h + 1) * D],
                                start=(nh == 0), stop=(nh == 1))
                        pe.inc(mm)

            @blk.vector
            def _(v):
                for cl in range(NCH):
                    p0c, d0c = pe0_by_cl[cl], dv0_by_cl[cl]
                    for i in range(2 * H):
                        h, nh = i // 2, i % 2
                        v.wait_ge(pe.h, p0c + i + 1)
                        ptr = ptrA if i % 2 == 0 else ptrB
                        dv.inc(v.tensor_scalar(
                            ktm[:, i * 128:(i + 1) * 128],
                            ptr[:, :],
                            self.kdec_sb[:, 2 * h + nh:2 * h + nh + 1],
                            None, ALU.mult))
                    for h in range(H):
                        v.wait_ge(pe.h, p0c + 32 + h + 1)
                        ps = ps_s0 if h % 2 == 0 else ps_s1
                        kvch = kvc[:, h * D:(h + 1) * D]
                        if cl == 0:
                            dv.inc(v.tensor_copy(kvch, ps[:, :]))
                        else:
                            dv.inc(v.scalar_tensor_tensor(
                                kvch, kvch, self.bd_sb[:, h:h + 1], ps[:, :],
                                ALU.mult, ALU.add))
                    if cl < NCH - 1:
                        if cl > 0:
                            v.wait_ge(sT.h, sT0 + 16 * cl)
                        v.wait_ge(dv.h, dv.v)
                        dv.inc(v.tensor_copy(kvl[:, :], kvc[:, :]))

            @blk.sync
            def _(sy):
                for cl in range(NCH - 1):
                    sy.wait_ge(dv.h, dv0_by_cl[cl] + 49)
                    self.st_kvl[cl] = sT.inc(sy.dma_start(
                        out=self.hdj(self.kvloc_sc[cl * H * D:(cl + 1) * H * D, :]),
                        in_=kvl[:, :].rearrange("p (h e) -> p h e", e=D)), 16)
                sy.wait_ge(dv.h, dv_final)
                self.st_T = sT.inc(sy.dma_start(
                    out=self.hdj(self.cc_in),
                    in_=kvc[:, :].rearrange("p (h e) -> p h e", e=D)), 16)
                if self.debug == 1:
                    sT.inc(sy.dma_start(
                        out=self.hdj(self.dbg["dbg_T"]),
                        in_=kvc[:, :].rearrange("p (h e) -> p h e", e=D)), 16)

            @blk.gpsimd
            def _(g):
                g.wait_ge(sT.h, self.st_T)
                self.cc.inc(g.collective_compute(
                    "AllGather", ALU.bypass,
                    replica_groups=[list(range(NCORES))],
                    ins=[self.cc_in.opt()],
                    outs=[self.cc_out.opt()]))

        assert pe.v == pe0_by_cl[NCH - 1] + 48
        assert dv.v == dv_final

    def phase_b(self):
        nc = self.nc
        gL, hL, sT, pe, dv, ac = self.gL, self.hL, self.sT, self.pe, self.dv, self.ac
        self.la.close()
        lb = self.lb = ExitStack()
        q_tile = self.sbuf("q_tile", [128, H * TILE], BF16, lb)
        g_tile = self.sbuf("g_tile", [128, H * TILE], BF16, lb)
        kv_in = self.sbuf("kv_in", [128, H * D], F32, lb)
        kv_bf = self.sbuf("kv_bf", [128, H * D], BF16, lb)
        kvl_in = self.sbuf("kvl_in", [128, H * D], BF16, lb)
        kch = self.sbuf("kch", [128, H * CH], BF16, lb)
        vch = self.sbuf("vch", [128, 2 * DM], BF16, lb)
        outbf = self.sbuf("outbf", [128, H * CH], BF16, lb)
        sqb = self.sbuf("sqb", [128, 2 * CH], BF16, lb)
        qd_scr = self.sbuf("qd_scr", [128, 2 * CH], BF16, lb)
        kqd_scr = self.sbuf("kqd_scr", [128, 2 * CH], BF16, lb)
        wqg = self.sbuf("wqg", [128, 2 * DM], BF16, lb)
        og_tile = self.sbuf("og_tile", [128, H * TILE], BF16, lb)
        z_sb = self.sbuf("z_sb", [128, KC * TILE], F32, lb)
        zb_scr = self.sbuf("zb_scr", [128, 2 * TILE], BF16, lb)
        zq_scr = self.sbuf("zq_scr", [128, 2 * TILE], BF16, lb)
        t1b = self.sbuf("t1b", [128, 2 * TILE], F32, lb)
        pa = self.psum("pba", [128, TILE], F32, lb)
        pb = self.psum("pbb", [128, TILE], F32, lb)
        ps_kq = self.psum("ps_kq", [128, TILE], F32, lb)
        ps_o0 = self.psum("ps_o0", [128, CH], F32, lb)
        ps_o1 = self.psum("ps_o1", [128, CH], F32, lb)
        pr0 = self.psum("pbr0", [1, TILE], F32, lb)
        pr1 = self.psum("pbr1", [1, TILE], F32, lb)
        pbc2 = self.psum("pbc2", [128, TILE], F32, lb)

        def b1_parts(t, p_base, a_base, g_w, init_ac_wait=None,
                     init_gp_waits=None):
            t0 = t * TILE

            def gp(g):
                for i, wd in enumerate([self.wq_d, self.wg_d]):
                    for fo in range(KC):
                        idx = i * KC + fo
                        if idx >= 2:
                            g.wait_ge(pe.h, p_base + idx - 1)
                        elif init_gp_waits is not None:
                            g.wait_ge(pe.h, init_gp_waits[idx])
                        slot = idx % 2
                        g_w[idx] = gL.inc(g.dma_start(
                            out=wqg[:, slot * DM:(slot + 1) * DM]
                                .rearrange("p (k b) -> p k b", b=128),
                            in_=wd[fo * DM:(fo + 1) * DM, :]
                                .rearrange("(k a) b -> a k b", a=128)), 16)

            def tp(t_):
                for idx in range(2 * KC):
                    t_.wait_ge(gL.h, g_w[idx])
                    wslot = idx % 2
                    ps = pa if idx % 2 == 0 else pb
                    if idx >= 2:
                        t_.wait_ge(ac.h, a_base + idx - 1)
                    elif init_ac_wait is not None:
                        t_.wait_ge(ac.h, init_ac_wait)
                    for k in range(KC):
                        mm = t_.matmul(
                            ps[:, :],
                            wqg[:, wslot * DM + k * 128:wslot * DM + (k + 1) * 128],
                            self.xln(k, t0, TILE),
                            start=(k == 0), stop=(k == KC - 1))
                    pe.inc(mm)

            def sp(sc):
                for idx in range(2 * KC):
                    fo = idx % KC
                    ps = pa if idx % 2 == 0 else pb
                    dst = q_tile if idx < KC else g_tile
                    fn = AF.Silu if idx < KC else AF.Sigmoid
                    sc.wait_ge(pe.h, p_base + idx + 1)
                    ac.inc(sc.activation(
                        dst[:, fo * TILE:(fo + 1) * TILE], ps[:, :], fn))

            return gp, tp, sp

        def b1_block(t, with_kv=False):
            p0, a0, d0 = pe.v, ac.v, dv.v
            g_w = {}
            h_T = {}
            gp, tp, sp = b1_parts(t, p0, a0, g_w)
            with nc.Block(f"B1_{t}") as blk:
                if with_kv:
                    @blk.sync
                    def _(sy):
                        sy.wait_ge(self.cc.h, self.cc.v)
                        issue_b2_loads(sy, 0)
                        for r in range(NCORES):
                            if r >= 1:
                                sy.wait_ge(dv.h, d0 + r)
                            h_T[r] = hL.inc(sy.dma_start(
                                out=z_sb[:, 0:H * D]
                                    .rearrange("p (h e) -> p h e", e=D),
                                in_=self.hdj(self.cc_out[r * H * D:
                                                         (r + 1) * H * D, :])), 16)
                        if self.debug == 1:
                            sy.wait_ge(dv.h, d0 + NCORES)
                            sT.inc(sy.dma_start(
                                out=self.hdj(self.dbg["dbg_kvin"]),
                                in_=kv_in[:, :]
                                    .rearrange("p (h e) -> p h e", e=D)), 16)

                @blk.gpsimd
                def _(g):
                    if with_kv:
                        g.wait_ge(self.cc.h, self.cc.v)
                    gp(g)
                    if self.debug == 1 and t == 0:
                        g.wait_ge(ac.h, a0 + 2 * KC)
                        sT.inc(g.dma_start(
                            out=self.fmj(self.dbg["dbg_q"]),
                            in_=q_tile[:, :].rearrange(
                                "p (h t) -> p h t", t=TILE)), 16)
                        sT.inc(g.dma_start(
                            out=self.fmj(self.dbg["dbg_g"]),
                            in_=g_tile[:, :].rearrange(
                                "p (h t) -> p h t", t=TILE)), 16)

                @blk.tensor
                def _(t_):
                    tp(t_)

                @blk.scalar
                def _(sc):
                    sp(sc)

                if with_kv:
                    @blk.vector
                    def _(v):
                        stg = z_sb[:, H * D:2 * H * D]
                        v.memset(kv_in[:, :], 0.0)
                        for r in range(NCORES):
                            v.wait_ge(hL.h, h_T[r])
                            for h in range(H):
                                ts = v.tensor_scalar(
                                    stg[:, h * D:(h + 1) * D],
                                    z_sb[:, h * D:(h + 1) * D],
                                    self.u_sb[:, r * H + h:r * H + h + 1],
                                    None, ALU.mult)
                            self.vstep(v, ts)
                            tt = v.tensor_add(kv_in[:, :], kv_in[:, :], stg)
                            dv.inc(tt)
                            v.wait_ge(dv.h, dv.v)
            assert pe.v == p0 + 2 * KC and ac.v == a0 + 2 * KC
            assert not with_kv or dv.v == d0 + NCORES

        def issue_b2_loads(sy, cl):
            loads = []
            sy.wait_ge(sT.h, self.st_k)
            loads.append(hL.inc(sy.dma_start(
                out=kch[:, :].rearrange("p (h t) -> p h t", t=CH),
                in_=self.fmj(self.k_sc)[:, :, cl * CH:(cl + 1) * CH]), 16))
            sy.wait_ge(sT.h, self.st_v)
            loads.append(hL.inc(sy.dma_start(
                out=vch[:, :].rearrange("p (tc e) -> p tc e", e=DM),
                in_=self.v_sc[cl * CH:(cl + 1) * CH, :]
                    .rearrange("(tc p) e -> p tc e", p=128)), 16))
            if cl > 0:
                sy.wait_ge(sT.h, self.st_kvl[cl - 1])
                loads.append(hL.inc(sy.dma_start(
                    out=kvl_in[:, :].rearrange("p (h e) -> p h e", e=D),
                    in_=self.hdj(self.kvloc_sc[(cl - 1) * H * D:
                                               cl * H * D, :])), 16))
            self.b2_loads[cl] = loads
            return loads

        self.issue_b2_loads = issue_b2_loads

        def b2_block(cl):
            t = cl // 2
            t0 = t * TILE
            co = (cl % 2) * CH
            p0, a0, d0 = pe.v, ac.v, dv.v
            loads = self.b2_loads[cl]
            with nc.Block(f"B2_{cl}") as blk:
                @blk.sync
                def _(sy):
                    if cl < NCH - 1:
                        sy.wait_ge(pe.h, p0 + 47)
                        sy.wait_ge(dv.h, d0 + 1)
                        issue_b2_loads(sy, cl + 1)

                qd_ev = lambda h: d0 + 2 * h + 2
                kqd_ev = lambda h: d0 + 2 * h + 3
                kq_ev = lambda h: p0 + 3 * h + 1
                out_ev = lambda h: p0 + 3 * h + 2
                stat_ev = lambda h: p0 + 3 * h + 3
                bc_ev = p0 + 49
                oev = lambda h: a0 + 2 * h + 1
                sq_ev = lambda h: a0 + 2 * h + 2

                @blk.tensor
                def _(t_):
                    for lv in loads:
                        t_.wait_ge(hL.h, lv)
                    t_.wait_ge(dv.h, d0 + 1)
                    for h in range(H):
                        if h >= 1:
                            t_.wait_ge(dv.h, kqd_ev(h - 1))
                        for nh in range(2):
                            mm = t_.matmul(
                                ps_kq[:, nh * CH:(nh + 1) * CH],
                                kch[:, h * CH + nh * 128:h * CH + (nh + 1) * 128],
                                q_tile[:, h * TILE + co:h * TILE + co + CH],
                                start=True, stop=True)
                        pe.inc(mm)
                        t_.wait_ge(dv.h, kqd_ev(h))
                        if h >= 2:
                            t_.wait_ge(ac.h, sq_ev(h - 2))
                        ps = ps_o0 if h % 2 == 0 else ps_o1
                        t_.matmul(ps[:, :], vch[:, 0 * DM + h * D:0 * DM + (h + 1) * D],
                                  kqd_scr[:, 0:CH], start=True, stop=False)
                        t_.matmul(ps[:, :], vch[:, 1 * DM + h * D:1 * DM + (h + 1) * D],
                                  kqd_scr[:, CH:2 * CH], start=False, stop=False)
                        qslot = h % 2
                        pe.inc(t_.matmul(ps[:, :], kv_bf[:, h * D:(h + 1) * D],
                                         qd_scr[:, qslot * CH:(qslot + 1) * CH],
                                         start=False, stop=True))
                        t_.wait_ge(ac.h, sq_ev(h))
                        slot = h % 2
                        pe.inc(t_.matmul(pr0[0:1, 0:CH], self.onesc_bf[:, :],
                                         sqb[:, slot * CH:(slot + 1) * CH],
                                         start=(h == 0), stop=(h == H - 1)))
                    t_.wait_ge(dv.h, d0 + 35)
                    t_.wait_ge(ac.h, oev(H - 1))
                    pe.inc(t_.matmul(ps_o0[:, :], self.ones_f[:, :],
                                     self.rsrow[:, 0:CH], start=True, stop=True))

                @blk.scalar
                def _(sc):
                    for h in range(H):
                        sc.wait_ge(pe.h, out_ev(h))
                        ps = ps_o0 if h % 2 == 0 else ps_o1
                        ac.inc(sc.activation(outbf[:, h * CH:(h + 1) * CH],
                                             ps[:, :], AF.Copy))
                        if h >= 2:
                            sc.wait_ge(pe.h, stat_ev(h - 2))
                        slot = h % 2
                        ac.inc(sc.activation(sqb[:, slot * CH:(slot + 1) * CH],
                                             ps[:, :], AF.Square))
                    sc.wait_ge(dv.h, d0 + 34)
                    ac.inc(sc.activation(self.r_t2[0:1, 0:CH],
                                         self.r_t1[0:1, 0:CH], AF.Sqrt))

                @blk.vector
                def _(v):
                    if cl == 0:
                        dv.inc(v.tensor_copy(kv_bf[:, :], kv_in[:, :]))
                    else:
                        v.wait_ge(hL.h, loads[2])
                        for h in range(H):
                            last = v.scalar_tensor_tensor(
                                kv_bf[:, h * D:(h + 1) * D],
                                kv_in[:, h * D:(h + 1) * D],
                                self.bdp_sb[:, cl * H + h:cl * H + h + 1],
                                kvl_in[:, h * D:(h + 1) * D],
                                ALU.mult, ALU.add)
                        dv.inc(last)
                    for h in range(H):
                        qslot = h % 2
                        if h >= 2:
                            v.wait_ge(pe.h, out_ev(h - 2))
                        dv.inc(v.tensor_mul(
                            qd_scr[:, qslot * CH:(qslot + 1) * CH],
                            q_tile[:, h * TILE + co:h * TILE + co + CH],
                            self.qdec_sb[:, h * CH:(h + 1) * CH]))
                        v.wait_ge(pe.h, kq_ev(h))
                        v.tensor_mul(kqd_scr[:, 0:CH], ps_kq[:, 0:CH],
                                     self.diag_sb[:, (2 * h) * CH:(2 * h + 1) * CH])
                        dv.inc(v.tensor_mul(
                            kqd_scr[:, CH:2 * CH], ps_kq[:, CH:2 * CH],
                            self.diag_sb[:, (2 * h + 1) * CH:(2 * h + 2) * CH]))
                    v.wait_ge(pe.h, stat_ev(H - 1))
                    dv.inc(v.tensor_scalar(self.r_t1[0:1, 0:CH], pr0[0:1, 0:CH],
                                           1.0 / DM, EPS_RMS, ALU.mult, ALU.add))
                    v.wait_ge(ac.h, a0 + 33)
                    dv.inc(v.reciprocal(self.rsrow[0:1, 0:CH], self.r_t2[0:1, 0:CH]))
                    v.wait_ge(pe.h, bc_ev)
                    for h in range(H):
                        self.vstep(v, v.scalar_tensor_tensor(
                            og_tile[:, h * TILE + co:h * TILE + co + CH],
                            outbf[:, h * CH:(h + 1) * CH],
                            self.nw_sb[:, h:h + 1],
                            ps_o0[:, :], ALU.mult, ALU.mult))
                        last = v.tensor_mul(
                            og_tile[:, h * TILE + co:h * TILE + co + CH],
                            og_tile[:, h * TILE + co:h * TILE + co + CH],
                            g_tile[:, h * TILE + co:h * TILE + co + CH])
                    dv.inc(last)

                if self.debug == 1 and cl == 0:
                    @blk.gpsimd
                    def _(g):
                        g.wait_ge(dv.h, dv.v)
                        sT.inc(g.dma_start(
                            out=self.fmj(self.dbg["dbg_ob"]),
                            in_=outbf[:, :].rearrange(
                                "p (h t) -> p h t", t=CH)), 16)
                        sT.inc(g.dma_start(
                            out=self.dbg["dbg_rms"],
                            in_=self.rsrow[0:1, 0:CH]), 16)

            assert pe.v == p0 + 49, (pe.v, p0)
            assert ac.v == a0 + 33
            assert dv.v == d0 + 36

        def b3_block(t, merge_b1=False):
            t0 = t * TILE
            p0, a0, d0 = pe.v, ac.v, dv.v
            g_w = {}
            g_w1 = {}
            g_acc = []
            nb1 = 32 if merge_b1 else 0
            bc_ev = p0 + 49
            ln2res = dict(ubs=zb_scr, uqs=zq_scr, pa=ps_kq, pb=pbc2,
                          pr0=pr0, pr1=pr1, t1s=t1b)
            ln2base = dict(pe=p0 + 49 + nb1, ac=a0 + 49 + nb1, dv=d0 + 3)
            sp2, tp2, vp2 = self.ln_f32_parts(
                ln2res,
                lambda k: z_sb[:, k * TILE:(k + 1) * TILE],
                TILE, ln2base,
                dst_slice=lambda k, t0=t0: self.xln(k, t0, TILE))
            if merge_b1:
                gp1, tp1, sp1 = b1_parts(1, p0 + 49, a0 + 49, g_w1,
                                         init_ac_wait=a0 + 48,
                                         init_gp_waits=(p0 + 3 * 14 + 1,
                                                        p0 + 3 * 15 + 1))
            wo_ev = lambda o: p0 + 3 * o + 1
            zstat_ev = lambda o: p0 + 3 * o + 3
            z_ev = lambda o: a0 + 3 * o + 1
            zb_ev = lambda o: a0 + 3 * o + 2
            zq_ev = lambda o: a0 + 3 * o + 3
            with nc.Block(f"B3_{t}") as blk:
                @blk.gpsimd
                def _(g):
                    for o in range(KC):
                        if o >= 2:
                            g.wait_ge(pe.h, wo_ev(o - 2))
                        slot = o % 2
                        g_w[o] = gL.inc(g.dma_start(
                            out=wqg[:, slot * DM:(slot + 1) * DM]
                                .rearrange("p (f b) -> p f b", b=128),
                            in_=self.wo_d[o * DM:(o + 1) * DM, :]
                                .rearrange("(f a) b -> a f b", a=128)), 16)
                    if merge_b1:
                        gp1(g)
                    if self.debug == 1:
                        sT.inc(g.dma_start(
                            out=self.fmj(self.dbg["dbg_og"])[:, :, t0:t0 + TILE],
                            in_=og_tile[:, :].rearrange("p (h t) -> p h t", t=TILE)),
                            16)
                    g.wait_ge(dv.h, d0 + 3)
                    g_acc.append(gL.inc(g.dma_start(
                        out=z_sb[:, :].rearrange("p (k t) -> p k t", t=TILE),
                        in_=self.fmj(self.x_d)[:, :, t0:t0 + TILE],
                        accum_op=ALU.add), 16))

                @blk.tensor
                def _(t_):
                    for o in range(KC):
                        t_.wait_ge(gL.h, g_w[o])
                        wslot = o % 2
                        ps = pa if o % 2 == 0 else pb
                        if o >= 2:
                            t_.wait_ge(ac.h, zq_ev(o - 2))
                        for f in range(KC):
                            mm = t_.matmul(
                                ps[:, :],
                                wqg[:, wslot * DM + f * 128:wslot * DM + (f + 1) * 128],
                                og_tile[:, f * TILE:(f + 1) * TILE],
                                start=(f == 0), stop=(f == KC - 1))
                        pe.inc(mm)
                        slot = o % 2
                        t_.wait_ge(ac.h, zb_ev(o))
                        pe.inc(t_.matmul(pr0[0:1, 0:TILE], self.onesc_bf[:, :],
                                         zb_scr[:, slot * TILE:(slot + 1) * TILE],
                                         start=(o == 0), stop=(o == KC - 1)))
                        t_.wait_ge(ac.h, zq_ev(o))
                        pe.inc(t_.matmul(pr1[0:1, 0:TILE], self.onesc_bf[:, :],
                                         zq_scr[:, slot * TILE:(slot + 1) * TILE],
                                         start=(o == 0), stop=(o == KC - 1)))
                    t_.wait_ge(dv.h, d0 + 2)
                    t_.matmul(ps_kq[:, :], self.ones_f[:, :], self.nmrow[:, 0:TILE],
                              start=True, stop=True)
                    pe.inc(t_.matmul(pbc2[:, :], self.ones_f[:, :],
                                     self.rsrow[:, 0:TILE], start=True, stop=True))
                    if merge_b1:
                        tp1(t_)
                    t_.wait_ge(dv.h, d0 + 1)
                    tp2(t_)

                @blk.scalar
                def _(sc):
                    if t > 0 and 0 in self.st_x2:
                        sc.wait_ge(sT.h, self.st_x2[0])
                    for o in range(KC):
                        sc.wait_ge(pe.h, wo_ev(o))
                        ps = pa if o % 2 == 0 else pb
                        slot = o % 2
                        ac.inc(sc.activation(z_sb[:, o * TILE:(o + 1) * TILE],
                                             ps[:, :], AF.Copy))
                        if o >= 2:
                            sc.wait_ge(pe.h, zstat_ev(o - 2))
                        ac.inc(sc.activation(zb_scr[:, slot * TILE:(slot + 1) * TILE],
                                             ps[:, :], AF.Copy))
                        ac.inc(sc.activation(zq_scr[:, slot * TILE:(slot + 1) * TILE],
                                             ps[:, :], AF.Square))
                    sc.wait_ge(dv.h, d0 + 1)
                    ac.inc(sc.activation(self.r_t2[0:1, 0:TILE],
                                         self.r_t1[0:1, 0:TILE], AF.Sqrt))
                    if merge_b1:
                        sp1(sc)
                    sc.wait_ge(gL.h, g_acc[0])
                    sp2(sc)

                @blk.vector
                def _(v):
                    v.wait_ge(pe.h, zstat_ev(KC - 1))
                    self.vstep(v, v.tensor_scalar(
                        self.nmrow[0:1, 0:TILE], pr0[0:1, 0:TILE],
                        -1.0 / DM, None, ALU.mult))
                    self.vstep(v, v.tensor_scalar(
                        self.r_t1[0:1, 0:TILE], pr1[0:1, 0:TILE],
                        1.0 / DM, EPS_LN, ALU.mult, ALU.add))
                    self.vstep(v, v.tensor_mul(
                        self.r_t2[0:1, 0:TILE], self.nmrow[0:1, 0:TILE],
                        self.nmrow[0:1, 0:TILE]))
                    dv.inc(v.tensor_tensor(self.r_t1[0:1, 0:TILE],
                                           self.r_t1[0:1, 0:TILE],
                                           self.r_t2[0:1, 0:TILE], ALU.subtract))
                    v.wait_ge(ac.h, a0 + 3 * KC + 1)
                    dv.inc(v.reciprocal(self.rsrow[0:1, 0:TILE],
                                        self.r_t2[0:1, 0:TILE]))
                    v.wait_ge(pe.h, bc_ev)
                    for k in range(KC):
                        t1 = t1b[:, (k % 2) * TILE:(k % 2 + 1) * TILE]
                        self.vstep(v, v.tensor_tensor(
                            t1, z_sb[:, k * TILE:(k + 1) * TILE],
                            ps_kq[:, :], ALU.add))
                        last = v.tensor_tensor(z_sb[:, k * TILE:(k + 1) * TILE],
                                               t1, pbc2[:, :], ALU.mult)
                    dv.inc(last)
                    if self.debug == 1:
                        v.wait_ge(sT.h, sT.v)
                    dv.inc(vp2(v))

                @blk.sync
                def _(sy):
                    sy.wait_ge(gL.h, g_acc[0])
                    self.st_x2[t] = sT.inc(sy.dma_start(
                        out=self.fmj(self.x2_sc)[:, :, t0:t0 + TILE],
                        in_=z_sb[:, :].rearrange("p (k t) -> p k t", t=TILE)), 16)
                    if self.debug == 1:
                        sT.inc(sy.dma_start(
                            out=self.fmj(self.dbg["dbg_x2"])[:, :, t0:t0 + TILE],
                            in_=z_sb[:, :].rearrange("p (k t) -> p k t", t=TILE)), 16)
            assert pe.v == p0 + 66 + nb1 and ac.v == a0 + 83 + nb1
            assert dv.v == d0 + 6

        b1_block(0, with_kv=True)
        if not self.lvl("B2"):
            return
        b2_block(0)
        b2_block(1)
        if not self.lvl("B3"):
            return
        b3_block(0, merge_b1=True)
        if not self.lvl("B"):
            return
        b2_block(2)
        b2_block(3)
        b3_block(1)

    def phase_c(self):
        nc = self.nc
        gL, hL, sT, pe, dv, ac = self.gL, self.hL, self.sT, self.pe, self.dv, self.ac
        self.lb.close()
        self.lab.close()
        lc = self.lc = ExitStack()
        pa = self.psum("pca", [128, TILE], F32, lc)
        pb = self.psum("pcb", [128, TILE], F32, lc)
        pr0 = self.psum("pcr0", [1, TILE], F32, lc)
        pr1 = self.psum("pcr1", [1, TILE], F32, lc)

        x2ln = self.xln

        if not self.lvl("C2"):
            return

        mid = self.sbuf("mid", [128, 32 * TCORE], BF16, lc)
        y_sb = self.sbuf("y_sb", [128, KC * TCORE], F32, lc)
        w0buf = self.sbuf("w0buf", [128, 2 * DM], BF16, lc)
        w1buf = self.sbuf("w1buf", [128, 2 * 32 * 128], BF16, lc)
        ubs = self.sbuf("ubs", [128, 2 * TILE], BF16, lc)
        uqs = self.sbuf("uqs", [128, 2 * TILE], BF16, lc)
        t1d = self.sbuf("t1d", [128, 2 * TILE], F32, lc)

        for h2 in range(2):
            p0, a0, d0 = pe.v, ac.v, dv.v
            pW1 = p0 + 64
            g_w0, g_w1 = {}, {}
            with nc.Block(f"C2_{h2}") as blk:
                @blk.gpsimd
                def _(g, h2=h2):
                    for jl in range(32):
                        if jl >= 2:
                            g.wait_ge(pe.h, p0 + 2 * (jl - 1))
                        slot = jl % 2
                        jo = h2 * 32 + jl
                        g_w0[jl] = gL.inc(g.dma_start(
                            out=w0buf[:, slot * DM:(slot + 1) * DM]
                                .rearrange("p (k b) -> p k b", b=128),
                            in_=self.w0_d[jo * DM:(jo + 1) * DM, :]
                                .rearrange("(k a) b -> a k b", a=128)), 16)
                    for o in range(KC):
                        if o >= 2:
                            g.wait_ge(pe.h, pW1 + 2 * (o - 1))
                        slot = o % 2
                        base_r = h2 * FF * 8 + o * 32 * 128
                        g_w1[o] = gL.inc(g.dma_start(
                            out=w1buf[:, slot * 32 * 128:(slot + 1) * 32 * 128]
                                .rearrange("p (j b) -> p j b", b=128),
                            in_=self.w1_d[base_r:base_r + 32 * 128, :]
                                .rearrange("(j a) b -> a j b", a=128)), 16)

                @blk.tensor
                def _(t_, h2=h2):
                    for jl in range(32):
                        t_.wait_ge(gL.h, g_w0[jl])
                        wslot = jl % 2
                        for nt in range(2):
                            idx = 2 * jl + nt
                            ps = pa if idx % 2 == 0 else pb
                            if idx >= 2:
                                t_.wait_ge(ac.h, a0 + idx - 1)
                            for k in range(KC):
                                mm = t_.matmul(
                                    ps[:, :],
                                    w0buf[:, wslot * DM + k * 128:
                                          wslot * DM + (k + 1) * 128],
                                    x2ln(k, nt * TILE, TILE),
                                    start=(k == 0), stop=(k == KC - 1))
                            pe.inc(mm)
                    t_.wait_ge(ac.h, a0 + 64)
                    for o in range(KC):
                        t_.wait_ge(gL.h, g_w1[o])
                        wslot = o % 2
                        for nt in range(2):
                            idx = 2 * o + nt
                            ps = pa if idx % 2 == 0 else pb
                            if idx >= 2:
                                if h2 == 0:
                                    t_.wait_ge(ac.h, a0 + 64 + idx - 1)
                                else:
                                    t_.wait_ge(dv.h, d0 + idx - 1)
                            for jl in range(32):
                                mm = t_.matmul(
                                    ps[:, :],
                                    w1buf[:, wslot * 32 * 128 + jl * 128:
                                          wslot * 32 * 128 + (jl + 1) * 128],
                                    mid[:, jl * TCORE + nt * TILE:
                                        jl * TCORE + (nt + 1) * TILE],
                                    start=(jl == 0), stop=(jl == 31))
                            pe.inc(mm)

                @blk.scalar
                def _(sc, h2=h2):
                    for jl in range(32):
                        jo = h2 * 32 + jl
                        for nt in range(2):
                            idx = 2 * jl + nt
                            ps = pa if idx % 2 == 0 else pb
                            sc.wait_ge(pe.h, p0 + idx + 1)
                            ac.inc(sc.activation(
                                mid[:, jl * TCORE + nt * TILE:
                                    jl * TCORE + (nt + 1) * TILE],
                                ps[:, :], AF.Gelu, bias=self.b0_sb[:, jo:jo + 1]))
                    if h2 == 0:
                        for o in range(KC):
                            for nt in range(2):
                                idx = 2 * o + nt
                                ps = pa if idx % 2 == 0 else pb
                                sc.wait_ge(pe.h, pW1 + idx + 1)
                                ac.inc(sc.activation(
                                    y_sb[:, o * TCORE + nt * TILE:
                                         o * TCORE + (nt + 1) * TILE],
                                    ps[:, :], AF.Copy))

                if h2 == 1:
                    @blk.vector
                    def _(v):
                        for o in range(KC):
                            for nt in range(2):
                                idx = 2 * o + nt
                                ps = pa if idx % 2 == 0 else pb
                                v.wait_ge(pe.h, pW1 + idx + 1)
                                ys = y_sb[:, o * TCORE + nt * TILE:
                                          o * TCORE + (nt + 1) * TILE]
                                dv.inc(v.tensor_tensor(ys, ys, ps[:, :], ALU.add))

        if not self.lvl("C"):
            return
        pe0, ac0, dv0 = pe.v, ac.v, dv.v
        bases = []
        sps, tps, vps = [], [], []
        for nt in range(2):
            base = dict(pe=pe0 + 17 * nt, ac=ac0 + 34 * nt, dv=dv0 + 3 * nt)
            bases.append(base)
            lnres2 = dict(ubs=ubs, uqs=uqs, pa=pa, pb=pb, pr0=pr0, pr1=pr1, t1s=t1d)
            sp, tp, vp = self.ln_f32_parts(
                lnres2,
                lambda k, t0=nt * TILE: y_sb[:, k * TCORE + t0:k * TCORE + t0 + TILE],
                TILE, base,
                bias_col=lambda k: self.b1_sb[:, k:k + 1])
            sps.append(sp)
            tps.append(tp)
            vps.append(vp)
        if self.debug == 2:
            with nc.Block("C3dbg") as blk:
                @blk.sync
                def _(sy):
                    v = sT.inc(sy.dma_start(
                        out=self.fmj(self.dbg["dbg_y"]),
                        in_=y_sb[:, :].rearrange("p (k t) -> p k t", t=TCORE)), 16)
                    sy.wait_ge(sT.h, v)

                @blk.gpsimd
                def _(g):
                    sT.inc(g.dma_start(
                        out=self.fmj(self.dbg["dbg_x2ln"]),
                        in_=self.xln_sb[:, :]
                            .rearrange("p (k t) -> p k t", t=TCORE)), 16)
                    sT.inc(g.dma_start(
                        out=self.dbg["dbg_mid"]
                            .rearrange("(j p) t -> p j t", p=128),
                        in_=mid[:, :].rearrange("p (j t) -> p j t", t=TCORE)), 16)
        g_acc = {}
        gL0, hL0, sT0 = gL.v, hL.v, sT.v
        with nc.Block("C3") as blk:
            @blk.gpsimd
            def _(g):
                for nt in range(2):
                    g.wait_ge(dv.h, dv0 + 3 * nt + 3)
                    t0 = nt * TILE
                    g_acc[nt] = gL.inc(g.dma_start(
                        out=y_sb[:, :].rearrange(
                            "p (k t) -> p k t", t=TCORE)[:, :, t0:t0 + TILE],
                        in_=self.fmj(self.x2_sc)[:, :, t0:t0 + TILE],
                        accum_op=ALU.add), 16)

            @blk.scalar
            def _(sc):
                sc.wait_ge(gL.h, gL0)
                sc.wait_ge(hL.h, hL0)
                sc.wait_ge(sT.h, sT0)
                sps[0](sc)
                sc.wait_ge(dv.h, dv0 + 3)
                sps[1](sc)

            @blk.tensor
            def _(t_):
                t_.wait_ge(gL.h, gL0)
                t_.wait_ge(hL.h, hL0)
                t_.wait_ge(sT.h, sT0)
                tps[0](t_)
                t_.wait_ge(dv.h, dv0 + 3)
                tps[1](t_)

            @blk.vector
            def _(v):
                dv.inc(vps[0](v))
                dv.inc(vps[1](v))

            @blk.sync
            def _(sy):
                for nt in range(2):
                    t0 = nt * TILE
                    sy.wait_ge(gL.h, g_acc[nt])
                    sT.inc(sy.dma_start(
                        out=self.fmj(self.out_d)[:, :, t0:t0 + TILE],
                        in_=y_sb[:, :].rearrange(
                            "p (k t) -> p k t", t=TCORE)[:, :, t0:t0 + TILE]), 16)
        assert pe.v == pe0 + 34 and ac.v == ac0 + 68 and dv.v == dv0 + 6

    def tail_block(self):
        nc = self.nc
        with nc.Block("tail") as blk:
            @blk.sync
            def _(sy):
                sy.wait_ge(self.sT.h, self.sT.v)
                if self.cc.v:
                    sy.wait_ge(self.cc.h, self.cc.v)

    LEVELS = ["S", "A1", "A2", "A", "B1", "Bkv", "B2", "B3", "B", "C1", "C2", "C"]

    def lvl(self, name):
        return self.LEVELS.index(self.upto) >= self.LEVELS.index(name)

    def build(self):
        self.alloc_a()
        self.setup_block()
        if self.lvl("A1"):
            self.phase_a()
        if self.lvl("B1"):
            self.phase_b()
        if self.lvl("C1"):
            self.phase_c()
        self.tail_block()
        return self.nc


_CACHE = {}


def _blocked_lhsT(wT, n_out, n_k=KC):
    return np.ascontiguousarray(
        wT.reshape(n_k, 128, n_out, 128).transpose(2, 0, 1, 3).reshape(-1, 128))


def _prep(inputs):
    x = np.ascontiguousarray(np.asarray(inputs["L0_attn_x"], np.float32))
    Wqkv = np.asarray(inputs["L0_attn_Wqkv"], np.float32).reshape(H, 3, D, DM)
    Wq = Wqkv[:, 0].reshape(H * D, DM)
    Wk = Wqkv[:, 1].reshape(H * D, DM)
    Wv = Wqkv[:, 2].reshape(H * D, DM)
    Wg = np.asarray(inputs["L0_attn_Wg"], np.float32)
    Wo = np.asarray(inputs["L0_attn_Wo"], np.float32)
    W0 = np.asarray(inputs["L0_ffn_W0"], np.float32)
    W1 = np.asarray(inputs["L0_ffn_W1"], np.float32)

    d = {}
    d["wq"] = _blocked_lhsT(np.ascontiguousarray(Wq.T), KC)
    d["wk"] = _blocked_lhsT(np.ascontiguousarray(Wk.T), KC)
    d["wv"] = np.ascontiguousarray(Wv.T)
    d["wg"] = _blocked_lhsT(np.ascontiguousarray(Wg.T), KC)
    d["wo"] = _blocked_lhsT(np.ascontiguousarray(Wo.T), KC)
    d["w0"] = _blocked_lhsT(np.ascontiguousarray(W0.T), NJ)
    W1T = np.ascontiguousarray(W1.T).reshape(2, 32, 128, KC, 128)
    d["w1"] = np.ascontiguousarray(W1T.transpose(0, 3, 1, 2, 4).reshape(-1, 128))

    slope = np.asarray(inputs["L0_attn_slope_rate"], np.float32).reshape(1, H)
    d["slope"] = np.ascontiguousarray(np.broadcast_to(slope, (128, H)))
    arr = np.arange(1, CH + 1, dtype=np.float32)
    d["arrb"] = np.ascontiguousarray(np.broadcast_to(arr[None, :], (128, CH)))
    p = np.arange(128, dtype=np.float32)
    d["arrm"] = np.ascontiguousarray(
        np.stack([p + 1 - CH, p + 129 - CH], axis=1))
    mm, nn = np.meshgrid(np.arange(CH, dtype=np.float32),
                         np.arange(CH, dtype=np.float32))
    diff = mm - nn
    d["idxT"] = np.maximum(diff, 0.0).astype(np.float32)
    d["maskT"] = (diff >= 0).astype(np.float32)
    d["ident"] = np.eye(128, dtype=np.float32)
    d["onesr"] = np.ones((1, 128), np.float32)
    d["onesc"] = np.ones((128, 1), np.float32)
    d["b0c"] = np.ascontiguousarray(
        np.asarray(inputs["L0_ffn_bias0"], np.float32).reshape(NJ, 128).T)
    d["b1c"] = np.ascontiguousarray(
        np.asarray(inputs["L0_ffn_bias1"], np.float32).reshape(KC, 128).T)
    d["nwc"] = np.ascontiguousarray(
        np.asarray(inputs["L0_attn_norm_w"], np.float32).reshape(KC, 128).T)
    return x, d


def kernel(**inputs):
    key = (DEBUG, UPTO)
    if key not in _CACHE:
        _CACHE[key] = Builder(debug=DEBUG, upto=UPTO).build()
    nc = _CACHE[key]

    x, shared = _prep(inputs)
    in_maps = []
    for core in range(NCORES):
        b, qq = core // 4, core % 4
        m = dict(shared)
        m["x"] = np.ascontiguousarray(x[b, qq * TCORE:(qq + 1) * TCORE, :].T)
        sel = np.zeros((128, NCORES * 4), np.float32)
        for pidx in range(qq):
            r = b * 4 + pidx
            sel[:, r * 4 + (qq - 1 - pidx)] = 1.0
        m["sel"] = sel
        in_maps.append(m)

    res = run_bass_kernel_spmd(nc, in_maps, core_ids=list(range(NCORES)))
    out = np.empty((2, 4096, DM), np.float32)
    for core in range(NCORES):
        b, qq = core // 4, core % 4
        out[b, qq * TCORE:(qq + 1) * TCORE, :] = res.results[core]["out"].T
    return out
